# version 01
# baseline (speedup 1.0000x reference)
import contextlib
from collections import defaultdict

import numpy as np
import ml_dtypes
import concourse.bass as bass
import concourse.mybir as mybir
from concourse.alu_op_type import AluOpType as OP
from concourse.bass_utils import run_bass_kernel_spmd

F32 = mybir.dt.float32
BF16 = mybir.dt.bfloat16
I32 = mybir.dt.int32
AF = mybir.ActivationFunctionType
AX = mybir.AxisListType.X

B, L, C = 4, 4096, 1024
TH = L // 2
NC8 = 8
H, D = 16, 64
bf16 = ml_dtypes.bfloat16

_builder_cache = {}


def f32c(x):
    return np.ascontiguousarray(np.asarray(x), dtype=np.float32)


ENGS = ("t", "v", "s", "g", "y")
SEMS = ("t", "v", "s", "g", "gd", "y")


class Sched:

    def __init__(self, nc):
        self.nc = nc
        self.q = defaultdict(list)
        self.cnt = dict.fromkeys(SEMS, 0)
        self.sem = {}

    def raw(self, e, fn):
        return self.inc(e, fn, 1)

    def inc(self, e, fn, k=1, selfwait=True, semkey=None):
        semkey = semkey or e
        self.q[e].append(("op", fn, k, selfwait, semkey))
        self.cnt[semkey] += k
        return self.cnt[semkey]

    def dma(self, e, out, in_, **kw):
        semkey = "gd" if e == "g" else e
        return self.inc(e, lambda eng: eng.dma_start(out, in_, **kw), 16,
                        selfwait=True, semkey=semkey)

    def wait(self, e, other, count):
        if count is None or count <= 0:
            return
        self.q[e].append(("w", other, count))

    def barrier(self):
        counts = dict(self.cnt)
        for e in ENGS:
            for o in SEMS:
                if o != e and not (e == "g" and o == "gd") and counts[o] > 0:
                    self.wait(e, o, counts[o])

    def _replay(self, e, eng):
        run = dict.fromkeys(SEMS, 0)
        last_wait = {}
        for ent in self.q[e]:
            if ent[0] == "w":
                _, o, c = ent
                if last_wait.get(o, -1) < c:
                    eng.wait_ge(self.sem[o], c)
                    last_wait[o] = c
            else:
                _, fn, k, sw, semkey = ent
                if sw and e != "t":
                    keys = (semkey, "g") if e == "g" else (semkey,)
                    for sk in keys:
                        if run[sk] > 0 and last_wait.get(sk, -1) < run[sk]:
                            eng.wait_ge(self.sem[sk], run[sk])
                            last_wait[sk] = run[sk]
                fn(eng).then_inc(self.sem[semkey], k)
                run[semkey] += k

    def finalize(self, es):
        if self.cnt["y"] > 0:
            self.wait("y", "y", self.cnt["y"])
        if self.cnt["gd"] > 0:
            self.wait("g", "gd", self.cnt["gd"])
        for e in SEMS:
            self.sem[e] = es.enter_context(self.nc.semaphore("sem_" + e))
        block = es.enter_context(self.nc.Block())

        @block.tensor
        def _(eng):
            self._replay("t", eng)

        @block.vector
        def _(eng):
            self._replay("v", eng)

        @block.scalar
        def _(eng):
            self._replay("s", eng)

        @block.gpsimd
        def _(eng):
            self._replay("g", eng)

        @block.sync
        def _(eng):
            self._replay("y", eng)


def mm_accum(sc, ps_ap, pairs):
    n = len(pairs)
    for i, (lt, rh) in enumerate(pairs):
        if i == n - 1:
            c = sc.inc("t", lambda e, _l=lt, _r=rh, _st=(i == 0):
                       e.matmul(ps_ap, _l, _r, start=_st, stop=True))
        else:
            sc.raw("t", lambda e, _l=lt, _r=rh, _st=(i == 0):
                   e.matmul(ps_ap, _l, _r, start=_st, stop=False))
    return c


def build_L0():
    nc = bass.Bass("TRN2", target_bir_lowering=False)
    _uid = [0]

    def _nm(n):
        _uid[0] += 1
        return f"{n}_u{_uid[0]}"

    def SBUF(n, shp, dt, side=None):
        return nc.sbuf_tensor(_nm(n), shp, dt, side=side)

    def SBUFR(n, shp, dt):
        return nc.sbuf_tensor(_nm(n), shp, dt, side="right")

    def PSUM(n, shp, dt):
        return nc.psum_tensor(_nm(n), shp, dt)
    x0T = nc.dram_tensor("x0T", [C, L], F32, kind="ExternalInput").ap()
    qoffd = nc.dram_tensor("qoff", [1, 1], F32, kind="ExternalInput").ap()
    wqT = nc.dram_tensor("wqT", [C, C], F32, kind="ExternalInput").ap()
    wkT = nc.dram_tensor("wkT", [C, C], F32, kind="ExternalInput").ap()
    wvT = nc.dram_tensor("wvT", [C, C], F32, kind="ExternalInput").ap()
    woT = nc.dram_tensor("woT", [C, C], F32, kind="ExternalInput").ap()
    bo_d = nc.dram_tensor("bo2", [128, 8], F32, kind="ExternalInput").ap()
    wfT = nc.dram_tensor("wfT", [C, C], F32, kind="ExternalInput").ap()
    x1T = nc.dram_tensor("x1T", [C, TH], F32, kind="ExternalOutput").ap()

    k_scr = nc.dram_tensor("k_scr", [H, 64, L], BF16)
    q_scr = nc.dram_tensor("q_scr", [H, 64, TH], BF16)
    g_scr = nc.dram_tensor("g_scr", [C, TH], BF16)
    kq_scr = nc.dram_tensor("kq_scr", [H, 2], F32)

    slopes = [0.7071067811865476 ** (h + 1) for h in range(H)]
    DW = 6272

    sc = Sched(nc)
    es = contextlib.ExitStack()
    with es:
        sb = lambda n, shp, dt: es.enter_context(SBUF(n, shp, dt))
        iot = sb("iot", [128, 1], F32)
        pcol0 = sb("pcol0", [128, 1], F32)
        pcol1 = sb("pcol1", [128, 1], F32)
        qcol = sb("qcol", [128, 1], F32)
        ones_cf = sb("ones_cf", [128, 1], F32)
        ones_cb = sb("ones_cb", [128, 1], BF16)
        ones64 = sb("ones64", [1, 64], F32)
        onesb_r = sb("onesb_r", [1, 128], BF16)
        hsel = sb("hsel", [128, 2], F32)
        negb = [sb(f"negb{h}", [128, 1], F32) for h in range(H)]

        eps1 = sb("eps1", [1, 1], F32)
        iota_i = sb("iota_i", [128, 1], I32)
        sc.raw("g", lambda e: e.iota(iota_i[:], pattern=[[0, 1]], base=0,
                                     channel_multiplier=1))
        sc.raw("g", lambda e: e.memset(eps1[:], 1e-5))
        sc.raw("g", lambda e: e.memset(ones_cf[:], 1.0))
        sc.raw("g", lambda e: e.memset(ones_cb[:], 1.0))
        sc.raw("g", lambda e: e.memset(ones64[:], 1.0))
        sc.raw("g", lambda e: e.memset(onesb_r[:], 1.0))
        sc.raw("g", lambda e: e.memset(hsel[0:64, 0:1], 1.0))
        sc.raw("g", lambda e: e.memset(hsel[64:128, 0:1], 0.0))
        sc.raw("g", lambda e: e.memset(hsel[0:64, 1:2], 0.0))
        g_const = sc.inc("g", lambda e: e.memset(hsel[64:128, 1:2], 1.0))
        sc.wait("v", "g", g_const)
        v_iot = sc.inc("v", lambda e: e.tensor_copy(iot[:], iota_i[:]))
        qld = sc.dma("y", qcol[0:1, :], qoffd)

        with PSUM("ps_q", [128, 1], F32) as ps_q:
            qrow_b = sb("qrow_b", [1, 1], BF16)
            sc.wait("v", "y", qld)
            v_q = sc.inc("v", lambda e: e.tensor_copy(qrow_b[:], qcol[0:1, :]))
            sc.wait("t", "v", v_q)
            sc.wait("t", "g", g_const)
            t_q = sc.inc("t", lambda e: e.matmul(ps_q[:], onesb_r[:],
                                                 qrow_b[:], start=True,
                                                 stop=True))
            sc.wait("v", "t", t_q)
            sc.inc("v", lambda e: e.tensor_copy(qcol[:], ps_q[:]))
        sc.inc("v", lambda e: e.tensor_scalar(pcol0[:], iot[:], 3968.0, None,
                                              OP.add))
        sc.inc("v", lambda e: e.tensor_scalar(pcol1[:], qcol[:], -2.0,
                                              3968.0 + 2048.0, OP.mult, OP.add))
        v_pc = sc.inc("v", lambda e: e.tensor_tensor(pcol1[:], pcol1[:],
                                                     iot[:], OP.add))
        t_psq_done = sc.cnt["t"]

        stA = contextlib.ExitStack()
        sa = lambda n, shp, dt: stA.enter_context(SBUFR(n, shp, dt))
        xn = [sa(f"xn{i}", [128, L], BF16) for i in range(8)]

        with contextlib.ExitStack() as ph:
            pb = lambda n, shp, dt: ph.enter_context(SBUFR(n, shp, dt))
            x0b = [pb(f"x0b{i}", [128, 2048], F32) for i in range(8)]
            sqb = pb("sqb", [128, 2048], F32)
            srow = pb("srow", [1, 2048], F32)
            s2row = pb("s2row", [1, 2048], F32)
            rrow = pb("rrow", [1, 2048], BF16)
            crow = pb("crow", [1, 2048], BF16)
            mrow = pb("mrow", [1, 2048], F32)
            vrow = pb("vrow", [1, 2048], F32)
            m2row = pb("m2row", [1, 2048], F32)
            tmpx = pb("tmpx", [128, 1024], F32)
            v_before = sc.cnt["v"]
            for half in range(2):
                t0 = half * 2048
                with contextlib.ExitStack() as pp:
                    ps_s = pp.enter_context(PSUM("ps_s", [1, 2048], F32))
                    ps_s2 = pp.enter_context(PSUM("ps_s2", [1, 2048], F32))
                    lds = []
                    for i in range(8):
                        lds.append(sc.dma("y", x0b[i][:],
                                          x0T[i * 128:(i + 1) * 128,
                                              t0:t0 + 2048]))
                    sc.wait("t", "v", v_pc)
                    prev_t = None
                    for i in range(8):
                        sc.wait("s", "y", lds[i])
                        if prev_t:
                            sc.wait("s", "t", prev_t)
                        s_sq = sc.inc("s", lambda e, _i=i: e.activation(
                            sqb[:], x0b[_i][:], AF.Square))
                        sc.wait("t", "s", s_sq)
                        sc.wait("t", "y", lds[i])
                        for nn in range(4):
                            sl = slice(nn * 512, nn * 512 + 512)
                            sc.raw("t", lambda e, _i=i, _sl=sl: e.matmul(
                                ps_s[:, _sl], ones_cf[:], x0b[_i][:, _sl],
                                start=(_i == 0), stop=(_i == 7)))
                            if nn < 3:
                                sc.raw("t", lambda e, _i=i, _sl=sl: e.matmul(
                                    ps_s2[:, _sl], ones_cf[:], sqb[:, _sl],
                                    start=(_i == 0), stop=(_i == 7)))
                            else:
                                prev_t = sc.inc(
                                    "t", lambda e, _i=i, _sl=sl: e.matmul(
                                        ps_s2[:, _sl], ones_cf[:], sqb[:, _sl],
                                        start=(_i == 0), stop=(_i == 7)))
                    sc.wait("v", "t", prev_t)
                    sc.inc("v", lambda e: e.tensor_copy(srow[:], ps_s[:]))
                    v_st = sc.inc("v", lambda e: e.tensor_copy(
                        s2row[:], ps_s2[:]))
                sl2 = slice(0, 2048)
                sc.inc("v", lambda e, _s=sl2: e.tensor_scalar(
                    mrow[0:1, _s], srow[0:1, _s], 1.0 / C, None, OP.mult))
                sc.inc("v", lambda e, _s=sl2: e.tensor_scalar(
                    vrow[0:1, _s], s2row[0:1, _s], 1.0 / C, None, OP.mult))
                sc.inc("v", lambda e, _s=sl2: e.tensor_tensor(
                    m2row[:], mrow[0:1, _s], mrow[0:1, _s], OP.mult))
                v_var = sc.inc("v", lambda e, _s=sl2: e.tensor_tensor(
                    vrow[0:1, _s], vrow[0:1, _s], m2row[:], OP.subtract))
                sc.wait("s", "v", v_var)
                sc.inc("s", lambda e, _s=sl2: e.activation(
                    vrow[0:1, _s], vrow[0:1, _s], AF.Ln, bias=eps1[:]))
                s_r = sc.inc("s", lambda e, _s=sl2: e.activation(
                    rrow[0:1, _s], vrow[0:1, _s], AF.Exp, scale=-0.5))
                sc.wait("v", "s", s_r)
                sc.inc("v", lambda e, _s=sl2: e.tensor_tensor(
                    m2row[:], mrow[0:1, _s], rrow[0:1, _s], OP.mult))
                v_cr = sc.inc("v", lambda e, _s=sl2: e.tensor_scalar(
                    crow[0:1, _s], m2row[:], -1.0, None, OP.mult))
                for quar in range(2):
                    q0 = t0 + quar * 1024
                    with contextlib.ExitStack() as pp:
                        ps_a = pp.enter_context(
                            PSUM("ps_a", [128, 1024], F32))
                        ps_c = pp.enter_context(
                            PSUM("ps_c", [128, 1024], F32))
                        sc.wait("t", "v", v_cr)
                        t_bc = None
                        for nn in range(2):
                            sl = slice(nn * 512, nn * 512 + 512)
                            sg = slice(q0 - t0 + nn * 512,
                                       q0 - t0 + nn * 512 + 512)
                            sc.raw("t", lambda e, _sl=sl, _sg=sg: e.matmul(
                                ps_a[:, _sl], onesb_r[:], rrow[0:1, _sg],
                                start=True, stop=True))
                            t_bc = sc.inc("t", lambda e, _sl=sl, _sg=sg:
                                          e.matmul(ps_c[:, _sl], onesb_r[:],
                                                   crow[0:1, _sg],
                                                   start=True, stop=True))
                        sc.wait("v", "t", t_bc)
                        v_norm = None
                        for i in range(8):
                            qq = slice(quar * 1024, quar * 1024 + 1024)
                            sg = slice(q0, q0 + 1024)
                            sc.inc("v", lambda e, _i=i, _q=qq: e.tensor_tensor(
                                tmpx[:], x0b[_i][:, _q], ps_a[:], OP.mult))
                            v_norm = sc.inc(
                                "v", lambda e, _i=i, _g=sg: e.tensor_tensor(
                                    xn[_i][:, _g], tmpx[:], ps_c[:], OP.add))
                        sc.wait("t", "v", v_norm)
                if half == 0:
                    sc.wait("y", "v", sc.cnt["v"])
        sc.barrier()

        stV = contextlib.ExitStack()
        v_t = stV.enter_context(SBUF("v_t", [128, 32 * H * 65], BF16))

        with contextlib.ExitStack() as ph:
            pb = lambda n, shp, dt: ph.enter_context(SBUFR(n, shp, dt))
            wbuf = [pb(f"wbuf{i}", [128, C], BF16) for i in range(8)]
            kpair = pb("kpair", [128, L], BF16)
            qpair = pb("qpair", [128, TH], BF16)
            sqq = pb("sqq", [128, 2048], F32)
            nmax = pb("nmax", [2, 1], F32)
            nmax2 = pb("nmax2", [2, 2], F32)

            def load_w(dram):
                cs = [sc.dma("g", wbuf[i][:], dram[i * 128:(i + 1) * 128, :])
                      for i in range(8)]
                return cs[-1]

            lw = load_w(wvT)
            sc.wait("v", "gd", lw)
            v_ones = sc.inc("v", lambda e: e.memset(v_t[:], 1.0))
            sc.wait("t", "gd", lw)
            sc.wait("t", "v", v_ones)
            with contextlib.ExitStack() as pp:
                psv = [pp.enter_context(PSUM(f"psv{j}", [128, 512],
                                                       F32)) for j in range(4)]
                ev = {}
                for tc in range(32):
                    for nv in range(2):
                        j = (tc * 2 + nv) % 4
                        sc.wait("t", "v", ev.get(j))
                        mm = mm_accum(sc, psv[j][:],
                                      [(xn[cc][:, tc * 128:(tc + 1) * 128],
                                        wbuf[cc][:, nv * 512:(nv + 1) * 512])
                                       for cc in range(8)])
                        sc.wait("v", "t", mm)
                        base = tc * (H * 65) + nv * 8 * 65
                        out_ap = v_t[:, base:base + 8 * 65].rearrange(
                            "p (h d) -> p h d", h=8)[:, :, 0:64]
                        in_ap = psv[j][:].rearrange("p (h d) -> p h d", h=8)
                        ev[j] = sc.inc("v", lambda e, _o=out_ap, _i=in_ap:
                                       e.tensor_copy(_o, _i))
                sc.wait("t", "v", sc.cnt["v"])

            sc.wait("g", "t", sc.cnt["t"])
            lw = load_w(wqT)
            sc.wait("t", "gd", lw)
            with contextlib.ExitStack() as pp:
                psq = [pp.enter_context(PSUM(f"psq{j}", [128, 512],
                                                       F32)) for j in range(4)]
                psn = pp.enter_context(PSUM("psn", [2, 2048], F32))
                prev_v = None
                prev_store = None
                prev_nrm = None
                prev_ssq = None
                prev_nstore = None
                for hp in range(8):
                    sc.wait("t", "v", prev_v)
                    sc.wait("v", "s", prev_ssq)
                    vq = None
                    for nq in range(4):
                        j = nq
                        mm = mm_accum(sc, psq[j][:],
                                      [(wbuf[cc][:, hp * 128:(hp + 1) * 128],
                                        xn[cc][:, nq * 512:(nq + 1) * 512])
                                       for cc in range(8)])
                        sc.wait("v", "t", mm)
                        sc.wait("v", "y", prev_store)
                        sl = slice(nq * 512, (nq + 1) * 512)
                        vq = sc.inc("v", lambda e, _j=j, _sl=sl: e.tensor_scalar(
                            qpair[:, _sl], psq[_j][:], 0.125, None, OP.mult))
                    prev_v = vq
                    sc.wait("y", "v", vq)
                    st1 = sc.dma("y", q_scr.ap()[2 * hp, :, :], qpair[0:64, :])
                    prev_store = sc.dma("y", q_scr.ap()[2 * hp + 1, :, :],
                                        qpair[64:128, :])
                    sc.wait("s", "v", vq)
                    sc.wait("s", "t", prev_nrm)
                    prev_ssq = s_sq = sc.inc("s", lambda e: e.activation(
                        sqq[:, 0:2048], qpair[:, 0:2048], AF.Square))
                    sc.wait("t", "s", s_sq)
                    for nn in range(4):
                        sl = slice(nn * 512, nn * 512 + 512)
                        if nn < 3:
                            sc.raw("t", lambda e, _sl=sl: e.matmul(
                                psn[:, _sl], hsel[:], sqq[:, _sl],
                                start=True, stop=True))
                        else:
                            prev_nrm = sc.inc("t", lambda e, _sl=sl: e.matmul(
                                psn[:, _sl], hsel[:], sqq[:, _sl],
                                start=True, stop=True))
                    sc.wait("v", "t", prev_nrm)
                    sc.wait("v", "y", prev_nstore)
                    v_mx = sc.inc("v", lambda e: e.reduce_max(
                        nmax[0:2, 0:1], psn[:, 0:2048], axis=AX))
                    sc.wait("y", "v", v_mx)
                    prev_nstore = sc.dma(
                        "y", kq_scr.ap()[2 * hp:2 * hp + 2, 0:1],
                        nmax[0:2, 0:1], allow_slow_non_contiguous=True)

            sc.wait("g", "t", sc.cnt["t"])
            lw = load_w(wkT)
            sc.wait("t", "gd", lw)
            with contextlib.ExitStack() as pp:
                psk = [pp.enter_context(PSUM(f"psk{j}", [128, 512],
                                                       F32)) for j in range(4)]
                psn = pp.enter_context(PSUM("psn2", [2, 2048], F32))
                prev_v = None
                prev_store = None
                prev_nrm = None
                prev_ssq = None
                for hp in range(8):
                    sc.wait("t", "v", prev_v)
                    sc.wait("v", "s", prev_ssq)
                    ev = {}
                    vk = None
                    for nk in range(8):
                        j = nk % 4
                        sc.wait("t", "v", ev.get(j))
                        mm = mm_accum(sc, psk[j][:],
                                      [(wbuf[cc][:, hp * 128:(hp + 1) * 128],
                                        xn[cc][:, nk * 512:(nk + 1) * 512])
                                       for cc in range(8)])
                        sc.wait("v", "t", mm)
                        sc.wait("v", "y", prev_store)
                        sl = slice(nk * 512, (nk + 1) * 512)
                        vk = ev[j] = sc.inc("v", lambda e, _j=j, _sl=sl:
                                            e.tensor_copy(kpair[:, _sl],
                                                          psk[_j][:]))
                    prev_v = vk
                    sc.wait("y", "v", vk)
                    st1 = sc.dma("y", k_scr.ap()[2 * hp, :, :], kpair[0:64, :])
                    prev_store = sc.dma("y", k_scr.ap()[2 * hp + 1, :, :],
                                        kpair[64:128, :])
                    t_lh = prev_nrm
                    ss = None
                    v_mx2 = None
                    for lh in range(2):
                        sc.wait("s", "v", vk)
                        sc.wait("s", "t", t_lh)
                        prev_ssq = ss = sc.inc(
                            "s", lambda e, _l=lh: e.activation(
                                sqq[:], kpair[:, _l * 2048:(_l + 1) * 2048],
                                AF.Square))
                        sc.wait("t", "s", ss)
                        sc.wait("t", "v", v_mx2)
                        for nn in range(4):
                            sl = slice(nn * 512, nn * 512 + 512)
                            if nn == 3:
                                t_lh = sc.inc(
                                    "t", lambda e, _sl=sl: e.matmul(
                                        psn[:, _sl], hsel[:], sqq[:, _sl],
                                        start=True, stop=True))
                            else:
                                sc.raw("t", lambda e, _sl=sl: e.matmul(
                                    psn[:, _sl], hsel[:], sqq[:, _sl],
                                    start=True, stop=True))
                        sc.wait("v", "t", t_lh)
                        if lh == 0:
                            sc.wait("v", "y", prev_nstore)
                        v_mx2 = sc.inc("v", lambda e, _l=lh: e.reduce_max(
                            nmax2[0:2, _l:_l + 1], psn[:, 0:2048], axis=AX))
                    prev_nrm = t_lh
                    v_mx = sc.inc("v", lambda e: e.reduce_max(
                        nmax[0:2, 0:1], nmax2[0:2, 0:2], axis=AX))
                    sc.wait("y", "v", v_mx)
                    prev_nstore = sc.dma(
                        "y", kq_scr.ap()[2 * hp:2 * hp + 2, 1:2],
                        nmax[0:2, 0:1], allow_slow_non_contiguous=True)

            qn2 = pb("qn2", [1, H], F32)
            kn2 = pb("kn2", [1, H], F32)
            btv = pb("btv", [1, H], BF16)
            ldn1 = sc.dma("y", qn2[:],
                          kq_scr.ap().rearrange("h c -> c h")[0:1, :],
                          allow_slow_non_contiguous=True)
            ldn2 = sc.dma("y", kn2[:],
                          kq_scr.ap().rearrange("h c -> c h")[1:2, :],
                          allow_slow_non_contiguous=True)
            sc.wait("s", "y", ldn2)
            sc.inc("s", lambda e: e.activation(qn2[:], qn2[:], AF.Sqrt))
            s_n = sc.inc("s", lambda e: e.activation(kn2[:], kn2[:], AF.Sqrt))
            sc.wait("v", "s", s_n)
            v_bt = sc.inc("v", lambda e: e.tensor_tensor(btv[:], qn2[:],
                                                         kn2[:], OP.mult))
            with PSUM("ps_bt", [128, H], F32) as ps_bt:
                sc.wait("t", "v", v_bt)
                t_bt = sc.inc("t", lambda e: e.matmul(ps_bt[:], onesb_r[:],
                                                      btv[:], start=True,
                                                      stop=True))
                sc.wait("s", "t", t_bt)
                for h in range(H):
                    sc.inc("s", lambda e, _h=h: e.activation(
                        negb[_h][:], ps_bt[:, _h:_h + 1], AF.Copy, scale=-1.0))
        sc.barrier()

        with contextlib.ExitStack() as ph:
            pb = lambda n, shp, dt: ph.enter_context(SBUFR(n, shp, dt))
            wf_f = [pb(f"wf_f{i}", [128, C], F32) for i in range(2)]
            wf_s = [pb(f"wf_s{i}", [128, C], BF16) for i in range(8)]
            gtile = pb("gtile", [128, TH], BF16)
            sgn = {}
            for i in range(8):
                if i >= 2:
                    sc.wait("y", "s", sgn[i - 2])
                ld = sc.dma("y", wf_f[i % 2][:], wfT[i * 128:(i + 1) * 128, :])
                sc.wait("s", "y", ld)
                sgn[i] = sc.inc("s", lambda e, _i=i: e.activation(
                    wf_s[_i][:], wf_f[_i % 2][:], AF.Sign))
            sc.wait("t", "s", sgn[7])
            with contextlib.ExitStack() as pp:
                psg = [pp.enter_context(PSUM(f"psg{j}", [128, 512],
                                                       F32)) for j in range(4)]
                ev = {}
                prev_store = None
                for mo in range(8):
                    sg = None
                    for nt in range(4):
                        j = nt
                        sc.wait("t", "s", ev.get(j))
                        mm = mm_accum(sc, psg[j][:],
                                      [(wf_s[cc][:, mo * 128:(mo + 1) * 128],
                                        xn[cc][:, nt * 512:(nt + 1) * 512])
                                       for cc in range(8)])
                        sc.wait("s", "t", mm)
                        sc.wait("s", "y", prev_store)
                        sl = slice(nt * 512, (nt + 1) * 512)
                        sg = ev[j] = sc.inc("s", lambda e, _j=j, _sl=sl:
                                            e.activation(gtile[:, _sl],
                                                         psg[_j][:], AF.Silu))
                    sc.wait("y", "s", sg)
                    prev_store = sc.dma(
                        "y", g_scr.ap()[mo * 128:(mo + 1) * 128, :], gtile[:])
        sc.barrier()
        stA.close()

        stOC = contextlib.ExitStack()
        ocat = [stOC.enter_context(SBUFR(f"ocat{i}", [128, TH], BF16))
                for i in range(8)]
        stB = contextlib.ExitStack()
        pb = lambda n, shp, dt: stB.enter_context(SBUFR(n, shp, dt))
        dwide = pb("dwide", [128, DW], BF16)
        dwide_o = pb("dwide_o", [128, DW], BF16)
        with contextlib.ExitStack() as pd:
            iotw_i = pd.enter_context(SBUFR("iotw_i", [128, DW], I32))
            iotw = pd.enter_context(SBUFR("iotw", [128, DW], F32))
            dtmp = pd.enter_context(SBUFR("dtmp", [128, DW], F32))
            g_iw = sc.inc("g", lambda e: e.iota(iotw_i[:], pattern=[[1, DW]],
                                                base=0, channel_multiplier=0))
            sc.wait("v", "g", g_iw)
            sc.inc("v", lambda e: e.tensor_copy(iotw[:], iotw_i[:]))
            v_d0 = sc.inc("v", lambda e: e.tensor_scalar(
                dtmp[:], iotw[:], pcol0[:], None, OP.subtract))
            sc.wait("s", "v", v_d0)
            s_ab0 = sc.inc("s", lambda e: e.activation(dwide[:], dtmp[:],
                                                       AF.Abs))
            sc.wait("v", "s", s_ab0)
            v_d1 = sc.inc("v", lambda e: e.tensor_scalar(
                dtmp[:], iotw[:], pcol1[:], None, OP.subtract))
            sc.wait("s", "v", v_d1)
            sc.inc("s", lambda e: e.activation(dwide_o[:], dtmp[:], AF.Abs))
        sc.barrier()

        kh = [pb(f"kh{j}", [64, L], BF16) for j in range(2)]
        qh = [pb(f"qh{j}", [64, TH], BF16) for j in range(2)]
        tb = [pb(f"tb{j}", [128, 1024], BF16) for j in range(2)]
        pt = [pb(f"pt{j}", [128, 1024], BF16) for j in range(2)]
        otmp = pb("otmp", [64, 1024], F32)
        rrec = pb("rrec", [1, 1024], F32)
        with contextlib.ExitStack() as pp:
            ps_sa = [pp.enter_context(PSUM(f"ps_sa{j}", [128, 1024],
                                                     F32)) for j in range(2)]
            ps_o = pp.enter_context(PSUM("ps_o", [65, 1024], F32))
            ps_rb = pp.enter_context(PSUM("ps_rb", [64, 1024], F32))
            kld = {0: None}
            kld[0] = (sc.dma("y", kh[0][:], k_scr.ap()[0, :, :]),
                      sc.dma("y", qh[0][:], q_scr.ap()[0, :, :]))
            t_scores_end = {}
            last_stt = defaultdict(lambda: None)
            last_exp = defaultdict(lambda: None)
            last_pv = defaultdict(lambda: None)
            last_o_use = None
            last_rb = None
            for h in range(H):
                if h + 1 < H:
                    sc.wait("y", "t", t_scores_end.get(h - 1))
                    kld[h + 1] = (sc.dma("y", kh[(h + 1) % 2][:],
                                         k_scr.ap()[h + 1, :, :]),
                                  sc.dma("y", qh[(h + 1) % 2][:],
                                         q_scr.ap()[h + 1, :, :]))
                sc.wait("t", "y", kld[h][0])
                sc.wait("t", "y", kld[h][1])
                khh = kh[h % 2]
                qhh = qh[h % 2]
                for qb in range(2):
                    qsl = slice(qb * 1024, (qb + 1) * 1024)
                    for scn in range(32):
                        par = scn % 2
                        sc.wait("t", "v", last_stt[par])
                        mm = None
                        for nn in range(2):
                            sl = slice(nn * 512, nn * 512 + 512)
                            qg = slice(qb * 1024 + nn * 512,
                                       qb * 1024 + nn * 512 + 512)
                            fn = lambda e, _s=sl, _q=qg, _p=par, _sc=scn, \
                                _k=khh, _qh=qhh: e.matmul(
                                    ps_sa[_p][:, _s],
                                    _k[:, _sc * 128:(_sc + 1) * 128],
                                    _qh[:, _q], start=True, stop=True)
                            if nn == 1:
                                mm = sc.inc("t", fn)
                            else:
                                sc.raw("t", fn)
                        sc.wait("v", "t", mm)
                        sc.wait("v", "s", last_exp[par])
                        dw = dwide if scn < 16 else dwide_o
                        scp = scn if scn < 16 else scn - 16
                        off = 3968 + 1024 * qb - 128 * scp
                        last_stt[par] = sc.inc(
                            "v", lambda e, _p=par, _d=dw, _o=off, _h=h:
                            e.scalar_tensor_tensor(
                                tb[_p][:], _d[:, _o:_o + 1024],
                                float(-slopes[_h]), ps_sa[_p][:],
                                OP.mult, OP.add))
                        sc.wait("s", "v", last_stt[par])
                        sc.wait("s", "t", last_pv[par])
                        last_exp[par] = sc.inc(
                            "s", lambda e, _p=par, _h=h: e.activation(
                                pt[_p][:], tb[_p][:], AF.Exp,
                                bias=negb[_h][:]))
                        sc.wait("t", "s", last_exp[par])
                        if scn == 0:
                            sc.wait("t", "v", last_o_use)
                        vbase = scn * (H * 65) + h * 65
                        for nn in range(2):
                            sl = slice(nn * 512, nn * 512 + 512)
                            fn = lambda e, _p=par, _vb=vbase, _s=sl, _sc=scn: \
                                e.matmul(ps_o[:, _s], v_t[:, _vb:_vb + 65],
                                         pt[_p][:, _s], start=(_sc == 0),
                                         stop=(_sc == 31))
                            if nn == 1:
                                last_pv[par] = sc.inc("t", fn)
                            else:
                                sc.raw("t", fn)
                    t_scores_end[h] = sc.cnt["t"]
                    sc.wait("v", "t", last_pv[31 % 2])
                    sc.wait("v", "t", last_pv[30 % 2])
                    v_rec = sc.inc("v", lambda e: e.reciprocal(
                        rrec[:], ps_o[64:65, :]))
                    sc.wait("t", "v", v_rec)
                    for nn in range(2):
                        sl = slice(nn * 512, nn * 512 + 512)
                        fn = lambda e, _s=sl: e.matmul(
                            ps_rb[:, _s], ones64[:], rrec[0:1, _s],
                            start=True, stop=True)
                        if nn == 1:
                            last_rb = sc.inc("t", fn)
                        else:
                            sc.raw("t", fn)
                    sc.wait("s", "t", last_rb)
                    sc.wait("s", "v", last_o_use)
                    s_ot = sc.inc("s", lambda e: e.copy(otmp[:], ps_o[0:64, :]))
                    sc.wait("v", "s", s_ot)
                    cc, r0 = h // 2, (h % 2) * 64
                    last_o_use = sc.inc(
                        "v", lambda e, _cc=cc, _r0=r0, _q=qsl: e.tensor_tensor(
                            ocat[_cc][_r0:_r0 + 64, _q], otmp[:], ps_rb[:],
                            OP.mult))
            sc.wait("t", "v", last_o_use)
        sc.barrier()
        stB.close()

        with contextlib.ExitStack() as ph:
            pb = lambda n, shp, dt: ph.enter_context(SBUFR(n, shp, dt))
            wo_s = [pb(f"wo_s{i}", [128, C], BF16) for i in range(8)]
            for i in range(8):
                sc.dma("g", wo_s[i][:], woT[i * 128:(i + 1) * 128, :])
            bo_sb = pb("bo_sb", [128, 8], F32)
            g_sb = pb("g_sb", [128, TH], BF16)
            x0_sb = pb("x0_sb", [128, TH], F32)
            tmp1 = pb("tmp1", [128, 512], F32)
            tmp2 = pb("tmp2", [128, 512], F32)
            xout = pb("xout", [128, TH], F32)
            sc.dma("g", bo_sb[:], bo_d)
            sc.barrier()
            with contextlib.ExitStack() as pp:
                pso = [pp.enter_context(PSUM(f"pso{j}", [128, 512],
                                                       F32)) for j in range(4)]
                prev_v = None
                for dc in range(8):
                    sc.wait("y", "v", prev_v)
                    sc.wait("t", "v", prev_v)
                    ldg = sc.dma("y", g_sb[:],
                                 g_scr.ap()[dc * 128:(dc + 1) * 128, :])
                    ldx = sc.dma("y", x0_sb[:],
                                 x0T[dc * 128:(dc + 1) * 128, 0:TH])
                    vv = None
                    for nt in range(4):
                        j = nt
                        mm = mm_accum(sc, pso[j][:],
                                      [(wo_s[cc][:, dc * 128:(dc + 1) * 128],
                                        ocat[cc][:, nt * 512:(nt + 1) * 512])
                                       for cc in range(8)])
                        sc.wait("v", "t", mm)
                        sc.wait("v", "y", ldx)
                        sc.wait("v", "y", ldg)
                        sl = slice(nt * 512, (nt + 1) * 512)
                        sc.inc("v", lambda e, _j=j, _dc=dc: e.tensor_scalar(
                            tmp1[:], pso[_j][:], bo_sb[:, _dc:_dc + 1], None,
                            OP.add))
                        sc.inc("v", lambda e, _sl=sl: e.tensor_tensor(
                            tmp2[:], tmp1[:], g_sb[:, _sl], OP.mult))
                        vv = sc.inc("v", lambda e, _sl=sl: e.tensor_tensor(
                            xout[:, _sl], tmp2[:], x0_sb[:, _sl], OP.add))
                    prev_v = vv
                    sc.wait("y", "v", vv)
                    stc = sc.dma("y", x1T[dc * 128:(dc + 1) * 128, :], xout[:])
                    sc.wait("v", "y", stc)
                sc.wait("t", "v", sc.cnt["v"])
        stOC.close()
        stV.close()
        sc.wait("y", "y", sc.cnt["y"])
        sc.finalize(es)
    return nc


def prep_L0_inputs(inputs):
    x = f32c(inputs["L0_attn_q"])
    wq = np.ascontiguousarray(f32c(inputs["L0_attn_Wq"]).T)
    wk = np.ascontiguousarray(f32c(inputs["L0_attn_Wk"]).T)
    wv = np.ascontiguousarray(f32c(inputs["L0_attn_Wv"]).T)
    wo = np.ascontiguousarray(f32c(inputs["L0_attn_Wo"]).T)
    bo = f32c(inputs["L0_attn_bo"])
    bo2 = np.ascontiguousarray(bo.reshape(8, 128).T)
    wf = np.ascontiguousarray(f32c(inputs["L0_ffn_weight"]).T)
    maps = []
    for core in range(NC8):
        b, th = core // 2, core % 2
        xT = x[b].T
        rot = np.concatenate([xT[:, th * TH:(th + 1) * TH],
                              xT[:, (1 - th) * TH:(2 - th) * TH]], axis=1)
        maps.append({
            "x0T": np.ascontiguousarray(rot),
            "qoff": np.array([[th * TH]], dtype=np.float32),
            "wqT": wq, "wkT": wk, "wvT": wv, "woT": wo,
            "bo2": bo2, "wfT": wf,
        })
    return maps


def run_L0(inputs, trace=False):
    if "L0" not in _builder_cache:
        _builder_cache["L0"] = build_L0()
    nc = _builder_cache["L0"]
    maps = prep_L0_inputs(inputs)
    res = run_bass_kernel_spmd(nc, maps, core_ids=list(range(NC8)),
                               trace=trace)
    x1 = np.empty((B, L, C), dtype=np.float32)
    for core in range(NC8):
        b, th = core // 2, core % 2
        x1[b, th * TH:(th + 1) * TH, :] = res.results[core]["x1T"].T
    return x1, res


def kernel(**inputs):
    x1, _ = run_L0(inputs)
    x2 = run_L1(inputs, x1)
    x3 = run_L2(inputs, x2)
    x4 = run_L3(inputs, x3)
    return x4


class Ctx:

    def __init__(self):
        self.nc = bass.Bass("TRN2", target_bir_lowering=False)
        self.sc = Sched(self.nc)
        self._uid = [0]

    def nm(self, n):
        self._uid[0] += 1
        return f"{n}_u{self._uid[0]}"

    def SBUF(self, n, shp, dt, side=None):
        return self.nc.sbuf_tensor(self.nm(n), shp, dt, side=side)

    def SBUFR(self, n, shp, dt):
        return self.nc.sbuf_tensor(self.nm(n), shp, dt, side="right")

    def PSUM(self, n, shp, dt):
        return self.nc.psum_tensor(self.nm(n), shp, dt)


def emit_ln_cm(cx, es_left, x_dram, ncols, gb_cols=None, eps=1e-5,
               keep_stats=None):
    sc, nc = cx.sc, cx.nc
    CB = 1024
    xn = [es_left.enter_context(cx.SBUF(f"xn{i}", [128, ncols], BF16))
          for i in range(8)]
    nh = ncols // CB
    with contextlib.ExitStack() as ph:
        pb = lambda n, shp, dt: ph.enter_context(cx.SBUFR(n, shp, dt))
        x0b = [pb(f"x0b{i}", [128, CB], F32) for i in range(8)]
        sqb = pb("sqb", [128, CB], F32)
        srow = pb("srow", [1, CB], F32)
        s2row = pb("s2row", [1, CB], F32)
        rrow = pb("rrow", [1, CB], BF16)
        rful = pb("rful", [1, CB], F32)
        crow = pb("crow", [1, CB], BF16)
        mrow = pb("mrow", [1, CB], F32)
        vrow = pb("vrow", [1, CB], F32)
        m2row = pb("m2row", [1, CB], F32)
        tmpx = pb("tmpx", [128, CB], F32)
        onescol_f = pb("onescol_f", [128, 1], F32)
        onesb_r2 = pb("onesb_r2", [1, 128], BF16)
        epsc = pb("epsc", [1, 1], F32)
        sc.inc("g", lambda e: e.memset(onescol_f[:], 1.0))
        sc.inc("g", lambda e: e.memset(onesb_r2[:], 1.0))
        gc = sc.inc("g", lambda e: e.memset(epsc[:], eps))
        sc.wait("t", "g", gc)
        sc.wait("s", "g", gc)
        sc.wait("v", "g", gc)
        prev_norm = None
        for half in range(nh):
            t0 = half * CB
            with contextlib.ExitStack() as pp:
                ps_s = pp.enter_context(cx.PSUM("ps_s", [1, CB], F32))
                ps_s2 = pp.enter_context(cx.PSUM("ps_s2", [1, CB], F32))
                lds = [sc.dma("y", x0b[i][:],
                              x_dram[i * 128:(i + 1) * 128, t0:t0 + CB])
                       for i in range(8)]
                sc.wait("t", "v", prev_norm)
                prev_t = None
                for i in range(8):
                    sc.wait("s", "y", lds[i])
                    if prev_t:
                        sc.wait("s", "t", prev_t)
                    s_sq = sc.inc("s", lambda e, _i=i: e.activation(
                        sqb[:], x0b[_i][:], AF.Square))
                    sc.wait("t", "s", s_sq)
                    sc.wait("t", "y", lds[i])
                    for nn in range(CB // 512):
                        sl = slice(nn * 512, nn * 512 + 512)
                        sc.raw("t", lambda e, _i=i, _sl=sl: e.matmul(
                            ps_s[:, _sl], onescol_f[:], x0b[_i][:, _sl],
                            start=(_i == 0), stop=(_i == 7)))
                        if nn < CB // 512 - 1:
                            sc.raw("t", lambda e, _i=i, _sl=sl: e.matmul(
                                ps_s2[:, _sl], onescol_f[:], sqb[:, _sl],
                                start=(_i == 0), stop=(_i == 7)))
                        else:
                            prev_t = sc.inc(
                                "t", lambda e, _i=i, _sl=sl: e.matmul(
                                    ps_s2[:, _sl], onescol_f[:], sqb[:, _sl],
                                    start=(_i == 0), stop=(_i == 7)))
                sc.wait("v", "t", prev_t)
                sc.inc("v", lambda e: e.tensor_copy(srow[:], ps_s[:]))
                sc.inc("v", lambda e: e.tensor_copy(s2row[:], ps_s2[:]))
            sc.inc("v", lambda e: e.tensor_scalar(
                mrow[:], srow[:], 1.0 / C, None, OP.mult))
            sc.inc("v", lambda e: e.tensor_scalar(
                vrow[:], s2row[:], 1.0 / C, None, OP.mult))
            sc.inc("v", lambda e: e.tensor_tensor(
                m2row[:], mrow[:], mrow[:], OP.mult))
            v_var = sc.inc("v", lambda e: e.tensor_tensor(
                vrow[:], vrow[:], m2row[:], OP.subtract))
            sc.wait("s", "v", v_var)
            sc.inc("s", lambda e: e.activation(vrow[:], vrow[:], AF.Ln,
                                               bias=epsc[:]))
            s_r = sc.inc("s", lambda e: e.activation(rful[:], vrow[:], AF.Exp,
                                                     scale=-0.5))
            sc.wait("v", "s", s_r)
            sc.inc("v", lambda e: e.tensor_copy(rrow[:], rful[:]))
            if keep_stats is not None and t0 < keep_stats[2]:
                mu_k, r_k, _ = keep_stats
                sc.inc("v", lambda e, _t=t0: e.tensor_copy(
                    mu_k[0:1, _t:_t + CB], mrow[:]))
                sc.inc("v", lambda e, _t=t0: e.tensor_copy(
                    r_k[0:1, _t:_t + CB], rful[:]))
            sc.inc("v", lambda e: e.tensor_tensor(m2row[:], mrow[:],
                                                  rrow[:], OP.mult))
            v_cr = sc.inc("v", lambda e: e.tensor_scalar(
                crow[:], m2row[:], -1.0, None, OP.mult))
            with contextlib.ExitStack() as pp:
                ps_a = pp.enter_context(cx.PSUM("ps_a", [128, CB], F32))
                ps_c = pp.enter_context(cx.PSUM("ps_c", [128, CB], F32))
                sc.wait("t", "v", v_cr)
                t_bc = None
                for nn in range(CB // 512):
                    sl = slice(nn * 512, nn * 512 + 512)
                    sc.raw("t", lambda e, _sl=sl: e.matmul(
                        ps_a[:, _sl], onesb_r2[:], rrow[0:1, _sl],
                        start=True, stop=True))
                    t_bc = sc.inc("t", lambda e, _sl=sl: e.matmul(
                        ps_c[:, _sl], onesb_r2[:], crow[0:1, _sl],
                        start=True, stop=True))
                sc.wait("v", "t", t_bc)
                v_norm = None
                for i in range(8):
                    sg = slice(t0, t0 + CB)
                    sc.inc("v", lambda e, _i=i: e.tensor_tensor(
                        tmpx[:], x0b[_i][:], ps_a[:], OP.mult))
                    v_norm = sc.inc(
                        "v", lambda e, _i=i, _g=sg: e.tensor_tensor(
                            xn[_i][:, _g], tmpx[:], ps_c[:], OP.add))
                sc.wait("t", "v", v_norm)
                prev_norm = v_norm
            if half + 1 < nh:
                sc.wait("y", "v", sc.cnt["v"])
    return xn


def emit_proj_cm(cx, w_tiles, rhs_tiles, rhs_cols, m_chunks, evac,
                 psn_width=512, nk=8):
    sc = cx.sc
    ncol_chunks = rhs_cols // psn_width
    with contextlib.ExitStack() as pp:
        ps = [pp.enter_context(cx.PSUM(f"psp{j}", [128, psn_width], F32))
              for j in range(4)]
        last = {}
        for mo in m_chunks:
            for nn in range(ncol_chunks):
                j = (mo * ncol_chunks + nn) % 4
                if j in last:
                    sc.wait("t", last[j][0], last[j][1])
                mm = mm_accum(sc, ps[j][:],
                              [(w_tiles[cc][:, mo * 128:(mo + 1) * 128],
                                rhs_tiles[cc][:, nn * psn_width:
                                              (nn + 1) * psn_width])
                               for cc in range(nk)])
                last[j] = evac(mo, nn, ps[j], mm)
        for j, (ek, c) in last.items():
            sc.wait("t", ek, c)


def build_L1a():
    cx = Ctx()
    nc, sc = cx.nc, cx.sc
    x1T = nc.dram_tensor("x1T", [C, L], F32, kind="ExternalInput").ap()
    qoffd = nc.dram_tensor("qoff", [1, 1], F32, kind="ExternalInput").ap()
    wkT = nc.dram_tensor("wkT", [C, C], F32, kind="ExternalInput").ap()
    wvT = nc.dram_tensor("wvT", [C, C], F32, kind="ExternalInput").ap()
    wrT = nc.dram_tensor("wrT", [C, C], F32, kind="ExternalInput").ap()
    woT = nc.dram_tensor("woT", [C, C], F32, kind="ExternalInput").ap()
    tmk2 = nc.dram_tensor("tmk2", [128, 8], F32, kind="ExternalInput").ap()
    tmv2 = nc.dram_tensor("tmv2", [128, 8], F32, kind="ExternalInput").ap()
    tmr2 = nc.dram_tensor("tmr2", [128, 8], F32, kind="ExternalInput").ap()
    td2 = nc.dram_tensor("td2", [128, 8], F32, kind="ExternalInput").ap()
    tf2 = nc.dram_tensor("tf2", [128, 8], F32, kind="ExternalInput").ap()
    lng2 = nc.dram_tensor("lng2", [128, 8], F32, kind="ExternalInput").ap()
    lnb2 = nc.dram_tensor("lnb2", [128, 8], F32, kind="ExternalInput").ap()
    wr8 = nc.dram_tensor("wr8", [C, 8], F32, kind="ExternalInput").ap()
    br8 = nc.dram_tensor("br8", [1, 8], F32, kind="ExternalInput").ap()
    idn = nc.dram_tensor("idn", [128, 128], F32, kind="ExternalInput").ap()

    s1T = nc.dram_tensor("s1T", [C, TH], F32, kind="ExternalOutput").ap()
    xtok = nc.dram_tensor("xtok", [TH, C], BF16, kind="ExternalOutput").ap()
    m1o = nc.dram_tensor("m1o", [TH, 8], BF16, kind="ExternalOutput").ap()
    m2o = nc.dram_tensor("m2o", [TH, 8], BF16, kind="ExternalOutput").ap()
    p1o = nc.dram_tensor("p1o", [TH, 1], F32, kind="ExternalOutput").ap()
    p2o = nc.dram_tensor("p2o", [TH, 1], F32, kind="ExternalOutput").ap()
    r_scr = nc.dram_tensor("r_scr", [C, TH], BF16)
    o_scr = nc.dram_tensor("o_scr", [C, TH], BF16)
    mr_scr = nc.dram_tensor("mr_scr", [2, TH], F32)

    es = contextlib.ExitStack()
    with es:
        sbL = lambda n, shp, dt: es.enter_context(cx.SBUF(n, shp, dt))
        qcol = sbL("qcol", [128, 1], F32)
        step1 = sbL("step1", [128, 1], F32)
        step2 = sbL("step2", [128, 1], F32)
        onesb_r = sbL("onesb_r", [1, 128], BF16)
        ident = sbL("ident", [128, 128], BF16)
        tmk = sbL("tmk", [128, 8], F32)
        itmk = sbL("itmk", [128, 8], F32)
        tmv = sbL("tmv", [128, 8], F32)
        itmv = sbL("itmv", [128, 8], F32)
        tmr = sbL("tmr", [128, 8], F32)
        itmr = sbL("itmr", [128, 8], F32)
        tds = sbL("tds", [128, 8], F32)
        tfs = sbL("tfs", [128, 8], F32)
        lng = sbL("lng", [128, 8], F32)
        lnb = sbL("lnb", [128, 8], F32)
        wrb = [sbL(f"wrb{i}", [128, 8], BF16) for i in range(8)]
        brb = sbL("brb", [1, 8], BF16)
        dcol = sbL("dcol", [128, 8], F32)
        carry = sbL("carry", [128, 8], F32)

        ld = sc.dma("y", qcol[0:1, :], qoffd)
        for dst, srcd in ((tmk, tmk2), (tmv, tmv2), (tmr, tmr2), (tds, td2),
                          (tfs, tf2), (lng, lng2), (lnb, lnb2)):
            ld = sc.dma("y", dst[:], srcd)
        for i in range(8):
            ld2 = sc.dma("g", wrb[i][:], wr8[i * 128:(i + 1) * 128, :])
        ld2 = sc.dma("g", brb[:], br8)
        ld2 = sc.dma("g", ident[:], idn)
        gsets = sc.inc("g", lambda e: e.memset(onesb_r[:], 1.0))
        sc.wait("v", "y", ld)
        sc.wait("v", "gd", ld2)
        sc.wait("v", "g", gsets)
        sc.inc("v", lambda e: e.tensor_single_scalar(step1[0:1, :],
                                                     qcol[0:1, :], 0.5,
                                                     OP.is_gt))
        sc.inc("v", lambda e: e.tensor_scalar(step2[0:1, :], step1[0:1, :],
                                              -1.0, 1.0, OP.mult, OP.add))
        for tm, itm in ((tmk, itmk), (tmv, itmv), (tmr, itmr)):
            sc.inc("v", lambda e, _a=tm, _b=itm: e.tensor_scalar(
                _b[:], _a[:], -1.0, 1.0, OP.mult, OP.add))
        with cx.PSUM("ps_st", [128, 2], F32) as ps_st:
            stp_b = sbL("stp_b", [1, 2], BF16)
            sc.inc("v", lambda e: e.tensor_copy(stp_b[0:1, 0:1], step1[0:1, :]))
            v_c = sc.inc("v", lambda e: e.tensor_copy(stp_b[0:1, 1:2],
                                                      step2[0:1, :]))
            sc.wait("t", "v", v_c)
            sc.wait("t", "g", gsets)
            t_b = sc.inc("t", lambda e: e.matmul(ps_st[:], onesb_r[:],
                                                 stp_b[:], start=True,
                                                 stop=True))
            sc.wait("v", "t", t_b)
            sc.inc("v", lambda e: e.tensor_copy(step1[:], ps_st[:, 0:1]))
            v_st = sc.inc("v", lambda e: e.tensor_copy(step2[:],
                                                       ps_st[:, 1:2]))
        sc.barrier()

        stMU = contextlib.ExitStack()
        mu_k = stMU.enter_context(cx.SBUFR("mu_k", [1, TH], F32))
        r_k = stMU.enter_context(cx.SBUFR("r_k", [1, TH], F32))
        stXN = contextlib.ExitStack()
        xn = emit_ln_cm(cx, stXN, x1T, L, keep_stats=(mu_k, r_k, TH))
        vk = sc.cnt["v"]
        sc.wait("y", "v", vk)
        sc.dma("y", mr_scr.ap()[0:1, :], mu_k[:])
        sc.dma("y", mr_scr.ap()[1:2, :], r_k[:])
        sc.barrier()
        stMU.close()

        with contextlib.ExitStack() as ph:
            xtk = ph.enter_context(cx.SBUFR("xtk", [128, C], BF16))
            with cx.PSUM("ps_tr", [128, C], BF16) as ps_tr:
                prev_st = None
                prev_cp = None
                for tc in range(16):
                    t_tr = None
                    sc.wait("t", "v", prev_cp)
                    for ccj in range(8):
                        t_tr = sc.inc("t", lambda e, _c=ccj, _tc=tc: e.transpose(
                            ps_tr[:, _c * 128:(_c + 1) * 128],
                            xn[_c][:, _tc * 128:(_tc + 1) * 128], ident[:]))
                    sc.wait("v", "t", t_tr)
                    sc.wait("v", "y", prev_st)
                    prev_cp = sc.inc("v", lambda e: e.tensor_copy(xtk[:],
                                                                  ps_tr[:]))
                    sc.wait("y", "v", prev_cp)
                    prev_st = sc.dma("y", xtok[tc * 128:(tc + 1) * 128, :],
                                     xtk[:])
        sc.barrier()

        with contextlib.ExitStack() as ph:
            pb = lambda n, shp, dt: ph.enter_context(cx.SBUFR(n, shp, dt))
            prb = pb("prb", [128, 8], F32)
            nmx = pb("nmx", [128, 1], F32)
            ssum = pb("ssum", [128, 1], F32)
            rsum = pb("rsum", [128, 1], F32)
            mk1 = pb("mk1", [128, 8], BF16)
            mk2 = pb("mk2", [128, 8], BF16)
            mk1f = pb("mk1f", [128, 8], F32)
            pm = pb("pm", [128, 8], F32)
            pv1 = pb("pv1", [128, 1], F32)
            pv2 = pb("pv2", [128, 1], F32)
            x1f = [pb(f"x1f{i}", [128, TH], F32) for i in range(8)]
            wrf = [pb(f"wrf{i}", [128, 8], F32) for i in range(8)]
            mcol = pb("mcol", [128, 16], F32)
            rcol = pb("rcol", [128, 16], F32)
            mrc = pb("mrc", [128, 16], F32)
            bsum = pb("bsum", [128, 8], F32)
            brf = pb("brf", [128, 8], F32)
            onescol_g = pb("onescol_g", [128, 1], F32)
            t1b = pb("t1b", [128, 8], F32)
            t2b = pb("t2b", [128, 8], F32)
            for i in range(8):
                sc.dma("y", x1f[i][:], x1T[i * 128:(i + 1) * 128, 0:TH])
                sc.dma("y", wrf[i][:], wr8[i * 128:(i + 1) * 128, :])
            sc.dma("y", mcol[:],
                   mr_scr.ap()[0:1, :].rearrange("a (tc p) -> (a p) tc",
                                                 p=128),
                   allow_slow_non_contiguous=True)
            ldm = sc.dma("y", rcol[:],
                         mr_scr.ap()[1:2, :].rearrange(
                             "a (tc p) -> (a p) tc", p=128),
                         allow_slow_non_contiguous=True)
            gc3 = sc.inc("g", lambda e: e.memset(onescol_g[:], 1.0))
            sc.wait("v", "y", ldm)
            sc.inc("v", lambda e: e.tensor_tensor(mrc[:], mcol[:], rcol[:],
                                                  OP.mult))
            bs_b = pb("bs_b", [1, 8], BF16)
            with cx.PSUM("ps_bs", [1, 8], F32) as ps_bs:
                sc.wait("t", "y", ldm)
                sc.wait("t", "g", gc3)
                t_bs = None
                for cc in range(8):
                    if cc < 7:
                        sc.raw("t", lambda e, _c=cc: e.matmul(
                            ps_bs[:], onescol_g[:], wrf[_c][:],
                            start=(_c == 0), stop=False))
                    else:
                        t_bs = sc.inc("t", lambda e, _c=cc: e.matmul(
                            ps_bs[:], onescol_g[:], wrf[_c][:],
                            start=False, stop=True))
                sc.wait("v", "t", t_bs)
                v_bs = sc.inc("v", lambda e: e.tensor_copy(bs_b[:], ps_bs[:]))
            with cx.PSUM("ps_b2", [128, 8], F32) as ps_b2:
                sc.wait("t", "v", v_bs)
                t_x = sc.inc("t", lambda e: e.matmul(
                    ps_b2[:], onesb_r[:], bs_b[:], start=True, stop=True))
                sc.wait("v", "t", t_x)
                v_x = sc.inc("v", lambda e: e.tensor_copy(bsum[:], ps_b2[:]))
                sc.wait("t", "v", v_x)
                t_x = sc.inc("t", lambda e: e.matmul(
                    ps_b2[:], onesb_r[:], brb[:], start=True, stop=True))
                sc.wait("v", "t", t_x)
                v_x = sc.inc("v", lambda e: e.tensor_copy(brf[:], ps_b2[:]))
                sc.wait("t", "v", v_x)
            with cx.PSUM("ps_lg", [128, 8], F32) as ps_lg:
                prev = None
                prev_store = None
                for tc in range(16):
                    sc.wait("t", "v", prev)
                    sc.wait("v", "y", prev_store)
                    t_lg = mm_accum(
                        sc, ps_lg[:],
                        [(x1f[cc][:, tc * 128:(tc + 1) * 128], wrf[cc][:])
                         for cc in range(8)])
                    sc.wait("v", "t", t_lg)
                    sc.inc("v", lambda e, _tc=tc: e.tensor_scalar(
                        t1b[:], ps_lg[:], rcol[:, _tc:_tc + 1], None,
                        OP.mult))
                    sc.inc("v", lambda e, _tc=tc: e.tensor_scalar(
                        t2b[:], bsum[:], mrc[:, _tc:_tc + 1], None, OP.mult))
                    sc.inc("v", lambda e: e.tensor_tensor(
                        t1b[:], t1b[:], t2b[:], OP.subtract))
                    t_lg2 = sc.inc("v", lambda e: e.tensor_tensor(
                        t1b[:], t1b[:], brf[:], OP.add))
                    sc.inc("v", lambda e: e.reduce_max(nmx[:], t1b[:],
                                                       axis=AX, negate=True))
                    v_m = sc.cnt["v"]
                    sc.wait("s", "v", v_m)
                    s_e = sc.inc("s", lambda e: e.activation(
                        prb[:], t1b[:], AF.Exp, bias=nmx[:]))
                    sc.wait("v", "s", s_e)
                    sc.inc("v", lambda e: e.reduce_sum(ssum[:], prb[:],
                                                       axis=AX))
                    sc.inc("v", lambda e: e.reciprocal(rsum[:], ssum[:]))
                    sc.inc("v", lambda e: e.reduce_max(nmx[:], prb[:],
                                                       axis=AX))
                    sc.inc("v", lambda e: e.tensor_scalar(
                        mk1[:], prb[:], nmx[:], None, OP.is_ge))
                    sc.inc("v", lambda e: e.tensor_copy(mk1f[:], mk1[:]))
                    sc.inc("v", lambda e: e.tensor_tensor(
                        pv1[:], nmx[:], rsum[:], OP.mult))
                    sc.inc("v", lambda e: e.scalar_tensor_tensor(
                        pm[:], mk1f[:], -1e9, prb[:], OP.mult, OP.add))
                    sc.inc("v", lambda e: e.reduce_max(nmx[:], pm[:], axis=AX))
                    sc.inc("v", lambda e: e.tensor_scalar(
                        mk2[:], pm[:], nmx[:], None, OP.is_ge))
                    v_p2 = sc.inc("v", lambda e: e.tensor_tensor(
                        pv2[:], nmx[:], rsum[:], OP.mult))
                    sc.wait("y", "v", v_p2)
                    sc.dma("y", m1o[tc * 128:(tc + 1) * 128, :], mk1[:])
                    sc.dma("y", m2o[tc * 128:(tc + 1) * 128, :], mk2[:])
                    sc.dma("y", p1o[tc * 128:(tc + 1) * 128, :], pv1[:])
                    prev = v_p2
                    prev_store = sc.dma("y", p2o[tc * 128:(tc + 1) * 128, :],
                                        pv2[:])
        sc.barrier()

        stMix = contextlib.ExitStack()
        mixtmp = stMix.enter_context(cx.SBUFR("mixtmp", [128, TH], F32))
        col1 = stMix.enter_context(cx.SBUFR("col1", [128, 1], F32))
        col2 = stMix.enter_context(cx.SBUFR("col2", [128, 1], F32))
        wk_s = [stMix.enter_context(cx.SBUFR(f"wk_s{i}", [128, C], BF16))
                for i in range(8)]
        wv_s = [stMix.enter_context(cx.SBUFR(f"wv_s{i}", [128, C], BF16))
                for i in range(8)]
        for i in range(8):
            sc.dma("g", wk_s[i][:], wkT[i * 128:(i + 1) * 128, :])
            lw = sc.dma("g", wv_s[i][:], wvT[i * 128:(i + 1) * 128, :])
        sc.wait("t", "gd", lw)

        def emit_mix(dst_tiles, tm, itm, seg0, boundary_src, stepc):
            vv = None
            for ccx in range(8):
                a = seg0
                sc.inc("v", lambda e, _c=ccx, _a=a: e.tensor_tensor(
                    mixtmp[:, 1:TH], xn[_c][:, _a + 1:_a + TH],
                    xn[_c][:, _a:_a + TH - 1], OP.subtract))
                sc.inc("v", lambda e, _c=ccx, _a=a: e.scalar_tensor_tensor(
                    dst_tiles[_c][:, 1:TH], mixtmp[:, 1:TH],
                    tm[:, _c:_c + 1], xn[_c][:, _a:_a + TH - 1],
                    OP.mult, OP.add))
                sc.inc("v", lambda e, _c=ccx, _a=a: e.tensor_scalar(
                    col1[:], xn[_c][:, _a:_a + 1], tm[:, _c:_c + 1], None,
                    OP.mult))
                sc.inc("v", lambda e, _c=ccx: e.tensor_scalar(
                    col2[:], xn[_c][:, boundary_src:boundary_src + 1],
                    itm[:, _c:_c + 1], None, OP.mult))
                sc.inc("v", lambda e, _c=ccx: e.tensor_tensor(
                    col2[:], col2[:], stepc[:], OP.mult))
                vv = sc.inc("v", lambda e, _c=ccx: e.tensor_tensor(
                    dst_tiles[_c][:, 0:1], col1[:], col2[:], OP.add))
            return vv

        def scan_half(seg0, boundary_src, stepc, init_mode, out_fn):
            with contextlib.ExitStack() as ph:
                pb = lambda n, shp, dt: ph.enter_context(cx.SBUFR(n, shp, dt))
                xk_t = [pb(f"xk{i}", [128, TH], BF16) for i in range(8)]
                xv_t = [pb(f"xv{i}", [128, TH], BF16) for i in range(8)]
                ktmp = pb("ktmp", [128, TH], BF16)
                kvb = pb("kvb", [128, TH], BF16)
                st_buf = pb("st_buf", [128, TH], F32)
                drep = pb("drep", [128, TH], F32)
                obuf = pb("obuf", [128, TH], BF16)
                rbuf = pb("rbuf", [128, TH], BF16)
                v_mix = emit_mix(xk_t, tmk, itmk, seg0, boundary_src, stepc)
                v_mix = emit_mix(xv_t, tmv, itmv, seg0, boundary_src, stepc)
                sc.wait("t", "v", v_mix)
                with contextlib.ExitStack() as pp:
                    psk = [pp.enter_context(cx.PSUM(f"psk{j}", [128, 512],
                                                    F32)) for j in range(4)]
                    psv = [pp.enter_context(cx.PSUM(f"psv{j}", [128, 512],
                                                    F32)) for j in range(4)]
                    prev_done = None
                    for dc in range(8):
                        sc.wait("t", "v", prev_done)
                        rld = None
                        ek = None
                        for nn in range(4):
                            mm = mm_accum(
                                sc, psk[nn][:],
                                [(wk_s[cc][:, dc * 128:(dc + 1) * 128],
                                  xk_t[cc][:, nn * 512:(nn + 1) * 512])
                                 for cc in range(8)])
                            sc.wait("v", "t", mm)
                            ek = sc.inc("v", lambda e, _n=nn: e.tensor_copy(
                                ktmp[:, _n * 512:(_n + 1) * 512], psk[_n][:]))
                        kv = None
                        for nn in range(4):
                            mm = mm_accum(
                                sc, psv[nn][:],
                                [(wv_s[cc][:, dc * 128:(dc + 1) * 128],
                                  xv_t[cc][:, nn * 512:(nn + 1) * 512])
                                 for cc in range(8)])
                            sc.wait("v", "t", mm)
                            kv = sc.inc(
                                "v", lambda e, _n=nn: e.tensor_tensor(
                                    kvb[:, _n * 512:(_n + 1) * 512], psv[_n][:],
                                    ktmp[:, _n * 512:(_n + 1) * 512], OP.mult))
                        s_d = sc.inc("s", lambda e, _dc=dc: e.activation(
                            dcol[:, _dc:_dc + 1], tds[:, _dc:_dc + 1], AF.Exp))
                        sc.wait("v", "s", s_d)
                        sc.inc("v", lambda e, _dc=dc: e.tensor_scalar(
                            drep[:], kvb[:], 0.0, dcol[:, _dc:_dc + 1],
                            OP.mult, OP.add))
                        if init_mode == "zero":
                            v_sc = sc.inc(
                                "v", lambda e: e.tensor_tensor_scan(
                                    st_buf[:], drep[:], kvb[:], 0.0,
                                    OP.mult, OP.add))
                        else:
                            v_sc = sc.inc(
                                "v", lambda e, _dc=dc: e.tensor_tensor_scan(
                                    st_buf[:], drep[:], kvb[:],
                                    carry[:, _dc:_dc + 1], OP.mult, OP.add))
                        prev_done = out_fn(dc, st_buf, kvb, rld, obuf)
            return

        def carry_out(dc, st_buf, kvb, rld, obuf):
            return sc.inc("v", lambda e, _dc=dc: e.tensor_tensor(
                carry[:, _dc:_dc + 1], st_buf[:, TH - 1:TH], step1[:],
                OP.mult))

        scan_half(TH, TH - 1, step2, "zero", carry_out)

        with contextlib.ExitStack() as ph:
            xr_t = [ph.enter_context(cx.SBUFR(f"xr{i}", [128, TH], BF16))
                    for i in range(8)]
            wr_s = [ph.enter_context(cx.SBUFR(f"wr_s{i}", [128, C], BF16))
                    for i in range(8)]
            rout = ph.enter_context(cx.SBUFR("rout", [128, TH], BF16))
            sc.wait("g", "v", sc.cnt["v"])
            for i in range(8):
                lw = sc.dma("g", wr_s[i][:], wrT[i * 128:(i + 1) * 128, :])
            v_mix = emit_mix(xr_t, tmr, itmr, 0, 2 * TH - 1, step1)
            sc.wait("t", "v", v_mix)
            sc.wait("t", "gd", lw)
            r_store = {}

            def evac_r(mo, nn, ps, mm):
                sc.wait("v", "t", mm)
                sc.wait("v", "y", r_store.get(mo))
                c = sc.inc("v", lambda e, _nn=nn: e.tensor_copy(
                    rout[:, _nn * 512:(_nn + 1) * 512], ps[:]))
                if nn == 3:
                    sc.wait("y", "v", c)
                    r_store[mo + 1] = sc.dma(
                        "y", r_scr.ap()[mo * 128:(mo + 1) * 128, :], rout[:])
                return ("v", c)

            emit_proj_cm(cx, wr_s, xr_t, TH, range(8), evac_r)
            sc.wait("v", "y", sc.cnt["y"])
        sc.barrier()

        o_store = {}

        def own_out(dc, st_buf, kvb, rld, obuf):
            sc.inc("v", lambda e, _dc=dc: e.scalar_tensor_tensor(
                st_buf[:], kvb[:], tfs[:, _dc:_dc + 1], st_buf[:],
                OP.mult, OP.add))
            sc.wait("v", "y", rld)
            sc.wait("v", "y", o_store.get(dc))
            c = sc.inc("v", lambda e, _dc=dc: e.tensor_tensor(
                obuf[:], st_buf[:], rbuf_holder[0][:], OP.mult))
            sc.wait("y", "v", c)
            o_store[dc + 1] = sc.dma(
                "y", o_scr.ap()[dc * 128:(dc + 1) * 128, :], obuf[:])
            return c

        rbuf_holder = [None]

        def own_out2(dc, st_buf, kvb, rld, obuf):
            return own_out(dc, st_buf, kvb, rld, obuf)

        def scan_half_own():
            with contextlib.ExitStack() as ph:
                pb = lambda n, shp, dt: ph.enter_context(cx.SBUFR(n, shp, dt))
                xk_t = [pb(f"xk{i}", [128, TH], BF16) for i in range(8)]
                xv_t = [pb(f"xv{i}", [128, TH], BF16) for i in range(8)]
                ktmp = pb("ktmp", [128, TH], BF16)
                kvb = pb("kvb", [128, TH], BF16)
                st_buf = pb("st_buf", [128, TH], F32)
                drep = pb("drep", [128, TH], F32)
                obuf = pb("obuf", [128, TH], BF16)
                rbuf = pb("rbuf", [128, TH], BF16)
                rbuf_holder[0] = rbuf
                v_mix = emit_mix(xk_t, tmk, itmk, 0, 2 * TH - 1, step1)
                v_mix = emit_mix(xv_t, tmv, itmv, 0, 2 * TH - 1, step1)
                sc.wait("t", "v", v_mix)
                with contextlib.ExitStack() as pp:
                    psk = [pp.enter_context(cx.PSUM(f"psk{j}", [128, 512],
                                                    F32)) for j in range(4)]
                    psv = [pp.enter_context(cx.PSUM(f"psv{j}", [128, 512],
                                                    F32)) for j in range(4)]
                    prev_done = None
                    for dc in range(8):
                        sc.wait("t", "v", prev_done)
                        rld = sc.dma("y", rbuf[:],
                                     r_scr.ap()[dc * 128:(dc + 1) * 128, :])
                        for nn in range(4):
                            mm = mm_accum(
                                sc, psk[nn][:],
                                [(wk_s[cc][:, dc * 128:(dc + 1) * 128],
                                  xk_t[cc][:, nn * 512:(nn + 1) * 512])
                                 for cc in range(8)])
                            sc.wait("v", "t", mm)
                            sc.inc("v", lambda e, _n=nn: e.tensor_copy(
                                ktmp[:, _n * 512:(_n + 1) * 512], psk[_n][:]))
                        for nn in range(4):
                            mm = mm_accum(
                                sc, psv[nn][:],
                                [(wv_s[cc][:, dc * 128:(dc + 1) * 128],
                                  xv_t[cc][:, nn * 512:(nn + 1) * 512])
                                 for cc in range(8)])
                            sc.wait("v", "t", mm)
                            sc.inc(
                                "v", lambda e, _n=nn: e.tensor_tensor(
                                    kvb[:, _n * 512:(_n + 1) * 512], psv[_n][:],
                                    ktmp[:, _n * 512:(_n + 1) * 512], OP.mult))
                        s_d = sc.inc("s", lambda e, _dc=dc: e.activation(
                            dcol[:, _dc:_dc + 1], tds[:, _dc:_dc + 1], AF.Exp))
                        sc.wait("v", "s", s_d)
                        sc.inc("v", lambda e, _dc=dc: e.tensor_scalar(
                            drep[:], kvb[:], 0.0, dcol[:, _dc:_dc + 1],
                            OP.mult, OP.add))
                        sc.inc(
                            "v", lambda e, _dc=dc: e.tensor_tensor_scan(
                                st_buf[:], drep[:], kvb[:],
                                carry[:, _dc:_dc + 1], OP.mult, OP.add))
                        prev_done = own_out(dc, st_buf, kvb, rld, obuf)

        scan_half_own()
        stMix.close()
        stXN.close()
        sc.barrier()

        stON = contextlib.ExitStack()
        o_n = [stON.enter_context(cx.SBUF(f"o_n{i}", [128, TH], BF16))
               for i in range(8)]
        with contextlib.ExitStack() as ph:
            pb = lambda n, shp, dt: ph.enter_context(cx.SBUFR(n, shp, dt))
            ob = [pb(f"ob{i}", [128, TH], BF16) for i in range(2)]
            sqb = pb("sqb", [128, TH], F32)
            srow = pb("srow", [1, TH], F32)
            s2row = pb("s2row", [1, TH], F32)
            rrow = pb("rrow", [1, TH], BF16)
            crow = pb("crow", [1, TH], BF16)
            mrow = pb("mrow", [1, TH], F32)
            vrow = pb("vrow", [1, TH], F32)
            m2row = pb("m2row", [1, TH], F32)
            tmpx = pb("tmpx", [128, 1024], F32)
            onescol_f = pb("onescol_f2", [128, 1], F32)
            onescol_b = pb("onescol_b2", [128, 1], BF16)
            onesb_r2 = pb("onesb_r3", [1, 128], BF16)
            epsc = pb("epsc2", [1, 1], F32)
            sc.inc("g", lambda e: e.memset(onescol_b[:], 1.0))
            sc.inc("g", lambda e: e.memset(onescol_f[:], 1.0))
            sc.inc("g", lambda e: e.memset(onesb_r2[:], 1.0))
            gc = sc.inc("g", lambda e: e.memset(epsc[:], 1e-5))
            sc.wait("t", "g", gc)
            sc.wait("s", "g", gc)
            with contextlib.ExitStack() as pp:
                ps_s = pp.enter_context(cx.PSUM("ps_s", [1, 2048], F32))
                ps_s2 = pp.enter_context(cx.PSUM("ps_s2", [1, 2048], F32))
                prev_t = None
                lds = {}
                sqs = {}
                lds[0] = sc.dma("y", ob[0][:], o_scr.ap()[0:128, :])
                for i in range(8):
                    if i + 1 < 8:
                        if prev_t and i >= 1:
                            sc.wait("y", "t", prev_t)
                        sc.wait("y", "s", sqs.get(i - 1))
                        lds[i + 1] = sc.dma(
                            "y", ob[(i + 1) % 2][:],
                            o_scr.ap()[(i + 1) * 128:(i + 2) * 128, :])
                    sc.wait("s", "y", lds[i])
                    if prev_t:
                        sc.wait("s", "t", prev_t)
                    sqs[i] = s_sq = sc.inc("s", lambda e, _i=i: e.activation(
                        sqb[:], ob[_i % 2][:], AF.Square))
                    sc.wait("t", "s", s_sq)
                    sc.wait("t", "y", lds[i])
                    for nn in range(4):
                        sl = slice(nn * 512, nn * 512 + 512)
                        sc.raw("t", lambda e, _i=i, _sl=sl: e.matmul(
                            ps_s[:, _sl], onescol_b[:], ob[_i % 2][:, _sl],
                            start=(_i == 0), stop=(_i == 7)))
                        if nn < 3:
                            sc.raw("t", lambda e, _i=i, _sl=sl: e.matmul(
                                ps_s2[:, _sl], onescol_f[:], sqb[:, _sl],
                                start=(_i == 0), stop=(_i == 7)))
                        else:
                            prev_t = sc.inc(
                                "t", lambda e, _i=i, _sl=sl: e.matmul(
                                    ps_s2[:, _sl], onescol_f[:], sqb[:, _sl],
                                    start=(_i == 0), stop=(_i == 7)))
                sc.wait("v", "t", prev_t)
                sc.inc("v", lambda e: e.tensor_copy(srow[:], ps_s[:]))
                sc.inc("v", lambda e: e.tensor_copy(s2row[:], ps_s2[:]))
            sc.inc("v", lambda e: e.tensor_scalar(
                mrow[:], srow[:], 1.0 / C, None, OP.mult))
            sc.inc("v", lambda e: e.tensor_scalar(
                vrow[:], s2row[:], 1.0 / C, None, OP.mult))
            sc.inc("v", lambda e: e.tensor_tensor(
                m2row[:], mrow[:], mrow[:], OP.mult))
            v_var = sc.inc("v", lambda e: e.tensor_tensor(
                vrow[:], vrow[:], m2row[:], OP.subtract))
            sc.wait("s", "v", v_var)
            sc.inc("s", lambda e: e.activation(vrow[:], vrow[:], AF.Ln,
                                               bias=epsc[:]))
            s_r = sc.inc("s", lambda e: e.activation(rrow[:], vrow[:],
                                                     AF.Exp, scale=-0.5))
            sc.wait("v", "s", s_r)
            sc.inc("v", lambda e: e.tensor_tensor(m2row[:], mrow[:], rrow[:],
                                                  OP.mult))
            v_cr = sc.inc("v", lambda e: e.tensor_scalar(
                crow[:], m2row[:], -1.0, None, OP.mult))
            sc.barrier()
            for quar in range(2):
                q0 = quar * 1024
                with contextlib.ExitStack() as pp:
                    ps_a = pp.enter_context(cx.PSUM("ps_a", [128, 1024], F32))
                    ps_c = pp.enter_context(cx.PSUM("ps_c", [128, 1024], F32))
                    sc.wait("t", "v", v_cr)
                    t_bc = None
                    for nn in range(2):
                        sl = slice(nn * 512, nn * 512 + 512)
                        sg = slice(q0 + nn * 512, q0 + nn * 512 + 512)
                        sc.raw("t", lambda e, _sl=sl, _sg=sg: e.matmul(
                            ps_a[:, _sl], onesb_r2[:], rrow[0:1, _sg],
                            start=True, stop=True))
                        t_bc = sc.inc("t", lambda e, _sl=sl, _sg=sg: e.matmul(
                            ps_c[:, _sl], onesb_r2[:], crow[0:1, _sg],
                            start=True, stop=True))
                    sc.wait("v", "t", t_bc)
                    v_norm = None
                    ldq = {}
                    if quar == 0:
                        vpar = {0: None, 1: None}
                    for i in range(8):
                        sc.wait("y", "v", vpar[i % 2])
                        ldq[i] = sc.dma("y", ob[i % 2][:, 0:1024],
                                        o_scr.ap()[i * 128:(i + 1) * 128,
                                                   q0:q0 + 1024])
                        sc.wait("v", "y", ldq[i])
                        sc.inc("v", lambda e, _i=i: e.tensor_tensor(
                            tmpx[:], ob[_i % 2][:, 0:1024], ps_a[:], OP.mult))
                        sc.inc("v", lambda e, _i=i: e.tensor_tensor(
                            tmpx[:], tmpx[:], ps_c[:], OP.add))
                        vpar[i % 2] = v_norm = sc.inc(
                            "v", lambda e, _i=i, _q=slice(q0, q0 + 1024):
                            e.tensor_scalar(
                                o_n[_i][:, _q], tmpx[:], lng[:, _i:_i + 1],
                                lnb[:, _i:_i + 1], OP.mult, OP.add))
                    sc.wait("t", "v", v_norm)
        sc.barrier()

        with contextlib.ExitStack() as ph:
            wo_s = [ph.enter_context(cx.SBUFR(f"wo_s{i}", [128, C], BF16))
                    for i in range(8)]
            sout = ph.enter_context(cx.SBUFR("sout", [128, TH], F32))
            for i in range(8):
                lw = sc.dma("g", wo_s[i][:], woT[i * 128:(i + 1) * 128, :])
            sc.wait("t", "gd", lw)
            store = {}

            def evac_s(mo, nn, ps, mm):
                sc.wait("v", "t", mm)
                sc.wait("v", "y", store.get(mo))
                c = sc.inc("v", lambda e, _nn=nn: e.tensor_copy(
                    sout[:, _nn * 512:(_nn + 1) * 512], ps[:]))
                if nn == 3:
                    sc.wait("y", "v", c)
                    store[mo + 1] = sc.dma(
                        "y", s1T[mo * 128:(mo + 1) * 128, :], sout[:])
                return ("v", c)

            emit_proj_cm(cx, wo_s, o_n, TH, range(8), evac_s)
            sc.wait("v", "y", sc.cnt["y"])
        stON.close()
        sc.wait("y", "y", sc.cnt["y"])
        sc.finalize(es)
    return cx.nc


def prep_L1a_inputs(inputs, x1):
    wk = np.ascontiguousarray(f32c(inputs["L1_attn_Wk"]).T)
    wv = np.ascontiguousarray(f32c(inputs["L1_attn_Wv"]).T)
    wr = np.ascontiguousarray(f32c(inputs["L1_attn_Wr"]).T)
    wo = np.ascontiguousarray(f32c(inputs["L1_attn_Wo"]).T)

    def col8(v):
        return np.ascontiguousarray(f32c(v).reshape(8, 128).T)

    tmk2 = col8(inputs["L1_attn_time_mix_k"])
    tmv2 = col8(inputs["L1_attn_time_mix_v"])
    tmr2 = col8(inputs["L1_attn_time_mix_r"])
    td2 = col8(inputs["L1_attn_time_decay"])
    tf2 = col8(inputs["L1_attn_time_first"])
    lng2 = col8(inputs["L1_attn_ln_g"])
    lnb2 = col8(inputs["L1_attn_ln_b"])
    wr8 = f32c(inputs["L1_ffn_Wr"])
    br8 = f32c(inputs["L1_ffn_br"]).reshape(1, 8)
    idn = np.eye(128, dtype=np.float32)
    maps = []
    for core in range(NC8):
        b, th = core // 2, core % 2
        xT = np.ascontiguousarray(x1[b].T)
        rot = np.concatenate([xT[:, th * TH:(th + 1) * TH],
                              xT[:, (1 - th) * TH:(2 - th) * TH]], axis=1)
        maps.append({
            "x1T": np.ascontiguousarray(rot),
            "qoff": np.array([[th * TH]], dtype=np.float32),
            "wkT": wk, "wvT": wv, "wrT": wr, "woT": wo,
            "tmk2": tmk2, "tmv2": tmv2, "tmr2": tmr2,
            "td2": td2, "tf2": tf2, "lng2": lng2, "lnb2": lnb2,
            "wr8": wr8, "br8": br8, "idn": idn,
        })
    return maps


def run_L1a(inputs, x1):
    if "L1a" not in _builder_cache:
        _builder_cache["L1a"] = build_L1a()
    nc = _builder_cache["L1a"]
    maps = prep_L1a_inputs(inputs, x1)
    res = run_bass_kernel_spmd(nc, maps, core_ids=list(range(NC8)))
    return res.results, maps


def emit_routing_slots(cx, m1d, m2d, p1d, p2d, es_right, nrows=L):
    sc, nc = cx.sc, cx.nc
    nch = nrows // 128
    pb = lambda n, shp, dt: es_right.enter_context(cx.SBUFR(n, shp, dt))
    pos1 = pb("pos1", [128, nch], F32)
    pos2 = pb("pos2", [128, nch], F32)
    e1t = pb("e1t", [128, nch], F32)
    e2t = pb("e2t", [128, nch], F32)
    k1t = pb("k1t", [128, nch], F32)
    k2t = pb("k2t", [128, nch], F32)
    p1t = pb("p1t", [128, nch], F32)
    p2t = pb("p2t", [128, nch], F32)
    with contextlib.ExitStack() as ph:
        qb = lambda n, shp, dt: ph.enter_context(cx.SBUFR(n, shp, dt))
        m1b = qb("m1b", [128, 8], BF16)
        m2b = qb("m2b", [128, 8], BF16)
        m1f = qb("m1f", [128, 8], F32)
        m2f = qb("m2f", [128, 8], F32)
        msum = qb("msum", [128, 8], BF16)
        sl_t = qb("sl_t", [128, 128], BF16)
        iotr = qb("iotr", [128, 128], I32)
        iotc = qb("iotc", [128, 1], I32)
        iotcf = qb("iotcf", [128, 1], F32)
        slf = qb("slf", [128, 128], F32)
        eid_i = qb("eid_i", [128, 8], I32)
        eidx = qb("eidx", [128, 8], F32)
        carry_b = qb("carry_b", [1, 8], BF16)
        carry_f = qb("carry_f", [1, 8], F32)
        onescol_f = qb("onescol_f", [128, 1], F32)
        onesb_r = qb("onesb_r", [1, 128], BF16)
        onesf_r = qb("onesf_r", [1, 128], F32)
        post = qb("post", [128, 8], F32)
        tmp8 = qb("tmp8", [128, 8], F32)
        sc.inc("g", lambda e: e.iota(iotr[:], pattern=[[1, 128]], base=0,
                                     channel_multiplier=0))
        sc.inc("g", lambda e: e.iota(iotc[:], pattern=[[0, 1]], base=0,
                                     channel_multiplier=1))
        sc.inc("g", lambda e: e.iota(eid_i[:], pattern=[[1, 8]], base=0,
                                     channel_multiplier=0))
        sc.inc("g", lambda e: e.memset(onescol_f[:], 1.0))
        sc.inc("g", lambda e: e.memset(onesb_r[:], 1.0))
        gc = sc.inc("g", lambda e: e.memset(onesf_r[:], 1.0))
        sc.wait("v", "g", gc)
        sc.inc("v", lambda e: e.tensor_copy(iotcf[:], iotc[:]))
        sc.inc("v", lambda e: e.tensor_copy(slf[:], iotr[:]))
        sc.inc("v", lambda e: e.tensor_scalar(slf[:], slf[:], iotcf[:], None,
                                              OP.is_gt))
        sc.inc("v", lambda e: e.tensor_copy(sl_t[:], slf[:]))
        v_c = sc.inc("v", lambda e: e.tensor_copy(eidx[:], eid_i[:]))
        sc.wait("t", "v", v_c)
        with contextlib.ExitStack() as pp:
            ps_pos = pp.enter_context(cx.PSUM("ps_pos", [128, 8], F32))
            ps_tot = pp.enter_context(cx.PSUM("ps_tot", [1, 8], F32))
            prev_v = None
            for i in range(nch):
                ld1 = sc.dma("y", m1b[:], m1d[i * 128:(i + 1) * 128, :])
                ld2 = sc.dma("y", m2b[:], m2d[i * 128:(i + 1) * 128, :])
                ld3 = sc.dma("y", p1t[:, i:i + 1], p1d[i * 128:(i + 1) * 128, :])
                ld4 = sc.dma("y", p2t[:, i:i + 1], p2d[i * 128:(i + 1) * 128, :])
                sc.wait("v", "y", ld4)
                sc.inc("v", lambda e: e.tensor_copy(m1f[:], m1b[:]))
                sc.inc("v", lambda e: e.tensor_copy(m2f[:], m2b[:]))
                v_ms = sc.inc("v", lambda e: e.tensor_tensor(
                    msum[:], m1b[:], m2b[:], OP.add))
                sc.wait("t", "v", v_ms)
                if i == 0:
                    t_pos = sc.inc("t", lambda e: e.matmul(
                        ps_pos[:], sl_t[:], msum[:], start=True, stop=True))
                else:
                    sc.raw("t", lambda e: e.matmul(ps_pos[:], sl_t[:],
                                                   msum[:], start=True,
                                                   stop=False))
                    t_pos = sc.inc("t", lambda e: e.matmul(
                        ps_pos[:], onesf_r[:], carry_f[:], start=False,
                        stop=True))
                sc.wait("v", "t", t_pos)
                for (mk, mf, pos, et, kt) in ((m1b, m1f, pos1, e1t, k1t),
                                              (m2b, m2f, pos2, e2t, k2t)):
                    sc.inc("v", lambda e, _m=mf: e.tensor_tensor(
                        tmp8[:], _m[:], ps_pos[:], OP.mult))
                    sc.inc("v", lambda e, _p=pos, _i=i: e.reduce_sum(
                        _p[:, _i:_i + 1], tmp8[:], axis=AX))
                    sc.inc("v", lambda e, _m=mf: e.tensor_tensor(
                        tmp8[:], _m[:], eidx[:], OP.mult))
                    sc.inc("v", lambda e, _e=et, _i=i: e.reduce_sum(
                        _e[:, _i:_i + 1], tmp8[:], axis=AX))
                    sc.inc("v", lambda e, _p=pos, _k=kt, _i=i:
                           e.tensor_single_scalar(_k[:, _i:_i + 1],
                                                  _p[:, _i:_i + 1], 639.5,
                                                  OP.is_lt))
                sc.raw("t", lambda e: e.matmul(
                    ps_tot[:], onescol_f[:], m1f[:], start=True, stop=False))
                t_tot = sc.inc("t", lambda e: e.matmul(
                    ps_tot[:], onescol_f[:], m2f[:], start=False, stop=True))
                sc.wait("v", "t", t_tot)
                if i == 0:
                    sc.inc("v", lambda e: e.tensor_copy(carry_f[:],
                                                        ps_tot[:]))
                else:
                    sc.inc("v", lambda e: e.tensor_tensor(
                        carry_f[:], carry_f[:], ps_tot[:], OP.add))
                prev_v = sc.cnt["v"]
                sc.wait("t", "v", prev_v)
                sc.wait("y", "v", prev_v)
    return pos1, pos2, e1t, e2t, k1t, k2t, p1t, p2t


def build_L1moe():
    cx = Ctx()
    nc, sc = cx.nc, cx.sc
    xtok = nc.dram_tensor("xtok", [L, C], BF16, kind="ExternalInput").ap()
    m1d = nc.dram_tensor("m1d", [L, 8], BF16, kind="ExternalInput").ap()
    m2d = nc.dram_tensor("m2d", [L, 8], BF16, kind="ExternalInput").ap()
    p1d = nc.dram_tensor("p1d", [L, 1], F32, kind="ExternalInput").ap()
    p2d = nc.dram_tensor("p2d", [L, 1], F32, kind="ExternalInput").ap()
    ebased = nc.dram_tensor("ebase", [1, 1], F32, kind="ExternalInput").ap()
    w1s = nc.dram_tensor("w1s", [4 * C, 4096], F32, kind="ExternalInput").ap()
    b1s = nc.dram_tensor("b1s", [128, 128], F32, kind="ExternalInput").ap()
    w2s = nc.dram_tensor("w2s", [4 * 4096, C], F32, kind="ExternalInput").ap()
    b2s = nc.dram_tensor("b2s", [128, 32], F32, kind="ExternalInput").ap()
    idn = nc.dram_tensor("idn", [128, 128], F32, kind="ExternalInput").ap()
    ybuf = nc.dram_tensor("ybuf", [2560, C], BF16, kind="ExternalOutput").ap()
    buf_scr = nc.dram_tensor("buf_scr", [2560, C], BF16)

    CAP = 640
    es = contextlib.ExitStack()
    with es:
        sbL = lambda n, shp, dt: es.enter_context(cx.SBUF(n, shp, dt))
        ident = sbL("ident", [128, 128], BF16)
        ebcol = sbL("ebcol", [128, 1], F32)
        b1sb = sbL("b1sb", [128, 128], F32)
        b2sb = sbL("b2sb", [128, 32], F32)
        ld = sc.dma("g", ident[:], idn)
        ld = sc.dma("y", b1sb[:], b1s)
        ld = sc.dma("y", b2sb[:], b2s)
        ldq = sc.dma("y", ebcol[0:1, :], ebased)
        onesb_r = sbL("onesb_r", [1, 128], BF16)
        gc = sc.inc("g", lambda e: e.memset(onesb_r[:], 1.0))
        with cx.PSUM("ps_eb", [128, 1], F32) as ps_eb:
            eb_b = sbL("eb_b", [1, 1], BF16)
            sc.wait("v", "y", ldq)
            v_c = sc.inc("v", lambda e: e.tensor_copy(eb_b[:], ebcol[0:1, :]))
            sc.wait("t", "v", v_c)
            sc.wait("t", "g", gc)
            t_b = sc.inc("t", lambda e: e.matmul(ps_eb[:], onesb_r[:], eb_b[:],
                                                 start=True, stop=True))
            sc.wait("v", "t", t_b)
            sc.inc("v", lambda e: e.tensor_copy(ebcol[:], ps_eb[:]))
        sc.barrier()

        stRT = contextlib.ExitStack()
        rt = emit_routing_slots(cx, m1d, m2d, p1d, p2d, stRT, L)
        pos1, pos2, e1t, e2t, k1t, k2t, p1t, p2t = rt
        sc.barrier()

        with contextlib.ExitStack() as ph:
            pb = lambda n, shp, dt: ph.enter_context(cx.SBUFR(n, shp, dt))
            zt = pb("zt", [128, C], BF16)
            xrow = pb("xrow", [128, C], BF16)
            idx1 = pb("idx1", [128, 1], I32)
            idxf = pb("idxf", [128, 1], F32)
            t1 = pb("t1", [128, 1], F32)
            t2 = pb("t2", [128, 1], F32)
            gz = sc.inc("g", lambda e: e.memset(zt[:], 0.0))
            sc.wait("y", "g", gz)
            prev = None
            for i in range(20):
                prev = sc.dma("y", buf_scr.ap()[i * 128:(i + 1) * 128, :],
                              zt[:])
            sc.wait("g", "y", prev)
            with nc.gpsimd.register("bc") as bc:
                sc.raw("g", lambda e: e.reg_mov(bc, 2559))
                prev_sc = None
                for i in range(L // 128):
                    ldx = sc.dma("y", xrow[:], xtok[i * 128:(i + 1) * 128, :])
                    for (pos, et, kt) in ((pos1, e1t, k1t), (pos2, e2t, k2t)):
                        sc.inc("v", lambda e, _e=et, _i=i: e.tensor_tensor(
                            t1[:], _e[:, _i:_i + 1], ebcol[:], OP.subtract))
                        sc.inc("v", lambda e: e.tensor_single_scalar(
                            t2[:], t1[:], -0.5, OP.is_gt))
                        sc.inc("v", lambda e, _k=kt, _i=i: e.tensor_tensor(
                            t2[:], t2[:], _k[:, _i:_i + 1], OP.mult))
                        sc.inc("v", lambda e: e.tensor_scalar(
                            idxf[:], t1[:], 3.5, None, OP.is_lt))
                        sc.inc("v", lambda e: e.tensor_tensor(
                            t2[:], t2[:], idxf[:], OP.mult))
                        sc.inc("v", lambda e, _p=pos, _i=i: e.tensor_scalar(
                            idxf[:], t1[:], float(CAP), None, OP.mult))
                        sc.inc("v", lambda e, _p=pos, _i=i: e.tensor_tensor(
                            idxf[:], idxf[:], _p[:, _i:_i + 1], OP.add))
                        sc.inc("v", lambda e: e.tensor_scalar(
                            idxf[:], idxf[:], -1e6, None, OP.add))
                        sc.inc("v", lambda e: e.tensor_tensor(
                            idxf[:], idxf[:], t2[:], OP.mult))
                        sc.inc("v", lambda e: e.tensor_scalar(
                            idxf[:], idxf[:], 1e6, None, OP.add))
                        v_i = sc.inc("v", lambda e: e.tensor_copy(idx1[:],
                                                                  idxf[:]))
                        sc.wait("g", "v", v_i)
                        sc.wait("g", "y", ldx)
                        prev_sc = sc.inc(
                            "g", lambda e: e.indirect_dma_start(
                                buf_scr.ap(),
                                bass.IndirectOffsetOnAxis(ap=idx1[:], axis=0),
                                xrow[:], None, bounds_check=bc,
                                oob_is_err=False), 16, semkey="gd")
                        sc.wait("v", "gd", prev_sc)
                    sc.wait("y", "gd", prev_sc)
        sc.barrier()
        stRT.close()

        stBT = contextlib.ExitStack()
        bufT = [stBT.enter_context(cx.SBUF(f"bufT{i}", [128, 2560], BF16))
                for i in range(8)]
        for i in range(8):
            lw = sc.dma("y", bufT[i][:],
                        buf_scr.ap()[:, i * 128:(i + 1) * 128], transpose=True)
        sc.barrier()
        with contextlib.ExitStack() as ph:
            pb = lambda n, shp, dt: ph.enter_context(cx.SBUFR(n, shp, dt))
            wbuf = [pb(f"wbuf{i}", [128, 2048], BF16) for i in range(16)]
            hT = [pb(f"hT{i}", [128, CAP], BF16) for i in range(32)]
            yT = [pb(f"yT{i}", [128, CAP], BF16) for i in range(8)]
            ytk = pb("ytk", [128, C], BF16)
            stPS = contextlib.ExitStack()
            pool = [stPS.enter_context(cx.PSUM(f"pool{j}", [128, CAP], F32))
                    for j in range(3)]
            pool_last = {}
            prev_y_done = None
            prev_h_done = None
            for e_i in range(4):
                sc.wait("g", "t", prev_h_done)
                prev_lw = None
                for fh in range(2):
                    for ccx in range(8):
                        prev_lw = sc.dma(
                            "g", wbuf[fh * 8 + ccx][:],
                            w1s[e_i * C + ccx * 128:
                                e_i * C + ccx * 128 + 128,
                                fh * 2048:(fh + 1) * 2048])
                sc.wait("t", "gd", prev_lw)
                if True:
                    for mo in range(32):
                        fh, moh = mo // 16, mo % 16
                        j = mo % 3
                        if j in pool_last:
                            sc.wait("t", pool_last[j][0], pool_last[j][1])
                        mm = None
                        for nn in range(2):
                            npart = 512 if nn == 0 else 128
                            mm = mm_accum(
                                sc, pool[j][:, nn * 512:nn * 512 + npart],
                                [(wbuf[fh * 8 + ccx][:,
                                       moh * 128:(moh + 1) * 128],
                                  bufT[ccx][:, e_i * CAP + nn * 512:
                                            e_i * CAP + nn * 512 + npart])
                                 for ccx in range(8)])
                        sc.wait("s", "t", mm)
                        pool_last[j] = ("s", sc.inc(
                            "s", lambda e, _j=j, _mo=mo, _e=e_i: e.activation(
                                hT[_mo][:], pool[_j][:], AF.Relu,
                                bias=b1sb[:, _e * 32 + _mo:
                                          _e * 32 + _mo + 1])))
                    s_h_done = None
                    for j, (ek, c) in pool_last.items():
                        if ek == "s":
                            sc.wait("t", "s", c)
                            s_h_done = max(s_h_done or 0, c)
                sc.wait("g", "s", s_h_done)
                sc.wait("g", "t", sc.cnt["t"])
                prev_lw = None
                for fc in range(32):
                    prev_lw = sc.dma(
                        "g", wbuf[fc // 2][:, (fc % 2) * 1024:
                                           (fc % 2) * 1024 + 1024],
                        w2s[e_i * 4096 + fc * 128:
                            e_i * 4096 + fc * 128 + 128, :])
                sc.wait("t", "gd", prev_lw)
                if True:
                    for mo in range(8):
                        j = mo % 3
                        if j in pool_last:
                            sc.wait("t", pool_last[j][0], pool_last[j][1])
                        mm = None
                        for nn in range(2):
                            npart = 512 if nn == 0 else 128
                            mm = mm_accum(
                                sc, pool[j][:, nn * 512:nn * 512 + npart],
                                [(wbuf[fc // 2][:, (fc % 2) * 1024 +
                                                mo * 128:
                                                (fc % 2) * 1024 +
                                                (mo + 1) * 128],
                                  hT[fc][:, nn * 512:nn * 512 + npart])
                                 for fc in range(32)], )
                        sc.wait("v", "t", mm)
                        pool_last[j] = ("v", sc.inc(
                            "v", lambda e, _j=j, _mo=mo, _e=e_i:
                            e.tensor_scalar(
                                yT[_mo][:], pool[_j][:],
                                b2sb[:, _e * 8 + _mo:_e * 8 + _mo + 1],
                                None, OP.add)))
                    v_y_done = None
                    for j, (ek, c) in pool_last.items():
                        if ek == "v":
                            sc.wait("t", "v", c)
                            v_y_done = max(v_y_done or 0, c)
                prev_h_done = sc.cnt["t"]
                with cx.PSUM("ps_ty", [128, C], BF16) as ps_ty:
                    prev_cp = None
                    prev_st = None
                    for sblk in range(5):
                        sc.wait("t", "v", prev_cp)
                        t_tr = None
                        for mo in range(8):
                            t_tr = sc.inc(
                                "t", lambda e, _m=mo, _s=sblk: e.transpose(
                                    ps_ty[:, _m * 128:(_m + 1) * 128],
                                    yT[_m][:, _s * 128:(_s + 1) * 128],
                                    ident[:]))
                        sc.wait("v", "t", t_tr)
                        sc.wait("v", "y", prev_st)
                        prev_cp = sc.inc("v", lambda e: e.tensor_copy(
                            ytk[:], ps_ty[:]))
                        sc.wait("y", "v", prev_cp)
                        prev_st = sc.dma(
                            "y", ybuf[e_i * CAP + sblk * 128:
                                      e_i * CAP + (sblk + 1) * 128, :],
                            ytk[:])
                sc.wait("v", "y", prev_st)
            stPS.close()
        stBT.close()
        sc.wait("y", "y", sc.cnt["y"])
        sc.finalize(es)
    return cx.nc



def build_L1comb():
    cx = Ctx()
    nc, sc = cx.nc, cx.sc
    yb = nc.dram_tensor("yb", [5120, C], BF16, kind="ExternalInput").ap()
    m1d = nc.dram_tensor("m1d", [L, 8], BF16, kind="ExternalInput").ap()
    m2d = nc.dram_tensor("m2d", [L, 8], BF16, kind="ExternalInput").ap()
    p1d = nc.dram_tensor("p1d", [L, 1], F32, kind="ExternalInput").ap()
    p2d = nc.dram_tensor("p2d", [L, 1], F32, kind="ExternalInput").ap()
    toffd = nc.dram_tensor("toff", [1, 1], F32, kind="ExternalInput").ap()
    x1T = nc.dram_tensor("x1Th", [C, TH], F32, kind="ExternalInput").ap()
    s1T = nc.dram_tensor("s1Th", [C, TH], F32, kind="ExternalInput").ap()
    idn = nc.dram_tensor("idn", [128, 128], F32, kind="ExternalInput").ap()
    x2T = nc.dram_tensor("x2T", [C, TH], F32, kind="ExternalOutput").ap()

    CAP = 640
    es = contextlib.ExitStack()
    with es:
        sbL = lambda n, shp, dt: es.enter_context(cx.SBUF(n, shp, dt))
        ident = sbL("ident", [128, 128], BF16)
        tcol = sbL("tcol", [128, 1], F32)
        ld = sc.dma("g", ident[:], idn)
        ldq = sc.dma("y", tcol[0:1, :], toffd)
        onesb_r = sbL("onesb_r", [1, 128], BF16)
        gc = sc.inc("g", lambda e: e.memset(onesb_r[:], 1.0))
        with cx.PSUM("ps_eb", [128, 1], F32) as ps_eb:
            eb_b = sbL("eb_b", [1, 1], BF16)
            sc.wait("v", "y", ldq)
            v_c = sc.inc("v", lambda e: e.tensor_copy(eb_b[:], tcol[0:1, :]))
            sc.wait("t", "v", v_c)
            sc.wait("t", "g", gc)
            t_b = sc.inc("t", lambda e: e.matmul(ps_eb[:], onesb_r[:],
                                                 eb_b[:], start=True,
                                                 stop=True))
            sc.wait("v", "t", t_b)
            sc.inc("v", lambda e: e.tensor_copy(tcol[:], ps_eb[:]))
        sc.barrier()

        stRT = contextlib.ExitStack()
        rt = emit_routing_slots(cx, m1d, m2d, p1d, p2d, stRT, L)
        pos1, pos2, e1t, e2t, k1t, k2t, p1t, p2t = rt
        sc.barrier()

        g_scr = nc.dram_tensor("g_scr2", [L, C], BF16)
        with contextlib.ExitStack() as ph:
            pb = lambda n, shp, dt: ph.enter_context(cx.SBUFR(n, shp, dt))
            y1 = pb("y1", [128, C], BF16)
            y2 = pb("y2", [128, C], BF16)
            idx1 = pb("idx1", [128, 1], I32)
            idx2 = pb("idx2", [128, 1], I32)
            f1 = pb("f1", [128, 1], F32)
            f2 = pb("f2", [128, 1], F32)
            idxf = pb("idxf", [128, 1], F32)
            t1c = pb("t1c", [128, 1], F32)
            moet = pb("moet", [128, C], F32)
            gtk = pb("gtk", [128, C], BF16)
            prev_g = None
            for i in range(L // 128):
                vi = None
                for (pos, et, kt, pt, idx, ff) in (
                        (pos1, e1t, k1t, p1t, idx1, f1),
                        (pos2, e2t, k2t, p2t, idx2, f2)):
                    sc.inc("v", lambda e, _p=pos, _i=i: e.tensor_single_scalar(
                        idxf[:], _p[:, _i:_i + 1], float(CAP - 1), OP.min))
                    sc.inc("v", lambda e, _e=et, _i=i: e.scalar_tensor_tensor(
                        idxf[:], _e[:, _i:_i + 1], float(CAP), idxf[:],
                        OP.mult, OP.add))
                    sc.inc("v", lambda e, _x=idx: e.tensor_copy(_x[:],
                                                                idxf[:]))
                    vi = sc.inc("v", lambda e, _k=kt, _pt=pt, _f=ff, _i=i:
                                e.tensor_tensor(_f[:], _pt[:, _i:_i + 1],
                                                _k[:, _i:_i + 1], OP.mult))
                sc.wait("g", "v", vi)
                g1 = sc.inc("g", lambda e: e.indirect_dma_start(
                    y1[:], None, yb, bass.IndirectOffsetOnAxis(ap=idx1[:],
                                                               axis=0)),
                    16, semkey="gd")
                g2 = sc.inc("g", lambda e: e.indirect_dma_start(
                    y2[:], None, yb, bass.IndirectOffsetOnAxis(ap=idx2[:],
                                                               axis=0)),
                    16, semkey="gd")
                sc.wait("v", "gd", g2)
                sc.inc("v", lambda e: e.tensor_scalar(
                    moet[:], y1[:], f1[:], None, OP.mult))
                v_m = sc.inc("v", lambda e: e.scalar_tensor_tensor(
                    moet[:], y2[:], f2[:], moet[:], OP.mult, OP.add))
                sc.wait("s", "v", v_m)
                sc.wait("s", "y", prev_g)
                s_g = sc.inc("s", lambda e: e.activation(gtk[:], moet[:],
                                                         AF.Silu))
                sc.wait("y", "s", s_g)
                prev_g = sc.dma("y", g_scr.ap()[i * 128:(i + 1) * 128, :],
                                gtk[:])
                sc.wait("v", "y", prev_g)
        sc.barrier()

        with contextlib.ExitStack() as ph:
            pb = lambda n, shp, dt: ph.enter_context(cx.SBUFR(n, shp, dt))
            grow = pb("grow", [128, C], BF16)
            idxr = pb("idxr", [128, 1], I32)
            idxf2 = pb("idxf2", [128, 1], F32)
            iotc = pb("iotc", [128, 1], I32)
            iotcf = pb("iotcf", [128, 1], F32)
            t1c = pb("t1c", [128, 1], F32)
            gT = [pb(f"gT{i}", [128, TH], BF16) for i in range(8)]
            x1b = pb("x1b", [128, TH], F32)
            s1b = pb("s1b", [128, TH], F32)
            tmpc = pb("tmpc", [128, 512], F32)
            xob = pb("xob", [128, TH], F32)
            gc = sc.inc("g", lambda e: e.iota(iotc[:], pattern=[[0, 1]],
                                              base=0, channel_multiplier=1))
            sc.wait("v", "g", gc)
            sc.inc("v", lambda e: e.tensor_copy(iotcf[:], iotc[:]))
            v_t1 = sc.inc("v", lambda e: e.tensor_tensor(
                t1c[:], iotcf[:], tcol[:], OP.add))
            with cx.PSUM("ps_tg", [128, C], BF16) as ps_tg:
                prev_cp = None
                prev_tr = None
                for tc in range(16):
                    sc.wait("v", "gd", prev_tr)
                    sc.inc("v", lambda e, _tc=tc: e.tensor_single_scalar(
                        idxf2[:], t1c[:], float(_tc * 128), OP.add))
                    v_i = sc.inc("v", lambda e: e.tensor_copy(idxr[:],
                                                              idxf2[:]))
                    sc.wait("g", "v", v_i)
                    sc.wait("g", "t", prev_tr_t if tc else None)
                    prev_tr = sc.inc("g", lambda e: e.indirect_dma_start(
                        grow[:], None, g_scr.ap(),
                        bass.IndirectOffsetOnAxis(ap=idxr[:], axis=0)),
                        16, semkey="gd")
                    sc.wait("t", "gd", prev_tr)
                    sc.wait("t", "v", prev_cp)
                    t_tr = None
                    for ccx in range(8):
                        t_tr = sc.inc(
                            "t", lambda e, _c=ccx: e.transpose(
                                ps_tg[:, _c * 128:(_c + 1) * 128],
                                grow[:, _c * 128:(_c + 1) * 128], ident[:]))
                    prev_tr_t = t_tr
                    sc.wait("v", "t", t_tr)
                    prev_cp = None
                    for ccx in range(8):
                        prev_cp = sc.inc(
                            "v", lambda e, _c=ccx, _tc=tc: e.tensor_copy(
                                gT[_c][:, _tc * 128:(_tc + 1) * 128],
                                ps_tg[:, _c * 128:(_c + 1) * 128]))
            prev_st = None
            for ccx in range(8):
                sc.wait("y", "v", prev_st_v if ccx else None)
                ld1 = sc.dma("y", x1b[:], x1T[ccx * 128:(ccx + 1) * 128, :])
                ld2 = sc.dma("y", s1b[:], s1T[ccx * 128:(ccx + 1) * 128, :])
                sc.wait("v", "y", ld2)
                vv = None
                for nn in range(4):
                    sl = slice(nn * 512, (nn + 1) * 512)
                    sc.inc("v", lambda e, _c=ccx, _sl=sl: e.tensor_tensor(
                        tmpc[:], s1b[:, _sl], gT[_c][:, _sl], OP.mult))
                    vv = sc.inc("v", lambda e, _sl=sl: e.tensor_tensor(
                        xob[:, _sl], tmpc[:], x1b[:, _sl], OP.add))
                prev_st_v = vv
                sc.wait("y", "v", vv)
                prev_st = sc.dma("y", x2T[ccx * 128:(ccx + 1) * 128, :],
                                 xob[:])
                sc.wait("v", "y", prev_st)
        sc.wait("y", "y", sc.cnt["y"])
        sc.finalize(es)
    return cx.nc



def prep_L1moe_inputs(inputs, bres):
    w1 = f32c(inputs["L1_ffn_W1"])
    b1 = f32c(inputs["L1_ffn_b1"])
    w2 = f32c(inputs["L1_ffn_W2"])
    b2 = f32c(inputs["L1_ffn_b2"])
    idn = np.eye(128, dtype=np.float32)
    maps = []
    for core in range(NC8):
        b, eh = core // 2, core % 2
        r0, r1 = bres[2 * b], bres[2 * b + 1]
        cat = lambda k: np.ascontiguousarray(
            np.concatenate([np.asarray(r0[k]), np.asarray(r1[k])], axis=0))
        es = slice(4 * eh, 4 * eh + 4)
        maps.append({
            "xtok": cat("xtok"), "m1d": cat("m1o"), "m2d": cat("m2o"),
            "p1d": cat("p1o"), "p2d": cat("p2o"),
            "ebase": np.array([[4.0 * eh]], dtype=np.float32),
            "w1s": np.ascontiguousarray(w1[es].reshape(4 * C, 4096)),
            "b1s": np.ascontiguousarray(
                b1[es].reshape(4 * 32, 128).T),
            "w2s": np.ascontiguousarray(w2[es].reshape(4 * 4096, C)),
            "b2s": np.ascontiguousarray(b2[es].reshape(4 * 8, 128).T),
            "idn": idn,
        })
    return maps


def prep_L1comb_inputs(inputs, x1, bres, cres):
    idn = np.eye(128, dtype=np.float32)
    maps = []
    for core in range(NC8):
        b, th = core // 2, core % 2
        r0, r1 = bres[2 * b], bres[2 * b + 1]
        cat = lambda k: np.ascontiguousarray(
            np.concatenate([np.asarray(r0[k]), np.asarray(r1[k])], axis=0))
        yb = np.ascontiguousarray(
            np.concatenate([np.asarray(cres[2 * b]["ybuf"]),
                            np.asarray(cres[2 * b + 1]["ybuf"])], axis=0))
        maps.append({
            "yb": yb, "m1d": cat("m1o"), "m2d": cat("m2o"),
            "p1d": cat("p1o"), "p2d": cat("p2o"),
            "toff": np.array([[th * TH]], dtype=np.float32),
            "x1Th": np.ascontiguousarray(x1[b, th * TH:(th + 1) * TH, :].T),
            "s1Th": np.asarray(bres[core]["s1T"]),
            "idn": idn,
        })
    return maps


def run_L1(inputs, x1):
    for nm, bld in (("L1a", build_L1a), ("L1moe", build_L1moe),
                    ("L1comb", build_L1comb)):
        if nm not in _builder_cache:
            _builder_cache[nm] = bld()
    mb = prep_L1a_inputs(inputs, x1)
    resb = run_bass_kernel_spmd(_builder_cache["L1a"], mb,
                                core_ids=list(range(NC8))).results
    mc = prep_L1moe_inputs(inputs, resb)
    resc = run_bass_kernel_spmd(_builder_cache["L1moe"], mc,
                                core_ids=list(range(NC8))).results
    md = prep_L1comb_inputs(inputs, x1, resb, resc)
    resd = run_bass_kernel_spmd(_builder_cache["L1comb"], md,
                                core_ids=list(range(NC8))).results
    x2 = np.empty((B, L, C), dtype=np.float32)
    for core in range(NC8):
        b, th = core // 2, core % 2
        x2[b, th * TH:(th + 1) * TH, :] = resd[core]["x2T"].T
    return x2


def build_L2a():
    cx = Ctx()
    nc, sc = cx.nc, cx.sc
    x2T = nc.dram_tensor("x2T", [C, L], F32, kind="ExternalInput").ap()
    winx = nc.dram_tensor("winx", [C, 512], F32, kind="ExternalInput").ap()
    wing = nc.dram_tensor("wing", [C, 512], F32, kind="ExternalInput").ap()
    binx = nc.dram_tensor("binx", [128, 4], F32, kind="ExternalInput").ap()
    bing = nc.dram_tensor("bing", [128, 4], F32, kind="ExternalInput").ap()
    a_d = nc.dram_tensor("a_d", [512, 64], F32, kind="ExternalInput").ap()
    bm_d = nc.dram_tensor("bm_d", [512, 64], F32, kind="ExternalInput").ap()
    cm_d = nc.dram_tensor("cm_d", [C, 64], F32, kind="ExternalInput").ap()
    dv_d = nc.dram_tensor("dv_d", [128, 4], F32, kind="ExternalInput").ap()
    gn_d = nc.dram_tensor("gn_d", [128, 8], F32, kind="ExternalInput").ap()
    yg = nc.dram_tensor("yg", [512, 2048], BF16, kind="ExternalOutput").ap()

    TS = 2048
    es = contextlib.ExitStack()
    with es:
        sbL = lambda n, shp, dt: es.enter_context(cx.SBUF(n, shp, dt))
        a_t = [sbL(f"a_t{i}", [128, 64], F32) for i in range(4)]
        bm_t = [sbL(f"bm_t{i}", [128, 64], F32) for i in range(4)]
        cb = sbL("cb", [128, 64], F32)
        dv = sbL("dv", [128, 4], F32)
        gn = sbL("gn", [128, 8], F32)
        binx_sb = sbL("binx_sb", [128, 4], F32)
        bing_sb = sbL("bing_sb", [128, 4], F32)
        onescol_f = sbL("onescol_f", [128, 1], F32)
        onesb_r = sbL("onesb_r", [1, 128], BF16)
        onesf_r = sbL("onesf_r", [1, 128], F32)
        for i in range(4):
            sc.dma("y", a_t[i][:], a_d[i * 128:(i + 1) * 128, :])
            sc.dma("y", bm_t[i][:], bm_d[i * 128:(i + 1) * 128, :])
        sc.dma("y", dv[:], dv_d)
        sc.dma("y", binx_sb[:], binx)
        sc.dma("y", bing_sb[:], bing)
        ldy = sc.dma("y", gn[:], gn_d)
        sc.inc("g", lambda e: e.memset(onescol_f[:], 1.0))
        sc.inc("g", lambda e: e.memset(onesf_r[:], 1.0))
        gc = sc.inc("g", lambda e: e.memset(onesb_r[:], 1.0))

        with contextlib.ExitStack() as ph:
            cmt = [ph.enter_context(cx.SBUFR(f"cmt{i}", [128, 64], F32))
                   for i in range(8)]
            csr = ph.enter_context(cx.SBUFR("csr", [1, 64], F32))
            for i in range(8):
                lc = sc.dma("y", cmt[i][:], cm_d[i * 128:(i + 1) * 128, :])
            with cx.PSUM("ps_cs", [1, 64], F32) as ps_cs, \
                    cx.PSUM("ps_cb", [128, 64], F32) as ps_cb:
                sc.wait("t", "y", lc)
                sc.wait("t", "g", gc)
                t_c = None
                for i in range(8):
                    if i < 7:
                        sc.raw("t", lambda e, _i=i: e.matmul(
                            ps_cs[:], onescol_f[:], cmt[_i][:],
                            start=(_i == 0), stop=False))
                    else:
                        t_c = sc.inc("t", lambda e, _i=i: e.matmul(
                            ps_cs[:], onescol_f[:], cmt[_i][:],
                            start=False, stop=True))
                sc.wait("v", "t", t_c)
                v_c = sc.inc("v", lambda e: e.tensor_copy(csr[:], ps_cs[:]))
                sc.wait("t", "v", v_c)
                t_c = sc.inc("t", lambda e: e.matmul(
                    ps_cb[:], onesf_r[:], csr[:], start=True, stop=True))
                sc.wait("v", "t", t_c)
                sc.inc("v", lambda e: e.tensor_copy(cb[:], ps_cb[:]))
        sc.barrier()

        stXN = contextlib.ExitStack()
        xn = emit_ln_cm(cx, stXN, x2T, L)
        sc.barrier()
        stXS = contextlib.ExitStack()
        xs = [stXS.enter_context(cx.SBUFR(f"xs{i}", [128, TS], BF16))
              for i in range(8)]
        v_cmp = None
        for i in range(8):
            v_cmp = sc.inc("v", lambda e, _i=i: e.tensor_copy(
                xs[_i][:], xn[_i].ap().rearrange(
                    "p (t two) -> p t two", two=2)[:, :, 0:1]))
        stXN.close()
        with contextlib.ExitStack() as ph:
            pb = lambda n, shp, dt: ph.enter_context(cx.SBUFR(n, shp, dt))
            sqb = pb("sqb", [128, TS], F32)
            rrow = pb("rrow", [1, TS], BF16)
            vrow = pb("vrow", [1, TS], F32)
            epsc = pb("epsc", [1, 1], F32)
            gcc = sc.inc("g", lambda e: e.memset(epsc[:], 1e-6))
            with contextlib.ExitStack() as pp:
                ps_s2 = pp.enter_context(cx.PSUM("ps_s2", [1, TS], F32))
                prev_t = None
                for i in range(8):
                    sc.wait("s", "v", v_cmp)
                    if prev_t:
                        sc.wait("s", "t", prev_t)
                    s_sq = sc.inc("s", lambda e, _i=i: e.activation(
                        sqb[:], xs[_i][:], AF.Square))
                    sc.wait("t", "s", s_sq)
                    for nn in range(4):
                        sl = slice(nn * 512, nn * 512 + 512)
                        if nn < 3:
                            sc.raw("t", lambda e, _i=i, _sl=sl: e.matmul(
                                ps_s2[:, _sl], onescol_f[:], sqb[:, _sl],
                                start=(_i == 0), stop=(_i == 7)))
                        else:
                            prev_t = sc.inc(
                                "t", lambda e, _i=i, _sl=sl: e.matmul(
                                    ps_s2[:, _sl], onescol_f[:], sqb[:, _sl],
                                    start=(_i == 0), stop=(_i == 7)))
                sc.wait("v", "t", prev_t)
                sc.inc("v", lambda e: e.tensor_scalar(
                    vrow[:], ps_s2[:], 1.0 / C, None, OP.mult))
            v_v = sc.cnt["v"]
            sc.wait("s", "v", v_v)
            sc.wait("s", "g", gcc)
            sc.inc("s", lambda e: e.activation(vrow[:], vrow[:], AF.Ln,
                                               bias=epsc[:]))
            s_r = sc.inc("s", lambda e: e.activation(rrow[:], vrow[:],
                                                     AF.Exp, scale=-0.5))
            for quar in range(2):
                q0 = quar * 1024
                with cx.PSUM("ps_a", [128, 1024], F32) as ps_a:
                    sc.wait("t", "s", s_r)
                    t_bc = None
                    for nn in range(2):
                        sl = slice(q0 + nn * 512, q0 + nn * 512 + 512)
                        sl2 = slice(nn * 512, nn * 512 + 512)
                        t_bc = sc.inc("t", lambda e, _s=sl, _s2=sl2: e.matmul(
                            ps_a[:, _s2], onesb_r[:], rrow[0:1, _s],
                            start=True, stop=True))
                    sc.wait("v", "t", t_bc)
                    v_n = None
                    for i in range(8):
                        sc.inc("v", lambda e, _i=i, _q=slice(q0, q0 + 1024):
                               e.tensor_tensor(sqb[:, 0:1024], xs[_i][:, _q],
                                               ps_a[:], OP.mult))
                        v_n = sc.inc(
                            "v", lambda e, _i=i, _q=slice(q0, q0 + 1024):
                            e.tensor_scalar(xs[_i][:, _q], sqb[:, 0:1024],
                                            gn[:, _i:_i + 1], None, OP.mult))
                    sc.wait("t", "v", v_n)
        sc.barrier()

        stP = contextlib.ExitStack()
        xp_t = [stP.enter_context(cx.SBUF(f"xp{i}", [128, TS], F32))
                for i in range(4)]
        gate_t = [stP.enter_context(cx.SBUF(f"gt{i}", [128, TS], BF16))
                  for i in range(4)]
        with contextlib.ExitStack() as ph:
            wb = [ph.enter_context(cx.SBUFR(f"wb{i}", [128, 512], BF16))
                  for i in range(8)]
            with contextlib.ExitStack() as pp:
              psp = [pp.enter_context(cx.PSUM(f"psp{j}", [128, 512],
                                              F32)) for j in range(4)]
              last = {}
              for wsrc, bsrc, dst, dodt in ((winx, binx, xp_t, F32),
                                            (wing, bing, gate_t, BF16)):
                sc.wait("g", "t", sc.cnt["t"])
                for i in range(8):
                    lw = sc.dma("g", wb[i][:], wsrc[i * 128:(i + 1) * 128, :])
                sc.wait("t", "gd", lw)
                if True:
                    for mo in range(4):
                        for nn in range(4):
                            j = nn
                            sc.wait("t", "v", last.get(j))
                            mm = mm_accum(
                                sc, psp[j][:],
                                [(wb[cc][:, mo * 128:(mo + 1) * 128],
                                  xs[cc][:, nn * 512:(nn + 1) * 512])
                                 for cc in range(8)])
                            sc.wait("v", "t", mm)
                            sl = slice(nn * 512, (nn + 1) * 512)
                            last[j] = sc.inc(
                                "v", lambda e, _j=j, _m=mo, _sl=sl, _d=dst,
                                _b=bsrc: e.tensor_scalar(
                                    _d[_m][:, _sl], psp[_j][:],
                                    binx_sb[:, _m:_m + 1] if _b is binx
                                    else bing_sb[:, _m:_m + 1],
                                    None, OP.add))
              for k, c in last.items():
                  sc.wait("t", "v", c)
        stXS.close()
        sc.barrier()

        with contextlib.ExitStack() as ph:
            pb = lambda n, shp, dt: ph.enter_context(cx.SBUFR(n, shp, dt))
            h_big = pb("h_big", [128, 256], F32)
            u_big = pb("u_big", [128, 256], F32)
            v_big = pb("v_big", [128, 256], F32)
            dmy = pb("dmy", [128, 64], F32)
            y_t = [pb(f"y_t{i}", [128, TS], F32) for i in range(4)]
            a_big = pb("a_big", [128, 256], F32)
            cbig = pb("cbig", [128, 256], F32)
            vv = None
            for g4 in range(4):
                sc.inc("v", lambda e, _g=g4: e.tensor_copy(
                    a_big[:, _g * 64:(_g + 1) * 64], a_t[_g][:]))
                vv = sc.inc("v", lambda e, _g=g4: e.tensor_copy(
                    cbig[:, _g * 64:(_g + 1) * 64], cb[:]))
            gz = sc.inc("v", lambda e: e.memset(h_big[:], 0.0))
            prev_s = None
            for t_i in range(TS):
                sc.wait("v", "s", prev_s)
                sc.inc("v", lambda e: e.tensor_tensor(v_big[:], h_big[:],
                                                      a_big[:], OP.mult))
                for g4 in range(4):
                    sc.inc("v", lambda e, _g=g4, _t=t_i:
                           e.scalar_tensor_tensor(
                               u_big[:, _g * 64:(_g + 1) * 64], bm_t[_g][:],
                               xp_t[_g][:, _t:_t + 1],
                               v_big[:, _g * 64:(_g + 1) * 64],
                               OP.mult, OP.add))
                v_u = sc.cnt["v"]
                sc.wait("s", "v", v_u)
                prev_s = sc.inc("s", lambda e: e.activation(
                    h_big[:], u_big[:], AF.Tanh))
                sc.wait("v", "s", prev_s)
                sc.inc("v", lambda e: e.tensor_tensor(v_big[:], h_big[:],
                                                      cbig[:], OP.mult))
                for g4 in range(4):
                    sc.inc("v", lambda e, _g=g4, _t=t_i: e.reduce_sum(
                        y_t[_g][:, _t:_t + 1],
                        v_big[:, _g * 64:(_g + 1) * 64], axis=AX))
            with contextlib.ExitStack() as p2:
                ob = p2.enter_context(cx.SBUFR("ob", [128, TS], BF16))
                sgb = p2.enter_context(cx.SBUFR("sgb", [128, TS], BF16))
                prev_st = None
                for g4 in range(4):
                    sc.inc("v", lambda e, _g=g4: e.scalar_tensor_tensor(
                        y_t[_g][:], xp_t[_g][:], dv[:, _g:_g + 1], y_t[_g][:],
                        OP.mult, OP.add))
                    v_y = sc.cnt["v"]
                    sc.wait("s", "v", v_y)
                    s_g = sc.inc("s", lambda e, _g=g4: e.activation(
                        sgb[:], gate_t[_g][:], AF.Silu))
                    sc.wait("v", "s", s_g)
                    sc.wait("v", "y", prev_st)
                    v_o = sc.inc("v", lambda e, _g=g4: e.tensor_tensor(
                        ob[:], y_t[_g][:], sgb[:], OP.mult))
                    sc.wait("y", "v", v_o)
                    prev_st = sc.dma("y", yg[g4 * 128:(g4 + 1) * 128, :],
                                     ob[:])
        stP.close()
        sc.wait("y", "y", sc.cnt["y"])
        sc.finalize(es)
    return cx.nc


def build_L2b():
    cx = Ctx()
    nc, sc = cx.nc, cx.sc
    x2Th = nc.dram_tensor("x2Th", [C, TH], F32, kind="ExternalInput").ap()
    ygh = nc.dram_tensor("ygh", [C, TH // 2], BF16, kind="ExternalInput").ap()
    woutT = nc.dram_tensor("woutT", [C, C], F32, kind="ExternalInput").ap()
    bout2 = nc.dram_tensor("bout2", [128, 8], F32, kind="ExternalInput").ap()
    gw8 = nc.dram_tensor("gw8", [C, 8], F32, kind="ExternalInput").ap()
    w1s = nc.dram_tensor("w1s", [8 * C, 4096], F32, kind="ExternalInput").ap()
    w2s = nc.dram_tensor("w2s", [8 * 4096, C], F32, kind="ExternalInput").ap()
    idn = nc.dram_tensor("idn", [128, 128], F32, kind="ExternalInput").ap()
    x3T = nc.dram_tensor("x3T", [C, TH], F32, kind="ExternalOutput").ap()
    mr_scr = nc.dram_tensor("mr_scr", [2, TH], F32)

    es = contextlib.ExitStack()
    with es:
        sbL = lambda n, shp, dt: es.enter_context(cx.SBUF(n, shp, dt))
        ident = sbL("ident", [128, 128], BF16)
        bout = sbL("bout", [128, 8], F32)
        onesb_r = sbL("onesb_r", [1, 128], BF16)
        cmb = sbL("cmb", [128, 128], F32)
        sc.dma("g", ident[:], idn)
        sc.dma("y", bout[:], bout2)
        gcn = sc.inc("g", lambda e: e.memset(onesb_r[:], 1.0))

        stMU = contextlib.ExitStack()
        mu_k = stMU.enter_context(cx.SBUFR("mu_k", [1, TH], F32))
        r_k = stMU.enter_context(cx.SBUFR("r_k", [1, TH], F32))
        stXN = contextlib.ExitStack()
        xn = emit_ln_cm(cx, stXN, x2Th, TH, keep_stats=(mu_k, r_k, TH))
        vk = sc.cnt["v"]
        sc.wait("y", "v", vk)
        sc.dma("y", mr_scr.ap()[0:1, :], mu_k[:])
        sc.dma("y", mr_scr.ap()[1:2, :], r_k[:])
        sc.barrier()
        stMU.close()

        with contextlib.ExitStack() as ph:
            pb = lambda n, shp, dt: ph.enter_context(cx.SBUFR(n, shp, dt))
            x1f = [pb(f"x1f{i}", [128, TH], F32) for i in range(8)]
            wrf = [pb(f"wrf{i}", [128, 8], F32) for i in range(8)]
            mcol = pb("mcol", [128, 16], F32)
            rcol = pb("rcol", [128, 16], F32)
            mrc = pb("mrc", [128, 16], F32)
            bsum = pb("bsum", [128, 8], F32)
            onescol_g = pb("onescol_g", [128, 1], F32)
            t1b = pb("t1b", [128, 8], F32)
            t2b = pb("t2b", [128, 8], F32)
            nmx = pb("nmx", [128, 1], F32)
            m1v = pb("m1v", [128, 1], F32)
            m2v = pb("m2v", [128, 1], F32)
            rsum = pb("rsum", [128, 1], F32)
            mk1 = pb("mk1", [128, 8], F32)
            pm = pb("pm", [128, 8], F32)
            bs_b = pb("bs_b", [1, 8], BF16)
            for i in range(8):
                sc.dma("y", x1f[i][:], x2Th[i * 128:(i + 1) * 128, :])
                sc.dma("y", wrf[i][:], gw8[i * 128:(i + 1) * 128, :])
            sc.dma("y", mcol[:],
                   mr_scr.ap()[0:1, :].rearrange("a (tc p) -> (a p) tc",
                                                 p=128),
                   allow_slow_non_contiguous=True)
            ldm = sc.dma("y", rcol[:],
                         mr_scr.ap()[1:2, :].rearrange(
                             "a (tc p) -> (a p) tc", p=128),
                         allow_slow_non_contiguous=True)
            gc3 = sc.inc("g", lambda e: e.memset(onescol_g[:], 1.0))
            sc.wait("v", "y", ldm)
            sc.inc("v", lambda e: e.tensor_tensor(mrc[:], mcol[:], rcol[:],
                                                  OP.mult))
            with cx.PSUM("ps_bs", [1, 8], F32) as ps_bs:
                sc.wait("t", "y", ldm)
                sc.wait("t", "g", gc3)
                t_bs = None
                for cc in range(8):
                    if cc < 7:
                        sc.raw("t", lambda e, _c=cc: e.matmul(
                            ps_bs[:], onescol_g[:], wrf[_c][:],
                            start=(_c == 0), stop=False))
                    else:
                        t_bs = sc.inc("t", lambda e, _c=cc: e.matmul(
                            ps_bs[:], onescol_g[:], wrf[_c][:],
                            start=False, stop=True))
                sc.wait("v", "t", t_bs)
                v_bs = sc.inc("v", lambda e: e.tensor_copy(bs_b[:], ps_bs[:]))
            with cx.PSUM("ps_b2", [128, 8], F32) as ps_b2:
                sc.wait("t", "v", v_bs)
                t_x = sc.inc("t", lambda e: e.matmul(
                    ps_b2[:], onesb_r[:], bs_b[:], start=True, stop=True))
                sc.wait("v", "t", t_x)
                v_x = sc.inc("v", lambda e: e.tensor_copy(bsum[:], ps_b2[:]))
                sc.wait("t", "v", v_x)
            with cx.PSUM("ps_lg", [128, 8], F32) as ps_lg:
                prev = None
                for tc in range(16):
                    sc.wait("t", "v", prev)
                    t_lg = mm_accum(
                        sc, ps_lg[:],
                        [(x1f[cc][:, tc * 128:(tc + 1) * 128], wrf[cc][:])
                         for cc in range(8)])
                    sc.wait("v", "t", t_lg)
                    sc.inc("v", lambda e, _tc=tc: e.tensor_scalar(
                        t1b[:], ps_lg[:], rcol[:, _tc:_tc + 1], None,
                        OP.mult))
                    sc.inc("v", lambda e, _tc=tc: e.tensor_scalar(
                        t2b[:], bsum[:], mrc[:, _tc:_tc + 1], None, OP.mult))
                    sc.inc("v", lambda e: e.tensor_tensor(
                        t1b[:], t1b[:], t2b[:], OP.subtract))
                    sc.inc("v", lambda e: e.reduce_max(nmx[:], t1b[:],
                                                       axis=AX))
                    sc.inc("v", lambda e: e.tensor_scalar(
                        mk1[:], t1b[:], nmx[:], None, OP.is_ge))
                    sc.inc("v", lambda e: e.tensor_copy(m1v[:], nmx[:]))
                    sc.inc("v", lambda e: e.scalar_tensor_tensor(
                        pm[:], mk1[:], -1e9, t1b[:], OP.mult, OP.add))
                    sc.inc("v", lambda e: e.reduce_max(nmx[:], pm[:],
                                                       axis=AX))
                    sc.inc("v", lambda e: e.tensor_scalar(
                        t2b[:], pm[:], nmx[:], None, OP.is_ge))
                    sc.inc("v", lambda e: e.tensor_tensor(
                        m2v[:], nmx[:], m1v[:], OP.subtract))
                    v_d = sc.cnt["v"]
                    sc.wait("s", "v", v_d)
                    s_sg = sc.inc("s", lambda e: e.activation(
                        rsum[:], m2v[:], AF.Sigmoid, scale=-1.0))
                    sc.wait("v", "s", s_sg)
                    sc.inc("v", lambda e: e.tensor_scalar(
                        m2v[:], rsum[:], -1.0, 1.0, OP.mult, OP.add))
                    sc.inc("v", lambda e, _tc=tc: e.tensor_scalar(
                        pm[:], mk1[:], rsum[:], None, OP.mult))
                    prev = sc.inc("v", lambda e, _tc=tc:
                                  e.scalar_tensor_tensor(
                                      cmb[:, _tc * 8:(_tc + 1) * 8], t2b[:],
                                      m2v[:], pm[:], OP.mult, OP.add))
        sc.barrier()

        stS2 = contextlib.ExitStack()
        s2_t = [stS2.enter_context(cx.SBUF(f"s2_{i}", [128, TH], BF16))
                for i in range(8)]
        with contextlib.ExitStack() as ph:
            pb = lambda n, shp, dt: ph.enter_context(cx.SBUFR(n, shp, dt))
            wo_s = [pb(f"wo_s{i}", [128, C], BF16) for i in range(8)]
            yg_s = [pb(f"yg_s{i}", [128, TH // 2], BF16) for i in range(8)]
            tme = pb("tme", [128, 512], F32)
            for i in range(8):
                sc.dma("g", wo_s[i][:], woutT[i * 128:(i + 1) * 128, :])
                lw = sc.dma("y", yg_s[i][:], ygh[i * 128:(i + 1) * 128, :])
            sc.wait("t", "gd", sc.cnt["gd"])
            sc.wait("t", "y", lw)

            def evac_s2(mo, nn, ps, mm):
                sc.wait("v", "t", mm)
                sl = slice(nn * 512, (nn + 1) * 512)
                sc.inc("v", lambda e, _mo=mo: e.tensor_scalar(
                    tme[:], ps[:], bout[:, _mo:_mo + 1], None, OP.add))
                so = nn * 1024
                sc.inc("v", lambda e, _mo=mo, _so=so: e.tensor_copy(
                    s2_t[_mo].ap().rearrange("p (t two) -> p t two", two=2)
                    [:, _so // 2:_so // 2 + 512, 0:1], tme[:]))
                c = sc.inc("v", lambda e, _mo=mo, _so=so: e.tensor_copy(
                    s2_t[_mo].ap().rearrange("p (t two) -> p t two", two=2)
                    [:, _so // 2:_so // 2 + 512, 1:2], tme[:]))
                return ("v", c)

            emit_proj_cm(cx, wo_s, yg_s, TH // 2, range(8), evac_s2)
        sc.barrier()

        stAC = contextlib.ExitStack()
        acc = [stAC.enter_context(cx.SBUF(f"acc{i}", [128, 1024], F32))
               for i in range(16)]
        with contextlib.ExitStack() as ph:
            pb = lambda n, shp, dt: ph.enter_context(cx.SBUFR(n, shp, dt))
            w1h = [pb(f"w1h{i}", [128, 2048], BF16) for i in range(8)]
            w2h = [pb(f"w2h{i}", [128, 2048], BF16) for i in range(8)]
            hb = [pb(f"hb{i}", [128, 256], BF16) for i in range(2)]
            vz = None
            for i in range(16):
                vz = sc.inc("v", lambda e, _i=i: e.memset(acc[_i][:], 0.0))
            with contextlib.ExitStack() as pp:
                ps_h = [pp.enter_context(cx.PSUM(f"ps_h{j}", [128, 256], F32))
                        for j in range(2)]
                ps_y = [pp.enter_context(cx.PSUM(f"ps_y{j}", [128, 512], F32))
                        for j in range(4)]
                last_h = {}
                last_y = {}
                prev_gel = {}
                for e_i in range(8):
                    for fh in range(2):
                        sc.wait("g", "t", sc.cnt["t"])
                        lw = None
                        for ccx in range(8):
                            sc.dma("g", w1h[ccx][:],
                                   w1s[e_i * C + ccx * 128:
                                       e_i * C + ccx * 128 + 128,
                                       fh * 2048:(fh + 1) * 2048])
                        for fcx in range(16):
                            fc = fh * 16 + fcx
                            lw = sc.dma("g", w2h[fcx // 2][:,
                                                           (fcx % 2) * 1024:
                                                           (fcx % 2) * 1024 + 1024],
                                        w2s[e_i * 4096 + fc * 128:
                                            e_i * 4096 + fc * 128 + 128, :])
                        sc.wait("t", "gd", lw)
                        for tb in range(8):
                            tsl = slice(tb * 256, (tb + 1) * 256)
                            yev = None
                            for fcx in range(16):
                                j = fcx % 2
                                sc.wait("t", "s", prev_gel.get(j))
                                mm = mm_accum(
                                    sc, ps_h[j][:],
                                    [(w1h[ccx][:, fcx * 128:(fcx + 1) * 128],
                                      xn[ccx][:, tsl])
                                     for ccx in range(8)])
                                sc.wait("s", "t", mm)
                                prev_gel[j] = sc.inc(
                                    "s", lambda e, _j=j: e.activation(
                                        hb[_j][:], ps_h[_j][:], AF.Gelu))
                                sc.wait("t", "s", prev_gel[j])
                                for sub in range(2):
                                    for nc2 in range(2):
                                        yj = sub * 2 + nc2
                                        if fcx == 0:
                                            sc.wait("t", "v",
                                                    last_y.get(yj))
                                        fn = lambda e, _j=j, _s=sub, _n=nc2, \
                                            _f=fcx: e.matmul(
                                                ps_y[_s * 2 + _n][:],
                                                hb[_j][:, _s * 128:
                                                       (_s + 1) * 128],
                                                w2h[_f // 2][:,
                                                             (_f % 2) * 1024 +
                                                             _n * 512:
                                                             (_f % 2) * 1024 +
                                                             (_n + 1) * 512],
                                                start=(_f == 0),
                                                stop=(_f == 15))
                                        if fcx == 15 and sub == 1 and nc2 == 1:
                                            mmy = sc.inc("t", fn)
                                        else:
                                            sc.raw("t", fn)
                            sc.wait("v", "t", mmy)
                            for sub in range(2):
                                ti = tb * 2 + sub
                                cwc = cmb[:, (ti // 1) * 8 + e_i:
                                          (ti) * 8 + e_i + 1]
                                for nc2 in range(2):
                                    last_y[sub * 2 + nc2] = sc.inc(
                                        "v", lambda e, _t=ti, _n=nc2, _s=sub,
                                        _e=e_i: e.scalar_tensor_tensor(
                                            acc[_t][:, _n * 512:(_n + 1) * 512],
                                            ps_y[_s * 2 + _n][:],
                                            cmb[:, _t * 8 + _e:_t * 8 + _e + 1],
                                            acc[_t][:, _n * 512:(_n + 1) * 512],
                                            OP.mult, OP.add))
        sc.barrier()

        with contextlib.ExitStack() as ph:
            pb = lambda n, shp, dt: ph.enter_context(cx.SBUFR(n, shp, dt))
            g2k = pb("g2k", [128, 1024], BF16)
            g2T = [pb(f"g2T{i}", [128, TH], BF16) for i in range(8)]
            x2b = pb("x2b", [128, TH], F32)
            tmpc = pb("tmpc", [128, 512], F32)
            xob = pb("xob", [128, TH], F32)
            with cx.PSUM("ps_tg", [128, 1024], BF16) as ps_tg:
                prev_cp = None
                for ti in range(16):
                    v_s = sc.cnt["v"]
                    sc.wait("s", "v", v_s)
                    s_g = sc.inc("s", lambda e, _t=ti: e.activation(
                        g2k[:], acc[_t][:], AF.Silu))
                    sc.wait("t", "s", s_g)
                    sc.wait("t", "v", prev_cp)
                    t_tr = None
                    for ccx in range(8):
                        t_tr = sc.inc("t", lambda e, _c=ccx: e.transpose(
                            ps_tg[:, _c * 128:(_c + 1) * 128],
                            g2k[:, _c * 128:(_c + 1) * 128], ident[:]))
                    sc.wait("v", "t", t_tr)
                    prev_cp = None
                    for ccx in range(8):
                        prev_cp = sc.inc(
                            "v", lambda e, _c=ccx, _t=ti: e.tensor_copy(
                                g2T[_c][:, _t * 128:(_t + 1) * 128],
                                ps_tg[:, _c * 128:(_c + 1) * 128]))
            prev_st = None
            prev_v = None
            for ccx in range(8):
                sc.wait("y", "v", prev_v)
                ldx = sc.dma("y", x2b[:], x2Th[ccx * 128:(ccx + 1) * 128, :])
                sc.wait("v", "y", ldx)
                vv = None
                for nn in range(4):
                    sl = slice(nn * 512, (nn + 1) * 512)
                    sc.inc("v", lambda e, _c=ccx, _sl=sl: e.tensor_tensor(
                        tmpc[:], s2_t[_c][:, _sl], g2T[_c][:, _sl], OP.mult))
                    vv = sc.inc("v", lambda e, _sl=sl: e.tensor_tensor(
                        xob[:, _sl], tmpc[:], x2b[:, _sl], OP.add))
                prev_v = vv
                sc.wait("y", "v", vv)
                prev_st = sc.dma("y", x3T[ccx * 128:(ccx + 1) * 128, :],
                                 xob[:])
                sc.wait("v", "y", prev_st)
        stAC.close()
        stS2.close()
        stXN.close()
        sc.wait("y", "y", sc.cnt["y"])
        sc.finalize(es)
    return cx.nc


def prep_L2_inputs(inputs, x2):
    w_in = f32c(inputs["L2_attn_w_in"])
    b_in = f32c(inputs["L2_attn_b_in"])
    A = f32c(inputs["L2_attn_A"])
    Bm = f32c(inputs["L2_attn_Bm"])
    Cm = f32c(inputs["L2_attn_Cm"])
    Dv = f32c(inputs["L2_attn_Dv"])
    wout = np.ascontiguousarray(f32c(inputs["L2_attn_w_out"]).T)
    bout = f32c(inputs["L2_attn_b_out"])
    gn = f32c(inputs["L2_attn_g_norm"])
    gw = f32c(inputs["L2_ffn_gate_w"])
    w1 = np.ascontiguousarray(f32c(inputs["L2_ffn_w1"]).reshape(8 * C, 4096))
    w2 = np.ascontiguousarray(f32c(inputs["L2_ffn_w2"]).reshape(8 * 4096, C))
    idn = np.eye(128, dtype=np.float32)
    winT = np.ascontiguousarray(w_in.T)
    mapsA, mapsB = [], []
    for core in range(NC8):
        b, dh = core // 2, core % 2
        rx = slice(512 * dh, 512 * dh + 512)
        rg = slice(1024 + 512 * dh, 1024 + 512 * dh + 512)
        mapsA.append({
            "x2T": np.ascontiguousarray(x2[b].T),
            "winx": np.ascontiguousarray(winT[:, rx]),
            "wing": np.ascontiguousarray(winT[:, rg]),
            "binx": np.ascontiguousarray(b_in[rx].reshape(4, 128).T),
            "bing": np.ascontiguousarray(b_in[rg].reshape(4, 128).T),
            "a_d": np.ascontiguousarray(A[rx]),
            "bm_d": np.ascontiguousarray(Bm[rx]),
            "cm_d": Cm,
            "dv_d": np.ascontiguousarray(Dv[rx].reshape(4, 128).T),
            "gn_d": np.ascontiguousarray(gn.reshape(8, 128).T),
        })
        th = core % 2
        mapsB.append({
            "x2Th": np.ascontiguousarray(x2[b, th * TH:(th + 1) * TH, :].T),
            "woutT": wout,
            "bout2": np.ascontiguousarray(bout.reshape(8, 128).T),
            "gw8": gw, "w1s": w1, "w2s": w2, "idn": idn,
        })
    return mapsA, mapsB


def run_L2(inputs, x2):
    for nm, bld in (("L2a", build_L2a), ("L2b", build_L2b)):
        if nm not in _builder_cache:
            _builder_cache[nm] = bld()
    mapsA, mapsB = prep_L2_inputs(inputs, x2)
    resA = run_bass_kernel_spmd(_builder_cache["L2a"], mapsA,
                                core_ids=list(range(NC8))).results
    for core in range(NC8):
        b, th = core // 2, core % 2
        ygb = np.concatenate([np.asarray(resA[2 * b]["yg"]),
                              np.asarray(resA[2 * b + 1]["yg"])], axis=0)
        mapsB[core]["ygh"] = np.ascontiguousarray(
            ygb[:, th * (TH // 2):(th + 1) * (TH // 2)])
    resB = run_bass_kernel_spmd(_builder_cache["L2b"], mapsB,
                                core_ids=list(range(NC8))).results
    x3 = np.empty((B, L, C), dtype=np.float32)
    for core in range(NC8):
        b, th = core // 2, core % 2
        x3[b, th * TH:(th + 1) * TH, :] = resB[core]["x3T"].T
    return x3


def build_L3a():
    cx = Ctx()
    nc, sc = cx.nc, cx.sc
    x3T = nc.dram_tensor("x3T", [C, L], F32, kind="ExternalInput").ap()
    qoffd = nc.dram_tensor("qoff", [1, 1], F32, kind="ExternalInput").ap()
    wuT = nc.dram_tensor("wuT", [C, 6272], F32, kind="ExternalInput").ap()
    cw4 = nc.dram_tensor("cw4", [128, 68], F32, kind="ExternalInput").ap()
    cb17 = nc.dram_tensor("cb17", [128, 17], F32, kind="ExternalInput").ap()
    dtb = nc.dram_tensor("dtb", [128, 1], F32, kind="ExternalInput").ap()
    alog = nc.dram_tensor("alog", [8, 1], F32, kind="ExternalInput").ap()
    dpar = nc.dram_tensor("dpar", [1, 8], F32, kind="ExternalInput").ap()
    idn = nc.dram_tensor("idn", [128, 128], F32, kind="ExternalInput").ap()

    y_scr = nc.dram_tensor("y_scr", [TH, 2048], BF16, kind="ExternalOutput")
    skip_scr = nc.dram_tensor("skip_scr", [2048, TH], BF16,
                              kind="ExternalOutput")
    dtq_o = nc.dram_tensor("dtq_o", [8, TH], F32, kind="ExternalOutput")
    clg_o = nc.dram_tensor("clg_o", [8, TH], F32, kind="ExternalOutput")
    hfin_o = nc.dram_tensor("hfin_o", [16, 2048], F32, kind="ExternalOutput")
    bc_o = nc.dram_tensor("bc_o", [32, TH], BF16, kind="ExternalOutput")
    xbc_scr = nc.dram_tensor("xbc_scr", [2176, 2051], BF16)
    z_scr = nc.dram_tensor("z_scr", [2048, TH], BF16)

    NH, PD, NS = 8, 256, 16
    es = contextlib.ExitStack()
    with es:
        sbL = lambda n, shp, dt: es.enter_context(cx.SBUF(n, shp, dt))
        ident = sbL("ident", [128, 128], BF16)
        step1 = sbL("step1", [128, 1], F32)
        qcol = sbL("qcol", [128, 1], F32)
        cwsb = sbL("cwsb", [128, 68], F32)
        cbsb = sbL("cbsb", [128, 17], F32)
        dtbsb = sbL("dtbsb", [128, 1], F32)
        acol = sbL("acol", [8, 1], F32)
        onesb_r = sbL("onesb_r", [1, 128], BF16)
        onesf_r = sbL("onesf_r", [1, 128], F32)
        ones16 = sbL("ones16", [1, 16], BF16)
        ones16f = sbL("ones16f", [1, 16], F32)
        dcol128 = sbL("dcol128", [128, 8], F32)
        tri = sbL("tri", [128, 128], BF16)
        dtq = sbL("dtq", [8, TH], F32)
        hstate = [sbL(f"hst{h}", [16, PD], F32) for h in range(NH)]

        ld = sc.dma("g", ident[:], idn)
        ld = sc.dma("y", qcol[0:1, :], qoffd)
        sc.dma("y", cwsb[:], cw4)
        sc.dma("y", cbsb[:], cb17)
        sc.dma("y", dtbsb[:], dtb)
        ldy = sc.dma("y", acol[:], alog)
        gc = sc.inc("g", lambda e: e.memset(onesb_r[:], 1.0))
        gc = sc.inc("g", lambda e: e.memset(onesf_r[:], 1.0))
        gc = sc.inc("g", lambda e: e.memset(ones16[:], 1.0))
        gc = sc.inc("g", lambda e: e.memset(ones16f[:], 1.0))
        iotr = sbL("iotr", [128, 128], I32)
        iotc = sbL("iotc", [128, 1], I32)
        gci = sc.inc("g", lambda e: e.iota(iotr[:], pattern=[[1, 128]],
                                           base=0, channel_multiplier=0))
        gci = sc.inc("g", lambda e: e.iota(iotc[:], pattern=[[0, 1]],
                                           base=0, channel_multiplier=1))
        trf = sbL("trf", [128, 128], F32)
        trc = sbL("trc", [128, 1], F32)
        sc.wait("v", "g", gci)
        sc.wait("v", "y", ldy)
        sc.inc("v", lambda e: e.tensor_copy(trf[:], iotr[:]))
        sc.inc("v", lambda e: e.tensor_copy(trc[:], iotc[:]))
        sc.inc("v", lambda e: e.tensor_scalar(trf[:], trf[:], trc[:], None,
                                              OP.is_ge))
        sc.inc("v", lambda e: e.tensor_copy(tri[:], trf[:]))
        sc.inc("v", lambda e: e.tensor_single_scalar(step1[0:1, :],
                                                     qcol[0:1, :], 0.5,
                                                     OP.is_gt))
        v_a = sc.cnt["v"]
        sc.wait("s", "v", v_a)
        s_a = sc.inc("s", lambda e: e.activation(acol[:], acol[:], AF.Exp,
                                                 scale=1.0))
        sc.wait("v", "s", s_a)
        sc.inc("v", lambda e: e.tensor_scalar(acol[:], acol[:], -1.0, None,
                                              OP.mult))
        with cx.PSUM("ps_d", [128, 8], F32) as ps_d:
            drow_b = sbL("drow_b", [1, 8], F32)
            ldd = sc.dma("y", drow_b[:], dpar)
            sc.wait("t", "y", ldd)
            sc.wait("t", "g", gc)
            t_d = sc.inc("t", lambda e: e.matmul(ps_d[:], onesf_r[:],
                                                 drow_b[:], start=True,
                                                 stop=True))
            sc.wait("v", "t", t_d)
            sc.inc("v", lambda e: e.tensor_copy(dcol128[:], ps_d[:]))
        with cx.PSUM("ps_st", [128, 1], F32) as ps_st:
            stp_b = sbL("stp_b", [1, 1], BF16)
            v_c = sc.inc("v", lambda e: e.tensor_copy(stp_b[:],
                                                      step1[0:1, :]))
            sc.wait("t", "v", v_c)
            t_b = sc.inc("t", lambda e: e.matmul(ps_st[:], onesb_r[:],
                                                 stp_b[:], start=True,
                                                 stop=True))
            sc.wait("v", "t", t_b)
            sc.inc("v", lambda e: e.tensor_copy(step1[:], ps_st[:]))
        sc.barrier()

        stXN = contextlib.ExitStack()
        xn = emit_ln_cm(cx, stXN, x3T, L)
        sc.barrier()

        with contextlib.ExitStack() as ph:
            pb = lambda n, shp, dt: ph.enter_context(cx.SBUFR(n, shp, dt))
            wub = [pb(f"wub{i}", [128, 1024], BF16) for i in range(8)]
            pre = pb("pre", [128, 2051], F32)
            cvt = pb("cvt", [128, 2048], F32)
            cv2 = pb("cv2", [128, 2048], F32)
            xbc_b = pb("xbc_b", [128, 2051], BF16)
            zldb = pb("zldb", [128, 2048], BF16)
            dts1 = pb("dts1", [128, 2051], F32)
            one32 = pb("one32", [128, 1], F32)
            gco = sc.inc("g", lambda e: e.memset(one32[:], 1.0))
            stPP = contextlib.ExitStack()
            ps_p = stPP.enter_context(cx.PSUM("ps_p", [128, 2051], F32))
            prev_v = None
            prev_st = None
            lw = None
            for mo in range(49):
                sc.wait("t", "v", sc.cnt["v"])
                sc.wait("t", "s", sc.cnt["s"])
                if mo % 8 == 0:
                    sc.wait("g", "t", sc.cnt["t"])
                    nmw = min(1024, (49 - mo) * 128)
                    for ccx in range(8):
                        lw = sc.dma("g", wub[ccx][:, 0:nmw],
                                    wuT[ccx * 128:(ccx + 1) * 128,
                                        mo * 128:mo * 128 + nmw])
                    sc.wait("t", "gd", lw)
                mk = (mo % 8) * 128
                if True:
                    t_mm = None
                    for nn in range(4):
                        sl = slice(nn * 512, (nn + 1) * 512)
                        t_mm = mm_accum(
                            sc, ps_p[:, sl],
                            [(wub[cc][:, mk:mk + 128],
                              xn[cc][:, nn * 512:(nn + 1) * 512])
                             for cc in range(8)])
                    t_mm = mm_accum(
                        sc, ps_p[:, 2048:2051],
                        [(wub[cc][:, mk:mk + 128],
                          xn[cc][:, 2 * TH - 3:2 * TH])
                         for cc in range(8)])
                    sc.wait("v", "t", t_mm)
                    sc.wait("v", "y", prev_st)
                    if mo < 16:
                        prev_v = sc.inc("v", lambda e: e.tensor_copy(
                            xbc_b[:, 0:2048], ps_p[:, 0:2048]))
                        sc.wait("y", "v", prev_v)
                        prev_st = sc.dma(
                            "y", z_scr.ap()[mo * 128:(mo + 1) * 128, :],
                            xbc_b[:, 0:2048])
                    elif mo < 32:
                        ldz = sc.dma("y", zldb[:],
                                     z_scr.ap()[(mo - 16) * 128:
                                                (mo - 15) * 128, :])
                        sc.wait("s", "y", ldz)
                        sc.wait("s", "v", prev_v)
                        s_z = sc.inc("s", lambda e: e.activation(
                            cvt[:], zldb[:], AF.Silu))
                        sc.wait("v", "s", s_z)
                        prev_v = sc.inc("v", lambda e: e.tensor_tensor(
                            xbc_b[:, 0:2048], ps_p[:, 0:2048], cvt[:],
                            OP.mult))
                        sc.wait("y", "v", prev_v)
                        prev_st = sc.dma(
                            "y", skip_scr.ap()[(mo - 16) * 128:
                                               (mo - 15) * 128, :],
                            xbc_b[:, 0:2048])
                    else:
                        ci = mo - 32
                        sc.inc("v", lambda e: e.tensor_copy(
                            pre[:, 3:2051], ps_p[:, 0:2048]))
                        sc.inc("v", lambda e: e.tensor_copy(
                            pre[:, 0:3], ps_p[:, 2048:2051]))
                        sc.inc("v", lambda e: e.tensor_scalar(
                            pre[:, 0:3], pre[:, 0:3], step1[:, 0:1], None,
                            OP.mult))
                        sc.inc("v", lambda e, _c=ci: e.tensor_scalar(
                            cvt[:], pre[:, 0:2048],
                            cwsb[:, _c * 4:_c * 4 + 1], None, OP.mult))
                        for j in range(1, 4):
                            sc.inc("v", lambda e, _c=ci, _j=j:
                                   e.scalar_tensor_tensor(
                                       cvt[:], pre[:, _j:_j + 2048],
                                       cwsb[:, _c * 4 + _j:_c * 4 + _j + 1],
                                       cvt[:], OP.mult, OP.add))
                        v_cv = sc.cnt["v"]
                        sc.wait("s", "v", v_cv)
                        s_cv = sc.inc("s", lambda e, _c=ci: e.activation(
                            xbc_b[:, 3:2051], cvt[:], AF.Silu,
                            bias=cbsb[:, _c:_c + 1]))
                        prev_v = v_cv
                        if ci == 16:
                            sc.inc("v", lambda e: e.tensor_scalar(
                                pre[32:64, :], pre[32:64, :],
                                dtbsb[32:64, 0:1], None, OP.add))
                            v_d1 = sc.cnt["v"]
                            sc.wait("s", "v", v_d1)
                            s_ab = sc.inc("s", lambda e: e.activation(
                                dts1[32:64, :], pre[32:64, :], AF.Abs))
                            sc.inc("s", lambda e: e.activation(
                                dts1[32:64, :], dts1[32:64, :], AF.Exp,
                                scale=-1.0))
                            s_d3 = sc.inc("s", lambda e: e.activation(
                                dts1[32:64, :], dts1[32:64, :], AF.Ln,
                                bias=one32[32:64, 0:1]))
                            sc.wait("v", "s", s_ab)
                            sc.inc("v", lambda e: e.tensor_single_scalar(
                                pre[32:64, :], pre[32:64, :], 0.0, OP.max))
                            sc.wait("v", "s", s_d3)
                            prev_v = sc.inc("v", lambda e: e.tensor_tensor(
                                pre[32:64, :], pre[32:64, :], dts1[32:64, :],
                                OP.add))
                            sc.wait("g", "v", prev_v)
                            gdt = sc.dma("g", dtq[:], pre[32:40, 3:2051])
                            sc.wait("y", "gd", gdt)
                            sc.dma("y", dtq_o.ap(), dtq[:])
                        sc.wait("y", "s", s_cv)
                        sc.wait("y", "v", prev_v)
                        prev_st = sc.dma(
                            "y", xbc_scr.ap()[ci * 128:(ci + 1) * 128, :],
                            xbc_b[:])
                        if ci == 16:
                            sc.dma("y", bc_o.ap(),
                                   xbc_b[0:32, 3:2051])
            stPP.close()
        sc.barrier()
        stXN.close()

        with contextlib.ExitStack() as ph:
            pb = lambda n, shp, dt: ph.enter_context(cx.SBUFR(n, shp, dt))
            dta = pb("dta", [8, TH], F32)
            cumL = pb("cumL", [8, TH], F32)
            ones8t = pb("ones8t", [8, 128], F32)
            lrows = pb("lrows", [1, 8 * TH], F32)
            bdt = pb("bdt", [8, TH], F32)
            bdtrows = pb("bdtrows", [1, 8 * TH], BF16)
            csf = pb("csf", [8, 16], F32)
            daflat = pb("daflat", [1, 128], F32)
            dacol16 = pb("dacol16", [16, 128], F32)
            dtTc = [pb(f"dtTc{i}", [128, 8], F32) for i in range(16)]
            bt16 = pb("bt16", [16, TH], BF16)
            idnf = pb("idnf", [128, 128], F32)
            negf_r = pb("negf_r", [1, 128], F32)
            mfree = pb("mfree", [16, 128], F32)
            lprevc = pb("lprevc", [16, 1], F32)
            hsb = [pb(f"hsb{h}", [16, 256], BF16) for h in range(8)]
            ct16 = pb("ct16", [16, TH], BF16)
            xh = [pb(f"xh{i}", [128, TH], BF16) for i in range(2)]
            xcnk = pb("xcnk", [128, 256], BF16)
            btl = pb("btl", [16, 128], BF16)
            ctl = pb("ctl", [16, 128], BF16)
            btr = pb("btr", [128, 16], BF16)
            mlog = pb("mlog", [128, 128], F32)
            mtl = pb("mtl", [128, 128], BF16)
            ytile = pb("ytile", [128, 256], BF16)
            gc0 = sc.inc("g", lambda e: e.memset(ones8t[:], 1.0))
            gc0 = sc.inc("g", lambda e: e.memset(negf_r[:], -1.0))
            sc.dma("y", idnf[:], idn)
            for h in range(NH):
                sc.inc("v", lambda e, _h=h: e.memset(hstate[_h][:], 0.0))
                sc.inc("v", lambda e, _h=h: e.memset(hsb[_h][:], 0.0))
            ldb = sc.dma("y", bt16[:], bc_o.ap()[0:16, :])
            ldb = sc.dma("y", ct16[:], bc_o.ap()[16:32, :])
            sc.wait("v", "g", gc0)
            sc.wait("v", "y", ldb)
            sc.inc("v", lambda e: e.tensor_scalar(
                dta[:], dtq[:], acol[:, 0:1], None, OP.mult))
            prevoff = None
            for g16 in range(16):
                sl = slice(g16 * 128, (g16 + 1) * 128)
                sc.inc("v", lambda e, _s=sl: e.tensor_tensor_scan(
                    cumL[:, _s], ones8t[:], dta[:, _s], 0.0,
                    OP.mult, OP.add))
                sc.inc("v", lambda e, _g=g16, _s=sl: e.tensor_copy(
                    csf[:, _g:_g + 1], cumL[:, _s][:, 127:128]))
            for g16 in range(1, 16):
                sl = slice(g16 * 128, (g16 + 1) * 128)
                sc.inc("v", lambda e, _g=g16, _s=sl: e.tensor_scalar(
                    cumL[:, _s], cumL[:, _s],
                    cumL[:, _g * 128 - 1:_g * 128], None, OP.add))
            v_cl = sc.cnt["v"]
            sc.wait("y", "v", v_cl)
            sc.dma("y", clg_o.ap(), cumL[:])
            s_b = None
            for g16 in range(16):
                sl = slice(g16 * 128, (g16 + 1) * 128)
                sc.wait("s", "v", v_cl)
                sc.wait("s", "v", sc.cnt["v"])
                s_b = sc.inc("s", lambda e, _s=sl, _g=g16: e.activation(
                    bdt[:, _s], cumL[:, _s], AF.Exp, scale=-1.0,
                    bias=cumL[:, (_g + 1) * 128 - 1:(_g + 1) * 128]))
            sc.wait("v", "s", s_b)
            v_b = sc.inc("v", lambda e: e.tensor_tensor(bdt[:], bdt[:],
                                                        dtq[:], OP.mult))
            sc.wait("s", "v", v_b)
            s_da = sc.inc("s", lambda e: e.activation(csf[:], csf[:], AF.Exp))
            sc.wait("g", "s", s_da)
            for h in range(NH):
                gdma = sc.dma("g", daflat[0:1, h * 16:(h + 1) * 16],
                              csf[h:h + 1, :])
            for h in range(NH):
                gdma = sc.dma("g", lrows[0:1, h * TH:(h + 1) * TH],
                              cumL[h:h + 1, :])
                gdma = sc.dma("g", bdtrows[0:1, h * TH:(h + 1) * TH],
                              bdt[h:h + 1, :])
            sc.wait("t", "v", v_b)
            with cx.PSUM("ps_dt", [128, 8], F32) as ps_dt:
                prev_cp = None
                for g16 in range(16):
                    sc.wait("t", "v", prev_cp)
                    t_tr = sc.inc("t", lambda e, _g=g16: e.transpose(
                        ps_dt[:, 0:8],
                        dtq[0:8, _g * 128:(_g + 1) * 128], idnf[0:8, 0:8]))
                    sc.wait("v", "t", t_tr)
                    prev_cp = sc.inc("v", lambda e, _g=g16: e.tensor_copy(
                        dtTc[_g][:], ps_dt[:, 0:8]))
            with cx.PSUM("ps_da", [16, 128], F32) as ps_da:
                sc.wait("t", "v", sc.cnt["v"])
                sc.wait("t", "gd", sc.cnt["gd"])
                t_da = sc.inc("t", lambda e: e.matmul(
                    ps_da[:], ones16f[:], daflat[:], start=True, stop=True))
                sc.wait("v", "t", t_da)
                sc.inc("v", lambda e: e.tensor_copy(dacol16[:], ps_da[:]))
            sc.barrier()
            with contextlib.ExitStack() as pp:
                ps_g = pp.enter_context(cx.PSUM("ps_g", [128, 128], F32))
                ps_ld = pp.enter_context(cx.PSUM("ps_ld", [128, 128], F32))
                ps_y = pp.enter_context(cx.PSUM("ps_y", [128, 256], F32))
                ps_bc = pp.enter_context(cx.PSUM("ps_bc", [16, 128], F32))
                ps_tr = pp.enter_context(cx.PSUM("ps_tr", [128, 256], BF16))
                ps_hs = pp.enter_context(cx.PSUM("ps_hs", [16, 256], F32))
                btrp = pp.enter_context(cx.PSUM("btrp", [128, 16], BF16))
                prev = {}
                for h in range(NH):
                    ldx = None
                    for half in range(2):
                        ldx = sc.dma("y", xh[half][:],
                                     xbc_scr.ap()[(2 * h + half) * 128:
                                                  (2 * h + half + 1) * 128,
                                                  3:2051])
                    for g16 in range(16):
                        sl = slice(g16 * 128, (g16 + 1) * 128)
                        sc.wait("t", "y", ldx)
                        sc.wait("t", "v", prev.get("xc"))
                        t_tr = None
                        for half in range(2):
                            t_tr = sc.inc("t", lambda e, _h=half, _g=g16:
                                          e.transpose(
                                              ps_tr[:, _h * 128:
                                                    (_h + 1) * 128],
                                              xh[_h][:, _g * 128:
                                                     (_g + 1) * 128],
                                              ident[:]))
                        sc.wait("v", "t", t_tr)
                        prev["xc"] = sc.inc("v", lambda e: e.tensor_copy(
                            xcnk[:], ps_tr[:]))
                        sc.wait("t", "v", prev.get("lp"))
                        t_b1 = sc.inc("t", lambda e, _h=h, _g=g16: e.matmul(
                            ps_bc[:], ones16[:],
                            bdtrows[0:1, _h * TH + _g * 128:
                                    _h * TH + (_g + 1) * 128],
                            start=True, stop=True))
                        sc.wait("v", "t", t_b1)
                        sc.inc("v", lambda e, _g=g16: e.tensor_tensor(
                            btl[:], bt16[:, _g * 128:(_g + 1) * 128],
                            ps_bc[:], OP.mult))
                        prev["bc"] = sc.cnt["v"]
                        sc.wait("t", "v", prev.get("m"))
                        sc.raw("t", lambda e, _g=g16: e.matmul(
                            ps_g[:], bt16[:, _g * 128:(_g + 1) * 128],
                            ct16[:, _g * 128:(_g + 1) * 128],
                            start=True, stop=True))
                        sc.raw("t", lambda e, _h=h, _g=g16: e.matmul(
                            ps_ld[:], onesf_r[0:1, 0:128],
                            lrows[0:1, _h * TH + _g * 128:
                                  _h * TH + (_g + 1) * 128],
                            start=True, stop=False))
                        t_ld = sc.inc("t", lambda e, _h=h, _g=g16: e.matmul(
                            ps_ld[:],
                            lrows[0:1, _h * TH + _g * 128:
                                  _h * TH + (_g + 1) * 128],
                            negf_r[0:1, 0:128], start=False, stop=True))
                        sc.wait("s", "t", t_ld)
                        sc.wait("s", "v", prev.get("m2", None))
                        s_e = sc.inc("s", lambda e: e.activation(
                            mlog[:], ps_ld[:], AF.Exp))
                        sc.wait("v", "s", s_e)
                        sc.inc("v", lambda e: e.tensor_tensor(
                            mlog[:], mlog[:], tri[:], OP.mult))
                        sc.inc("v", lambda e, _g=g16, _h=h: e.tensor_scalar(
                            mlog[:], mlog[:], dtTc[_g][:, _h:_h + 1], None,
                            OP.mult))
                        prev["m2"] = prev["m"] = sc.inc(
                            "v", lambda e: e.tensor_tensor(
                                mtl[:], mlog[:], ps_g[:], OP.mult))
                        sc.wait("t", "v", prev["m"])
                        sc.wait("t", "v", prev["xc"])
                        sc.raw("t", lambda e: e.matmul(
                            ps_y[:], mtl[:], xcnk[:], start=True,
                            stop=False))
                        sc.wait("t", "v", prev.get("bc"))
                        t_ci = sc.inc("t", lambda e, _h=h, _g=g16: e.matmul(
                            ps_bc[:], ones16f[:],
                            lrows[0:1, _h * TH + _g * 128:
                                  _h * TH + (_g + 1) * 128],
                            start=True, stop=True))
                        sc.wait("v", "t", t_ci)
                        sc.wait("v", "s", prev.get("sec"))
                        if g16 == 0:
                            sc.inc("v", lambda e: e.tensor_copy(
                                mfree[:], ps_bc[:]))
                        else:
                            sc.inc("v", lambda e, _h=h, _g=g16:
                                   e.tensor_scalar(
                                       mfree[:], ps_bc[:],
                                       lprevc[:, 0:1], None, OP.subtract))
                        v_mf = sc.cnt["v"]
                        sc.wait("s", "v", v_mf)
                        prev["sec"] = s_ec = sc.inc(
                            "s", lambda e: e.activation(
                                mfree[:], mfree[:], AF.Exp))
                        sc.wait("v", "s", s_ec)
                        sc.inc("v", lambda e, _g=g16: e.tensor_tensor(
                            ctl[:], ct16[:, _g * 128:(_g + 1) * 128],
                            mfree[:], OP.mult))
                        v_ct = sc.cnt["v"]
                        sc.wait("t", "v", v_ct)
                        t_y = sc.inc("t", lambda e, _h=h: e.matmul(
                            ps_y[:], ctl[:], hsb[_h][:], start=False,
                            stop=True))
                        sc.wait("v", "t", t_y)
                        sc.wait("v", "y", prev.get("yst"))
                        v_y = sc.inc("v", lambda e, _h=h: e.scalar_tensor_tensor(
                            ytile[:], xcnk[:], dcol128[:, _h:_h + 1],
                            ps_y[:], OP.mult, OP.add))
                        sc.wait("y", "v", v_y)
                        prev["yst"] = sc.dma(
                            "y", y_scr.ap()[g16 * 128:(g16 + 1) * 128,
                                            h * 256:(h + 1) * 256],
                            ytile[:])
                        sc.wait("t", "v", prev["bc"])
                        t_tb = sc.inc("t", lambda e: e.transpose(
                            btrp[:, 0:16], btl[:], ident[0:16, 0:16]))
                        sc.wait("v", "t", t_tb)
                        v_tb = sc.inc("v", lambda e: e.tensor_copy(
                            btr[:], btrp[:, 0:16]))
                        sc.wait("t", "v", v_tb)
                        t_hs = sc.inc("t", lambda e: e.matmul(
                            ps_hs[:], btr[:], xcnk[:], start=True,
                            stop=True))
                        sc.wait("v", "t", t_hs)
                        sc.inc("v", lambda e, _h=h, _g=g16:
                               e.scalar_tensor_tensor(
                                   hstate[_h][:], hstate[_h][:],
                                   dacol16[:, _h * 16 + _g:
                                           _h * 16 + _g + 1],
                                   ps_hs[:], OP.mult, OP.add))
                        prev["hs"] = sc.inc(
                            "v", lambda e, _h=h: e.tensor_copy(
                                hsb[_h][:], hstate[_h][:]))
                        prev["lp"] = sc.inc(
                            "v", lambda e, _h=h, _g=g16: e.tensor_copy(
                                lprevc[:, 0:1],
                                ps_bc[:, 127:128]))
                v_f = sc.cnt["v"]
                sc.wait("y", "v", v_f)
                for h in range(NH):
                    sc.dma("y", hfin_o.ap()[:, h * 256:(h + 1) * 256],
                           hstate[h][:])
        sc.finalize(es)
    return cx.nc


def build_L3b():
    cx = Ctx()
    nc, sc = cx.nc, cx.sc
    y_d = nc.dram_tensor("y_d", [TH, 2048], BF16, kind="ExternalInput").ap()
    ct_d = nc.dram_tensor("ct_d", [16, TH], BF16, kind="ExternalInput").ap()
    clg_d = nc.dram_tensor("clg_d", [8, TH], F32, kind="ExternalInput").ap()
    hin_d = nc.dram_tensor("hin_d", [16, 2048], F32, kind="ExternalInput").ap()
    qoffd = nc.dram_tensor("qoff", [1, 1], F32, kind="ExternalInput").ap()
    lng_d = nc.dram_tensor("lng_d", [1, 2048], F32, kind="ExternalInput").ap()
    lnb_d = nc.dram_tensor("lnb_d", [1, 2048], F32, kind="ExternalInput").ap()
    wgT = nc.dram_tensor("wgT", [2048, 2048], F32, kind="ExternalInput").ap()
    bg_d = nc.dram_tensor("bg_d", [128, 16], F32, kind="ExternalInput").ap()
    skip_d = nc.dram_tensor("skip_d", [2048, TH], BF16,
                            kind="ExternalInput").ap()
    wo3T = nc.dram_tensor("wo3T", [4096, C], F32, kind="ExternalInput").ap()
    idn = nc.dram_tensor("idn", [128, 128], F32, kind="ExternalInput").ap()
    s3T = nc.dram_tensor("s3T", [C, TH], F32, kind="ExternalOutput").ap()
    gtd_scr = nc.dram_tensor("gtd_scr", [2048, TH], BF16)

    NH = 8
    es = contextlib.ExitStack()
    with es:
        sbL = lambda n, shp, dt: es.enter_context(cx.SBUF(n, shp, dt))
        ident = sbL("ident", [128, 128], BF16)
        step1 = sbL("step1", [128, 1], F32)
        qcol = sbL("qcol", [128, 1], F32)
        onesb_r = sbL("onesb_r", [1, 128], BF16)
        ones16 = sbL("ones16", [1, 16], BF16)
        bgsb = sbL("bgsb", [128, 16], F32)
        sc.dma("g", ident[:], idn)
        sc.dma("y", bgsb[:], bg_d)
        ldq = sc.dma("y", qcol[0:1, :], qoffd)
        gc = sc.inc("g", lambda e: e.memset(onesb_r[:], 1.0))
        gc = sc.inc("g", lambda e: e.memset(ones16[:], 1.0))
        sc.wait("v", "y", ldq)
        sc.wait("v", "g", gc)
        sc.inc("v", lambda e: e.tensor_single_scalar(step1[0:1, :],
                                                     qcol[0:1, :], 0.5,
                                                     OP.is_gt))
        with cx.PSUM("ps_st", [128, 1], F32) as ps_st:
            stp_b = sbL("stp_b", [1, 1], BF16)
            v_c = sc.inc("v", lambda e: e.tensor_copy(stp_b[:],
                                                      step1[0:1, :]))
            sc.wait("t", "v", v_c)
            sc.wait("t", "g", gc)
            t_b = sc.inc("t", lambda e: e.matmul(ps_st[:], onesb_r[:],
                                                 stp_b[:], start=True,
                                                 stop=True))
            sc.wait("v", "t", t_b)
            sc.inc("v", lambda e: e.tensor_copy(step1[:], ps_st[:]))
        sc.barrier()

        stYT = contextlib.ExitStack()
        ynT = [stYT.enter_context(cx.SBUF(f"ynT{i}", [128, TH], BF16))
               for i in range(16)]
        with contextlib.ExitStack() as ph:
            pb = lambda n, shp, dt: ph.enter_context(cx.SBUFR(n, shp, dt))
            ct16 = pb("ct16", [16, TH], BF16)
            clg = pb("clg", [8, TH], F32)
            elg = pb("elg", [8, TH], BF16)
            elrows = pb("elrows", [1, 8 * TH], BF16)
            hsc = [pb(f"hsc{h}", [16, 256], BF16) for h in range(NH)]
            hraw = pb("hraw", [16, 2048], F32)
            ytk = pb("ytk", [128, 2048], F32)
            ytb = pb("ytb", [128, 2048], BF16)
            ctl = pb("ctl", [16, 128], BF16)
            gamb = pb("gamb", [128, 2048], BF16)
            betb = pb("betb", [128, 2048], BF16)
            gb_row = pb("gb_row", [1, 2048], BF16)
            mrow = pb("mrow", [128, 1], F32)
            vrow = pb("vrow", [128, 1], F32)
            sqr = pb("sqr", [128, 2048], F32)
            epsc = pb("epsc", [128, 1], F32)
            sc.inc("g", lambda e: e.memset(epsc[:], 1e-5))
            ld1 = sc.dma("y", ct16[:], ct_d)
            ld1 = sc.dma("y", clg[:], clg_d)
            ld1 = sc.dma("y", hraw[:], hin_d)
            sc.wait("v", "y", ld1)
            for h in range(NH):
                sc.inc("v", lambda e, _h=h: e.tensor_scalar(
                    hsc[_h][:], hraw[:, _h * 256:(_h + 1) * 256],
                    step1[0:16, 0:1], None, OP.mult))
            v_e = sc.cnt["v"]
            sc.wait("s", "v", v_e)
            s_e = sc.inc("s", lambda e: e.activation(elg[:], clg[:], AF.Exp))
            sc.wait("g", "s", s_e)
            for h in range(NH):
                gd = sc.dma("g", elrows[0:1, h * TH:(h + 1) * TH],
                            elg[h:h + 1, :])
            with cx.PSUM("ps_gb", [128, 2048], F32) as ps_gb:
                for (dst, srcd) in ((gamb, lng_d), (betb, lnb_d)):
                    sc.wait("g", "t", sc.cnt["t"])
                    lg = sc.dma("g", gb_row[:], srcd)
                    sc.wait("t", "gd", lg)
                    sc.wait("t", "v", sc.cnt["v"])
                    t_g = None
                    for nn in range(4):
                        sl = slice(nn * 512, (nn + 1) * 512)
                        t_g = sc.inc("t", lambda e, _sl=sl: e.matmul(
                            ps_gb[:, _sl], onesb_r[:], gb_row[0:1, _sl],
                            start=True, stop=True))
                    sc.wait("v", "t", t_g)
                    sc.inc("v", lambda e, _d=dst: e.tensor_copy(_d[:],
                                                                ps_gb[:]))
            sc.barrier()
            with contextlib.ExitStack() as pp:
                ps_yc = pp.enter_context(cx.PSUM("ps_yc", [128, 256], F32))
                ps_bc = pp.enter_context(cx.PSUM("ps_bc", [16, 128], F32))
                ps_tr = pp.enter_context(cx.PSUM("ps_tr", [128, 2048], BF16))
                prev = {}
                for g16 in range(16):
                    sc.wait("g", "v", sc.cnt["v"])
                    ldy = sc.dma("g", ytk[:],
                                 y_d[g16 * 128:(g16 + 1) * 128, :])
                    for h in range(NH):
                        sc.wait("t", "v", prev.get("ctl"))
                        t_b1 = sc.inc("t", lambda e, _h=h, _g=g16: e.matmul(
                            ps_bc[:], ones16[:],
                            elrows[0:1, _h * TH + _g * 128:
                                   _h * TH + (_g + 1) * 128],
                            start=True, stop=True))
                        sc.wait("v", "t", t_b1)
                        sc.wait("v", "t", prev.get("yc"))
                        prev["ctl"] = sc.inc(
                            "v", lambda e, _g=g16: e.tensor_tensor(
                                ctl[:], ct16[:, _g * 128:(_g + 1) * 128],
                                ps_bc[:], OP.mult))
                        sc.wait("t", "v", prev["ctl"])
                        prev["yc"] = sc.inc("t", lambda e, _h=h: e.matmul(
                            ps_yc[:], ctl[:], hsc[_h][:], start=True,
                            stop=True))
                        sc.wait("v", "t", prev["yc"])
                        sc.wait("v", "gd", ldy)
                        sc.inc("v", lambda e, _h=h: e.tensor_tensor(
                            ytk[:, _h * 256:(_h + 1) * 256],
                            ytk[:, _h * 256:(_h + 1) * 256], ps_yc[:],
                            OP.add))
                        prev["yadd"] = sc.cnt["v"]
                    sc.inc("v", lambda e: e.reduce_sum(mrow[:], ytk[:],
                                                       axis=AX))
                    sc.inc("v", lambda e: e.tensor_scalar(
                        mrow[:], mrow[:], 1.0 / 2048, None, OP.mult))
                    v_m = sc.cnt["v"]
                    sc.wait("s", "v", v_m)
                    s_sq = sc.inc("s", lambda e: e.activation(
                        sqr[:], ytk[:], AF.Square))
                    sc.wait("v", "s", s_sq)
                    sc.inc("v", lambda e: e.reduce_sum(vrow[:], sqr[:],
                                                       axis=AX))
                    sc.inc("v", lambda e: e.tensor_scalar(
                        vrow[:], vrow[:], 1.0 / 2048, None, OP.mult))
                    sc.inc("v", lambda e: e.tensor_tensor(
                        sqr[:, 0:1], mrow[:], mrow[:], OP.mult))
                    v_v = sc.inc("v", lambda e: e.tensor_tensor(
                        vrow[:], vrow[:], sqr[:, 0:1], OP.subtract))
                    sc.wait("s", "v", v_v)
                    sc.inc("s", lambda e: e.activation(vrow[:], vrow[:],
                                                       AF.Ln, bias=epsc[:]))
                    s_r = sc.inc("s", lambda e: e.activation(
                        vrow[:], vrow[:], AF.Exp, scale=-0.5))
                    sc.wait("v", "s", s_r)
                    sc.inc("v", lambda e: e.tensor_scalar(
                        ytk[:], ytk[:], mrow[:], vrow[:], OP.subtract,
                        OP.mult))
                    sc.inc("v", lambda e: e.tensor_tensor(
                        ytk[:], ytk[:], gamb[:], OP.mult))
                    v_n = sc.inc("v", lambda e: e.tensor_tensor(
                        ytb[:], ytk[:], betb[:], OP.add))
                    sc.wait("t", "v", v_n)
                    sc.wait("t", "v", prev.get("trc"))
                    t_tr = None
                    for ccx in range(16):
                        t_tr = sc.inc("t", lambda e, _c=ccx: e.transpose(
                            ps_tr[:, _c * 128:(_c + 1) * 128],
                            ytb[:, _c * 128:(_c + 1) * 128], ident[:]))
                    sc.wait("v", "t", t_tr)
                    for ccx in range(16):
                        prev["trc"] = sc.inc(
                            "v", lambda e, _c=ccx, _g=g16: e.tensor_copy(
                                ynT[_c][:, _g * 128:(_g + 1) * 128],
                                ps_tr[:, _c * 128:(_c + 1) * 128]))
        sc.barrier()

        with contextlib.ExitStack() as ph:
            pb = lambda n, shp, dt: ph.enter_context(cx.SBUFR(n, shp, dt))
            wg_s = [pb(f"wg_s{i}", [128, 2048], BF16) for i in range(16)]
            sgb = pb("sgb", [128, TH], BF16)
            for i in range(16):
                lw = sc.dma("g", wg_s[i][:], wgT[i * 128:(i + 1) * 128, :])
            sc.wait("t", "gd", lw)
            with contextlib.ExitStack() as pp:
                psg = [pp.enter_context(cx.PSUM(f"psg{j}", [128, 512], F32))
                       for j in range(4)]
                last = {}
                for mo in range(16):
                    for nn in range(4):
                        j = nn
                        sc.wait("t", "y", last.get(j))
                        mm = mm_accum(
                            sc, psg[j][:],
                            [(wg_s[cc][:, mo * 128:(mo + 1) * 128],
                              ynT[cc][:, nn * 512:(nn + 1) * 512])
                             for cc in range(16)], )
                        sc.wait("s", "t", mm)
                        sl = slice(nn * 512, (nn + 1) * 512)
                        s_g = sc.inc("s", lambda e, _j=j, _mo=mo, _sl=sl:
                                     e.activation(sgb[:, _sl], psg[_j][:],
                                                  AF.Sigmoid,
                                                  bias=bgsb[:, _mo:_mo + 1]))
                        sc.wait("v", "s", s_g)
                        v_g = sc.inc("v", lambda e, _mo=mo, _sl=sl:
                                     e.tensor_tensor(sgb[:, _sl], sgb[:, _sl],
                                                     ynT[_mo][:, _sl],
                                                     OP.mult))
                        sc.wait("y", "v", v_g)
                        last[j] = sc.dma(
                            "y", gtd_scr.ap()[mo * 128:(mo + 1) * 128,
                                              nn * 512:(nn + 1) * 512],
                            sgb[:, sl])
        stYT.close()
        sc.barrier()

        with contextlib.ExitStack() as ph:
            pb = lambda n, shp, dt: ph.enter_context(cx.SBUFR(n, shp, dt))
            wo_s = [pb(f"wo_s{i}", [128, C], BF16) for i in range(32)]
            rb = [pb(f"rb{i}", [128, TH], BF16) for i in range(2)]
            sout = pb("sout", [128, TH], F32)
            for i in range(32):
                lw = sc.dma("g", wo_s[i][:],
                            wo3T[i * 128:(i + 1) * 128, :])
            sc.wait("t", "gd", lw)
            with contextlib.ExitStack() as pp:
                pso = [pp.enter_context(cx.PSUM(f"pso{j}", [128, 512], F32))
                       for j in range(4)]
                prev_ev = None
                prev_st = None
                for mo in range(8):
                    sc.wait("y", "v", prev_ev)
                    ldk = {}
                    ldk[0] = sc.dma("y", rb[0][:], skip_d[0:128, :])
                    for kc in range(32):
                        if kc + 1 < 32:
                            if kc >= 1:
                                sc.wait("y", "t", tk_done.get(kc - 1))
                            srcd = (skip_d if kc + 1 < 16 else gtd_scr.ap())
                            krow = ((kc + 1) * 128 if kc + 1 < 16 else
                                    (kc + 1 - 16) * 128)
                            ldk[kc + 1] = sc.dma(
                                "y", rb[(kc + 1) % 2][:],
                                srcd[krow:krow + 128, :])
                        sc.wait("t", "y", ldk[kc])
                        if kc == 0:
                            tk_done = {}
                        t_last = None
                        for nn in range(4):
                            sl = slice(nn * 512, (nn + 1) * 512)
                            fn = lambda e, _kc=kc, _mo=mo, _sl=sl, _n=nn: \
                                e.matmul(pso[_n][:],
                                         wo_s[_kc][:, _mo * 128:
                                                   (_mo + 1) * 128],
                                         rb[_kc % 2][:, _sl],
                                         start=(_kc == 0), stop=(_kc == 31))
                            if nn == 3:
                                t_last = sc.inc("t", fn)
                            else:
                                sc.raw("t", fn)
                        tk_done[kc] = t_last
                    sc.wait("v", "t", t_last)
                    sc.wait("v", "y", prev_st)
                    vv = None
                    for nn in range(4):
                        sl = slice(nn * 512, (nn + 1) * 512)
                        vv = sc.inc("v", lambda e, _n=nn, _sl=sl:
                                    e.tensor_copy(sout[:, _sl], pso[_n][:]))
                    prev_ev = vv
                    sc.wait("y", "v", vv)
                    prev_st = sc.dma("y", s3T[mo * 128:(mo + 1) * 128, :],
                                     sout[:])
        sc.wait("y", "y", sc.cnt["y"])
        sc.finalize(es)
    return cx.nc



def build_L3f():
    cx = Ctx()
    nc, sc = cx.nc, cx.sc
    x3T = nc.dram_tensor("x3T", [C, L], F32, kind="ExternalInput").ap()
    s3Th = nc.dram_tensor("s3Th", [C, TH], F32, kind="ExternalInput").ap()
    cnw1 = nc.dram_tensor("cnw1", [C, C], F32, kind="ExternalInput").ap()
    cnb1 = nc.dram_tensor("cnb1", [128, 8], F32, kind="ExternalInput").ap()
    cnw2 = nc.dram_tensor("cnw2", [C, 512], F32, kind="ExternalInput").ap()
    cnb2 = nc.dram_tensor("cnb2", [128, 4], F32, kind="ExternalInput").ap()
    cnw3 = nc.dram_tensor("cnw3", [512, 256], F32, kind="ExternalInput").ap()
    cnb3 = nc.dram_tensor("cnb3", [128, 2], F32, kind="ExternalInput").ap()
    cnw4 = nc.dram_tensor("cnw4", [256, 1], F32, kind="ExternalInput").ap()
    cnb4 = nc.dram_tensor("cnb4", [1, 1], F32, kind="ExternalInput").ap()
    gcw1 = nc.dram_tensor("gcw1", [C, 512], F32, kind="ExternalInput").ap()
    gcb1 = nc.dram_tensor("gcb1", [128, 4], F32, kind="ExternalInput").ap()
    gcw2 = nc.dram_tensor("gcw2", [512, 256], F32, kind="ExternalInput").ap()
    gcb2 = nc.dram_tensor("gcb2", [128, 2], F32, kind="ExternalInput").ap()
    gcw3 = nc.dram_tensor("gcw3", [256, 1], F32, kind="ExternalInput").ap()
    gcb3 = nc.dram_tensor("gcb3", [1, 1], F32, kind="ExternalInput").ap()
    ar_d = nc.dram_tensor("ar_d", [1, 1], F32, kind="ExternalInput").ap()
    x4T = nc.dram_tensor("x4T", [C, TH], F32, kind="ExternalOutput").ap()

    es = contextlib.ExitStack()
    with es:
        sbL = lambda n, shp, dt: es.enter_context(cx.SBUF(n, shp, dt))
        onesb_r = sbL("onesb_r", [1, 128], BF16)
        w1b = [sbL(f"w1b{i}", [128, C], BF16) for i in range(8)]
        w2b = [sbL(f"w2b{i}", [128, 512], BF16) for i in range(8)]
        w3b = [sbL(f"w3b{i}", [128, 256], BF16) for i in range(4)]
        w4b = [sbL(f"w4b{i}", [128, 1], BF16) for i in range(2)]
        b1c = sbL("b1c", [128, 8], F32)
        b2c = sbL("b2c", [128, 4], F32)
        b3c = sbL("b3c", [128, 2], F32)
        b4c = sbL("b4c", [1, 1], F32)
        arv = sbL("arv", [1, 1], F32)
        gfv = sbL("gfv", [1, 1], F32)
        gf7 = sbL("gf7", [1, 1], F32)
        gc8 = [sbL(f"gc8_{i}", [128, 1], F32) for i in range(8)]
        gc = sc.inc("g", lambda e: e.memset(onesb_r[:], 1.0))
        for i in range(8):
            sc.dma("g", w1b[i][:], cnw1[i * 128:(i + 1) * 128, :])
            sc.dma("g", w2b[i][:], cnw2[i * 128:(i + 1) * 128, :])
        for i in range(4):
            sc.dma("g", w3b[i][:], cnw3[i * 128:(i + 1) * 128, :])
        for i in range(2):
            sc.dma("g", w4b[i][:], cnw4[i * 128:(i + 1) * 128, :])
        sc.dma("y", b1c[:], cnb1)
        sc.dma("y", b2c[:], cnb2)
        sc.dma("y", b3c[:], cnb3)
        sc.dma("y", b4c[:], cnb4)
        sc.dma("y", arv[:], ar_d)

        stXN = contextlib.ExitStack()
        xn = emit_ln_cm(cx, stXN, x3T, L)
        sc.barrier()
        vv = None
        for i in range(8):
            sc.inc("v", lambda e, _i=i: e.reduce_sum(gc8[_i][:], xn[_i][:],
                                                     axis=AX))
            vv = sc.inc("v", lambda e, _i=i: e.tensor_scalar(
                gc8[_i][:], gc8[_i][:], 1.0 / L, None, OP.mult))
        stOU = contextlib.ExitStack()
        outA = [stOU.enter_context(cx.SBUFR(f"outA{i}", [128, TH], BF16))
                for i in range(8)]
        outB = [stOU.enter_context(cx.SBUFR(f"outB{i}", [128, TH], BF16))
                for i in range(8)]
        for i in range(8):
            vv = sc.inc("v", lambda e, _i=i: e.tensor_copy(
                outA[_i][:], xn[_i][:, 0:TH]))
        sc.barrier()
        stXN.close()

        with contextlib.ExitStack() as ph:
            pb = lambda n, shp, dt: ph.enter_context(cx.SBUFR(n, shp, dt))
            g1b = [pb(f"g1b{i}", [128, 512], BF16) for i in range(8)]
            g2b = [pb(f"g2b{i}", [128, 256], BF16) for i in range(4)]
            g3b = [pb(f"g3b{i}", [128, 1], BF16) for i in range(2)]
            gb1c = pb("gb1c", [128, 4], F32)
            gb2c = pb("gb2c", [128, 2], F32)
            gb3c = pb("gb3c", [1, 1], F32)
            hh1 = pb("hh1", [128, 4], BF16)
            hh2 = pb("hh2", [128, 2], BF16)
            gcc = pb("gcc", [128, 8], BF16)
            for i in range(8):
                sc.dma("g", g1b[i][:], gcw1[i * 128:(i + 1) * 128, :])
            for i in range(4):
                sc.dma("g", g2b[i][:], gcw2[i * 128:(i + 1) * 128, :])
            for i in range(2):
                lw = sc.dma("g", g3b[i][:], gcw3[i * 128:(i + 1) * 128, :])
            sc.dma("y", gb1c[:], gcb1)
            sc.dma("y", gb2c[:], gcb2)
            ldb = sc.dma("y", gb3c[:], gcb3)
            sc.wait("v", "gd", lw)
            sc.wait("v", "y", ldb)
            vv = None
            for i in range(8):
                vv = sc.inc("v", lambda e, _i=i: e.tensor_copy(
                    gcc[:, _i:_i + 1], gc8[_i][:]))
            sc.wait("t", "v", vv)
            with cx.PSUM("ps_g1", [128, 4], F32) as ps1, \
                    cx.PSUM("ps_g2", [128, 2], F32) as ps2, \
                    cx.PSUM("ps_g3", [1, 1], F32) as ps3:
                t1 = None
                for mo in range(4):
                    t1 = mm_accum(sc, ps1[:, mo:mo + 1],
                                  [(g1b[cc][:, mo * 128:(mo + 1) * 128],
                                    gcc[:, cc:cc + 1]) for cc in range(8)])
                sc.wait("s", "t", t1)
                s1 = None
                for mo in range(4):
                    s1 = sc.inc("s", lambda e, _m=mo: e.activation(
                        hh1[:, _m:_m + 1], ps1[:, _m:_m + 1], AF.Gelu,
                        bias=gb1c[:, _m:_m + 1]))
                sc.wait("t", "s", s1)
                t2 = None
                for mo in range(2):
                    t2 = mm_accum(sc, ps2[:, mo:mo + 1],
                                  [(g2b[cc][:, mo * 128:(mo + 1) * 128],
                                    hh1[:, cc:cc + 1]) for cc in range(4)],)
                sc.wait("s", "t", t2)
                s2 = None
                for mo in range(2):
                    s2 = sc.inc("s", lambda e, _m=mo: e.activation(
                        hh2[:, _m:_m + 1], ps2[:, _m:_m + 1], AF.Gelu,
                        bias=gb2c[:, _m:_m + 1]))
                sc.wait("t", "s", s2)
                t3 = mm_accum(sc, ps3[:],
                              [(g3b[cc][:], hh2[:, cc:cc + 1])
                               for cc in range(2)])
                sc.wait("s", "t", t3)
                s3 = sc.inc("s", lambda e: e.activation(
                    gfv[:], ps3[:], AF.Sigmoid, bias=gb3c[:]))
        sc.barrier()

        sigma = [4, 5, 6, 7, 0, 1, 2, 3]
        signs = [1, 1, -1, -1, -1, -1, 1, 1]
        cur, nxt = outA, outB
        with contextlib.ExitStack() as ph:
            pb = lambda n, shp, dt: ph.enter_context(cx.SBUFR(n, shp, dt))
            h1t = [pb(f"h1t{i}", [128, TH], BF16) for i in range(8)]
            h2t = [pb(f"h2t{i}", [128, TH], BF16) for i in range(4)]
            h3t = [pb(f"h3t{i}", [128, TH], BF16) for i in range(2)]
            tfr = pb("tfr", [1, TH], F32)
            cpr = pb("cpr", [1, TH], BF16)
            tmpc = pb("tmpc", [128, 512], F32)
            cbt = pb("cbt", [128, TH], F32)
            with contextlib.ExitStack() as pp:
                psc = [pp.enter_context(cx.PSUM(f"psc{j}", [128, 512],
                                                F32)) for j in range(4)]
                ps_cb = pp.enter_context(cx.PSUM("ps_cb", [128, 512], F32))
                lastc = {}
                for cyc in range(3):
                    sc.wait("t", "s", sc.cnt["s"])
                    sc.wait("t", "v", sc.cnt["v"])
                    for mo in range(8):
                        for nn in range(4):
                            j = nn
                            sc.wait("t", "s", lastc.get(j))
                            mm = mm_accum(
                                sc, psc[j][:],
                                [(w1b[cc][:, mo * 128:(mo + 1) * 128],
                                  cur[cc][:, nn * 512:(nn + 1) * 512])
                                 for cc in range(8)])
                            sc.wait("s", "t", mm)
                            sl = slice(nn * 512, (nn + 1) * 512)
                            lastc[j] = sc.inc(
                                "s", lambda e, _j=j, _m=mo, _sl=sl:
                                e.activation(h1t[_m][:, _sl], psc[_j][:],
                                             AF.Gelu,
                                             bias=b1c[:, _m:_m + 1]))
                    for mo in range(4):
                        for nn in range(4):
                            j = nn
                            sc.wait("t", "s", lastc.get(j))
                            mm = mm_accum(
                                sc, psc[j][:],
                                [(w2b[cc][:, mo * 128:(mo + 1) * 128],
                                  h1t[cc][:, nn * 512:(nn + 1) * 512])
                                 for cc in range(8)])
                            sc.wait("s", "t", mm)
                            sl = slice(nn * 512, (nn + 1) * 512)
                            lastc[j] = sc.inc(
                                "s", lambda e, _j=j, _m=mo, _sl=sl:
                                e.activation(h2t[_m][:, _sl], psc[_j][:],
                                             AF.Gelu,
                                             bias=b2c[:, _m:_m + 1]))
                    for mo in range(2):
                        for nn in range(4):
                            j = nn
                            sc.wait("t", "s", lastc.get(j))
                            mm = mm_accum(
                                sc, psc[j][:],
                                [(w3b[cc][:, mo * 128:(mo + 1) * 128],
                                  h2t[cc][:, nn * 512:(nn + 1) * 512])
                                 for cc in range(4)])
                            sc.wait("s", "t", mm)
                            sl = slice(nn * 512, (nn + 1) * 512)
                            lastc[j] = sc.inc(
                                "s", lambda e, _j=j, _m=mo, _sl=sl:
                                e.activation(h3t[_m][:, _sl], psc[_j][:],
                                             AF.Gelu,
                                             bias=b3c[:, _m:_m + 1]))
                    s_tf = None
                    for nn in range(4):
                        sl = slice(nn * 512, (nn + 1) * 512)
                        sc.wait("t", "s", sc.cnt["s"])
                        t4 = mm_accum(
                            sc, ps_cb[0:1, :],
                            [(w4b[cc][:], h3t[cc][:, sl])
                             for cc in range(2)])
                        sc.wait("s", "t", t4)
                        s_tf = sc.inc("s", lambda e, _sl=sl: e.activation(
                            tfr[:, _sl], ps_cb[0:1, :], AF.Sigmoid,
                            bias=b4c[:]))
                    sc.wait("v", "s", s_tf)
                    sc.inc("v", lambda e: e.tensor_single_scalar(
                        tfr[:], tfr[:], 0.3, OP.mult))
                    sc.inc("v", lambda e: e.tensor_scalar(
                        gf7[:], gfv[:], 0.7, None, OP.mult))
                    sc.inc("v", lambda e: e.tensor_scalar(
                        tfr[:], tfr[:], gf7[0:1, 0:1], -0.5, OP.add, OP.add))
                    sc.inc("v", lambda e: e.tensor_scalar(
                        tfr[:], tfr[:], 2.0, None, OP.mult))
                    v_cp = sc.inc("v", lambda e: e.tensor_scalar(
                        cpr[:], tfr[:], arv[0:1, 0:1], 0.1, OP.mult, OP.add))
                    sc.wait("t", "v", v_cp)
                    sc.wait("t", "s", sc.cnt["s"])
                    v_bc = None
                    for nn in range(4):
                        sl = slice(nn * 512, (nn + 1) * 512)
                        sc.wait("t", "v", v_bc)
                        t_cb = sc.inc("t", lambda e, _sl=sl: e.matmul(
                            ps_cb[:], onesb_r[:], cpr[0:1, _sl],
                            start=True, stop=True))
                        sc.wait("v", "t", t_cb)
                        v_bc = sc.inc("v", lambda e, _sl=sl: e.tensor_copy(
                            cbt[:, _sl], ps_cb[:]))
                    vv = None
                    for i in range(8):
                        si = sigma[i]
                        op2 = OP.add if signs[i] > 0 else OP.subtract
                        for nn in range(4):
                            sl = slice(nn * 512, (nn + 1) * 512)
                            sc.inc("v", lambda e, _s=cur[si], _sl=sl:
                                   e.tensor_tensor(tmpc[:],
                                                   _s[:, _sl],
                                                   cbt[:, _sl], OP.mult))
                            vv = sc.inc("v", lambda e, _n=nxt[i], _c=cur[i],
                                        _sl=sl, _o=op2:
                                        e.tensor_tensor(_n[:, _sl],
                                                        _c[:, _sl],
                                                        tmpc[:], _o))
                    cur, nxt = nxt, cur
        sc.barrier()

        with contextlib.ExitStack() as ph:
            pb = lambda n, shp, dt: ph.enter_context(cx.SBUFR(n, shp, dt))
            x3b = pb("x3b", [128, TH], F32)
            s3b = pb("s3b", [128, TH], F32)
            gsl = pb("gsl", [128, TH], BF16)
            tmpc2 = pb("tmpc2", [128, 512], F32)
            xob = pb("xob", [128, TH], F32)
            prev_st = None
            prev_v = None
            for ccx in range(8):
                sc.wait("y", "v", prev_v)
                ldx = sc.dma("y", x3b[:], x3T[ccx * 128:(ccx + 1) * 128,
                                              0:TH])
                lds = sc.dma("y", s3b[:], s3Th[ccx * 128:(ccx + 1) * 128, :])
                v_c = sc.cnt["v"]
                sc.wait("s", "v", v_c)
                sc.wait("s", "y", lds)
                s_g = sc.inc("s", lambda e, _c=ccx: e.activation(
                    gsl[:], cur[_c][:], AF.Silu))
                sc.wait("v", "s", s_g)
                sc.wait("v", "y", lds)
                vv = None
                for nn in range(4):
                    sl = slice(nn * 512, (nn + 1) * 512)
                    sc.inc("v", lambda e, _sl=sl: e.tensor_tensor(
                        tmpc2[:], s3b[:, _sl], gsl[:, _sl], OP.mult))
                    vv = sc.inc("v", lambda e, _sl=sl: e.tensor_tensor(
                        xob[:, _sl], tmpc2[:], x3b[:, _sl], OP.add))
                prev_v = vv
                sc.wait("y", "v", vv)
                prev_st = sc.dma("y", x4T[ccx * 128:(ccx + 1) * 128, :],
                                 xob[:])
                sc.wait("v", "y", prev_st)
        stOU.close()
        sc.wait("y", "y", sc.cnt["y"])
        sc.finalize(es)
    return cx.nc



def prep_L3_inputs(inputs, x3):
    W_in = f32c(inputs["L3_attn_W_in"])
    rows = np.concatenate([W_in[0:4096], W_in[6144:8232]], axis=0)
    wuT = np.zeros((C, 6272), np.float32)
    wuT[:, 0:6184] = rows.T
    cw = f32c(inputs["L3_attn_conv_w"]).reshape(2080, 4)
    cwp = np.zeros((2176, 4), np.float32)
    cwp[0:2080] = cw
    cw4 = np.ascontiguousarray(
        cwp.reshape(17, 128, 4).transpose(1, 0, 2).reshape(128, 68))
    cbp = np.zeros((2176,), np.float32)
    cbp[0:2080] = f32c(inputs["L3_attn_conv_b"])
    cb17 = np.ascontiguousarray(cbp.reshape(17, 128).T)
    dtb = np.zeros((128, 1), np.float32)
    dtb[32:40, 0] = f32c(inputs["L3_attn_dt_bias"])
    alog = f32c(inputs["L3_attn_A_log"]).reshape(8, 1)
    dpar = f32c(inputs["L3_attn_D_param"]).reshape(1, 8)
    idn = np.eye(128, dtype=np.float32)
    lng = f32c(inputs["L3_attn_ln_gamma"]).reshape(1, 2048)
    lnb = f32c(inputs["L3_attn_ln_beta"]).reshape(1, 2048)
    wgT = np.ascontiguousarray(f32c(inputs["L3_attn_W_gate"]).T)
    bg = np.ascontiguousarray(
        f32c(inputs["L3_attn_b_gate"]).reshape(16, 128).T)
    wo3T = np.ascontiguousarray(f32c(inputs["L3_attn_W_out"]).T)
    mapsG, mapsH = [], []
    for core in range(NC8):
        b, th = core // 2, core % 2
        xT = np.ascontiguousarray(x3[b].T)
        rot = np.concatenate([xT[:, th * TH:(th + 1) * TH],
                              xT[:, (1 - th) * TH:(2 - th) * TH]], axis=1)
        mapsG.append({
            "x3T": np.ascontiguousarray(rot),
            "qoff": np.array([[th * TH]], dtype=np.float32),
            "wuT": wuT, "cw4": cw4, "cb17": cb17, "dtb": dtb,
            "alog": alog, "dpar": dpar, "idn": idn,
        })
        mapsH.append({
            "qoff": np.array([[th * TH]], dtype=np.float32),
            "lng_d": lng, "lnb_d": lnb, "wgT": wgT, "bg_d": bg,
            "wo3T": wo3T, "idn": idn,
        })
    return mapsG, mapsH


def prep_L3f_inputs(inputs, x3, resH):
    def cb(v, nch):
        v = f32c(v)
        return np.ascontiguousarray(v.reshape(nch, 128).T) if nch > 0 else v
    maps = []
    for core in range(NC8):
        b, th = core // 2, core % 2
        xT = np.ascontiguousarray(x3[b].T)
        rot = np.concatenate([xT[:, th * TH:(th + 1) * TH],
                              xT[:, (1 - th) * TH:(2 - th) * TH]], axis=1)
        maps.append({
            "x3T": rot,
            "s3Th": np.asarray(resH[core]["s3T"]),
            "cnw1": f32c(inputs["L3_ffn_cn_w1"]),
            "cnb1": cb(inputs["L3_ffn_cn_b1"], 8),
            "cnw2": f32c(inputs["L3_ffn_cn_w2"]),
            "cnb2": cb(inputs["L3_ffn_cn_b2"], 4),
            "cnw3": f32c(inputs["L3_ffn_cn_w3"]),
            "cnb3": cb(inputs["L3_ffn_cn_b3"], 2),
            "cnw4": f32c(inputs["L3_ffn_cn_w4"]),
            "cnb4": f32c(inputs["L3_ffn_cn_b4"]).reshape(1, 1),
            "gcw1": f32c(inputs["L3_ffn_gc_w1"]),
            "gcb1": cb(inputs["L3_ffn_gc_b1"], 4),
            "gcw2": f32c(inputs["L3_ffn_gc_w2"]),
            "gcb2": cb(inputs["L3_ffn_gc_b2"], 2),
            "gcw3": f32c(inputs["L3_ffn_gc_w3"]),
            "gcb3": f32c(inputs["L3_ffn_gc_b3"]).reshape(1, 1),
            "ar_d": f32c(inputs["L3_ffn_adaptive_range"]).reshape(1, 1),
        })
    return maps


def run_L3(inputs, x3):
    for nm, bld in (("L3a", build_L3a), ("L3b", build_L3b),
                    ("L3f", build_L3f)):
        if nm not in _builder_cache:
            _builder_cache[nm] = bld()
    mapsG, mapsH = prep_L3_inputs(inputs, x3)
    resG = run_bass_kernel_spmd(_builder_cache["L3a"], mapsG,
                                core_ids=list(range(NC8))).results
    for core in range(NC8):
        b, th = core // 2, core % 2
        pair = 2 * b + (1 - th)
        mapsH[core]["y_d"] = np.asarray(resG[core]["y_scr"])
        mapsH[core]["ct_d"] = np.ascontiguousarray(
            np.asarray(resG[core]["bc_o"])[16:32])
        mapsH[core]["clg_d"] = np.asarray(resG[core]["clg_o"])
        mapsH[core]["hin_d"] = np.asarray(resG[pair]["hfin_o"])
        mapsH[core]["skip_d"] = np.asarray(resG[core]["skip_scr"])
    resH = run_bass_kernel_spmd(_builder_cache["L3b"], mapsH,
                                core_ids=list(range(NC8))).results
    mapsI = prep_L3f_inputs(inputs, x3, resH)
    resI = run_bass_kernel_spmd(_builder_cache["L3f"], mapsI,
                                core_ids=list(range(NC8))).results
    x4 = np.empty((B, L, C), dtype=np.float32)
    for core in range(NC8):
        b, th = core // 2, core % 2
        x4[b, th * TH:(th + 1) * TH, :] = resI[core]["x4T"].T
    return x4


# version 02
# speedup vs baseline: 1.1579x; 1.1579x over previous
import contextlib
from collections import defaultdict

import numpy as np
import ml_dtypes
import concourse.bass as bass
import concourse.mybir as mybir
from concourse.alu_op_type import AluOpType as OP
from concourse.bass_utils import run_bass_kernel_spmd

F32 = mybir.dt.float32
BF16 = mybir.dt.bfloat16
I32 = mybir.dt.int32
AF = mybir.ActivationFunctionType
AX = mybir.AxisListType.X

B, L, C = 4, 4096, 1024
TH = L // 2
NC8 = 8
H, D = 16, 64
bf16 = ml_dtypes.bfloat16

_builder_cache = {}


def f32c(x):
    return np.ascontiguousarray(np.asarray(x), dtype=np.float32)


ENGS = ("t", "v", "s", "g", "y")
SEMS = ("t", "v", "s", "g", "gd", "y")


class Sched:

    def __init__(self, nc):
        self.nc = nc
        self.q = defaultdict(list)
        self.cnt = dict.fromkeys(SEMS, 0)
        self.sem = {}

    def raw(self, e, fn):
        return self.inc(e, fn, 1)

    def inc(self, e, fn, k=1, selfwait=True, semkey=None):
        semkey = semkey or e
        self.q[e].append(("op", fn, k, selfwait, semkey))
        self.cnt[semkey] += k
        return self.cnt[semkey]

    def dma(self, e, out, in_, **kw):
        semkey = "gd" if e == "g" else e
        return self.inc(e, lambda eng: eng.dma_start(out, in_, **kw), 16,
                        selfwait=True, semkey=semkey)

    def wait(self, e, other, count):
        if count is None or count <= 0:
            return
        self.q[e].append(("w", other, count))

    def barrier(self):
        counts = dict(self.cnt)
        for e in ENGS:
            for o in SEMS:
                if o != e and not (e == "g" and o == "gd") and counts[o] > 0:
                    self.wait(e, o, counts[o])

    def _replay(self, e, eng):
        run = dict.fromkeys(SEMS, 0)
        last_wait = {}
        for ent in self.q[e]:
            if ent[0] == "w":
                _, o, c = ent
                if last_wait.get(o, -1) < c:
                    eng.wait_ge(self.sem[o], c)
                    last_wait[o] = c
            else:
                _, fn, k, sw, semkey = ent
                if sw and e != "t":
                    keys = (semkey, "g") if e == "g" else (semkey,)
                    for sk in keys:
                        if run[sk] > 0 and last_wait.get(sk, -1) < run[sk]:
                            eng.wait_ge(self.sem[sk], run[sk])
                            last_wait[sk] = run[sk]
                fn(eng).then_inc(self.sem[semkey], k)
                run[semkey] += k

    def finalize(self, es):
        if self.cnt["y"] > 0:
            self.wait("y", "y", self.cnt["y"])
        if self.cnt["gd"] > 0:
            self.wait("g", "gd", self.cnt["gd"])
        for e in SEMS:
            self.sem[e] = es.enter_context(self.nc.semaphore("sem_" + e))
        block = es.enter_context(self.nc.Block())

        @block.tensor
        def _(eng):
            self._replay("t", eng)

        @block.vector
        def _(eng):
            self._replay("v", eng)

        @block.scalar
        def _(eng):
            self._replay("s", eng)

        @block.gpsimd
        def _(eng):
            self._replay("g", eng)

        @block.sync
        def _(eng):
            self._replay("y", eng)


def mm_accum(sc, ps_ap, pairs):
    n = len(pairs)
    for i, (lt, rh) in enumerate(pairs):
        if i == n - 1:
            c = sc.inc("t", lambda e, _l=lt, _r=rh, _st=(i == 0):
                       e.matmul(ps_ap, _l, _r, start=_st, stop=True))
        else:
            sc.raw("t", lambda e, _l=lt, _r=rh, _st=(i == 0):
                   e.matmul(ps_ap, _l, _r, start=_st, stop=False))
    return c


def build_L0():
    nc = bass.Bass("TRN2", target_bir_lowering=False)
    _uid = [0]

    def _nm(n):
        _uid[0] += 1
        return f"{n}_u{_uid[0]}"

    def SBUF(n, shp, dt, side=None):
        return nc.sbuf_tensor(_nm(n), shp, dt, side=side)

    def SBUFR(n, shp, dt):
        return nc.sbuf_tensor(_nm(n), shp, dt, side="right")

    def PSUM(n, shp, dt):
        return nc.psum_tensor(_nm(n), shp, dt)
    x0T = nc.dram_tensor("x0T", [C, L], F32, kind="ExternalInput").ap()
    qoffd = nc.dram_tensor("qoff", [1, 1], F32, kind="ExternalInput").ap()
    wqT = nc.dram_tensor("wqT", [C, C], F32, kind="ExternalInput").ap()
    wkT = nc.dram_tensor("wkT", [C, C], F32, kind="ExternalInput").ap()
    wvT = nc.dram_tensor("wvT", [C, C], F32, kind="ExternalInput").ap()
    woT = nc.dram_tensor("woT", [C, C], F32, kind="ExternalInput").ap()
    bo_d = nc.dram_tensor("bo2", [128, 8], F32, kind="ExternalInput").ap()
    wfT = nc.dram_tensor("wfT", [C, C], F32, kind="ExternalInput").ap()
    x1T = nc.dram_tensor("x1T", [C, TH], F32, kind="ExternalOutput").ap()

    k_scr = nc.dram_tensor("k_scr", [H, 64, L], BF16)
    q_scr = nc.dram_tensor("q_scr", [H, 64, TH], BF16)
    g_scr = nc.dram_tensor("g_scr", [C, TH], BF16)
    kq_scr = nc.dram_tensor("kq_scr", [H, 2], F32)

    slopes = [0.7071067811865476 ** (h + 1) for h in range(H)]
    DW = 6272

    sc = Sched(nc)
    es = contextlib.ExitStack()
    with es:
        sb = lambda n, shp, dt: es.enter_context(SBUF(n, shp, dt))
        iot = sb("iot", [128, 1], F32)
        pcol0 = sb("pcol0", [128, 1], F32)
        pcol1 = sb("pcol1", [128, 1], F32)
        qcol = sb("qcol", [128, 1], F32)
        ones_cf = sb("ones_cf", [128, 1], F32)
        ones_cb = sb("ones_cb", [128, 1], BF16)
        ones64 = sb("ones64", [1, 64], F32)
        onesb_r = sb("onesb_r", [1, 128], BF16)
        hsel = sb("hsel", [128, 2], F32)
        negb = [sb(f"negb{h}", [128, 1], F32) for h in range(H)]

        eps1 = sb("eps1", [1, 1], F32)
        iota_i = sb("iota_i", [128, 1], I32)
        sc.raw("g", lambda e: e.iota(iota_i[:], pattern=[[0, 1]], base=0,
                                     channel_multiplier=1))
        sc.raw("g", lambda e: e.memset(eps1[:], 1e-5))
        sc.raw("g", lambda e: e.memset(ones_cf[:], 1.0))
        sc.raw("g", lambda e: e.memset(ones_cb[:], 1.0))
        sc.raw("g", lambda e: e.memset(ones64[:], 1.0))
        sc.raw("g", lambda e: e.memset(onesb_r[:], 1.0))
        sc.raw("g", lambda e: e.memset(hsel[0:64, 0:1], 1.0))
        sc.raw("g", lambda e: e.memset(hsel[64:128, 0:1], 0.0))
        sc.raw("g", lambda e: e.memset(hsel[0:64, 1:2], 0.0))
        g_const = sc.inc("g", lambda e: e.memset(hsel[64:128, 1:2], 1.0))
        sc.wait("v", "g", g_const)
        v_iot = sc.inc("v", lambda e: e.tensor_copy(iot[:], iota_i[:]))
        qld = sc.dma("y", qcol[0:1, :], qoffd)

        with PSUM("ps_q", [128, 1], F32) as ps_q:
            qrow_b = sb("qrow_b", [1, 1], BF16)
            sc.wait("v", "y", qld)
            v_q = sc.inc("v", lambda e: e.tensor_copy(qrow_b[:], qcol[0:1, :]))
            sc.wait("t", "v", v_q)
            sc.wait("t", "g", g_const)
            t_q = sc.inc("t", lambda e: e.matmul(ps_q[:], onesb_r[:],
                                                 qrow_b[:], start=True,
                                                 stop=True))
            sc.wait("v", "t", t_q)
            sc.inc("v", lambda e: e.tensor_copy(qcol[:], ps_q[:]))
        sc.inc("v", lambda e: e.tensor_scalar(pcol0[:], iot[:], 3968.0, None,
                                              OP.add))
        sc.inc("v", lambda e: e.tensor_scalar(pcol1[:], qcol[:], -2.0,
                                              3968.0 + 2048.0, OP.mult, OP.add))
        v_pc = sc.inc("v", lambda e: e.tensor_tensor(pcol1[:], pcol1[:],
                                                     iot[:], OP.add))
        t_psq_done = sc.cnt["t"]

        stA = contextlib.ExitStack()
        sa = lambda n, shp, dt: stA.enter_context(SBUFR(n, shp, dt))
        xn = [sa(f"xn{i}", [128, L], BF16) for i in range(8)]

        with contextlib.ExitStack() as ph:
            pb = lambda n, shp, dt: ph.enter_context(SBUFR(n, shp, dt))
            x0b = [pb(f"x0b{i}", [128, 2048], F32) for i in range(8)]
            sqb = pb("sqb", [128, 2048], F32)
            srow = pb("srow", [1, 2048], F32)
            s2row = pb("s2row", [1, 2048], F32)
            rrow = pb("rrow", [1, 2048], BF16)
            crow = pb("crow", [1, 2048], BF16)
            mrow = pb("mrow", [1, 2048], F32)
            vrow = pb("vrow", [1, 2048], F32)
            m2row = pb("m2row", [1, 2048], F32)
            tmpx = pb("tmpx", [128, 1024], F32)
            v_before = sc.cnt["v"]
            for half in range(2):
                t0 = half * 2048
                with contextlib.ExitStack() as pp:
                    ps_s = pp.enter_context(PSUM("ps_s", [1, 2048], F32))
                    ps_s2 = pp.enter_context(PSUM("ps_s2", [1, 2048], F32))
                    lds = []
                    for i in range(8):
                        lds.append(sc.dma("y", x0b[i][:],
                                          x0T[i * 128:(i + 1) * 128,
                                              t0:t0 + 2048]))
                    sc.wait("t", "v", v_pc)
                    prev_t = None
                    for i in range(8):
                        sc.wait("s", "y", lds[i])
                        if prev_t:
                            sc.wait("s", "t", prev_t)
                        s_sq = sc.inc("s", lambda e, _i=i: e.activation(
                            sqb[:], x0b[_i][:], AF.Square))
                        sc.wait("t", "s", s_sq)
                        sc.wait("t", "y", lds[i])
                        for nn in range(4):
                            sl = slice(nn * 512, nn * 512 + 512)
                            sc.raw("t", lambda e, _i=i, _sl=sl: e.matmul(
                                ps_s[:, _sl], ones_cf[:], x0b[_i][:, _sl],
                                start=(_i == 0), stop=(_i == 7)))
                            if nn < 3:
                                sc.raw("t", lambda e, _i=i, _sl=sl: e.matmul(
                                    ps_s2[:, _sl], ones_cf[:], sqb[:, _sl],
                                    start=(_i == 0), stop=(_i == 7)))
                            else:
                                prev_t = sc.inc(
                                    "t", lambda e, _i=i, _sl=sl: e.matmul(
                                        ps_s2[:, _sl], ones_cf[:], sqb[:, _sl],
                                        start=(_i == 0), stop=(_i == 7)))
                    sc.wait("v", "t", prev_t)
                    sc.inc("v", lambda e: e.tensor_copy(srow[:], ps_s[:]))
                    v_st = sc.inc("v", lambda e: e.tensor_copy(
                        s2row[:], ps_s2[:]))
                sl2 = slice(0, 2048)
                sc.inc("v", lambda e, _s=sl2: e.tensor_scalar(
                    mrow[0:1, _s], srow[0:1, _s], 1.0 / C, None, OP.mult))
                sc.inc("v", lambda e, _s=sl2: e.tensor_scalar(
                    vrow[0:1, _s], s2row[0:1, _s], 1.0 / C, None, OP.mult))
                sc.inc("v", lambda e, _s=sl2: e.tensor_tensor(
                    m2row[:], mrow[0:1, _s], mrow[0:1, _s], OP.mult))
                v_var = sc.inc("v", lambda e, _s=sl2: e.tensor_tensor(
                    vrow[0:1, _s], vrow[0:1, _s], m2row[:], OP.subtract))
                sc.wait("s", "v", v_var)
                sc.inc("s", lambda e, _s=sl2: e.activation(
                    vrow[0:1, _s], vrow[0:1, _s], AF.Ln, bias=eps1[:]))
                s_r = sc.inc("s", lambda e, _s=sl2: e.activation(
                    rrow[0:1, _s], vrow[0:1, _s], AF.Exp, scale=-0.5))
                sc.wait("v", "s", s_r)
                sc.inc("v", lambda e, _s=sl2: e.tensor_tensor(
                    m2row[:], mrow[0:1, _s], rrow[0:1, _s], OP.mult))
                v_cr = sc.inc("v", lambda e, _s=sl2: e.tensor_scalar(
                    crow[0:1, _s], m2row[:], -1.0, None, OP.mult))
                for quar in range(2):
                    q0 = t0 + quar * 1024
                    with contextlib.ExitStack() as pp:
                        ps_a = pp.enter_context(
                            PSUM("ps_a", [128, 1024], F32))
                        ps_c = pp.enter_context(
                            PSUM("ps_c", [128, 1024], F32))
                        sc.wait("t", "v", v_cr)
                        t_bc = None
                        for nn in range(2):
                            sl = slice(nn * 512, nn * 512 + 512)
                            sg = slice(q0 - t0 + nn * 512,
                                       q0 - t0 + nn * 512 + 512)
                            sc.raw("t", lambda e, _sl=sl, _sg=sg: e.matmul(
                                ps_a[:, _sl], onesb_r[:], rrow[0:1, _sg],
                                start=True, stop=True))
                            t_bc = sc.inc("t", lambda e, _sl=sl, _sg=sg:
                                          e.matmul(ps_c[:, _sl], onesb_r[:],
                                                   crow[0:1, _sg],
                                                   start=True, stop=True))
                        sc.wait("v", "t", t_bc)
                        v_norm = None
                        for i in range(8):
                            qq = slice(quar * 1024, quar * 1024 + 1024)
                            sg = slice(q0, q0 + 1024)
                            sc.inc("v", lambda e, _i=i, _q=qq: e.tensor_tensor(
                                tmpx[:], x0b[_i][:, _q], ps_a[:], OP.mult))
                            v_norm = sc.inc(
                                "v", lambda e, _i=i, _g=sg: e.tensor_tensor(
                                    xn[_i][:, _g], tmpx[:], ps_c[:], OP.add))
                        sc.wait("t", "v", v_norm)
                if half == 0:
                    sc.wait("y", "v", sc.cnt["v"])
        sc.barrier()

        stV = contextlib.ExitStack()
        v_t = stV.enter_context(SBUF("v_t", [128, 32 * H * 65], BF16))

        with contextlib.ExitStack() as ph:
            pb = lambda n, shp, dt: ph.enter_context(SBUFR(n, shp, dt))
            wbuf = [pb(f"wbuf{i}", [128, C], BF16) for i in range(8)]
            kpair = pb("kpair", [128, L], BF16)
            qpair = pb("qpair", [128, TH], BF16)
            sqq = pb("sqq", [128, 2048], F32)
            nmax = pb("nmax", [2, 1], F32)
            nmax2 = pb("nmax2", [2, 2], F32)

            def load_w(dram):
                cs = [sc.dma("g", wbuf[i][:], dram[i * 128:(i + 1) * 128, :])
                      for i in range(8)]
                return cs[-1]

            lw = load_w(wvT)
            sc.wait("v", "gd", lw)
            v_ones = sc.inc("v", lambda e: e.memset(v_t[:], 1.0))
            sc.wait("t", "gd", lw)
            sc.wait("t", "v", v_ones)
            with contextlib.ExitStack() as pp:
                psv = [pp.enter_context(PSUM(f"psv{j}", [128, 512],
                                                       F32)) for j in range(4)]
                ev = {}
                for tc in range(32):
                    for nv in range(2):
                        j = (tc * 2 + nv) % 4
                        sc.wait("t", "v", ev.get(j))
                        mm = mm_accum(sc, psv[j][:],
                                      [(xn[cc][:, tc * 128:(tc + 1) * 128],
                                        wbuf[cc][:, nv * 512:(nv + 1) * 512])
                                       for cc in range(8)])
                        sc.wait("v", "t", mm)
                        base = tc * (H * 65) + nv * 8 * 65
                        out_ap = v_t[:, base:base + 8 * 65].rearrange(
                            "p (h d) -> p h d", h=8)[:, :, 0:64]
                        in_ap = psv[j][:].rearrange("p (h d) -> p h d", h=8)
                        ev[j] = sc.inc("v", lambda e, _o=out_ap, _i=in_ap:
                                       e.tensor_copy(_o, _i))
                sc.wait("t", "v", sc.cnt["v"])

            sc.wait("g", "t", sc.cnt["t"])
            lw = load_w(wqT)
            sc.wait("t", "gd", lw)
            with contextlib.ExitStack() as pp:
                psq = [pp.enter_context(PSUM(f"psq{j}", [128, 512],
                                                       F32)) for j in range(4)]
                psn = pp.enter_context(PSUM("psn", [2, 2048], F32))
                prev_v = None
                prev_store = None
                prev_nrm = None
                prev_ssq = None
                prev_nstore = None
                for hp in range(8):
                    sc.wait("t", "v", prev_v)
                    sc.wait("v", "s", prev_ssq)
                    vq = None
                    for nq in range(4):
                        j = nq
                        mm = mm_accum(sc, psq[j][:],
                                      [(wbuf[cc][:, hp * 128:(hp + 1) * 128],
                                        xn[cc][:, nq * 512:(nq + 1) * 512])
                                       for cc in range(8)])
                        sc.wait("v", "t", mm)
                        sc.wait("v", "y", prev_store)
                        sl = slice(nq * 512, (nq + 1) * 512)
                        vq = sc.inc("v", lambda e, _j=j, _sl=sl: e.tensor_scalar(
                            qpair[:, _sl], psq[_j][:], 0.125, None, OP.mult))
                    prev_v = vq
                    sc.wait("y", "v", vq)
                    st1 = sc.dma("y", q_scr.ap()[2 * hp, :, :], qpair[0:64, :])
                    prev_store = sc.dma("y", q_scr.ap()[2 * hp + 1, :, :],
                                        qpair[64:128, :])
                    sc.wait("s", "v", vq)
                    sc.wait("s", "t", prev_nrm)
                    prev_ssq = s_sq = sc.inc("s", lambda e: e.activation(
                        sqq[:, 0:2048], qpair[:, 0:2048], AF.Square))
                    sc.wait("t", "s", s_sq)
                    for nn in range(4):
                        sl = slice(nn * 512, nn * 512 + 512)
                        if nn < 3:
                            sc.raw("t", lambda e, _sl=sl: e.matmul(
                                psn[:, _sl], hsel[:], sqq[:, _sl],
                                start=True, stop=True))
                        else:
                            prev_nrm = sc.inc("t", lambda e, _sl=sl: e.matmul(
                                psn[:, _sl], hsel[:], sqq[:, _sl],
                                start=True, stop=True))
                    sc.wait("v", "t", prev_nrm)
                    sc.wait("v", "y", prev_nstore)
                    v_mx = sc.inc("v", lambda e: e.reduce_max(
                        nmax[0:2, 0:1], psn[:, 0:2048], axis=AX))
                    sc.wait("y", "v", v_mx)
                    prev_nstore = sc.dma(
                        "y", kq_scr.ap()[2 * hp:2 * hp + 2, 0:1],
                        nmax[0:2, 0:1], allow_slow_non_contiguous=True)

            sc.wait("g", "t", sc.cnt["t"])
            lw = load_w(wkT)
            sc.wait("t", "gd", lw)
            with contextlib.ExitStack() as pp:
                psk = [pp.enter_context(PSUM(f"psk{j}", [128, 512],
                                                       F32)) for j in range(4)]
                psn = pp.enter_context(PSUM("psn2", [2, 2048], F32))
                prev_v = None
                prev_store = None
                prev_nrm = None
                prev_ssq = None
                for hp in range(8):
                    sc.wait("t", "v", prev_v)
                    sc.wait("v", "s", prev_ssq)
                    ev = {}
                    vk = None
                    for nk in range(8):
                        j = nk % 4
                        sc.wait("t", "v", ev.get(j))
                        mm = mm_accum(sc, psk[j][:],
                                      [(wbuf[cc][:, hp * 128:(hp + 1) * 128],
                                        xn[cc][:, nk * 512:(nk + 1) * 512])
                                       for cc in range(8)])
                        sc.wait("v", "t", mm)
                        sc.wait("v", "y", prev_store)
                        sl = slice(nk * 512, (nk + 1) * 512)
                        vk = ev[j] = sc.inc("v", lambda e, _j=j, _sl=sl:
                                            e.tensor_copy(kpair[:, _sl],
                                                          psk[_j][:]))
                    prev_v = vk
                    sc.wait("y", "v", vk)
                    st1 = sc.dma("y", k_scr.ap()[2 * hp, :, :], kpair[0:64, :])
                    prev_store = sc.dma("y", k_scr.ap()[2 * hp + 1, :, :],
                                        kpair[64:128, :])
                    t_lh = prev_nrm
                    ss = None
                    v_mx2 = None
                    for lh in range(2):
                        sc.wait("s", "v", vk)
                        sc.wait("s", "t", t_lh)
                        prev_ssq = ss = sc.inc(
                            "s", lambda e, _l=lh: e.activation(
                                sqq[:], kpair[:, _l * 2048:(_l + 1) * 2048],
                                AF.Square))
                        sc.wait("t", "s", ss)
                        sc.wait("t", "v", v_mx2)
                        for nn in range(4):
                            sl = slice(nn * 512, nn * 512 + 512)
                            if nn == 3:
                                t_lh = sc.inc(
                                    "t", lambda e, _sl=sl: e.matmul(
                                        psn[:, _sl], hsel[:], sqq[:, _sl],
                                        start=True, stop=True))
                            else:
                                sc.raw("t", lambda e, _sl=sl: e.matmul(
                                    psn[:, _sl], hsel[:], sqq[:, _sl],
                                    start=True, stop=True))
                        sc.wait("v", "t", t_lh)
                        if lh == 0:
                            sc.wait("v", "y", prev_nstore)
                        v_mx2 = sc.inc("v", lambda e, _l=lh: e.reduce_max(
                            nmax2[0:2, _l:_l + 1], psn[:, 0:2048], axis=AX))
                    prev_nrm = t_lh
                    v_mx = sc.inc("v", lambda e: e.reduce_max(
                        nmax[0:2, 0:1], nmax2[0:2, 0:2], axis=AX))
                    sc.wait("y", "v", v_mx)
                    prev_nstore = sc.dma(
                        "y", kq_scr.ap()[2 * hp:2 * hp + 2, 1:2],
                        nmax[0:2, 0:1], allow_slow_non_contiguous=True)

            qn2 = pb("qn2", [1, H], F32)
            kn2 = pb("kn2", [1, H], F32)
            btv = pb("btv", [1, H], BF16)
            ldn1 = sc.dma("y", qn2[:],
                          kq_scr.ap().rearrange("h c -> c h")[0:1, :],
                          allow_slow_non_contiguous=True)
            ldn2 = sc.dma("y", kn2[:],
                          kq_scr.ap().rearrange("h c -> c h")[1:2, :],
                          allow_slow_non_contiguous=True)
            sc.wait("s", "y", ldn2)
            sc.inc("s", lambda e: e.activation(qn2[:], qn2[:], AF.Sqrt))
            s_n = sc.inc("s", lambda e: e.activation(kn2[:], kn2[:], AF.Sqrt))
            sc.wait("v", "s", s_n)
            v_bt = sc.inc("v", lambda e: e.tensor_tensor(btv[:], qn2[:],
                                                         kn2[:], OP.mult))
            with PSUM("ps_bt", [128, H], F32) as ps_bt:
                sc.wait("t", "v", v_bt)
                t_bt = sc.inc("t", lambda e: e.matmul(ps_bt[:], onesb_r[:],
                                                      btv[:], start=True,
                                                      stop=True))
                sc.wait("s", "t", t_bt)
                for h in range(H):
                    sc.inc("s", lambda e, _h=h: e.activation(
                        negb[_h][:], ps_bt[:, _h:_h + 1], AF.Copy, scale=-1.0))
        sc.barrier()

        with contextlib.ExitStack() as ph:
            pb = lambda n, shp, dt: ph.enter_context(SBUFR(n, shp, dt))
            wf_f = [pb(f"wf_f{i}", [128, C], F32) for i in range(2)]
            wf_s = [pb(f"wf_s{i}", [128, C], BF16) for i in range(8)]
            gtile = pb("gtile", [128, TH], BF16)
            sgn = {}
            for i in range(8):
                if i >= 2:
                    sc.wait("y", "s", sgn[i - 2])
                ld = sc.dma("y", wf_f[i % 2][:], wfT[i * 128:(i + 1) * 128, :])
                sc.wait("s", "y", ld)
                sgn[i] = sc.inc("s", lambda e, _i=i: e.activation(
                    wf_s[_i][:], wf_f[_i % 2][:], AF.Sign))
            sc.wait("t", "s", sgn[7])
            with contextlib.ExitStack() as pp:
                psg = [pp.enter_context(PSUM(f"psg{j}", [128, 512],
                                                       F32)) for j in range(4)]
                ev = {}
                prev_store = None
                for mo in range(8):
                    sg = None
                    for nt in range(4):
                        j = nt
                        sc.wait("t", "s", ev.get(j))
                        mm = mm_accum(sc, psg[j][:],
                                      [(wf_s[cc][:, mo * 128:(mo + 1) * 128],
                                        xn[cc][:, nt * 512:(nt + 1) * 512])
                                       for cc in range(8)])
                        sc.wait("s", "t", mm)
                        sc.wait("s", "y", prev_store)
                        sl = slice(nt * 512, (nt + 1) * 512)
                        sg = ev[j] = sc.inc("s", lambda e, _j=j, _sl=sl:
                                            e.activation(gtile[:, _sl],
                                                         psg[_j][:], AF.Silu))
                    sc.wait("y", "s", sg)
                    prev_store = sc.dma(
                        "y", g_scr.ap()[mo * 128:(mo + 1) * 128, :], gtile[:])
        sc.barrier()
        stA.close()

        stOC = contextlib.ExitStack()
        ocat = [stOC.enter_context(SBUFR(f"ocat{i}", [128, TH], BF16))
                for i in range(8)]
        stB = contextlib.ExitStack()
        pb = lambda n, shp, dt: stB.enter_context(SBUFR(n, shp, dt))
        dwide = pb("dwide", [128, DW], BF16)
        dwide_o = pb("dwide_o", [128, DW], BF16)
        with contextlib.ExitStack() as pd:
            iotw_i = pd.enter_context(SBUFR("iotw_i", [128, DW], I32))
            iotw = pd.enter_context(SBUFR("iotw", [128, DW], F32))
            dtmp = pd.enter_context(SBUFR("dtmp", [128, DW], F32))
            g_iw = sc.inc("g", lambda e: e.iota(iotw_i[:], pattern=[[1, DW]],
                                                base=0, channel_multiplier=0))
            sc.wait("v", "g", g_iw)
            sc.inc("v", lambda e: e.tensor_copy(iotw[:], iotw_i[:]))
            v_d0 = sc.inc("v", lambda e: e.tensor_scalar(
                dtmp[:], iotw[:], pcol0[:], None, OP.subtract))
            sc.wait("s", "v", v_d0)
            s_ab0 = sc.inc("s", lambda e: e.activation(dwide[:], dtmp[:],
                                                       AF.Abs))
            sc.wait("v", "s", s_ab0)
            v_d1 = sc.inc("v", lambda e: e.tensor_scalar(
                dtmp[:], iotw[:], pcol1[:], None, OP.subtract))
            sc.wait("s", "v", v_d1)
            sc.inc("s", lambda e: e.activation(dwide_o[:], dtmp[:], AF.Abs))
        sc.barrier()

        kh = [pb(f"kh{j}", [64, L], BF16) for j in range(2)]
        qh = [pb(f"qh{j}", [64, TH], BF16) for j in range(2)]
        tb = [pb(f"tb{j}", [128, 1024], BF16) for j in range(2)]
        pt = [pb(f"pt{j}", [128, 1024], BF16) for j in range(2)]
        otmp = pb("otmp", [64, 1024], F32)
        rrec = pb("rrec", [1, 1024], F32)
        with contextlib.ExitStack() as pp:
            ps_sa = [pp.enter_context(PSUM(f"ps_sa{j}", [128, 1024],
                                                     F32)) for j in range(2)]
            ps_o = pp.enter_context(PSUM("ps_o", [65, 1024], F32))
            ps_rb = pp.enter_context(PSUM("ps_rb", [64, 1024], F32))
            kld = {0: None}
            kld[0] = (sc.dma("y", kh[0][:], k_scr.ap()[0, :, :]),
                      sc.dma("y", qh[0][:], q_scr.ap()[0, :, :]))
            t_scores_end = {}
            last_stt = defaultdict(lambda: None)
            last_exp = defaultdict(lambda: None)
            last_pv = defaultdict(lambda: None)
            last_o_use = None
            last_rb = None
            for h in range(H):
                if h + 1 < H:
                    sc.wait("y", "t", t_scores_end.get(h - 1))
                    kld[h + 1] = (sc.dma("y", kh[(h + 1) % 2][:],
                                         k_scr.ap()[h + 1, :, :]),
                                  sc.dma("y", qh[(h + 1) % 2][:],
                                         q_scr.ap()[h + 1, :, :]))
                sc.wait("t", "y", kld[h][0])
                sc.wait("t", "y", kld[h][1])
                khh = kh[h % 2]
                qhh = qh[h % 2]
                for qb in range(2):
                    qsl = slice(qb * 1024, (qb + 1) * 1024)
                    for scn in range(32):
                        par = scn % 2
                        sc.wait("t", "v", last_stt[par])
                        mm = None
                        for nn in range(2):
                            sl = slice(nn * 512, nn * 512 + 512)
                            qg = slice(qb * 1024 + nn * 512,
                                       qb * 1024 + nn * 512 + 512)
                            fn = lambda e, _s=sl, _q=qg, _p=par, _sc=scn, \
                                _k=khh, _qh=qhh: e.matmul(
                                    ps_sa[_p][:, _s],
                                    _k[:, _sc * 128:(_sc + 1) * 128],
                                    _qh[:, _q], start=True, stop=True)
                            if nn == 1:
                                mm = sc.inc("t", fn)
                            else:
                                sc.raw("t", fn)
                        sc.wait("v", "t", mm)
                        sc.wait("v", "s", last_exp[par])
                        dw = dwide if scn < 16 else dwide_o
                        scp = scn if scn < 16 else scn - 16
                        off = 3968 + 1024 * qb - 128 * scp
                        last_stt[par] = sc.inc(
                            "v", lambda e, _p=par, _d=dw, _o=off, _h=h:
                            e.scalar_tensor_tensor(
                                tb[_p][:], _d[:, _o:_o + 1024],
                                float(-slopes[_h]), ps_sa[_p][:],
                                OP.mult, OP.add))
                        sc.wait("s", "v", last_stt[par])
                        sc.wait("s", "t", last_pv[par])
                        last_exp[par] = sc.inc(
                            "s", lambda e, _p=par, _h=h: e.activation(
                                pt[_p][:], tb[_p][:], AF.Exp,
                                bias=negb[_h][:]))
                        sc.wait("t", "s", last_exp[par])
                        if scn == 0:
                            sc.wait("t", "v", last_o_use)
                        vbase = scn * (H * 65) + h * 65
                        for nn in range(2):
                            sl = slice(nn * 512, nn * 512 + 512)
                            fn = lambda e, _p=par, _vb=vbase, _s=sl, _sc=scn: \
                                e.matmul(ps_o[:, _s], v_t[:, _vb:_vb + 65],
                                         pt[_p][:, _s], start=(_sc == 0),
                                         stop=(_sc == 31))
                            if nn == 1:
                                last_pv[par] = sc.inc("t", fn)
                            else:
                                sc.raw("t", fn)
                    t_scores_end[h] = sc.cnt["t"]
                    sc.wait("v", "t", last_pv[31 % 2])
                    sc.wait("v", "t", last_pv[30 % 2])
                    v_rec = sc.inc("v", lambda e: e.reciprocal(
                        rrec[:], ps_o[64:65, :]))
                    sc.wait("t", "v", v_rec)
                    for nn in range(2):
                        sl = slice(nn * 512, nn * 512 + 512)
                        fn = lambda e, _s=sl: e.matmul(
                            ps_rb[:, _s], ones64[:], rrec[0:1, _s],
                            start=True, stop=True)
                        if nn == 1:
                            last_rb = sc.inc("t", fn)
                        else:
                            sc.raw("t", fn)
                    sc.wait("s", "t", last_rb)
                    sc.wait("s", "v", last_o_use)
                    s_ot = sc.inc("s", lambda e: e.copy(otmp[:], ps_o[0:64, :]))
                    sc.wait("v", "s", s_ot)
                    cc, r0 = h // 2, (h % 2) * 64
                    last_o_use = sc.inc(
                        "v", lambda e, _cc=cc, _r0=r0, _q=qsl: e.tensor_tensor(
                            ocat[_cc][_r0:_r0 + 64, _q], otmp[:], ps_rb[:],
                            OP.mult))
            sc.wait("t", "v", last_o_use)
        sc.barrier()
        stB.close()

        with contextlib.ExitStack() as ph:
            pb = lambda n, shp, dt: ph.enter_context(SBUFR(n, shp, dt))
            wo_s = [pb(f"wo_s{i}", [128, C], BF16) for i in range(8)]
            for i in range(8):
                sc.dma("g", wo_s[i][:], woT[i * 128:(i + 1) * 128, :])
            bo_sb = pb("bo_sb", [128, 8], F32)
            g_sb = pb("g_sb", [128, TH], BF16)
            x0_sb = pb("x0_sb", [128, TH], F32)
            tmp1 = pb("tmp1", [128, 512], F32)
            tmp2 = pb("tmp2", [128, 512], F32)
            xout = pb("xout", [128, TH], F32)
            sc.dma("g", bo_sb[:], bo_d)
            sc.barrier()
            with contextlib.ExitStack() as pp:
                pso = [pp.enter_context(PSUM(f"pso{j}", [128, 512],
                                                       F32)) for j in range(4)]
                prev_v = None
                for dc in range(8):
                    sc.wait("y", "v", prev_v)
                    sc.wait("t", "v", prev_v)
                    ldg = sc.dma("y", g_sb[:],
                                 g_scr.ap()[dc * 128:(dc + 1) * 128, :])
                    ldx = sc.dma("y", x0_sb[:],
                                 x0T[dc * 128:(dc + 1) * 128, 0:TH])
                    vv = None
                    for nt in range(4):
                        j = nt
                        mm = mm_accum(sc, pso[j][:],
                                      [(wo_s[cc][:, dc * 128:(dc + 1) * 128],
                                        ocat[cc][:, nt * 512:(nt + 1) * 512])
                                       for cc in range(8)])
                        sc.wait("v", "t", mm)
                        sc.wait("v", "y", ldx)
                        sc.wait("v", "y", ldg)
                        sl = slice(nt * 512, (nt + 1) * 512)
                        sc.inc("v", lambda e, _j=j, _dc=dc: e.tensor_scalar(
                            tmp1[:], pso[_j][:], bo_sb[:, _dc:_dc + 1], None,
                            OP.add))
                        sc.inc("v", lambda e, _sl=sl: e.tensor_tensor(
                            tmp2[:], tmp1[:], g_sb[:, _sl], OP.mult))
                        vv = sc.inc("v", lambda e, _sl=sl: e.tensor_tensor(
                            xout[:, _sl], tmp2[:], x0_sb[:, _sl], OP.add))
                    prev_v = vv
                    sc.wait("y", "v", vv)
                    stc = sc.dma("y", x1T[dc * 128:(dc + 1) * 128, :], xout[:])
                    sc.wait("v", "y", stc)
                sc.wait("t", "v", sc.cnt["v"])
        stOC.close()
        stV.close()
        sc.wait("y", "y", sc.cnt["y"])
        sc.finalize(es)
    return nc


def prep_L0_inputs(inputs):
    x = f32c(inputs["L0_attn_q"])
    wq = np.ascontiguousarray(f32c(inputs["L0_attn_Wq"]).T)
    wk = np.ascontiguousarray(f32c(inputs["L0_attn_Wk"]).T)
    wv = np.ascontiguousarray(f32c(inputs["L0_attn_Wv"]).T)
    wo = np.ascontiguousarray(f32c(inputs["L0_attn_Wo"]).T)
    bo = f32c(inputs["L0_attn_bo"])
    bo2 = np.ascontiguousarray(bo.reshape(8, 128).T)
    wf = np.ascontiguousarray(f32c(inputs["L0_ffn_weight"]).T)
    maps = []
    for core in range(NC8):
        b, th = core // 2, core % 2
        xT = x[b].T
        rot = np.concatenate([xT[:, th * TH:(th + 1) * TH],
                              xT[:, (1 - th) * TH:(2 - th) * TH]], axis=1)
        maps.append({
            "x0T": np.ascontiguousarray(rot),
            "qoff": np.array([[th * TH]], dtype=np.float32),
            "wqT": wq, "wkT": wk, "wvT": wv, "woT": wo,
            "bo2": bo2, "wfT": wf,
        })
    return maps


def run_L0(inputs, trace=False):
    if "L0" not in _builder_cache:
        _builder_cache["L0"] = build_L0()
    nc = _builder_cache["L0"]
    maps = prep_L0_inputs(inputs)
    res = run_bass_kernel_spmd(nc, maps, core_ids=list(range(NC8)),
                               trace=trace)
    x1 = np.empty((B, L, C), dtype=np.float32)
    for core in range(NC8):
        b, th = core // 2, core % 2
        x1[b, th * TH:(th + 1) * TH, :] = res.results[core]["x1T"].T
    return x1, res


def kernel(**inputs):
    x1, _ = run_L0(inputs)
    x2 = run_L1(inputs, x1)
    x3 = run_L2(inputs, x2)
    x4 = run_L3(inputs, x3)
    return x4


class Ctx:

    def __init__(self):
        self.nc = bass.Bass("TRN2", target_bir_lowering=False)
        self.sc = Sched(self.nc)
        self._uid = [0]

    def nm(self, n):
        self._uid[0] += 1
        return f"{n}_u{self._uid[0]}"

    def SBUF(self, n, shp, dt, side=None):
        return self.nc.sbuf_tensor(self.nm(n), shp, dt, side=side)

    def SBUFR(self, n, shp, dt):
        return self.nc.sbuf_tensor(self.nm(n), shp, dt, side="right")

    def PSUM(self, n, shp, dt):
        return self.nc.psum_tensor(self.nm(n), shp, dt)


def emit_ln_cm(cx, es_left, x_dram, ncols, gb_cols=None, eps=1e-5,
               keep_stats=None):
    sc, nc = cx.sc, cx.nc
    CB = 1024
    xn = [es_left.enter_context(cx.SBUF(f"xn{i}", [128, ncols], BF16))
          for i in range(8)]
    nh = ncols // CB
    with contextlib.ExitStack() as ph:
        pb = lambda n, shp, dt: ph.enter_context(cx.SBUFR(n, shp, dt))
        x0b = [pb(f"x0b{i}", [128, CB], F32) for i in range(8)]
        sqb = pb("sqb", [128, CB], F32)
        srow = pb("srow", [1, CB], F32)
        s2row = pb("s2row", [1, CB], F32)
        rrow = pb("rrow", [1, CB], BF16)
        rful = pb("rful", [1, CB], F32)
        crow = pb("crow", [1, CB], BF16)
        mrow = pb("mrow", [1, CB], F32)
        vrow = pb("vrow", [1, CB], F32)
        m2row = pb("m2row", [1, CB], F32)
        tmpx = pb("tmpx", [128, CB], F32)
        onescol_f = pb("onescol_f", [128, 1], F32)
        onesb_r2 = pb("onesb_r2", [1, 128], BF16)
        epsc = pb("epsc", [1, 1], F32)
        sc.inc("g", lambda e: e.memset(onescol_f[:], 1.0))
        sc.inc("g", lambda e: e.memset(onesb_r2[:], 1.0))
        gc = sc.inc("g", lambda e: e.memset(epsc[:], eps))
        sc.wait("t", "g", gc)
        sc.wait("s", "g", gc)
        sc.wait("v", "g", gc)
        prev_norm = None
        for half in range(nh):
            t0 = half * CB
            with contextlib.ExitStack() as pp:
                ps_s = pp.enter_context(cx.PSUM("ps_s", [1, CB], F32))
                ps_s2 = pp.enter_context(cx.PSUM("ps_s2", [1, CB], F32))
                lds = [sc.dma("y", x0b[i][:],
                              x_dram[i * 128:(i + 1) * 128, t0:t0 + CB])
                       for i in range(8)]
                sc.wait("t", "v", prev_norm)
                prev_t = None
                for i in range(8):
                    sc.wait("s", "y", lds[i])
                    if prev_t:
                        sc.wait("s", "t", prev_t)
                    s_sq = sc.inc("s", lambda e, _i=i: e.activation(
                        sqb[:], x0b[_i][:], AF.Square))
                    sc.wait("t", "s", s_sq)
                    sc.wait("t", "y", lds[i])
                    for nn in range(CB // 512):
                        sl = slice(nn * 512, nn * 512 + 512)
                        sc.raw("t", lambda e, _i=i, _sl=sl: e.matmul(
                            ps_s[:, _sl], onescol_f[:], x0b[_i][:, _sl],
                            start=(_i == 0), stop=(_i == 7)))
                        if nn < CB // 512 - 1:
                            sc.raw("t", lambda e, _i=i, _sl=sl: e.matmul(
                                ps_s2[:, _sl], onescol_f[:], sqb[:, _sl],
                                start=(_i == 0), stop=(_i == 7)))
                        else:
                            prev_t = sc.inc(
                                "t", lambda e, _i=i, _sl=sl: e.matmul(
                                    ps_s2[:, _sl], onescol_f[:], sqb[:, _sl],
                                    start=(_i == 0), stop=(_i == 7)))
                sc.wait("v", "t", prev_t)
                sc.inc("v", lambda e: e.tensor_copy(srow[:], ps_s[:]))
                sc.inc("v", lambda e: e.tensor_copy(s2row[:], ps_s2[:]))
            sc.inc("v", lambda e: e.tensor_scalar(
                mrow[:], srow[:], 1.0 / C, None, OP.mult))
            sc.inc("v", lambda e: e.tensor_scalar(
                vrow[:], s2row[:], 1.0 / C, None, OP.mult))
            sc.inc("v", lambda e: e.tensor_tensor(
                m2row[:], mrow[:], mrow[:], OP.mult))
            v_var = sc.inc("v", lambda e: e.tensor_tensor(
                vrow[:], vrow[:], m2row[:], OP.subtract))
            sc.wait("s", "v", v_var)
            sc.inc("s", lambda e: e.activation(vrow[:], vrow[:], AF.Ln,
                                               bias=epsc[:]))
            s_r = sc.inc("s", lambda e: e.activation(rful[:], vrow[:], AF.Exp,
                                                     scale=-0.5))
            sc.wait("v", "s", s_r)
            sc.inc("v", lambda e: e.tensor_copy(rrow[:], rful[:]))
            if keep_stats is not None and t0 < keep_stats[2]:
                mu_k, r_k, _ = keep_stats
                sc.inc("v", lambda e, _t=t0: e.tensor_copy(
                    mu_k[0:1, _t:_t + CB], mrow[:]))
                sc.inc("v", lambda e, _t=t0: e.tensor_copy(
                    r_k[0:1, _t:_t + CB], rful[:]))
            sc.inc("v", lambda e: e.tensor_tensor(m2row[:], mrow[:],
                                                  rrow[:], OP.mult))
            v_cr = sc.inc("v", lambda e: e.tensor_scalar(
                crow[:], m2row[:], -1.0, None, OP.mult))
            with contextlib.ExitStack() as pp:
                ps_a = pp.enter_context(cx.PSUM("ps_a", [128, CB], F32))
                ps_c = pp.enter_context(cx.PSUM("ps_c", [128, CB], F32))
                sc.wait("t", "v", v_cr)
                t_bc = None
                for nn in range(CB // 512):
                    sl = slice(nn * 512, nn * 512 + 512)
                    sc.raw("t", lambda e, _sl=sl: e.matmul(
                        ps_a[:, _sl], onesb_r2[:], rrow[0:1, _sl],
                        start=True, stop=True))
                    t_bc = sc.inc("t", lambda e, _sl=sl: e.matmul(
                        ps_c[:, _sl], onesb_r2[:], crow[0:1, _sl],
                        start=True, stop=True))
                sc.wait("v", "t", t_bc)
                v_norm = None
                for i in range(8):
                    sg = slice(t0, t0 + CB)
                    sc.inc("v", lambda e, _i=i: e.tensor_tensor(
                        tmpx[:], x0b[_i][:], ps_a[:], OP.mult))
                    v_norm = sc.inc(
                        "v", lambda e, _i=i, _g=sg: e.tensor_tensor(
                            xn[_i][:, _g], tmpx[:], ps_c[:], OP.add))
                sc.wait("t", "v", v_norm)
                prev_norm = v_norm
            if half + 1 < nh:
                sc.wait("y", "v", sc.cnt["v"])
    return xn


def emit_proj_cm(cx, w_tiles, rhs_tiles, rhs_cols, m_chunks, evac,
                 psn_width=512, nk=8):
    sc = cx.sc
    ncol_chunks = rhs_cols // psn_width
    with contextlib.ExitStack() as pp:
        ps = [pp.enter_context(cx.PSUM(f"psp{j}", [128, psn_width], F32))
              for j in range(4)]
        last = {}
        for mo in m_chunks:
            for nn in range(ncol_chunks):
                j = (mo * ncol_chunks + nn) % 4
                if j in last:
                    sc.wait("t", last[j][0], last[j][1])
                mm = mm_accum(sc, ps[j][:],
                              [(w_tiles[cc][:, mo * 128:(mo + 1) * 128],
                                rhs_tiles[cc][:, nn * psn_width:
                                              (nn + 1) * psn_width])
                               for cc in range(nk)])
                last[j] = evac(mo, nn, ps[j], mm)
        for j, (ek, c) in last.items():
            sc.wait("t", ek, c)


def build_L1a():
    cx = Ctx()
    nc, sc = cx.nc, cx.sc
    x1T = nc.dram_tensor("x1T", [C, L], F32, kind="ExternalInput").ap()
    qoffd = nc.dram_tensor("qoff", [1, 1], F32, kind="ExternalInput").ap()
    wkT = nc.dram_tensor("wkT", [C, C], F32, kind="ExternalInput").ap()
    wvT = nc.dram_tensor("wvT", [C, C], F32, kind="ExternalInput").ap()
    wrT = nc.dram_tensor("wrT", [C, C], F32, kind="ExternalInput").ap()
    woT = nc.dram_tensor("woT", [C, C], F32, kind="ExternalInput").ap()
    tmk2 = nc.dram_tensor("tmk2", [128, 8], F32, kind="ExternalInput").ap()
    tmv2 = nc.dram_tensor("tmv2", [128, 8], F32, kind="ExternalInput").ap()
    tmr2 = nc.dram_tensor("tmr2", [128, 8], F32, kind="ExternalInput").ap()
    td2 = nc.dram_tensor("td2", [128, 8], F32, kind="ExternalInput").ap()
    tf2 = nc.dram_tensor("tf2", [128, 8], F32, kind="ExternalInput").ap()
    lng2 = nc.dram_tensor("lng2", [128, 8], F32, kind="ExternalInput").ap()
    lnb2 = nc.dram_tensor("lnb2", [128, 8], F32, kind="ExternalInput").ap()
    wr8 = nc.dram_tensor("wr8", [C, 8], F32, kind="ExternalInput").ap()
    br8 = nc.dram_tensor("br8", [1, 8], F32, kind="ExternalInput").ap()
    idn = nc.dram_tensor("idn", [128, 128], F32, kind="ExternalInput").ap()

    s1T = nc.dram_tensor("s1T", [C, TH], F32, kind="ExternalOutput").ap()
    xtok = nc.dram_tensor("xtok", [TH, C], BF16, kind="ExternalOutput").ap()
    m1o = nc.dram_tensor("m1o", [TH, 8], BF16, kind="ExternalOutput").ap()
    m2o = nc.dram_tensor("m2o", [TH, 8], BF16, kind="ExternalOutput").ap()
    p1o = nc.dram_tensor("p1o", [TH, 1], F32, kind="ExternalOutput").ap()
    p2o = nc.dram_tensor("p2o", [TH, 1], F32, kind="ExternalOutput").ap()
    r_scr = nc.dram_tensor("r_scr", [C, TH], BF16)
    o_scr = nc.dram_tensor("o_scr", [C, TH], BF16)
    mr_scr = nc.dram_tensor("mr_scr", [2, TH], F32)

    es = contextlib.ExitStack()
    with es:
        sbL = lambda n, shp, dt: es.enter_context(cx.SBUF(n, shp, dt))
        qcol = sbL("qcol", [128, 1], F32)
        step1 = sbL("step1", [128, 1], F32)
        step2 = sbL("step2", [128, 1], F32)
        onesb_r = sbL("onesb_r", [1, 128], BF16)
        ident = sbL("ident", [128, 128], BF16)
        tmk = sbL("tmk", [128, 8], F32)
        itmk = sbL("itmk", [128, 8], F32)
        tmv = sbL("tmv", [128, 8], F32)
        itmv = sbL("itmv", [128, 8], F32)
        tmr = sbL("tmr", [128, 8], F32)
        itmr = sbL("itmr", [128, 8], F32)
        tds = sbL("tds", [128, 8], F32)
        tfs = sbL("tfs", [128, 8], F32)
        lng = sbL("lng", [128, 8], F32)
        lnb = sbL("lnb", [128, 8], F32)
        wrb = [sbL(f"wrb{i}", [128, 8], BF16) for i in range(8)]
        brb = sbL("brb", [1, 8], BF16)
        dcol = sbL("dcol", [128, 8], F32)
        carry = sbL("carry", [128, 8], F32)

        ld = sc.dma("y", qcol[0:1, :], qoffd)
        for dst, srcd in ((tmk, tmk2), (tmv, tmv2), (tmr, tmr2), (tds, td2),
                          (tfs, tf2), (lng, lng2), (lnb, lnb2)):
            ld = sc.dma("y", dst[:], srcd)
        for i in range(8):
            ld2 = sc.dma("g", wrb[i][:], wr8[i * 128:(i + 1) * 128, :])
        ld2 = sc.dma("g", brb[:], br8)
        ld2 = sc.dma("g", ident[:], idn)
        gsets = sc.inc("g", lambda e: e.memset(onesb_r[:], 1.0))
        sc.wait("v", "y", ld)
        sc.wait("v", "gd", ld2)
        sc.wait("v", "g", gsets)
        sc.inc("v", lambda e: e.tensor_single_scalar(step1[0:1, :],
                                                     qcol[0:1, :], 0.5,
                                                     OP.is_gt))
        sc.inc("v", lambda e: e.tensor_scalar(step2[0:1, :], step1[0:1, :],
                                              -1.0, 1.0, OP.mult, OP.add))
        for tm, itm in ((tmk, itmk), (tmv, itmv), (tmr, itmr)):
            sc.inc("v", lambda e, _a=tm, _b=itm: e.tensor_scalar(
                _b[:], _a[:], -1.0, 1.0, OP.mult, OP.add))
        with cx.PSUM("ps_st", [128, 2], F32) as ps_st:
            stp_b = sbL("stp_b", [1, 2], BF16)
            sc.inc("v", lambda e: e.tensor_copy(stp_b[0:1, 0:1], step1[0:1, :]))
            v_c = sc.inc("v", lambda e: e.tensor_copy(stp_b[0:1, 1:2],
                                                      step2[0:1, :]))
            sc.wait("t", "v", v_c)
            sc.wait("t", "g", gsets)
            t_b = sc.inc("t", lambda e: e.matmul(ps_st[:], onesb_r[:],
                                                 stp_b[:], start=True,
                                                 stop=True))
            sc.wait("v", "t", t_b)
            sc.inc("v", lambda e: e.tensor_copy(step1[:], ps_st[:, 0:1]))
            v_st = sc.inc("v", lambda e: e.tensor_copy(step2[:],
                                                       ps_st[:, 1:2]))
        sc.barrier()

        stMU = contextlib.ExitStack()
        mu_k = stMU.enter_context(cx.SBUFR("mu_k", [1, TH], F32))
        r_k = stMU.enter_context(cx.SBUFR("r_k", [1, TH], F32))
        stXN = contextlib.ExitStack()
        xn = emit_ln_cm(cx, stXN, x1T, L, keep_stats=(mu_k, r_k, TH))
        vk = sc.cnt["v"]
        sc.wait("y", "v", vk)
        sc.dma("y", mr_scr.ap()[0:1, :], mu_k[:])
        sc.dma("y", mr_scr.ap()[1:2, :], r_k[:])
        sc.barrier()
        stMU.close()

        with contextlib.ExitStack() as ph:
            xtk = ph.enter_context(cx.SBUFR("xtk", [128, C], BF16))
            with cx.PSUM("ps_tr", [128, C], BF16) as ps_tr:
                prev_st = None
                prev_cp = None
                for tc in range(16):
                    t_tr = None
                    sc.wait("t", "v", prev_cp)
                    for ccj in range(8):
                        t_tr = sc.inc("t", lambda e, _c=ccj, _tc=tc: e.transpose(
                            ps_tr[:, _c * 128:(_c + 1) * 128],
                            xn[_c][:, _tc * 128:(_tc + 1) * 128], ident[:]))
                    sc.wait("v", "t", t_tr)
                    sc.wait("v", "y", prev_st)
                    prev_cp = sc.inc("v", lambda e: e.tensor_copy(xtk[:],
                                                                  ps_tr[:]))
                    sc.wait("y", "v", prev_cp)
                    prev_st = sc.dma("y", xtok[tc * 128:(tc + 1) * 128, :],
                                     xtk[:])
        sc.barrier()

        with contextlib.ExitStack() as ph:
            pb = lambda n, shp, dt: ph.enter_context(cx.SBUFR(n, shp, dt))
            prb = pb("prb", [128, 8], F32)
            nmx = pb("nmx", [128, 1], F32)
            ssum = pb("ssum", [128, 1], F32)
            rsum = pb("rsum", [128, 1], F32)
            mk1 = pb("mk1", [128, 8], BF16)
            mk2 = pb("mk2", [128, 8], BF16)
            mk1f = pb("mk1f", [128, 8], F32)
            pm = pb("pm", [128, 8], F32)
            pv1 = pb("pv1", [128, 1], F32)
            pv2 = pb("pv2", [128, 1], F32)
            x1f = [pb(f"x1f{i}", [128, TH], F32) for i in range(8)]
            wrf = [pb(f"wrf{i}", [128, 8], F32) for i in range(8)]
            mcol = pb("mcol", [128, 16], F32)
            rcol = pb("rcol", [128, 16], F32)
            mrc = pb("mrc", [128, 16], F32)
            bsum = pb("bsum", [128, 8], F32)
            brf = pb("brf", [128, 8], F32)
            onescol_g = pb("onescol_g", [128, 1], F32)
            t1b = pb("t1b", [128, 8], F32)
            t2b = pb("t2b", [128, 8], F32)
            for i in range(8):
                sc.dma("y", x1f[i][:], x1T[i * 128:(i + 1) * 128, 0:TH])
                sc.dma("y", wrf[i][:], wr8[i * 128:(i + 1) * 128, :])
            sc.dma("y", mcol[:],
                   mr_scr.ap()[0:1, :].rearrange("a (tc p) -> (a p) tc",
                                                 p=128),
                   allow_slow_non_contiguous=True)
            ldm = sc.dma("y", rcol[:],
                         mr_scr.ap()[1:2, :].rearrange(
                             "a (tc p) -> (a p) tc", p=128),
                         allow_slow_non_contiguous=True)
            gc3 = sc.inc("g", lambda e: e.memset(onescol_g[:], 1.0))
            sc.wait("v", "y", ldm)
            sc.inc("v", lambda e: e.tensor_tensor(mrc[:], mcol[:], rcol[:],
                                                  OP.mult))
            bs_b = pb("bs_b", [1, 8], BF16)
            with cx.PSUM("ps_bs", [1, 8], F32) as ps_bs:
                sc.wait("t", "y", ldm)
                sc.wait("t", "g", gc3)
                t_bs = None
                for cc in range(8):
                    if cc < 7:
                        sc.raw("t", lambda e, _c=cc: e.matmul(
                            ps_bs[:], onescol_g[:], wrf[_c][:],
                            start=(_c == 0), stop=False))
                    else:
                        t_bs = sc.inc("t", lambda e, _c=cc: e.matmul(
                            ps_bs[:], onescol_g[:], wrf[_c][:],
                            start=False, stop=True))
                sc.wait("v", "t", t_bs)
                v_bs = sc.inc("v", lambda e: e.tensor_copy(bs_b[:], ps_bs[:]))
            with cx.PSUM("ps_b2", [128, 8], F32) as ps_b2:
                sc.wait("t", "v", v_bs)
                t_x = sc.inc("t", lambda e: e.matmul(
                    ps_b2[:], onesb_r[:], bs_b[:], start=True, stop=True))
                sc.wait("v", "t", t_x)
                v_x = sc.inc("v", lambda e: e.tensor_copy(bsum[:], ps_b2[:]))
                sc.wait("t", "v", v_x)
                t_x = sc.inc("t", lambda e: e.matmul(
                    ps_b2[:], onesb_r[:], brb[:], start=True, stop=True))
                sc.wait("v", "t", t_x)
                v_x = sc.inc("v", lambda e: e.tensor_copy(brf[:], ps_b2[:]))
                sc.wait("t", "v", v_x)
            with cx.PSUM("ps_lg", [128, 8], F32) as ps_lg:
                prev = None
                prev_store = None
                for tc in range(16):
                    sc.wait("t", "v", prev)
                    sc.wait("v", "y", prev_store)
                    t_lg = mm_accum(
                        sc, ps_lg[:],
                        [(x1f[cc][:, tc * 128:(tc + 1) * 128], wrf[cc][:])
                         for cc in range(8)])
                    sc.wait("v", "t", t_lg)
                    sc.inc("v", lambda e, _tc=tc: e.tensor_scalar(
                        t1b[:], ps_lg[:], rcol[:, _tc:_tc + 1], None,
                        OP.mult))
                    sc.inc("v", lambda e, _tc=tc: e.tensor_scalar(
                        t2b[:], bsum[:], mrc[:, _tc:_tc + 1], None, OP.mult))
                    sc.inc("v", lambda e: e.tensor_tensor(
                        t1b[:], t1b[:], t2b[:], OP.subtract))
                    t_lg2 = sc.inc("v", lambda e: e.tensor_tensor(
                        t1b[:], t1b[:], brf[:], OP.add))
                    sc.inc("v", lambda e: e.reduce_max(nmx[:], t1b[:],
                                                       axis=AX, negate=True))
                    v_m = sc.cnt["v"]
                    sc.wait("s", "v", v_m)
                    s_e = sc.inc("s", lambda e: e.activation(
                        prb[:], t1b[:], AF.Exp, bias=nmx[:]))
                    sc.wait("v", "s", s_e)
                    sc.inc("v", lambda e: e.reduce_sum(ssum[:], prb[:],
                                                       axis=AX))
                    sc.inc("v", lambda e: e.reciprocal(rsum[:], ssum[:]))
                    sc.inc("v", lambda e: e.reduce_max(nmx[:], prb[:],
                                                       axis=AX))
                    sc.inc("v", lambda e: e.tensor_scalar(
                        mk1[:], prb[:], nmx[:], None, OP.is_ge))
                    sc.inc("v", lambda e: e.tensor_copy(mk1f[:], mk1[:]))
                    sc.inc("v", lambda e: e.tensor_tensor(
                        pv1[:], nmx[:], rsum[:], OP.mult))
                    sc.inc("v", lambda e: e.scalar_tensor_tensor(
                        pm[:], mk1f[:], -1e9, prb[:], OP.mult, OP.add))
                    sc.inc("v", lambda e: e.reduce_max(nmx[:], pm[:], axis=AX))
                    sc.inc("v", lambda e: e.tensor_scalar(
                        mk2[:], pm[:], nmx[:], None, OP.is_ge))
                    v_p2 = sc.inc("v", lambda e: e.tensor_tensor(
                        pv2[:], nmx[:], rsum[:], OP.mult))
                    sc.wait("y", "v", v_p2)
                    sc.dma("y", m1o[tc * 128:(tc + 1) * 128, :], mk1[:])
                    sc.dma("y", m2o[tc * 128:(tc + 1) * 128, :], mk2[:])
                    sc.dma("y", p1o[tc * 128:(tc + 1) * 128, :], pv1[:])
                    prev = v_p2
                    prev_store = sc.dma("y", p2o[tc * 128:(tc + 1) * 128, :],
                                        pv2[:])
        sc.barrier()

        stMix = contextlib.ExitStack()
        mixtmp = stMix.enter_context(cx.SBUFR("mixtmp", [128, TH], F32))
        col1 = stMix.enter_context(cx.SBUFR("col1", [128, 1], F32))
        col2 = stMix.enter_context(cx.SBUFR("col2", [128, 1], F32))
        wk_s = [stMix.enter_context(cx.SBUFR(f"wk_s{i}", [128, C], BF16))
                for i in range(8)]
        wv_s = [stMix.enter_context(cx.SBUFR(f"wv_s{i}", [128, C], BF16))
                for i in range(8)]
        for i in range(8):
            sc.dma("g", wk_s[i][:], wkT[i * 128:(i + 1) * 128, :])
            lw = sc.dma("g", wv_s[i][:], wvT[i * 128:(i + 1) * 128, :])
        sc.wait("t", "gd", lw)

        def emit_mix(dst_tiles, tm, itm, seg0, boundary_src, stepc):
            vv = None
            for ccx in range(8):
                a = seg0
                sc.inc("v", lambda e, _c=ccx, _a=a: e.tensor_tensor(
                    mixtmp[:, 1:TH], xn[_c][:, _a + 1:_a + TH],
                    xn[_c][:, _a:_a + TH - 1], OP.subtract))
                sc.inc("v", lambda e, _c=ccx, _a=a: e.scalar_tensor_tensor(
                    dst_tiles[_c][:, 1:TH], mixtmp[:, 1:TH],
                    tm[:, _c:_c + 1], xn[_c][:, _a:_a + TH - 1],
                    OP.mult, OP.add))
                sc.inc("v", lambda e, _c=ccx, _a=a: e.tensor_scalar(
                    col1[:], xn[_c][:, _a:_a + 1], tm[:, _c:_c + 1], None,
                    OP.mult))
                sc.inc("v", lambda e, _c=ccx: e.tensor_scalar(
                    col2[:], xn[_c][:, boundary_src:boundary_src + 1],
                    itm[:, _c:_c + 1], None, OP.mult))
                sc.inc("v", lambda e, _c=ccx: e.tensor_tensor(
                    col2[:], col2[:], stepc[:], OP.mult))
                vv = sc.inc("v", lambda e, _c=ccx: e.tensor_tensor(
                    dst_tiles[_c][:, 0:1], col1[:], col2[:], OP.add))
            return vv

        def scan_half(seg0, boundary_src, stepc, init_mode, out_fn):
            with contextlib.ExitStack() as ph:
                pb = lambda n, shp, dt: ph.enter_context(cx.SBUFR(n, shp, dt))
                xk_t = [pb(f"xk{i}", [128, TH], BF16) for i in range(8)]
                xv_t = [pb(f"xv{i}", [128, TH], BF16) for i in range(8)]
                ktmp = pb("ktmp", [128, TH], BF16)
                kvb = pb("kvb", [128, TH], BF16)
                st_buf = pb("st_buf", [128, TH], F32)
                drep = pb("drep", [128, TH], F32)
                obuf = pb("obuf", [128, TH], BF16)
                rbuf = pb("rbuf", [128, TH], BF16)
                v_mix = emit_mix(xk_t, tmk, itmk, seg0, boundary_src, stepc)
                v_mix = emit_mix(xv_t, tmv, itmv, seg0, boundary_src, stepc)
                sc.wait("t", "v", v_mix)
                with contextlib.ExitStack() as pp:
                    psk = [pp.enter_context(cx.PSUM(f"psk{j}", [128, 512],
                                                    F32)) for j in range(4)]
                    psv = [pp.enter_context(cx.PSUM(f"psv{j}", [128, 512],
                                                    F32)) for j in range(4)]
                    prev_done = None
                    for dc in range(8):
                        sc.wait("t", "v", prev_done)
                        rld = None
                        ek = None
                        for nn in range(4):
                            mm = mm_accum(
                                sc, psk[nn][:],
                                [(wk_s[cc][:, dc * 128:(dc + 1) * 128],
                                  xk_t[cc][:, nn * 512:(nn + 1) * 512])
                                 for cc in range(8)])
                            sc.wait("v", "t", mm)
                            ek = sc.inc("v", lambda e, _n=nn: e.tensor_copy(
                                ktmp[:, _n * 512:(_n + 1) * 512], psk[_n][:]))
                        kv = None
                        for nn in range(4):
                            mm = mm_accum(
                                sc, psv[nn][:],
                                [(wv_s[cc][:, dc * 128:(dc + 1) * 128],
                                  xv_t[cc][:, nn * 512:(nn + 1) * 512])
                                 for cc in range(8)])
                            sc.wait("v", "t", mm)
                            kv = sc.inc(
                                "v", lambda e, _n=nn: e.tensor_tensor(
                                    kvb[:, _n * 512:(_n + 1) * 512], psv[_n][:],
                                    ktmp[:, _n * 512:(_n + 1) * 512], OP.mult))
                        s_d = sc.inc("s", lambda e, _dc=dc: e.activation(
                            dcol[:, _dc:_dc + 1], tds[:, _dc:_dc + 1], AF.Exp))
                        sc.wait("v", "s", s_d)
                        sc.inc("v", lambda e, _dc=dc: e.tensor_scalar(
                            drep[:], kvb[:], 0.0, dcol[:, _dc:_dc + 1],
                            OP.mult, OP.add))
                        if init_mode == "zero":
                            v_sc = sc.inc(
                                "v", lambda e: e.tensor_tensor_scan(
                                    st_buf[:], drep[:], kvb[:], 0.0,
                                    OP.mult, OP.add))
                        else:
                            v_sc = sc.inc(
                                "v", lambda e, _dc=dc: e.tensor_tensor_scan(
                                    st_buf[:], drep[:], kvb[:],
                                    carry[:, _dc:_dc + 1], OP.mult, OP.add))
                        prev_done = out_fn(dc, st_buf, kvb, rld, obuf)
            return

        def carry_out(dc, st_buf, kvb, rld, obuf):
            return sc.inc("v", lambda e, _dc=dc: e.tensor_tensor(
                carry[:, _dc:_dc + 1], st_buf[:, TH - 1:TH], step1[:],
                OP.mult))

        scan_half(TH, TH - 1, step2, "zero", carry_out)

        with contextlib.ExitStack() as ph:
            xr_t = [ph.enter_context(cx.SBUFR(f"xr{i}", [128, TH], BF16))
                    for i in range(8)]
            wr_s = [ph.enter_context(cx.SBUFR(f"wr_s{i}", [128, C], BF16))
                    for i in range(8)]
            rout = ph.enter_context(cx.SBUFR("rout", [128, TH], BF16))
            sc.wait("g", "v", sc.cnt["v"])
            for i in range(8):
                lw = sc.dma("g", wr_s[i][:], wrT[i * 128:(i + 1) * 128, :])
            v_mix = emit_mix(xr_t, tmr, itmr, 0, 2 * TH - 1, step1)
            sc.wait("t", "v", v_mix)
            sc.wait("t", "gd", lw)
            r_store = {}

            def evac_r(mo, nn, ps, mm):
                sc.wait("v", "t", mm)
                sc.wait("v", "y", r_store.get(mo))
                c = sc.inc("v", lambda e, _nn=nn: e.tensor_copy(
                    rout[:, _nn * 512:(_nn + 1) * 512], ps[:]))
                if nn == 3:
                    sc.wait("y", "v", c)
                    r_store[mo + 1] = sc.dma(
                        "y", r_scr.ap()[mo * 128:(mo + 1) * 128, :], rout[:])
                return ("v", c)

            emit_proj_cm(cx, wr_s, xr_t, TH, range(8), evac_r)
            sc.wait("v", "y", sc.cnt["y"])
        sc.barrier()

        o_store = {}

        def own_out(dc, st_buf, kvb, rld, obuf):
            sc.inc("v", lambda e, _dc=dc: e.scalar_tensor_tensor(
                st_buf[:], kvb[:], tfs[:, _dc:_dc + 1], st_buf[:],
                OP.mult, OP.add))
            sc.wait("v", "y", rld)
            sc.wait("v", "y", o_store.get(dc))
            c = sc.inc("v", lambda e, _dc=dc: e.tensor_tensor(
                obuf[:], st_buf[:], rbuf_holder[0][:], OP.mult))
            sc.wait("y", "v", c)
            o_store[dc + 1] = sc.dma(
                "y", o_scr.ap()[dc * 128:(dc + 1) * 128, :], obuf[:])
            return c

        rbuf_holder = [None]

        def own_out2(dc, st_buf, kvb, rld, obuf):
            return own_out(dc, st_buf, kvb, rld, obuf)

        def scan_half_own():
            with contextlib.ExitStack() as ph:
                pb = lambda n, shp, dt: ph.enter_context(cx.SBUFR(n, shp, dt))
                xk_t = [pb(f"xk{i}", [128, TH], BF16) for i in range(8)]
                xv_t = [pb(f"xv{i}", [128, TH], BF16) for i in range(8)]
                ktmp = pb("ktmp", [128, TH], BF16)
                kvb = pb("kvb", [128, TH], BF16)
                st_buf = pb("st_buf", [128, TH], F32)
                drep = pb("drep", [128, TH], F32)
                obuf = pb("obuf", [128, TH], BF16)
                rbuf = pb("rbuf", [128, TH], BF16)
                rbuf_holder[0] = rbuf
                v_mix = emit_mix(xk_t, tmk, itmk, 0, 2 * TH - 1, step1)
                v_mix = emit_mix(xv_t, tmv, itmv, 0, 2 * TH - 1, step1)
                sc.wait("t", "v", v_mix)
                with contextlib.ExitStack() as pp:
                    psk = [pp.enter_context(cx.PSUM(f"psk{j}", [128, 512],
                                                    F32)) for j in range(4)]
                    psv = [pp.enter_context(cx.PSUM(f"psv{j}", [128, 512],
                                                    F32)) for j in range(4)]
                    prev_done = None
                    for dc in range(8):
                        sc.wait("t", "v", prev_done)
                        rld = sc.dma("y", rbuf[:],
                                     r_scr.ap()[dc * 128:(dc + 1) * 128, :])
                        for nn in range(4):
                            mm = mm_accum(
                                sc, psk[nn][:],
                                [(wk_s[cc][:, dc * 128:(dc + 1) * 128],
                                  xk_t[cc][:, nn * 512:(nn + 1) * 512])
                                 for cc in range(8)])
                            sc.wait("v", "t", mm)
                            sc.inc("v", lambda e, _n=nn: e.tensor_copy(
                                ktmp[:, _n * 512:(_n + 1) * 512], psk[_n][:]))
                        for nn in range(4):
                            mm = mm_accum(
                                sc, psv[nn][:],
                                [(wv_s[cc][:, dc * 128:(dc + 1) * 128],
                                  xv_t[cc][:, nn * 512:(nn + 1) * 512])
                                 for cc in range(8)])
                            sc.wait("v", "t", mm)
                            sc.inc(
                                "v", lambda e, _n=nn: e.tensor_tensor(
                                    kvb[:, _n * 512:(_n + 1) * 512], psv[_n][:],
                                    ktmp[:, _n * 512:(_n + 1) * 512], OP.mult))
                        s_d = sc.inc("s", lambda e, _dc=dc: e.activation(
                            dcol[:, _dc:_dc + 1], tds[:, _dc:_dc + 1], AF.Exp))
                        sc.wait("v", "s", s_d)
                        sc.inc("v", lambda e, _dc=dc: e.tensor_scalar(
                            drep[:], kvb[:], 0.0, dcol[:, _dc:_dc + 1],
                            OP.mult, OP.add))
                        sc.inc(
                            "v", lambda e, _dc=dc: e.tensor_tensor_scan(
                                st_buf[:], drep[:], kvb[:],
                                carry[:, _dc:_dc + 1], OP.mult, OP.add))
                        prev_done = own_out(dc, st_buf, kvb, rld, obuf)

        scan_half_own()
        stMix.close()
        stXN.close()
        sc.barrier()

        stON = contextlib.ExitStack()
        o_n = [stON.enter_context(cx.SBUF(f"o_n{i}", [128, TH], BF16))
               for i in range(8)]
        with contextlib.ExitStack() as ph:
            pb = lambda n, shp, dt: ph.enter_context(cx.SBUFR(n, shp, dt))
            ob = [pb(f"ob{i}", [128, TH], BF16) for i in range(2)]
            sqb = pb("sqb", [128, TH], F32)
            srow = pb("srow", [1, TH], F32)
            s2row = pb("s2row", [1, TH], F32)
            rrow = pb("rrow", [1, TH], BF16)
            crow = pb("crow", [1, TH], BF16)
            mrow = pb("mrow", [1, TH], F32)
            vrow = pb("vrow", [1, TH], F32)
            m2row = pb("m2row", [1, TH], F32)
            tmpx = pb("tmpx", [128, 1024], F32)
            onescol_f = pb("onescol_f2", [128, 1], F32)
            onescol_b = pb("onescol_b2", [128, 1], BF16)
            onesb_r2 = pb("onesb_r3", [1, 128], BF16)
            epsc = pb("epsc2", [1, 1], F32)
            sc.inc("g", lambda e: e.memset(onescol_b[:], 1.0))
            sc.inc("g", lambda e: e.memset(onescol_f[:], 1.0))
            sc.inc("g", lambda e: e.memset(onesb_r2[:], 1.0))
            gc = sc.inc("g", lambda e: e.memset(epsc[:], 1e-5))
            sc.wait("t", "g", gc)
            sc.wait("s", "g", gc)
            with contextlib.ExitStack() as pp:
                ps_s = pp.enter_context(cx.PSUM("ps_s", [1, 2048], F32))
                ps_s2 = pp.enter_context(cx.PSUM("ps_s2", [1, 2048], F32))
                prev_t = None
                lds = {}
                sqs = {}
                lds[0] = sc.dma("y", ob[0][:], o_scr.ap()[0:128, :])
                for i in range(8):
                    if i + 1 < 8:
                        if prev_t and i >= 1:
                            sc.wait("y", "t", prev_t)
                        sc.wait("y", "s", sqs.get(i - 1))
                        lds[i + 1] = sc.dma(
                            "y", ob[(i + 1) % 2][:],
                            o_scr.ap()[(i + 1) * 128:(i + 2) * 128, :])
                    sc.wait("s", "y", lds[i])
                    if prev_t:
                        sc.wait("s", "t", prev_t)
                    sqs[i] = s_sq = sc.inc("s", lambda e, _i=i: e.activation(
                        sqb[:], ob[_i % 2][:], AF.Square))
                    sc.wait("t", "s", s_sq)
                    sc.wait("t", "y", lds[i])
                    for nn in range(4):
                        sl = slice(nn * 512, nn * 512 + 512)
                        sc.raw("t", lambda e, _i=i, _sl=sl: e.matmul(
                            ps_s[:, _sl], onescol_b[:], ob[_i % 2][:, _sl],
                            start=(_i == 0), stop=(_i == 7)))
                        if nn < 3:
                            sc.raw("t", lambda e, _i=i, _sl=sl: e.matmul(
                                ps_s2[:, _sl], onescol_f[:], sqb[:, _sl],
                                start=(_i == 0), stop=(_i == 7)))
                        else:
                            prev_t = sc.inc(
                                "t", lambda e, _i=i, _sl=sl: e.matmul(
                                    ps_s2[:, _sl], onescol_f[:], sqb[:, _sl],
                                    start=(_i == 0), stop=(_i == 7)))
                sc.wait("v", "t", prev_t)
                sc.inc("v", lambda e: e.tensor_copy(srow[:], ps_s[:]))
                sc.inc("v", lambda e: e.tensor_copy(s2row[:], ps_s2[:]))
            sc.inc("v", lambda e: e.tensor_scalar(
                mrow[:], srow[:], 1.0 / C, None, OP.mult))
            sc.inc("v", lambda e: e.tensor_scalar(
                vrow[:], s2row[:], 1.0 / C, None, OP.mult))
            sc.inc("v", lambda e: e.tensor_tensor(
                m2row[:], mrow[:], mrow[:], OP.mult))
            v_var = sc.inc("v", lambda e: e.tensor_tensor(
                vrow[:], vrow[:], m2row[:], OP.subtract))
            sc.wait("s", "v", v_var)
            sc.inc("s", lambda e: e.activation(vrow[:], vrow[:], AF.Ln,
                                               bias=epsc[:]))
            s_r = sc.inc("s", lambda e: e.activation(rrow[:], vrow[:],
                                                     AF.Exp, scale=-0.5))
            sc.wait("v", "s", s_r)
            sc.inc("v", lambda e: e.tensor_tensor(m2row[:], mrow[:], rrow[:],
                                                  OP.mult))
            v_cr = sc.inc("v", lambda e: e.tensor_scalar(
                crow[:], m2row[:], -1.0, None, OP.mult))
            sc.barrier()
            for quar in range(2):
                q0 = quar * 1024
                with contextlib.ExitStack() as pp:
                    ps_a = pp.enter_context(cx.PSUM("ps_a", [128, 1024], F32))
                    ps_c = pp.enter_context(cx.PSUM("ps_c", [128, 1024], F32))
                    sc.wait("t", "v", v_cr)
                    t_bc = None
                    for nn in range(2):
                        sl = slice(nn * 512, nn * 512 + 512)
                        sg = slice(q0 + nn * 512, q0 + nn * 512 + 512)
                        sc.raw("t", lambda e, _sl=sl, _sg=sg: e.matmul(
                            ps_a[:, _sl], onesb_r2[:], rrow[0:1, _sg],
                            start=True, stop=True))
                        t_bc = sc.inc("t", lambda e, _sl=sl, _sg=sg: e.matmul(
                            ps_c[:, _sl], onesb_r2[:], crow[0:1, _sg],
                            start=True, stop=True))
                    sc.wait("v", "t", t_bc)
                    v_norm = None
                    ldq = {}
                    if quar == 0:
                        vpar = {0: None, 1: None}
                    for i in range(8):
                        sc.wait("y", "v", vpar[i % 2])
                        ldq[i] = sc.dma("y", ob[i % 2][:, 0:1024],
                                        o_scr.ap()[i * 128:(i + 1) * 128,
                                                   q0:q0 + 1024])
                        sc.wait("v", "y", ldq[i])
                        sc.inc("v", lambda e, _i=i: e.tensor_tensor(
                            tmpx[:], ob[_i % 2][:, 0:1024], ps_a[:], OP.mult))
                        sc.inc("v", lambda e, _i=i: e.tensor_tensor(
                            tmpx[:], tmpx[:], ps_c[:], OP.add))
                        vpar[i % 2] = v_norm = sc.inc(
                            "v", lambda e, _i=i, _q=slice(q0, q0 + 1024):
                            e.tensor_scalar(
                                o_n[_i][:, _q], tmpx[:], lng[:, _i:_i + 1],
                                lnb[:, _i:_i + 1], OP.mult, OP.add))
                    sc.wait("t", "v", v_norm)
        sc.barrier()

        with contextlib.ExitStack() as ph:
            wo_s = [ph.enter_context(cx.SBUFR(f"wo_s{i}", [128, C], BF16))
                    for i in range(8)]
            sout = ph.enter_context(cx.SBUFR("sout", [128, TH], F32))
            for i in range(8):
                lw = sc.dma("g", wo_s[i][:], woT[i * 128:(i + 1) * 128, :])
            sc.wait("t", "gd", lw)
            store = {}

            def evac_s(mo, nn, ps, mm):
                sc.wait("v", "t", mm)
                sc.wait("v", "y", store.get(mo))
                c = sc.inc("v", lambda e, _nn=nn: e.tensor_copy(
                    sout[:, _nn * 512:(_nn + 1) * 512], ps[:]))
                if nn == 3:
                    sc.wait("y", "v", c)
                    store[mo + 1] = sc.dma(
                        "y", s1T[mo * 128:(mo + 1) * 128, :], sout[:])
                return ("v", c)

            emit_proj_cm(cx, wo_s, o_n, TH, range(8), evac_s)
            sc.wait("v", "y", sc.cnt["y"])
        stON.close()
        sc.wait("y", "y", sc.cnt["y"])
        sc.finalize(es)
    return cx.nc


def prep_L1a_inputs(inputs, x1):
    wk = np.ascontiguousarray(f32c(inputs["L1_attn_Wk"]).T)
    wv = np.ascontiguousarray(f32c(inputs["L1_attn_Wv"]).T)
    wr = np.ascontiguousarray(f32c(inputs["L1_attn_Wr"]).T)
    wo = np.ascontiguousarray(f32c(inputs["L1_attn_Wo"]).T)

    def col8(v):
        return np.ascontiguousarray(f32c(v).reshape(8, 128).T)

    tmk2 = col8(inputs["L1_attn_time_mix_k"])
    tmv2 = col8(inputs["L1_attn_time_mix_v"])
    tmr2 = col8(inputs["L1_attn_time_mix_r"])
    td2 = col8(inputs["L1_attn_time_decay"])
    tf2 = col8(inputs["L1_attn_time_first"])
    lng2 = col8(inputs["L1_attn_ln_g"])
    lnb2 = col8(inputs["L1_attn_ln_b"])
    wr8 = f32c(inputs["L1_ffn_Wr"])
    br8 = f32c(inputs["L1_ffn_br"]).reshape(1, 8)
    idn = np.eye(128, dtype=np.float32)
    maps = []
    for core in range(NC8):
        b, th = core // 2, core % 2
        xT = np.ascontiguousarray(x1[b].T)
        rot = np.concatenate([xT[:, th * TH:(th + 1) * TH],
                              xT[:, (1 - th) * TH:(2 - th) * TH]], axis=1)
        maps.append({
            "x1T": np.ascontiguousarray(rot),
            "qoff": np.array([[th * TH]], dtype=np.float32),
            "wkT": wk, "wvT": wv, "wrT": wr, "woT": wo,
            "tmk2": tmk2, "tmv2": tmv2, "tmr2": tmr2,
            "td2": td2, "tf2": tf2, "lng2": lng2, "lnb2": lnb2,
            "wr8": wr8, "br8": br8, "idn": idn,
        })
    return maps


def run_L1a(inputs, x1):
    if "L1a" not in _builder_cache:
        _builder_cache["L1a"] = build_L1a()
    nc = _builder_cache["L1a"]
    maps = prep_L1a_inputs(inputs, x1)
    res = run_bass_kernel_spmd(nc, maps, core_ids=list(range(NC8)))
    return res.results, maps


def emit_routing_slots(cx, m1d, m2d, p1d, p2d, es_right, nrows=L):
    sc, nc = cx.sc, cx.nc
    nch = nrows // 128
    pb = lambda n, shp, dt: es_right.enter_context(cx.SBUFR(n, shp, dt))
    pos1 = pb("pos1", [128, nch], F32)
    pos2 = pb("pos2", [128, nch], F32)
    e1t = pb("e1t", [128, nch], F32)
    e2t = pb("e2t", [128, nch], F32)
    k1t = pb("k1t", [128, nch], F32)
    k2t = pb("k2t", [128, nch], F32)
    p1t = pb("p1t", [128, nch], F32)
    p2t = pb("p2t", [128, nch], F32)
    with contextlib.ExitStack() as ph:
        qb = lambda n, shp, dt: ph.enter_context(cx.SBUFR(n, shp, dt))
        m1b = qb("m1b", [128, 8], BF16)
        m2b = qb("m2b", [128, 8], BF16)
        m1f = qb("m1f", [128, 8], F32)
        m2f = qb("m2f", [128, 8], F32)
        msum = qb("msum", [128, 8], BF16)
        sl_t = qb("sl_t", [128, 128], BF16)
        iotr = qb("iotr", [128, 128], I32)
        iotc = qb("iotc", [128, 1], I32)
        iotcf = qb("iotcf", [128, 1], F32)
        slf = qb("slf", [128, 128], F32)
        eid_i = qb("eid_i", [128, 8], I32)
        eidx = qb("eidx", [128, 8], F32)
        carry_b = qb("carry_b", [1, 8], BF16)
        carry_f = qb("carry_f", [1, 8], F32)
        onescol_f = qb("onescol_f", [128, 1], F32)
        onesb_r = qb("onesb_r", [1, 128], BF16)
        onesf_r = qb("onesf_r", [1, 128], F32)
        post = qb("post", [128, 8], F32)
        tmp8 = qb("tmp8", [128, 8], F32)
        sc.inc("g", lambda e: e.iota(iotr[:], pattern=[[1, 128]], base=0,
                                     channel_multiplier=0))
        sc.inc("g", lambda e: e.iota(iotc[:], pattern=[[0, 1]], base=0,
                                     channel_multiplier=1))
        sc.inc("g", lambda e: e.iota(eid_i[:], pattern=[[1, 8]], base=0,
                                     channel_multiplier=0))
        sc.inc("g", lambda e: e.memset(onescol_f[:], 1.0))
        sc.inc("g", lambda e: e.memset(onesb_r[:], 1.0))
        gc = sc.inc("g", lambda e: e.memset(onesf_r[:], 1.0))
        sc.wait("v", "g", gc)
        sc.inc("v", lambda e: e.tensor_copy(iotcf[:], iotc[:]))
        sc.inc("v", lambda e: e.tensor_copy(slf[:], iotr[:]))
        sc.inc("v", lambda e: e.tensor_scalar(slf[:], slf[:], iotcf[:], None,
                                              OP.is_gt))
        sc.inc("v", lambda e: e.tensor_copy(sl_t[:], slf[:]))
        v_c = sc.inc("v", lambda e: e.tensor_copy(eidx[:], eid_i[:]))
        sc.wait("t", "v", v_c)
        with contextlib.ExitStack() as pp:
            ps_pos = pp.enter_context(cx.PSUM("ps_pos", [128, 8], F32))
            ps_tot = pp.enter_context(cx.PSUM("ps_tot", [1, 8], F32))
            prev_v = None
            for i in range(nch):
                ld1 = sc.dma("y", m1b[:], m1d[i * 128:(i + 1) * 128, :])
                ld2 = sc.dma("y", m2b[:], m2d[i * 128:(i + 1) * 128, :])
                ld3 = sc.dma("y", p1t[:, i:i + 1], p1d[i * 128:(i + 1) * 128, :])
                ld4 = sc.dma("y", p2t[:, i:i + 1], p2d[i * 128:(i + 1) * 128, :])
                sc.wait("v", "y", ld4)
                sc.inc("v", lambda e: e.tensor_copy(m1f[:], m1b[:]))
                sc.inc("v", lambda e: e.tensor_copy(m2f[:], m2b[:]))
                v_ms = sc.inc("v", lambda e: e.tensor_tensor(
                    msum[:], m1b[:], m2b[:], OP.add))
                sc.wait("t", "v", v_ms)
                if i == 0:
                    t_pos = sc.inc("t", lambda e: e.matmul(
                        ps_pos[:], sl_t[:], msum[:], start=True, stop=True))
                else:
                    sc.raw("t", lambda e: e.matmul(ps_pos[:], sl_t[:],
                                                   msum[:], start=True,
                                                   stop=False))
                    t_pos = sc.inc("t", lambda e: e.matmul(
                        ps_pos[:], onesf_r[:], carry_f[:], start=False,
                        stop=True))
                sc.wait("v", "t", t_pos)
                for (mk, mf, pos, et, kt) in ((m1b, m1f, pos1, e1t, k1t),
                                              (m2b, m2f, pos2, e2t, k2t)):
                    sc.inc("v", lambda e, _m=mf: e.tensor_tensor(
                        tmp8[:], _m[:], ps_pos[:], OP.mult))
                    sc.inc("v", lambda e, _p=pos, _i=i: e.reduce_sum(
                        _p[:, _i:_i + 1], tmp8[:], axis=AX))
                    sc.inc("v", lambda e, _m=mf: e.tensor_tensor(
                        tmp8[:], _m[:], eidx[:], OP.mult))
                    sc.inc("v", lambda e, _e=et, _i=i: e.reduce_sum(
                        _e[:, _i:_i + 1], tmp8[:], axis=AX))
                    sc.inc("v", lambda e, _p=pos, _k=kt, _i=i:
                           e.tensor_single_scalar(_k[:, _i:_i + 1],
                                                  _p[:, _i:_i + 1], 639.5,
                                                  OP.is_lt))
                sc.raw("t", lambda e: e.matmul(
                    ps_tot[:], onescol_f[:], m1f[:], start=True, stop=False))
                t_tot = sc.inc("t", lambda e: e.matmul(
                    ps_tot[:], onescol_f[:], m2f[:], start=False, stop=True))
                sc.wait("v", "t", t_tot)
                if i == 0:
                    sc.inc("v", lambda e: e.tensor_copy(carry_f[:],
                                                        ps_tot[:]))
                else:
                    sc.inc("v", lambda e: e.tensor_tensor(
                        carry_f[:], carry_f[:], ps_tot[:], OP.add))
                prev_v = sc.cnt["v"]
                sc.wait("t", "v", prev_v)
                sc.wait("y", "v", prev_v)
    return pos1, pos2, e1t, e2t, k1t, k2t, p1t, p2t


def build_L1moe():
    cx = Ctx()
    nc, sc = cx.nc, cx.sc
    xtok = nc.dram_tensor("xtok", [L, C], BF16, kind="ExternalInput").ap()
    m1d = nc.dram_tensor("m1d", [L, 8], BF16, kind="ExternalInput").ap()
    m2d = nc.dram_tensor("m2d", [L, 8], BF16, kind="ExternalInput").ap()
    p1d = nc.dram_tensor("p1d", [L, 1], F32, kind="ExternalInput").ap()
    p2d = nc.dram_tensor("p2d", [L, 1], F32, kind="ExternalInput").ap()
    ebased = nc.dram_tensor("ebase", [1, 1], F32, kind="ExternalInput").ap()
    w1s = nc.dram_tensor("w1s", [4 * C, 4096], F32, kind="ExternalInput").ap()
    b1s = nc.dram_tensor("b1s", [128, 128], F32, kind="ExternalInput").ap()
    w2s = nc.dram_tensor("w2s", [4 * 4096, C], F32, kind="ExternalInput").ap()
    b2s = nc.dram_tensor("b2s", [128, 32], F32, kind="ExternalInput").ap()
    idn = nc.dram_tensor("idn", [128, 128], F32, kind="ExternalInput").ap()
    ybuf = nc.dram_tensor("ybuf", [2560, C], BF16, kind="ExternalOutput").ap()
    buf_scr = nc.dram_tensor("buf_scr", [2560, C], BF16)

    CAP = 640
    es = contextlib.ExitStack()
    with es:
        sbL = lambda n, shp, dt: es.enter_context(cx.SBUF(n, shp, dt))
        ident = sbL("ident", [128, 128], BF16)
        ebcol = sbL("ebcol", [128, 1], F32)
        b1sb = sbL("b1sb", [128, 128], F32)
        b2sb = sbL("b2sb", [128, 32], F32)
        ld = sc.dma("g", ident[:], idn)
        ld = sc.dma("y", b1sb[:], b1s)
        ld = sc.dma("y", b2sb[:], b2s)
        ldq = sc.dma("y", ebcol[0:1, :], ebased)
        onesb_r = sbL("onesb_r", [1, 128], BF16)
        gc = sc.inc("g", lambda e: e.memset(onesb_r[:], 1.0))
        with cx.PSUM("ps_eb", [128, 1], F32) as ps_eb:
            eb_b = sbL("eb_b", [1, 1], BF16)
            sc.wait("v", "y", ldq)
            v_c = sc.inc("v", lambda e: e.tensor_copy(eb_b[:], ebcol[0:1, :]))
            sc.wait("t", "v", v_c)
            sc.wait("t", "g", gc)
            t_b = sc.inc("t", lambda e: e.matmul(ps_eb[:], onesb_r[:], eb_b[:],
                                                 start=True, stop=True))
            sc.wait("v", "t", t_b)
            sc.inc("v", lambda e: e.tensor_copy(ebcol[:], ps_eb[:]))
        sc.barrier()

        stRT = contextlib.ExitStack()
        rt = emit_routing_slots(cx, m1d, m2d, p1d, p2d, stRT, L)
        pos1, pos2, e1t, e2t, k1t, k2t, p1t, p2t = rt
        sc.barrier()

        with contextlib.ExitStack() as ph:
            pb = lambda n, shp, dt: ph.enter_context(cx.SBUFR(n, shp, dt))
            zt = pb("zt", [128, C], BF16)
            xrow = pb("xrow", [128, C], BF16)
            idx1 = pb("idx1", [128, 1], I32)
            idxf = pb("idxf", [128, 1], F32)
            t1 = pb("t1", [128, 1], F32)
            t2 = pb("t2", [128, 1], F32)
            gz = sc.inc("g", lambda e: e.memset(zt[:], 0.0))
            sc.wait("y", "g", gz)
            prev = None
            for i in range(20):
                prev = sc.dma("y", buf_scr.ap()[i * 128:(i + 1) * 128, :],
                              zt[:])
            sc.wait("g", "y", prev)
            with nc.gpsimd.register("bc") as bc:
                sc.raw("g", lambda e: e.reg_mov(bc, 2559))
                prev_sc = None
                for i in range(L // 128):
                    ldx = sc.dma("y", xrow[:], xtok[i * 128:(i + 1) * 128, :])
                    for (pos, et, kt) in ((pos1, e1t, k1t), (pos2, e2t, k2t)):
                        sc.inc("v", lambda e, _e=et, _i=i: e.tensor_tensor(
                            t1[:], _e[:, _i:_i + 1], ebcol[:], OP.subtract))
                        sc.inc("v", lambda e: e.tensor_single_scalar(
                            t2[:], t1[:], -0.5, OP.is_gt))
                        sc.inc("v", lambda e, _k=kt, _i=i: e.tensor_tensor(
                            t2[:], t2[:], _k[:, _i:_i + 1], OP.mult))
                        sc.inc("v", lambda e: e.tensor_scalar(
                            idxf[:], t1[:], 3.5, None, OP.is_lt))
                        sc.inc("v", lambda e: e.tensor_tensor(
                            t2[:], t2[:], idxf[:], OP.mult))
                        sc.inc("v", lambda e, _p=pos, _i=i: e.tensor_scalar(
                            idxf[:], t1[:], float(CAP), None, OP.mult))
                        sc.inc("v", lambda e, _p=pos, _i=i: e.tensor_tensor(
                            idxf[:], idxf[:], _p[:, _i:_i + 1], OP.add))
                        sc.inc("v", lambda e: e.tensor_scalar(
                            idxf[:], idxf[:], -1e6, None, OP.add))
                        sc.inc("v", lambda e: e.tensor_tensor(
                            idxf[:], idxf[:], t2[:], OP.mult))
                        sc.inc("v", lambda e: e.tensor_scalar(
                            idxf[:], idxf[:], 1e6, None, OP.add))
                        v_i = sc.inc("v", lambda e: e.tensor_copy(idx1[:],
                                                                  idxf[:]))
                        sc.wait("g", "v", v_i)
                        sc.wait("g", "y", ldx)
                        prev_sc = sc.inc(
                            "g", lambda e: e.indirect_dma_start(
                                buf_scr.ap(),
                                bass.IndirectOffsetOnAxis(ap=idx1[:], axis=0),
                                xrow[:], None, bounds_check=bc,
                                oob_is_err=False), 16, semkey="gd")
                        sc.wait("v", "gd", prev_sc)
                    sc.wait("y", "gd", prev_sc)
        sc.barrier()
        stRT.close()

        stBT = contextlib.ExitStack()
        bufT = [stBT.enter_context(cx.SBUF(f"bufT{i}", [128, 2560], BF16))
                for i in range(8)]
        for i in range(8):
            lw = sc.dma("y", bufT[i][:],
                        buf_scr.ap()[:, i * 128:(i + 1) * 128], transpose=True)
        sc.barrier()
        with contextlib.ExitStack() as ph:
            pb = lambda n, shp, dt: ph.enter_context(cx.SBUFR(n, shp, dt))
            wbuf = [pb(f"wbuf{i}", [128, 2048], BF16) for i in range(16)]
            hT = [pb(f"hT{i}", [128, CAP], BF16) for i in range(32)]
            yT = [pb(f"yT{i}", [128, CAP], BF16) for i in range(8)]
            ytk = pb("ytk", [128, C], BF16)
            stPS = contextlib.ExitStack()
            pool = [stPS.enter_context(cx.PSUM(f"pool{j}", [128, CAP], F32))
                    for j in range(3)]
            pool_last = {}
            prev_y_done = None
            prev_h_done = None
            for e_i in range(4):
                sc.wait("g", "t", prev_h_done)
                prev_lw = None
                for fh in range(2):
                    for ccx in range(8):
                        prev_lw = sc.dma(
                            "g", wbuf[fh * 8 + ccx][:],
                            w1s[e_i * C + ccx * 128:
                                e_i * C + ccx * 128 + 128,
                                fh * 2048:(fh + 1) * 2048])
                sc.wait("t", "gd", prev_lw)
                if True:
                    for mo in range(32):
                        fh, moh = mo // 16, mo % 16
                        j = mo % 3
                        if j in pool_last:
                            sc.wait("t", pool_last[j][0], pool_last[j][1])
                        mm = None
                        for nn in range(2):
                            npart = 512 if nn == 0 else 128
                            mm = mm_accum(
                                sc, pool[j][:, nn * 512:nn * 512 + npart],
                                [(wbuf[fh * 8 + ccx][:,
                                       moh * 128:(moh + 1) * 128],
                                  bufT[ccx][:, e_i * CAP + nn * 512:
                                            e_i * CAP + nn * 512 + npart])
                                 for ccx in range(8)])
                        sc.wait("s", "t", mm)
                        pool_last[j] = ("s", sc.inc(
                            "s", lambda e, _j=j, _mo=mo, _e=e_i: e.activation(
                                hT[_mo][:], pool[_j][:], AF.Relu,
                                bias=b1sb[:, _e * 32 + _mo:
                                          _e * 32 + _mo + 1])))
                    s_h_done = None
                    for j, (ek, c) in pool_last.items():
                        if ek == "s":
                            sc.wait("t", "s", c)
                            s_h_done = max(s_h_done or 0, c)
                sc.wait("g", "s", s_h_done)
                sc.wait("g", "t", sc.cnt["t"])
                prev_lw = None
                for fc in range(32):
                    prev_lw = sc.dma(
                        "g", wbuf[fc // 2][:, (fc % 2) * 1024:
                                           (fc % 2) * 1024 + 1024],
                        w2s[e_i * 4096 + fc * 128:
                            e_i * 4096 + fc * 128 + 128, :])
                sc.wait("t", "gd", prev_lw)
                if True:
                    for mo in range(8):
                        j = mo % 3
                        if j in pool_last:
                            sc.wait("t", pool_last[j][0], pool_last[j][1])
                        mm = None
                        for nn in range(2):
                            npart = 512 if nn == 0 else 128
                            mm = mm_accum(
                                sc, pool[j][:, nn * 512:nn * 512 + npart],
                                [(wbuf[fc // 2][:, (fc % 2) * 1024 +
                                                mo * 128:
                                                (fc % 2) * 1024 +
                                                (mo + 1) * 128],
                                  hT[fc][:, nn * 512:nn * 512 + npart])
                                 for fc in range(32)], )
                        sc.wait("v", "t", mm)
                        pool_last[j] = ("v", sc.inc(
                            "v", lambda e, _j=j, _mo=mo, _e=e_i:
                            e.tensor_scalar(
                                yT[_mo][:], pool[_j][:],
                                b2sb[:, _e * 8 + _mo:_e * 8 + _mo + 1],
                                None, OP.add)))
                    v_y_done = None
                    for j, (ek, c) in pool_last.items():
                        if ek == "v":
                            sc.wait("t", "v", c)
                            v_y_done = max(v_y_done or 0, c)
                prev_h_done = sc.cnt["t"]
                with cx.PSUM("ps_ty", [128, C], BF16) as ps_ty:
                    prev_cp = None
                    prev_st = None
                    for sblk in range(5):
                        sc.wait("t", "v", prev_cp)
                        t_tr = None
                        for mo in range(8):
                            t_tr = sc.inc(
                                "t", lambda e, _m=mo, _s=sblk: e.transpose(
                                    ps_ty[:, _m * 128:(_m + 1) * 128],
                                    yT[_m][:, _s * 128:(_s + 1) * 128],
                                    ident[:]))
                        sc.wait("v", "t", t_tr)
                        sc.wait("v", "y", prev_st)
                        prev_cp = sc.inc("v", lambda e: e.tensor_copy(
                            ytk[:], ps_ty[:]))
                        sc.wait("y", "v", prev_cp)
                        prev_st = sc.dma(
                            "y", ybuf[e_i * CAP + sblk * 128:
                                      e_i * CAP + (sblk + 1) * 128, :],
                            ytk[:])
                sc.wait("v", "y", prev_st)
            stPS.close()
        stBT.close()
        sc.wait("y", "y", sc.cnt["y"])
        sc.finalize(es)
    return cx.nc



def build_L1comb():
    cx = Ctx()
    nc, sc = cx.nc, cx.sc
    yb = nc.dram_tensor("yb", [5120, C], BF16, kind="ExternalInput").ap()
    m1d = nc.dram_tensor("m1d", [L, 8], BF16, kind="ExternalInput").ap()
    m2d = nc.dram_tensor("m2d", [L, 8], BF16, kind="ExternalInput").ap()
    p1d = nc.dram_tensor("p1d", [L, 1], F32, kind="ExternalInput").ap()
    p2d = nc.dram_tensor("p2d", [L, 1], F32, kind="ExternalInput").ap()
    toffd = nc.dram_tensor("toff", [1, 1], F32, kind="ExternalInput").ap()
    x1T = nc.dram_tensor("x1Th", [C, TH], F32, kind="ExternalInput").ap()
    s1T = nc.dram_tensor("s1Th", [C, TH], F32, kind="ExternalInput").ap()
    idn = nc.dram_tensor("idn", [128, 128], F32, kind="ExternalInput").ap()
    x2T = nc.dram_tensor("x2T", [C, TH], F32, kind="ExternalOutput").ap()

    CAP = 640
    es = contextlib.ExitStack()
    with es:
        sbL = lambda n, shp, dt: es.enter_context(cx.SBUF(n, shp, dt))
        ident = sbL("ident", [128, 128], BF16)
        tcol = sbL("tcol", [128, 1], F32)
        ld = sc.dma("g", ident[:], idn)
        ldq = sc.dma("y", tcol[0:1, :], toffd)
        onesb_r = sbL("onesb_r", [1, 128], BF16)
        gc = sc.inc("g", lambda e: e.memset(onesb_r[:], 1.0))
        with cx.PSUM("ps_eb", [128, 1], F32) as ps_eb:
            eb_b = sbL("eb_b", [1, 1], BF16)
            sc.wait("v", "y", ldq)
            v_c = sc.inc("v", lambda e: e.tensor_copy(eb_b[:], tcol[0:1, :]))
            sc.wait("t", "v", v_c)
            sc.wait("t", "g", gc)
            t_b = sc.inc("t", lambda e: e.matmul(ps_eb[:], onesb_r[:],
                                                 eb_b[:], start=True,
                                                 stop=True))
            sc.wait("v", "t", t_b)
            sc.inc("v", lambda e: e.tensor_copy(tcol[:], ps_eb[:]))
        sc.barrier()

        stRT = contextlib.ExitStack()
        rt = emit_routing_slots(cx, m1d, m2d, p1d, p2d, stRT, L)
        pos1, pos2, e1t, e2t, k1t, k2t, p1t, p2t = rt
        sc.barrier()

        g_scr = nc.dram_tensor("g_scr2", [L, C], BF16)
        with contextlib.ExitStack() as ph:
            pb = lambda n, shp, dt: ph.enter_context(cx.SBUFR(n, shp, dt))
            y1 = pb("y1", [128, C], BF16)
            y2 = pb("y2", [128, C], BF16)
            idx1 = pb("idx1", [128, 1], I32)
            idx2 = pb("idx2", [128, 1], I32)
            f1 = pb("f1", [128, 1], F32)
            f2 = pb("f2", [128, 1], F32)
            idxf = pb("idxf", [128, 1], F32)
            t1c = pb("t1c", [128, 1], F32)
            moet = pb("moet", [128, C], F32)
            gtk = pb("gtk", [128, C], BF16)
            prev_g = None
            for i in range(L // 128):
                vi = None
                for (pos, et, kt, pt, idx, ff) in (
                        (pos1, e1t, k1t, p1t, idx1, f1),
                        (pos2, e2t, k2t, p2t, idx2, f2)):
                    sc.inc("v", lambda e, _p=pos, _i=i: e.tensor_single_scalar(
                        idxf[:], _p[:, _i:_i + 1], float(CAP - 1), OP.min))
                    sc.inc("v", lambda e, _e=et, _i=i: e.scalar_tensor_tensor(
                        idxf[:], _e[:, _i:_i + 1], float(CAP), idxf[:],
                        OP.mult, OP.add))
                    sc.inc("v", lambda e, _x=idx: e.tensor_copy(_x[:],
                                                                idxf[:]))
                    vi = sc.inc("v", lambda e, _k=kt, _pt=pt, _f=ff, _i=i:
                                e.tensor_tensor(_f[:], _pt[:, _i:_i + 1],
                                                _k[:, _i:_i + 1], OP.mult))
                sc.wait("g", "v", vi)
                g1 = sc.inc("g", lambda e: e.indirect_dma_start(
                    y1[:], None, yb, bass.IndirectOffsetOnAxis(ap=idx1[:],
                                                               axis=0)),
                    16, semkey="gd")
                g2 = sc.inc("g", lambda e: e.indirect_dma_start(
                    y2[:], None, yb, bass.IndirectOffsetOnAxis(ap=idx2[:],
                                                               axis=0)),
                    16, semkey="gd")
                sc.wait("v", "gd", g2)
                sc.inc("v", lambda e: e.tensor_scalar(
                    moet[:], y1[:], f1[:], None, OP.mult))
                v_m = sc.inc("v", lambda e: e.scalar_tensor_tensor(
                    moet[:], y2[:], f2[:], moet[:], OP.mult, OP.add))
                sc.wait("s", "v", v_m)
                sc.wait("s", "y", prev_g)
                s_g = sc.inc("s", lambda e: e.activation(gtk[:], moet[:],
                                                         AF.Silu))
                sc.wait("y", "s", s_g)
                prev_g = sc.dma("y", g_scr.ap()[i * 128:(i + 1) * 128, :],
                                gtk[:])
                sc.wait("v", "y", prev_g)
        sc.barrier()

        with contextlib.ExitStack() as ph:
            pb = lambda n, shp, dt: ph.enter_context(cx.SBUFR(n, shp, dt))
            grow = pb("grow", [128, C], BF16)
            idxr = pb("idxr", [128, 1], I32)
            idxf2 = pb("idxf2", [128, 1], F32)
            iotc = pb("iotc", [128, 1], I32)
            iotcf = pb("iotcf", [128, 1], F32)
            t1c = pb("t1c", [128, 1], F32)
            gT = [pb(f"gT{i}", [128, TH], BF16) for i in range(8)]
            x1b = pb("x1b", [128, TH], F32)
            s1b = pb("s1b", [128, TH], F32)
            tmpc = pb("tmpc", [128, 512], F32)
            xob = pb("xob", [128, TH], F32)
            gc = sc.inc("g", lambda e: e.iota(iotc[:], pattern=[[0, 1]],
                                              base=0, channel_multiplier=1))
            sc.wait("v", "g", gc)
            sc.inc("v", lambda e: e.tensor_copy(iotcf[:], iotc[:]))
            v_t1 = sc.inc("v", lambda e: e.tensor_tensor(
                t1c[:], iotcf[:], tcol[:], OP.add))
            with cx.PSUM("ps_tg", [128, C], BF16) as ps_tg:
                prev_cp = None
                prev_tr = None
                for tc in range(16):
                    sc.wait("v", "gd", prev_tr)
                    sc.inc("v", lambda e, _tc=tc: e.tensor_single_scalar(
                        idxf2[:], t1c[:], float(_tc * 128), OP.add))
                    v_i = sc.inc("v", lambda e: e.tensor_copy(idxr[:],
                                                              idxf2[:]))
                    sc.wait("g", "v", v_i)
                    sc.wait("g", "t", prev_tr_t if tc else None)
                    prev_tr = sc.inc("g", lambda e: e.indirect_dma_start(
                        grow[:], None, g_scr.ap(),
                        bass.IndirectOffsetOnAxis(ap=idxr[:], axis=0)),
                        16, semkey="gd")
                    sc.wait("t", "gd", prev_tr)
                    sc.wait("t", "v", prev_cp)
                    t_tr = None
                    for ccx in range(8):
                        t_tr = sc.inc(
                            "t", lambda e, _c=ccx: e.transpose(
                                ps_tg[:, _c * 128:(_c + 1) * 128],
                                grow[:, _c * 128:(_c + 1) * 128], ident[:]))
                    prev_tr_t = t_tr
                    sc.wait("v", "t", t_tr)
                    prev_cp = None
                    for ccx in range(8):
                        prev_cp = sc.inc(
                            "v", lambda e, _c=ccx, _tc=tc: e.tensor_copy(
                                gT[_c][:, _tc * 128:(_tc + 1) * 128],
                                ps_tg[:, _c * 128:(_c + 1) * 128]))
            prev_st = None
            for ccx in range(8):
                sc.wait("y", "v", prev_st_v if ccx else None)
                ld1 = sc.dma("y", x1b[:], x1T[ccx * 128:(ccx + 1) * 128, :])
                ld2 = sc.dma("y", s1b[:], s1T[ccx * 128:(ccx + 1) * 128, :])
                sc.wait("v", "y", ld2)
                vv = None
                for nn in range(4):
                    sl = slice(nn * 512, (nn + 1) * 512)
                    sc.inc("v", lambda e, _c=ccx, _sl=sl: e.tensor_tensor(
                        tmpc[:], s1b[:, _sl], gT[_c][:, _sl], OP.mult))
                    vv = sc.inc("v", lambda e, _sl=sl: e.tensor_tensor(
                        xob[:, _sl], tmpc[:], x1b[:, _sl], OP.add))
                prev_st_v = vv
                sc.wait("y", "v", vv)
                prev_st = sc.dma("y", x2T[ccx * 128:(ccx + 1) * 128, :],
                                 xob[:])
                sc.wait("v", "y", prev_st)
        sc.wait("y", "y", sc.cnt["y"])
        sc.finalize(es)
    return cx.nc



def prep_L1moe_inputs(inputs, bres):
    w1 = f32c(inputs["L1_ffn_W1"])
    b1 = f32c(inputs["L1_ffn_b1"])
    w2 = f32c(inputs["L1_ffn_W2"])
    b2 = f32c(inputs["L1_ffn_b2"])
    idn = np.eye(128, dtype=np.float32)
    maps = []
    for core in range(NC8):
        b, eh = core // 2, core % 2
        r0, r1 = bres[2 * b], bres[2 * b + 1]
        cat = lambda k: np.ascontiguousarray(
            np.concatenate([np.asarray(r0[k]), np.asarray(r1[k])], axis=0))
        es = slice(4 * eh, 4 * eh + 4)
        maps.append({
            "xtok": cat("xtok"), "m1d": cat("m1o"), "m2d": cat("m2o"),
            "p1d": cat("p1o"), "p2d": cat("p2o"),
            "ebase": np.array([[4.0 * eh]], dtype=np.float32),
            "w1s": np.ascontiguousarray(w1[es].reshape(4 * C, 4096)),
            "b1s": np.ascontiguousarray(
                b1[es].reshape(4 * 32, 128).T),
            "w2s": np.ascontiguousarray(w2[es].reshape(4 * 4096, C)),
            "b2s": np.ascontiguousarray(b2[es].reshape(4 * 8, 128).T),
            "idn": idn,
        })
    return maps


def prep_L1comb_inputs(inputs, x1, bres, cres):
    idn = np.eye(128, dtype=np.float32)
    maps = []
    for core in range(NC8):
        b, th = core // 2, core % 2
        r0, r1 = bres[2 * b], bres[2 * b + 1]
        cat = lambda k: np.ascontiguousarray(
            np.concatenate([np.asarray(r0[k]), np.asarray(r1[k])], axis=0))
        yb = np.ascontiguousarray(
            np.concatenate([np.asarray(cres[2 * b]["ybuf"]),
                            np.asarray(cres[2 * b + 1]["ybuf"])], axis=0))
        maps.append({
            "yb": yb, "m1d": cat("m1o"), "m2d": cat("m2o"),
            "p1d": cat("p1o"), "p2d": cat("p2o"),
            "toff": np.array([[th * TH]], dtype=np.float32),
            "x1Th": np.ascontiguousarray(x1[b, th * TH:(th + 1) * TH, :].T),
            "s1Th": np.asarray(bres[core]["s1T"]),
            "idn": idn,
        })
    return maps


def run_L1(inputs, x1):
    for nm, bld in (("L1a", build_L1a), ("L1moe", build_L1moe),
                    ("L1comb", build_L1comb)):
        if nm not in _builder_cache:
            _builder_cache[nm] = bld()
    mb = prep_L1a_inputs(inputs, x1)
    resb = run_bass_kernel_spmd(_builder_cache["L1a"], mb,
                                core_ids=list(range(NC8))).results
    mc = prep_L1moe_inputs(inputs, resb)
    resc = run_bass_kernel_spmd(_builder_cache["L1moe"], mc,
                                core_ids=list(range(NC8))).results
    md = prep_L1comb_inputs(inputs, x1, resb, resc)
    resd = run_bass_kernel_spmd(_builder_cache["L1comb"], md,
                                core_ids=list(range(NC8))).results
    x2 = np.empty((B, L, C), dtype=np.float32)
    for core in range(NC8):
        b, th = core // 2, core % 2
        x2[b, th * TH:(th + 1) * TH, :] = resd[core]["x2T"].T
    return x2


def build_L2a():
    cx = Ctx()
    nc, sc = cx.nc, cx.sc
    x2T = nc.dram_tensor("x2T", [C, L], F32, kind="ExternalInput").ap()
    winx = nc.dram_tensor("winx", [C, 512], F32, kind="ExternalInput").ap()
    wing = nc.dram_tensor("wing", [C, 512], F32, kind="ExternalInput").ap()
    binx = nc.dram_tensor("binx", [128, 4], F32, kind="ExternalInput").ap()
    bing = nc.dram_tensor("bing", [128, 4], F32, kind="ExternalInput").ap()
    a_d = nc.dram_tensor("a_d", [512, 64], F32, kind="ExternalInput").ap()
    bm_d = nc.dram_tensor("bm_d", [512, 64], F32, kind="ExternalInput").ap()
    cm_d = nc.dram_tensor("cm_d", [C, 64], F32, kind="ExternalInput").ap()
    dv_d = nc.dram_tensor("dv_d", [128, 4], F32, kind="ExternalInput").ap()
    gn_d = nc.dram_tensor("gn_d", [128, 8], F32, kind="ExternalInput").ap()
    yg = nc.dram_tensor("yg", [512, 2048], BF16, kind="ExternalOutput").ap()

    TS = 2048
    es = contextlib.ExitStack()
    with es:
        sbL = lambda n, shp, dt: es.enter_context(cx.SBUF(n, shp, dt))
        a_t = [sbL(f"a_t{i}", [128, 64], F32) for i in range(4)]
        bm_t = [sbL(f"bm_t{i}", [128, 64], F32) for i in range(4)]
        cb = sbL("cb", [128, 64], F32)
        dv = sbL("dv", [128, 4], F32)
        gn = sbL("gn", [128, 8], F32)
        binx_sb = sbL("binx_sb", [128, 4], F32)
        bing_sb = sbL("bing_sb", [128, 4], F32)
        onescol_f = sbL("onescol_f", [128, 1], F32)
        onesb_r = sbL("onesb_r", [1, 128], BF16)
        onesf_r = sbL("onesf_r", [1, 128], F32)
        for i in range(4):
            sc.dma("y", a_t[i][:], a_d[i * 128:(i + 1) * 128, :])
            sc.dma("y", bm_t[i][:], bm_d[i * 128:(i + 1) * 128, :])
        sc.dma("y", dv[:], dv_d)
        sc.dma("y", binx_sb[:], binx)
        sc.dma("y", bing_sb[:], bing)
        ldy = sc.dma("y", gn[:], gn_d)
        sc.inc("g", lambda e: e.memset(onescol_f[:], 1.0))
        sc.inc("g", lambda e: e.memset(onesf_r[:], 1.0))
        gc = sc.inc("g", lambda e: e.memset(onesb_r[:], 1.0))

        with contextlib.ExitStack() as ph:
            cmt = [ph.enter_context(cx.SBUFR(f"cmt{i}", [128, 64], F32))
                   for i in range(8)]
            csr = ph.enter_context(cx.SBUFR("csr", [1, 64], F32))
            for i in range(8):
                lc = sc.dma("y", cmt[i][:], cm_d[i * 128:(i + 1) * 128, :])
            with cx.PSUM("ps_cs", [1, 64], F32) as ps_cs, \
                    cx.PSUM("ps_cb", [128, 64], F32) as ps_cb:
                sc.wait("t", "y", lc)
                sc.wait("t", "g", gc)
                t_c = None
                for i in range(8):
                    if i < 7:
                        sc.raw("t", lambda e, _i=i: e.matmul(
                            ps_cs[:], onescol_f[:], cmt[_i][:],
                            start=(_i == 0), stop=False))
                    else:
                        t_c = sc.inc("t", lambda e, _i=i: e.matmul(
                            ps_cs[:], onescol_f[:], cmt[_i][:],
                            start=False, stop=True))
                sc.wait("v", "t", t_c)
                v_c = sc.inc("v", lambda e: e.tensor_copy(csr[:], ps_cs[:]))
                sc.wait("t", "v", v_c)
                t_c = sc.inc("t", lambda e: e.matmul(
                    ps_cb[:], onesf_r[:], csr[:], start=True, stop=True))
                sc.wait("v", "t", t_c)
                sc.inc("v", lambda e: e.tensor_copy(cb[:], ps_cb[:]))
        sc.barrier()

        stXN = contextlib.ExitStack()
        xn = emit_ln_cm(cx, stXN, x2T, L)
        sc.barrier()
        stXS = contextlib.ExitStack()
        xs = [stXS.enter_context(cx.SBUFR(f"xs{i}", [128, TS], BF16))
              for i in range(8)]
        v_cmp = None
        for i in range(8):
            v_cmp = sc.inc("v", lambda e, _i=i: e.tensor_copy(
                xs[_i][:], xn[_i].ap().rearrange(
                    "p (t two) -> p t two", two=2)[:, :, 0:1]))
        stXN.close()
        with contextlib.ExitStack() as ph:
            pb = lambda n, shp, dt: ph.enter_context(cx.SBUFR(n, shp, dt))
            sqb = pb("sqb", [128, TS], F32)
            rrow = pb("rrow", [1, TS], BF16)
            vrow = pb("vrow", [1, TS], F32)
            epsc = pb("epsc", [1, 1], F32)
            gcc = sc.inc("g", lambda e: e.memset(epsc[:], 1e-6))
            with contextlib.ExitStack() as pp:
                ps_s2 = pp.enter_context(cx.PSUM("ps_s2", [1, TS], F32))
                prev_t = None
                for i in range(8):
                    sc.wait("s", "v", v_cmp)
                    if prev_t:
                        sc.wait("s", "t", prev_t)
                    s_sq = sc.inc("s", lambda e, _i=i: e.activation(
                        sqb[:], xs[_i][:], AF.Square))
                    sc.wait("t", "s", s_sq)
                    for nn in range(4):
                        sl = slice(nn * 512, nn * 512 + 512)
                        if nn < 3:
                            sc.raw("t", lambda e, _i=i, _sl=sl: e.matmul(
                                ps_s2[:, _sl], onescol_f[:], sqb[:, _sl],
                                start=(_i == 0), stop=(_i == 7)))
                        else:
                            prev_t = sc.inc(
                                "t", lambda e, _i=i, _sl=sl: e.matmul(
                                    ps_s2[:, _sl], onescol_f[:], sqb[:, _sl],
                                    start=(_i == 0), stop=(_i == 7)))
                sc.wait("v", "t", prev_t)
                sc.inc("v", lambda e: e.tensor_scalar(
                    vrow[:], ps_s2[:], 1.0 / C, None, OP.mult))
            v_v = sc.cnt["v"]
            sc.wait("s", "v", v_v)
            sc.wait("s", "g", gcc)
            sc.inc("s", lambda e: e.activation(vrow[:], vrow[:], AF.Ln,
                                               bias=epsc[:]))
            s_r = sc.inc("s", lambda e: e.activation(rrow[:], vrow[:],
                                                     AF.Exp, scale=-0.5))
            for quar in range(2):
                q0 = quar * 1024
                with cx.PSUM("ps_a", [128, 1024], F32) as ps_a:
                    sc.wait("t", "s", s_r)
                    t_bc = None
                    for nn in range(2):
                        sl = slice(q0 + nn * 512, q0 + nn * 512 + 512)
                        sl2 = slice(nn * 512, nn * 512 + 512)
                        t_bc = sc.inc("t", lambda e, _s=sl, _s2=sl2: e.matmul(
                            ps_a[:, _s2], onesb_r[:], rrow[0:1, _s],
                            start=True, stop=True))
                    sc.wait("v", "t", t_bc)
                    v_n = None
                    for i in range(8):
                        sc.inc("v", lambda e, _i=i, _q=slice(q0, q0 + 1024):
                               e.tensor_tensor(sqb[:, 0:1024], xs[_i][:, _q],
                                               ps_a[:], OP.mult))
                        v_n = sc.inc(
                            "v", lambda e, _i=i, _q=slice(q0, q0 + 1024):
                            e.tensor_scalar(xs[_i][:, _q], sqb[:, 0:1024],
                                            gn[:, _i:_i + 1], None, OP.mult))
                    sc.wait("t", "v", v_n)
        sc.barrier()

        stP = contextlib.ExitStack()
        xp_t = [stP.enter_context(cx.SBUF(f"xp{i}", [128, TS], F32))
                for i in range(4)]
        gate_t = [stP.enter_context(cx.SBUF(f"gt{i}", [128, TS], BF16))
                  for i in range(4)]
        with contextlib.ExitStack() as ph:
            wb = [ph.enter_context(cx.SBUFR(f"wb{i}", [128, 512], BF16))
                  for i in range(8)]
            with contextlib.ExitStack() as pp:
              psp = [pp.enter_context(cx.PSUM(f"psp{j}", [128, 512],
                                              F32)) for j in range(4)]
              last = {}
              for wsrc, bsrc, dst, dodt in ((winx, binx, xp_t, F32),
                                            (wing, bing, gate_t, BF16)):
                sc.wait("g", "t", sc.cnt["t"])
                for i in range(8):
                    lw = sc.dma("g", wb[i][:], wsrc[i * 128:(i + 1) * 128, :])
                sc.wait("t", "gd", lw)
                if True:
                    for mo in range(4):
                        for nn in range(4):
                            j = nn
                            sc.wait("t", "v", last.get(j))
                            mm = mm_accum(
                                sc, psp[j][:],
                                [(wb[cc][:, mo * 128:(mo + 1) * 128],
                                  xs[cc][:, nn * 512:(nn + 1) * 512])
                                 for cc in range(8)])
                            sc.wait("v", "t", mm)
                            sl = slice(nn * 512, (nn + 1) * 512)
                            last[j] = sc.inc(
                                "v", lambda e, _j=j, _m=mo, _sl=sl, _d=dst,
                                _b=bsrc: e.tensor_scalar(
                                    _d[_m][:, _sl], psp[_j][:],
                                    binx_sb[:, _m:_m + 1] if _b is binx
                                    else bing_sb[:, _m:_m + 1],
                                    None, OP.add))
              for k, c in last.items():
                  sc.wait("t", "v", c)
        stXS.close()
        sc.barrier()

        with contextlib.ExitStack() as ph:
            pb = lambda n, shp, dt: ph.enter_context(cx.SBUFR(n, shp, dt))
            h2 = [pb(f"h2_{g}", [128, 128], F32) for g in range(2)]
            u2 = [pb(f"u2_{g}", [128, 128], F32) for g in range(2)]
            vb2 = [pb(f"vb2_{g}", [128, 128], F32) for g in range(2)]
            yG = [pb(f"yG{g}", [128, 2 * TS], F32) for g in range(2)]
            y_t = [pb(f"y_t{i}", [128, TS], F32) for i in range(4)]
            a_big = pb("a_big", [128, 256], F32)
            cbig = pb("cbig", [128, 256], F32)
            for g4 in range(4):
                sc.inc("v", lambda e, _g=g4: e.tensor_copy(
                    a_big[:, _g * 64:(_g + 1) * 64], a_t[_g][:]))
                sc.inc("v", lambda e, _g=g4: e.tensor_copy(
                    cbig[:, _g * 64:(_g + 1) * 64], cb[:]))
            for g in range(2):
                sc.inc("v", lambda e, _g=g: e.memset(h2[_g][:], 0.0))
            tail_cnt = sc.cnt["v"]
            s_prev = [None, None]
            for t_i in range(TS):
                sc.wait("v", "v", tail_cnt)
                c_ymul = [None, None]
                c_umul = [None, None]
                for g in range(2):
                    sc.wait("v", "s", s_prev[g])
                    if t_i > 0:
                        c_ymul[g] = sc.inc(
                            "v", lambda e, _g=g: e.tensor_tensor(
                                vb2[_g][:], h2[_g][:],
                                cbig[:, _g * 128:(_g + 1) * 128], OP.mult),
                            selfwait=False)
                    c_umul[g] = sc.inc(
                        "v", lambda e, _g=g: e.tensor_tensor(
                            u2[_g][:], h2[_g][:],
                            a_big[:, _g * 128:(_g + 1) * 128], OP.mult),
                        selfwait=False)
                for g in range(2):
                    if t_i > 0:
                        sc.wait("v", "v", c_ymul[g])
                        sc.inc("v", lambda e, _g=g, _t=t_i - 1: e.reduce_sum(
                            yG[_g][:, 2 * _t:2 * _t + 2],
                            vb2[_g][:].rearrange("p (b s) -> p b s", b=2),
                            axis=AX), selfwait=False)
                    sc.wait("v", "v", c_umul[g])
                    sc.inc("v", lambda e, _g=g, _t=t_i:
                           e.scalar_tensor_tensor(
                               u2[_g][:, 0:64], bm_t[2 * _g][:],
                               xp_t[2 * _g][:, _t:_t + 1], u2[_g][:, 0:64],
                               OP.mult, OP.add), selfwait=False)
                    cu = sc.inc("v", lambda e, _g=g, _t=t_i:
                                e.scalar_tensor_tensor(
                                    u2[_g][:, 64:128], bm_t[2 * _g + 1][:],
                                    xp_t[2 * _g + 1][:, _t:_t + 1],
                                    u2[_g][:, 64:128],
                                    OP.mult, OP.add), selfwait=False)
                    sc.wait("s", "v", cu)
                    s_prev[g] = sc.inc("s", lambda e, _g=g: e.activation(
                        h2[_g][:], u2[_g][:], AF.Tanh))
                tail_cnt = sc.cnt["v"]
            for g in range(2):
                sc.wait("v", "s", s_prev[g])
                cm = sc.inc("v", lambda e, _g=g: e.tensor_tensor(
                    vb2[_g][:], h2[_g][:],
                    cbig[:, _g * 128:(_g + 1) * 128], OP.mult))
                sc.inc("v", lambda e, _g=g, _t=TS - 1: e.reduce_sum(
                    yG[_g][:, 2 * _t:2 * _t + 2],
                    vb2[_g][:].rearrange("p (b s) -> p b s", b=2),
                    axis=AX))
            for g4 in range(4):
                sc.inc("v", lambda e, _g=g4: e.tensor_copy(
                    y_t[_g][:],
                    yG[_g // 2][:].rearrange("p (t two) -> p t two", two=2)
                    [:, :, _g % 2:_g % 2 + 1]))
            with contextlib.ExitStack() as p2:
                ob = p2.enter_context(cx.SBUFR("ob", [128, TS], BF16))
                sgb = p2.enter_context(cx.SBUFR("sgb", [128, TS], BF16))
                prev_st = None
                for g4 in range(4):
                    sc.inc("v", lambda e, _g=g4: e.scalar_tensor_tensor(
                        y_t[_g][:], xp_t[_g][:], dv[:, _g:_g + 1], y_t[_g][:],
                        OP.mult, OP.add))
                    v_y = sc.cnt["v"]
                    sc.wait("s", "v", v_y)
                    s_g = sc.inc("s", lambda e, _g=g4: e.activation(
                        sgb[:], gate_t[_g][:], AF.Silu))
                    sc.wait("v", "s", s_g)
                    sc.wait("v", "y", prev_st)
                    v_o = sc.inc("v", lambda e, _g=g4: e.tensor_tensor(
                        ob[:], y_t[_g][:], sgb[:], OP.mult))
                    sc.wait("y", "v", v_o)
                    prev_st = sc.dma("y", yg[g4 * 128:(g4 + 1) * 128, :],
                                     ob[:])
        stP.close()
        sc.wait("y", "y", sc.cnt["y"])
        sc.finalize(es)
    return cx.nc


def build_L2b():
    cx = Ctx()
    nc, sc = cx.nc, cx.sc
    x2Th = nc.dram_tensor("x2Th", [C, TH], F32, kind="ExternalInput").ap()
    ygh = nc.dram_tensor("ygh", [C, TH // 2], BF16, kind="ExternalInput").ap()
    woutT = nc.dram_tensor("woutT", [C, C], F32, kind="ExternalInput").ap()
    bout2 = nc.dram_tensor("bout2", [128, 8], F32, kind="ExternalInput").ap()
    gw8 = nc.dram_tensor("gw8", [C, 8], F32, kind="ExternalInput").ap()
    w1s = nc.dram_tensor("w1s", [8 * C, 4096], F32, kind="ExternalInput").ap()
    w2s = nc.dram_tensor("w2s", [8 * 4096, C], F32, kind="ExternalInput").ap()
    idn = nc.dram_tensor("idn", [128, 128], F32, kind="ExternalInput").ap()
    x3T = nc.dram_tensor("x3T", [C, TH], F32, kind="ExternalOutput").ap()
    mr_scr = nc.dram_tensor("mr_scr", [2, TH], F32)

    es = contextlib.ExitStack()
    with es:
        sbL = lambda n, shp, dt: es.enter_context(cx.SBUF(n, shp, dt))
        ident = sbL("ident", [128, 128], BF16)
        bout = sbL("bout", [128, 8], F32)
        onesb_r = sbL("onesb_r", [1, 128], BF16)
        cmb = sbL("cmb", [128, 128], F32)
        sc.dma("g", ident[:], idn)
        sc.dma("y", bout[:], bout2)
        gcn = sc.inc("g", lambda e: e.memset(onesb_r[:], 1.0))

        stMU = contextlib.ExitStack()
        mu_k = stMU.enter_context(cx.SBUFR("mu_k", [1, TH], F32))
        r_k = stMU.enter_context(cx.SBUFR("r_k", [1, TH], F32))
        stXN = contextlib.ExitStack()
        xn = emit_ln_cm(cx, stXN, x2Th, TH, keep_stats=(mu_k, r_k, TH))
        vk = sc.cnt["v"]
        sc.wait("y", "v", vk)
        sc.dma("y", mr_scr.ap()[0:1, :], mu_k[:])
        sc.dma("y", mr_scr.ap()[1:2, :], r_k[:])
        sc.barrier()
        stMU.close()

        with contextlib.ExitStack() as ph:
            pb = lambda n, shp, dt: ph.enter_context(cx.SBUFR(n, shp, dt))
            x1f = [pb(f"x1f{i}", [128, TH], F32) for i in range(8)]
            wrf = [pb(f"wrf{i}", [128, 8], F32) for i in range(8)]
            mcol = pb("mcol", [128, 16], F32)
            rcol = pb("rcol", [128, 16], F32)
            mrc = pb("mrc", [128, 16], F32)
            bsum = pb("bsum", [128, 8], F32)
            onescol_g = pb("onescol_g", [128, 1], F32)
            t1b = pb("t1b", [128, 8], F32)
            t2b = pb("t2b", [128, 8], F32)
            nmx = pb("nmx", [128, 1], F32)
            m1v = pb("m1v", [128, 1], F32)
            m2v = pb("m2v", [128, 1], F32)
            rsum = pb("rsum", [128, 1], F32)
            mk1 = pb("mk1", [128, 8], F32)
            pm = pb("pm", [128, 8], F32)
            bs_b = pb("bs_b", [1, 8], BF16)
            for i in range(8):
                sc.dma("y", x1f[i][:], x2Th[i * 128:(i + 1) * 128, :])
                sc.dma("y", wrf[i][:], gw8[i * 128:(i + 1) * 128, :])
            sc.dma("y", mcol[:],
                   mr_scr.ap()[0:1, :].rearrange("a (tc p) -> (a p) tc",
                                                 p=128),
                   allow_slow_non_contiguous=True)
            ldm = sc.dma("y", rcol[:],
                         mr_scr.ap()[1:2, :].rearrange(
                             "a (tc p) -> (a p) tc", p=128),
                         allow_slow_non_contiguous=True)
            gc3 = sc.inc("g", lambda e: e.memset(onescol_g[:], 1.0))
            sc.wait("v", "y", ldm)
            sc.inc("v", lambda e: e.tensor_tensor(mrc[:], mcol[:], rcol[:],
                                                  OP.mult))
            with cx.PSUM("ps_bs", [1, 8], F32) as ps_bs:
                sc.wait("t", "y", ldm)
                sc.wait("t", "g", gc3)
                t_bs = None
                for cc in range(8):
                    if cc < 7:
                        sc.raw("t", lambda e, _c=cc: e.matmul(
                            ps_bs[:], onescol_g[:], wrf[_c][:],
                            start=(_c == 0), stop=False))
                    else:
                        t_bs = sc.inc("t", lambda e, _c=cc: e.matmul(
                            ps_bs[:], onescol_g[:], wrf[_c][:],
                            start=False, stop=True))
                sc.wait("v", "t", t_bs)
                v_bs = sc.inc("v", lambda e: e.tensor_copy(bs_b[:], ps_bs[:]))
            with cx.PSUM("ps_b2", [128, 8], F32) as ps_b2:
                sc.wait("t", "v", v_bs)
                t_x = sc.inc("t", lambda e: e.matmul(
                    ps_b2[:], onesb_r[:], bs_b[:], start=True, stop=True))
                sc.wait("v", "t", t_x)
                v_x = sc.inc("v", lambda e: e.tensor_copy(bsum[:], ps_b2[:]))
                sc.wait("t", "v", v_x)
            with cx.PSUM("ps_lg", [128, 8], F32) as ps_lg:
                prev = None
                for tc in range(16):
                    sc.wait("t", "v", prev)
                    t_lg = mm_accum(
                        sc, ps_lg[:],
                        [(x1f[cc][:, tc * 128:(tc + 1) * 128], wrf[cc][:])
                         for cc in range(8)])
                    sc.wait("v", "t", t_lg)
                    sc.inc("v", lambda e, _tc=tc: e.tensor_scalar(
                        t1b[:], ps_lg[:], rcol[:, _tc:_tc + 1], None,
                        OP.mult))
                    sc.inc("v", lambda e, _tc=tc: e.tensor_scalar(
                        t2b[:], bsum[:], mrc[:, _tc:_tc + 1], None, OP.mult))
                    sc.inc("v", lambda e: e.tensor_tensor(
                        t1b[:], t1b[:], t2b[:], OP.subtract))
                    sc.inc("v", lambda e: e.reduce_max(nmx[:], t1b[:],
                                                       axis=AX))
                    sc.inc("v", lambda e: e.tensor_scalar(
                        mk1[:], t1b[:], nmx[:], None, OP.is_ge))
                    sc.inc("v", lambda e: e.tensor_copy(m1v[:], nmx[:]))
                    sc.inc("v", lambda e: e.scalar_tensor_tensor(
                        pm[:], mk1[:], -1e9, t1b[:], OP.mult, OP.add))
                    sc.inc("v", lambda e: e.reduce_max(nmx[:], pm[:],
                                                       axis=AX))
                    sc.inc("v", lambda e: e.tensor_scalar(
                        t2b[:], pm[:], nmx[:], None, OP.is_ge))
                    sc.inc("v", lambda e: e.tensor_tensor(
                        m2v[:], nmx[:], m1v[:], OP.subtract))
                    v_d = sc.cnt["v"]
                    sc.wait("s", "v", v_d)
                    s_sg = sc.inc("s", lambda e: e.activation(
                        rsum[:], m2v[:], AF.Sigmoid, scale=-1.0))
                    sc.wait("v", "s", s_sg)
                    sc.inc("v", lambda e: e.tensor_scalar(
                        m2v[:], rsum[:], -1.0, 1.0, OP.mult, OP.add))
                    sc.inc("v", lambda e, _tc=tc: e.tensor_scalar(
                        pm[:], mk1[:], rsum[:], None, OP.mult))
                    prev = sc.inc("v", lambda e, _tc=tc:
                                  e.scalar_tensor_tensor(
                                      cmb[:, _tc * 8:(_tc + 1) * 8], t2b[:],
                                      m2v[:], pm[:], OP.mult, OP.add))
        sc.barrier()

        stS2 = contextlib.ExitStack()
        s2_t = [stS2.enter_context(cx.SBUF(f"s2_{i}", [128, TH], BF16))
                for i in range(8)]
        with contextlib.ExitStack() as ph:
            pb = lambda n, shp, dt: ph.enter_context(cx.SBUFR(n, shp, dt))
            wo_s = [pb(f"wo_s{i}", [128, C], BF16) for i in range(8)]
            yg_s = [pb(f"yg_s{i}", [128, TH // 2], BF16) for i in range(8)]
            tme = pb("tme", [128, 512], F32)
            for i in range(8):
                sc.dma("g", wo_s[i][:], woutT[i * 128:(i + 1) * 128, :])
                lw = sc.dma("y", yg_s[i][:], ygh[i * 128:(i + 1) * 128, :])
            sc.wait("t", "gd", sc.cnt["gd"])
            sc.wait("t", "y", lw)

            def evac_s2(mo, nn, ps, mm):
                sc.wait("v", "t", mm)
                sl = slice(nn * 512, (nn + 1) * 512)
                sc.inc("v", lambda e, _mo=mo: e.tensor_scalar(
                    tme[:], ps[:], bout[:, _mo:_mo + 1], None, OP.add))
                so = nn * 1024
                sc.inc("v", lambda e, _mo=mo, _so=so: e.tensor_copy(
                    s2_t[_mo].ap().rearrange("p (t two) -> p t two", two=2)
                    [:, _so // 2:_so // 2 + 512, 0:1], tme[:]))
                c = sc.inc("v", lambda e, _mo=mo, _so=so: e.tensor_copy(
                    s2_t[_mo].ap().rearrange("p (t two) -> p t two", two=2)
                    [:, _so // 2:_so // 2 + 512, 1:2], tme[:]))
                return ("v", c)

            emit_proj_cm(cx, wo_s, yg_s, TH // 2, range(8), evac_s2)
        sc.barrier()

        stAC = contextlib.ExitStack()
        acc = [stAC.enter_context(cx.SBUF(f"acc{i}", [128, 1024], F32))
               for i in range(16)]
        with contextlib.ExitStack() as ph:
            pb = lambda n, shp, dt: ph.enter_context(cx.SBUFR(n, shp, dt))
            w1h = [pb(f"w1h{i}", [128, 2048], BF16) for i in range(8)]
            w2h = [pb(f"w2h{i}", [128, 2048], BF16) for i in range(8)]
            hb = [pb(f"hb{i}", [128, 256], BF16) for i in range(2)]
            vz = None
            for i in range(16):
                vz = sc.inc("v", lambda e, _i=i: e.memset(acc[_i][:], 0.0))
            with contextlib.ExitStack() as pp:
                ps_h = [pp.enter_context(cx.PSUM(f"ps_h{j}", [128, 256], F32))
                        for j in range(2)]
                ps_y = [pp.enter_context(cx.PSUM(f"ps_y{j}", [128, 512], F32))
                        for j in range(4)]
                last_h = {}
                last_y = {}
                prev_gel = {}
                for e_i in range(8):
                    for fh in range(2):
                        sc.wait("g", "t", sc.cnt["t"])
                        lw = None
                        for ccx in range(8):
                            sc.dma("g", w1h[ccx][:],
                                   w1s[e_i * C + ccx * 128:
                                       e_i * C + ccx * 128 + 128,
                                       fh * 2048:(fh + 1) * 2048])
                        for fcx in range(16):
                            fc = fh * 16 + fcx
                            lw = sc.dma("g", w2h[fcx // 2][:,
                                                           (fcx % 2) * 1024:
                                                           (fcx % 2) * 1024 + 1024],
                                        w2s[e_i * 4096 + fc * 128:
                                            e_i * 4096 + fc * 128 + 128, :])
                        sc.wait("t", "gd", lw)
                        for tb in range(8):
                            tsl = slice(tb * 256, (tb + 1) * 256)
                            yev = None
                            for fcx in range(16):
                                j = fcx % 2
                                sc.wait("t", "s", prev_gel.get(j))
                                mm = mm_accum(
                                    sc, ps_h[j][:],
                                    [(w1h[ccx][:, fcx * 128:(fcx + 1) * 128],
                                      xn[ccx][:, tsl])
                                     for ccx in range(8)])
                                sc.wait("s", "t", mm)
                                prev_gel[j] = sc.inc(
                                    "s", lambda e, _j=j: e.activation(
                                        hb[_j][:], ps_h[_j][:], AF.Gelu))
                                sc.wait("t", "s", prev_gel[j])
                                for sub in range(2):
                                    for nc2 in range(2):
                                        yj = sub * 2 + nc2
                                        if fcx == 0:
                                            sc.wait("t", "v",
                                                    last_y.get(yj))
                                        fn = lambda e, _j=j, _s=sub, _n=nc2, \
                                            _f=fcx: e.matmul(
                                                ps_y[_s * 2 + _n][:],
                                                hb[_j][:, _s * 128:
                                                       (_s + 1) * 128],
                                                w2h[_f // 2][:,
                                                             (_f % 2) * 1024 +
                                                             _n * 512:
                                                             (_f % 2) * 1024 +
                                                             (_n + 1) * 512],
                                                start=(_f == 0),
                                                stop=(_f == 15))
                                        if fcx == 15 and sub == 1 and nc2 == 1:
                                            mmy = sc.inc("t", fn)
                                        else:
                                            sc.raw("t", fn)
                            sc.wait("v", "t", mmy)
                            for sub in range(2):
                                ti = tb * 2 + sub
                                cwc = cmb[:, (ti // 1) * 8 + e_i:
                                          (ti) * 8 + e_i + 1]
                                for nc2 in range(2):
                                    last_y[sub * 2 + nc2] = sc.inc(
                                        "v", lambda e, _t=ti, _n=nc2, _s=sub,
                                        _e=e_i: e.scalar_tensor_tensor(
                                            acc[_t][:, _n * 512:(_n + 1) * 512],
                                            ps_y[_s * 2 + _n][:],
                                            cmb[:, _t * 8 + _e:_t * 8 + _e + 1],
                                            acc[_t][:, _n * 512:(_n + 1) * 512],
                                            OP.mult, OP.add))
        sc.barrier()

        with contextlib.ExitStack() as ph:
            pb = lambda n, shp, dt: ph.enter_context(cx.SBUFR(n, shp, dt))
            g2k = pb("g2k", [128, 1024], BF16)
            g2T = [pb(f"g2T{i}", [128, TH], BF16) for i in range(8)]
            x2b = pb("x2b", [128, TH], F32)
            tmpc = pb("tmpc", [128, 512], F32)
            xob = pb("xob", [128, TH], F32)
            with cx.PSUM("ps_tg", [128, 1024], BF16) as ps_tg:
                prev_cp = None
                for ti in range(16):
                    v_s = sc.cnt["v"]
                    sc.wait("s", "v", v_s)
                    s_g = sc.inc("s", lambda e, _t=ti: e.activation(
                        g2k[:], acc[_t][:], AF.Silu))
                    sc.wait("t", "s", s_g)
                    sc.wait("t", "v", prev_cp)
                    t_tr = None
                    for ccx in range(8):
                        t_tr = sc.inc("t", lambda e, _c=ccx: e.transpose(
                            ps_tg[:, _c * 128:(_c + 1) * 128],
                            g2k[:, _c * 128:(_c + 1) * 128], ident[:]))
                    sc.wait("v", "t", t_tr)
                    prev_cp = None
                    for ccx in range(8):
                        prev_cp = sc.inc(
                            "v", lambda e, _c=ccx, _t=ti: e.tensor_copy(
                                g2T[_c][:, _t * 128:(_t + 1) * 128],
                                ps_tg[:, _c * 128:(_c + 1) * 128]))
            prev_st = None
            prev_v = None
            for ccx in range(8):
                sc.wait("y", "v", prev_v)
                ldx = sc.dma("y", x2b[:], x2Th[ccx * 128:(ccx + 1) * 128, :])
                sc.wait("v", "y", ldx)
                vv = None
                for nn in range(4):
                    sl = slice(nn * 512, (nn + 1) * 512)
                    sc.inc("v", lambda e, _c=ccx, _sl=sl: e.tensor_tensor(
                        tmpc[:], s2_t[_c][:, _sl], g2T[_c][:, _sl], OP.mult))
                    vv = sc.inc("v", lambda e, _sl=sl: e.tensor_tensor(
                        xob[:, _sl], tmpc[:], x2b[:, _sl], OP.add))
                prev_v = vv
                sc.wait("y", "v", vv)
                prev_st = sc.dma("y", x3T[ccx * 128:(ccx + 1) * 128, :],
                                 xob[:])
                sc.wait("v", "y", prev_st)
        stAC.close()
        stS2.close()
        stXN.close()
        sc.wait("y", "y", sc.cnt["y"])
        sc.finalize(es)
    return cx.nc


def prep_L2_inputs(inputs, x2):
    w_in = f32c(inputs["L2_attn_w_in"])
    b_in = f32c(inputs["L2_attn_b_in"])
    A = f32c(inputs["L2_attn_A"])
    Bm = f32c(inputs["L2_attn_Bm"])
    Cm = f32c(inputs["L2_attn_Cm"])
    Dv = f32c(inputs["L2_attn_Dv"])
    wout = np.ascontiguousarray(f32c(inputs["L2_attn_w_out"]).T)
    bout = f32c(inputs["L2_attn_b_out"])
    gn = f32c(inputs["L2_attn_g_norm"])
    gw = f32c(inputs["L2_ffn_gate_w"])
    w1 = np.ascontiguousarray(f32c(inputs["L2_ffn_w1"]).reshape(8 * C, 4096))
    w2 = np.ascontiguousarray(f32c(inputs["L2_ffn_w2"]).reshape(8 * 4096, C))
    idn = np.eye(128, dtype=np.float32)
    winT = np.ascontiguousarray(w_in.T)
    mapsA, mapsB = [], []
    for core in range(NC8):
        b, dh = core // 2, core % 2
        rx = slice(512 * dh, 512 * dh + 512)
        rg = slice(1024 + 512 * dh, 1024 + 512 * dh + 512)
        mapsA.append({
            "x2T": np.ascontiguousarray(x2[b].T),
            "winx": np.ascontiguousarray(winT[:, rx]),
            "wing": np.ascontiguousarray(winT[:, rg]),
            "binx": np.ascontiguousarray(b_in[rx].reshape(4, 128).T),
            "bing": np.ascontiguousarray(b_in[rg].reshape(4, 128).T),
            "a_d": np.ascontiguousarray(A[rx]),
            "bm_d": np.ascontiguousarray(Bm[rx]),
            "cm_d": Cm,
            "dv_d": np.ascontiguousarray(Dv[rx].reshape(4, 128).T),
            "gn_d": np.ascontiguousarray(gn.reshape(8, 128).T),
        })
        th = core % 2
        mapsB.append({
            "x2Th": np.ascontiguousarray(x2[b, th * TH:(th + 1) * TH, :].T),
            "woutT": wout,
            "bout2": np.ascontiguousarray(bout.reshape(8, 128).T),
            "gw8": gw, "w1s": w1, "w2s": w2, "idn": idn,
        })
    return mapsA, mapsB


def run_L2(inputs, x2):
    for nm, bld in (("L2a", build_L2a), ("L2b", build_L2b)):
        if nm not in _builder_cache:
            _builder_cache[nm] = bld()
    mapsA, mapsB = prep_L2_inputs(inputs, x2)
    resA = run_bass_kernel_spmd(_builder_cache["L2a"], mapsA,
                                core_ids=list(range(NC8))).results
    for core in range(NC8):
        b, th = core // 2, core % 2
        ygb = np.concatenate([np.asarray(resA[2 * b]["yg"]),
                              np.asarray(resA[2 * b + 1]["yg"])], axis=0)
        mapsB[core]["ygh"] = np.ascontiguousarray(
            ygb[:, th * (TH // 2):(th + 1) * (TH // 2)])
    resB = run_bass_kernel_spmd(_builder_cache["L2b"], mapsB,
                                core_ids=list(range(NC8))).results
    x3 = np.empty((B, L, C), dtype=np.float32)
    for core in range(NC8):
        b, th = core // 2, core % 2
        x3[b, th * TH:(th + 1) * TH, :] = resB[core]["x3T"].T
    return x3


def build_L3a():
    cx = Ctx()
    nc, sc = cx.nc, cx.sc
    x3T = nc.dram_tensor("x3T", [C, L], F32, kind="ExternalInput").ap()
    qoffd = nc.dram_tensor("qoff", [1, 1], F32, kind="ExternalInput").ap()
    wuT = nc.dram_tensor("wuT", [C, 6272], F32, kind="ExternalInput").ap()
    cw4 = nc.dram_tensor("cw4", [128, 68], F32, kind="ExternalInput").ap()
    cb17 = nc.dram_tensor("cb17", [128, 17], F32, kind="ExternalInput").ap()
    dtb = nc.dram_tensor("dtb", [128, 1], F32, kind="ExternalInput").ap()
    alog = nc.dram_tensor("alog", [8, 1], F32, kind="ExternalInput").ap()
    dpar = nc.dram_tensor("dpar", [1, 8], F32, kind="ExternalInput").ap()
    idn = nc.dram_tensor("idn", [128, 128], F32, kind="ExternalInput").ap()

    y_scr = nc.dram_tensor("y_scr", [TH, 2048], BF16, kind="ExternalOutput")
    skip_scr = nc.dram_tensor("skip_scr", [2048, TH], BF16,
                              kind="ExternalOutput")
    dtq_o = nc.dram_tensor("dtq_o", [8, TH], F32, kind="ExternalOutput")
    clg_o = nc.dram_tensor("clg_o", [8, TH], F32, kind="ExternalOutput")
    hfin_o = nc.dram_tensor("hfin_o", [16, 2048], F32, kind="ExternalOutput")
    bc_o = nc.dram_tensor("bc_o", [32, TH], BF16, kind="ExternalOutput")
    xbc_scr = nc.dram_tensor("xbc_scr", [2176, 2051], BF16)
    z_scr = nc.dram_tensor("z_scr", [2048, TH], BF16)

    NH, PD, NS = 8, 256, 16
    es = contextlib.ExitStack()
    with es:
        sbL = lambda n, shp, dt: es.enter_context(cx.SBUF(n, shp, dt))
        ident = sbL("ident", [128, 128], BF16)
        step1 = sbL("step1", [128, 1], F32)
        qcol = sbL("qcol", [128, 1], F32)
        cwsb = sbL("cwsb", [128, 68], F32)
        cbsb = sbL("cbsb", [128, 17], F32)
        dtbsb = sbL("dtbsb", [128, 1], F32)
        acol = sbL("acol", [8, 1], F32)
        onesb_r = sbL("onesb_r", [1, 128], BF16)
        onesf_r = sbL("onesf_r", [1, 128], F32)
        ones16 = sbL("ones16", [1, 16], BF16)
        ones16f = sbL("ones16f", [1, 16], F32)
        dcol128 = sbL("dcol128", [128, 8], F32)
        tri = sbL("tri", [128, 128], BF16)
        dtq = sbL("dtq", [8, TH], F32)
        hstate = [sbL(f"hst{h}", [16, PD], F32) for h in range(NH)]

        ld = sc.dma("g", ident[:], idn)
        ld = sc.dma("y", qcol[0:1, :], qoffd)
        sc.dma("y", cwsb[:], cw4)
        sc.dma("y", cbsb[:], cb17)
        sc.dma("y", dtbsb[:], dtb)
        ldy = sc.dma("y", acol[:], alog)
        gc = sc.inc("g", lambda e: e.memset(onesb_r[:], 1.0))
        gc = sc.inc("g", lambda e: e.memset(onesf_r[:], 1.0))
        gc = sc.inc("g", lambda e: e.memset(ones16[:], 1.0))
        gc = sc.inc("g", lambda e: e.memset(ones16f[:], 1.0))
        iotr = sbL("iotr", [128, 128], I32)
        iotc = sbL("iotc", [128, 1], I32)
        gci = sc.inc("g", lambda e: e.iota(iotr[:], pattern=[[1, 128]],
                                           base=0, channel_multiplier=0))
        gci = sc.inc("g", lambda e: e.iota(iotc[:], pattern=[[0, 1]],
                                           base=0, channel_multiplier=1))
        trf = sbL("trf", [128, 128], F32)
        trc = sbL("trc", [128, 1], F32)
        sc.wait("v", "g", gci)
        sc.wait("v", "y", ldy)
        sc.inc("v", lambda e: e.tensor_copy(trf[:], iotr[:]))
        sc.inc("v", lambda e: e.tensor_copy(trc[:], iotc[:]))
        sc.inc("v", lambda e: e.tensor_scalar(trf[:], trf[:], trc[:], None,
                                              OP.is_ge))
        sc.inc("v", lambda e: e.tensor_copy(tri[:], trf[:]))
        sc.inc("v", lambda e: e.tensor_single_scalar(step1[0:1, :],
                                                     qcol[0:1, :], 0.5,
                                                     OP.is_gt))
        v_a = sc.cnt["v"]
        sc.wait("s", "v", v_a)
        s_a = sc.inc("s", lambda e: e.activation(acol[:], acol[:], AF.Exp,
                                                 scale=1.0))
        sc.wait("v", "s", s_a)
        sc.inc("v", lambda e: e.tensor_scalar(acol[:], acol[:], -1.0, None,
                                              OP.mult))
        with cx.PSUM("ps_d", [128, 8], F32) as ps_d:
            drow_b = sbL("drow_b", [1, 8], F32)
            ldd = sc.dma("y", drow_b[:], dpar)
            sc.wait("t", "y", ldd)
            sc.wait("t", "g", gc)
            t_d = sc.inc("t", lambda e: e.matmul(ps_d[:], onesf_r[:],
                                                 drow_b[:], start=True,
                                                 stop=True))
            sc.wait("v", "t", t_d)
            sc.inc("v", lambda e: e.tensor_copy(dcol128[:], ps_d[:]))
        with cx.PSUM("ps_st", [128, 1], F32) as ps_st:
            stp_b = sbL("stp_b", [1, 1], BF16)
            v_c = sc.inc("v", lambda e: e.tensor_copy(stp_b[:],
                                                      step1[0:1, :]))
            sc.wait("t", "v", v_c)
            t_b = sc.inc("t", lambda e: e.matmul(ps_st[:], onesb_r[:],
                                                 stp_b[:], start=True,
                                                 stop=True))
            sc.wait("v", "t", t_b)
            sc.inc("v", lambda e: e.tensor_copy(step1[:], ps_st[:]))
        sc.barrier()

        stXN = contextlib.ExitStack()
        xn = emit_ln_cm(cx, stXN, x3T, L)
        sc.barrier()

        with contextlib.ExitStack() as ph:
            pb = lambda n, shp, dt: ph.enter_context(cx.SBUFR(n, shp, dt))
            wub = [pb(f"wub{i}", [128, 1024], BF16) for i in range(8)]
            pre = pb("pre", [128, 2051], F32)
            cvt = pb("cvt", [128, 2048], F32)
            cv2 = pb("cv2", [128, 2048], F32)
            xbc_b = pb("xbc_b", [128, 2051], BF16)
            zldb = pb("zldb", [128, 2048], BF16)
            dts1 = pb("dts1", [128, 2051], F32)
            one32 = pb("one32", [128, 1], F32)
            gco = sc.inc("g", lambda e: e.memset(one32[:], 1.0))
            stPP = contextlib.ExitStack()
            ps_p = stPP.enter_context(cx.PSUM("ps_p", [128, 2051], F32))
            prev_v = None
            prev_st = None
            lw = None
            for mo in range(49):
                sc.wait("t", "v", sc.cnt["v"])
                sc.wait("t", "s", sc.cnt["s"])
                if mo % 8 == 0:
                    sc.wait("g", "t", sc.cnt["t"])
                    nmw = min(1024, (49 - mo) * 128)
                    for ccx in range(8):
                        lw = sc.dma("g", wub[ccx][:, 0:nmw],
                                    wuT[ccx * 128:(ccx + 1) * 128,
                                        mo * 128:mo * 128 + nmw])
                    sc.wait("t", "gd", lw)
                mk = (mo % 8) * 128
                if True:
                    t_mm = None
                    for nn in range(4):
                        sl = slice(nn * 512, (nn + 1) * 512)
                        t_mm = mm_accum(
                            sc, ps_p[:, sl],
                            [(wub[cc][:, mk:mk + 128],
                              xn[cc][:, nn * 512:(nn + 1) * 512])
                             for cc in range(8)])
                    t_mm = mm_accum(
                        sc, ps_p[:, 2048:2051],
                        [(wub[cc][:, mk:mk + 128],
                          xn[cc][:, 2 * TH - 3:2 * TH])
                         for cc in range(8)])
                    sc.wait("v", "t", t_mm)
                    sc.wait("v", "y", prev_st)
                    if mo < 16:
                        prev_v = sc.inc("v", lambda e: e.tensor_copy(
                            xbc_b[:, 0:2048], ps_p[:, 0:2048]))
                        sc.wait("y", "v", prev_v)
                        prev_st = sc.dma(
                            "y", z_scr.ap()[mo * 128:(mo + 1) * 128, :],
                            xbc_b[:, 0:2048])
                    elif mo < 32:
                        ldz = sc.dma("y", zldb[:],
                                     z_scr.ap()[(mo - 16) * 128:
                                                (mo - 15) * 128, :])
                        sc.wait("s", "y", ldz)
                        sc.wait("s", "v", prev_v)
                        s_z = sc.inc("s", lambda e: e.activation(
                            cvt[:], zldb[:], AF.Silu))
                        sc.wait("v", "s", s_z)
                        prev_v = sc.inc("v", lambda e: e.tensor_tensor(
                            xbc_b[:, 0:2048], ps_p[:, 0:2048], cvt[:],
                            OP.mult))
                        sc.wait("y", "v", prev_v)
                        prev_st = sc.dma(
                            "y", skip_scr.ap()[(mo - 16) * 128:
                                               (mo - 15) * 128, :],
                            xbc_b[:, 0:2048])
                    else:
                        ci = mo - 32
                        sc.inc("v", lambda e: e.tensor_copy(
                            pre[:, 3:2051], ps_p[:, 0:2048]))
                        sc.inc("v", lambda e: e.tensor_copy(
                            pre[:, 0:3], ps_p[:, 2048:2051]))
                        sc.inc("v", lambda e: e.tensor_scalar(
                            pre[:, 0:3], pre[:, 0:3], step1[:, 0:1], None,
                            OP.mult))
                        sc.inc("v", lambda e, _c=ci: e.tensor_scalar(
                            cvt[:], pre[:, 0:2048],
                            cwsb[:, _c * 4:_c * 4 + 1], None, OP.mult))
                        for j in range(1, 4):
                            sc.inc("v", lambda e, _c=ci, _j=j:
                                   e.scalar_tensor_tensor(
                                       cvt[:], pre[:, _j:_j + 2048],
                                       cwsb[:, _c * 4 + _j:_c * 4 + _j + 1],
                                       cvt[:], OP.mult, OP.add))
                        v_cv = sc.cnt["v"]
                        sc.wait("s", "v", v_cv)
                        s_cv = sc.inc("s", lambda e, _c=ci: e.activation(
                            xbc_b[:, 3:2051], cvt[:], AF.Silu,
                            bias=cbsb[:, _c:_c + 1]))
                        prev_v = v_cv
                        if ci == 16:
                            sc.inc("v", lambda e: e.tensor_scalar(
                                pre[32:64, :], pre[32:64, :],
                                dtbsb[32:64, 0:1], None, OP.add))
                            v_d1 = sc.cnt["v"]
                            sc.wait("s", "v", v_d1)
                            s_ab = sc.inc("s", lambda e: e.activation(
                                dts1[32:64, :], pre[32:64, :], AF.Abs))
                            sc.inc("s", lambda e: e.activation(
                                dts1[32:64, :], dts1[32:64, :], AF.Exp,
                                scale=-1.0))
                            s_d3 = sc.inc("s", lambda e: e.activation(
                                dts1[32:64, :], dts1[32:64, :], AF.Ln,
                                bias=one32[32:64, 0:1]))
                            sc.wait("v", "s", s_ab)
                            sc.inc("v", lambda e: e.tensor_single_scalar(
                                pre[32:64, :], pre[32:64, :], 0.0, OP.max))
                            sc.wait("v", "s", s_d3)
                            prev_v = sc.inc("v", lambda e: e.tensor_tensor(
                                pre[32:64, :], pre[32:64, :], dts1[32:64, :],
                                OP.add))
                            sc.wait("g", "v", prev_v)
                            gdt = sc.dma("g", dtq[:], pre[32:40, 3:2051])
                            sc.wait("y", "gd", gdt)
                            sc.dma("y", dtq_o.ap(), dtq[:])
                        sc.wait("y", "s", s_cv)
                        sc.wait("y", "v", prev_v)
                        prev_st = sc.dma(
                            "y", xbc_scr.ap()[ci * 128:(ci + 1) * 128, :],
                            xbc_b[:])
                        if ci == 16:
                            sc.dma("y", bc_o.ap(),
                                   xbc_b[0:32, 3:2051])
            stPP.close()
        sc.barrier()
        stXN.close()

        with contextlib.ExitStack() as ph:
            pb = lambda n, shp, dt: ph.enter_context(cx.SBUFR(n, shp, dt))
            dta = pb("dta", [8, TH], F32)
            cumL = pb("cumL", [8, TH], F32)
            ones8t = pb("ones8t", [8, 128], F32)
            lrows = pb("lrows", [1, 8 * TH], F32)
            bdt = pb("bdt", [8, TH], F32)
            bdtrows = pb("bdtrows", [1, 8 * TH], BF16)
            csf = pb("csf", [8, 16], F32)
            daflat = pb("daflat", [1, 128], F32)
            dacol16 = pb("dacol16", [16, 128], F32)
            dtTc = [pb(f"dtTc{i}", [128, 8], F32) for i in range(16)]
            bt16 = pb("bt16", [16, TH], BF16)
            idnf = pb("idnf", [128, 128], F32)
            negf_r = pb("negf_r", [1, 128], F32)
            mfree = pb("mfree", [16, 128], F32)
            lprevc = pb("lprevc", [16, 1], F32)
            hsb = [pb(f"hsb{h}", [16, 256], BF16) for h in range(8)]
            ct16 = pb("ct16", [16, TH], BF16)
            xh = [pb(f"xh{i}", [128, TH], BF16) for i in range(2)]
            xcnk = pb("xcnk", [128, 256], BF16)
            btl = pb("btl", [16, 128], BF16)
            ctl = pb("ctl", [16, 128], BF16)
            btr = pb("btr", [128, 16], BF16)
            mlog = pb("mlog", [128, 128], F32)
            mtl = pb("mtl", [128, 128], BF16)
            ytile = pb("ytile", [128, 256], BF16)
            gc0 = sc.inc("g", lambda e: e.memset(ones8t[:], 1.0))
            gc0 = sc.inc("g", lambda e: e.memset(negf_r[:], -1.0))
            sc.dma("y", idnf[:], idn)
            for h in range(NH):
                sc.inc("v", lambda e, _h=h: e.memset(hstate[_h][:], 0.0))
                sc.inc("v", lambda e, _h=h: e.memset(hsb[_h][:], 0.0))
            ldb = sc.dma("y", bt16[:], bc_o.ap()[0:16, :])
            ldb = sc.dma("y", ct16[:], bc_o.ap()[16:32, :])
            sc.wait("v", "g", gc0)
            sc.wait("v", "y", ldb)
            sc.inc("v", lambda e: e.tensor_scalar(
                dta[:], dtq[:], acol[:, 0:1], None, OP.mult))
            prevoff = None
            for g16 in range(16):
                sl = slice(g16 * 128, (g16 + 1) * 128)
                sc.inc("v", lambda e, _s=sl: e.tensor_tensor_scan(
                    cumL[:, _s], ones8t[:], dta[:, _s], 0.0,
                    OP.mult, OP.add))
                sc.inc("v", lambda e, _g=g16, _s=sl: e.tensor_copy(
                    csf[:, _g:_g + 1], cumL[:, _s][:, 127:128]))
            for g16 in range(1, 16):
                sl = slice(g16 * 128, (g16 + 1) * 128)
                sc.inc("v", lambda e, _g=g16, _s=sl: e.tensor_scalar(
                    cumL[:, _s], cumL[:, _s],
                    cumL[:, _g * 128 - 1:_g * 128], None, OP.add))
            v_cl = sc.cnt["v"]
            sc.wait("y", "v", v_cl)
            sc.dma("y", clg_o.ap(), cumL[:])
            s_b = None
            for g16 in range(16):
                sl = slice(g16 * 128, (g16 + 1) * 128)
                sc.wait("s", "v", v_cl)
                sc.wait("s", "v", sc.cnt["v"])
                s_b = sc.inc("s", lambda e, _s=sl, _g=g16: e.activation(
                    bdt[:, _s], cumL[:, _s], AF.Exp, scale=-1.0,
                    bias=cumL[:, (_g + 1) * 128 - 1:(_g + 1) * 128]))
            sc.wait("v", "s", s_b)
            v_b = sc.inc("v", lambda e: e.tensor_tensor(bdt[:], bdt[:],
                                                        dtq[:], OP.mult))
            sc.wait("s", "v", v_b)
            s_da = sc.inc("s", lambda e: e.activation(csf[:], csf[:], AF.Exp))
            sc.wait("g", "s", s_da)
            for h in range(NH):
                gdma = sc.dma("g", daflat[0:1, h * 16:(h + 1) * 16],
                              csf[h:h + 1, :])
            for h in range(NH):
                gdma = sc.dma("g", lrows[0:1, h * TH:(h + 1) * TH],
                              cumL[h:h + 1, :])
                gdma = sc.dma("g", bdtrows[0:1, h * TH:(h + 1) * TH],
                              bdt[h:h + 1, :])
            sc.wait("t", "v", v_b)
            with cx.PSUM("ps_dt", [128, 8], F32) as ps_dt:
                prev_cp = None
                for g16 in range(16):
                    sc.wait("t", "v", prev_cp)
                    t_tr = sc.inc("t", lambda e, _g=g16: e.transpose(
                        ps_dt[:, 0:8],
                        dtq[0:8, _g * 128:(_g + 1) * 128], idnf[0:8, 0:8]))
                    sc.wait("v", "t", t_tr)
                    prev_cp = sc.inc("v", lambda e, _g=g16: e.tensor_copy(
                        dtTc[_g][:], ps_dt[:, 0:8]))
            with cx.PSUM("ps_da", [16, 128], F32) as ps_da:
                sc.wait("t", "v", sc.cnt["v"])
                sc.wait("t", "gd", sc.cnt["gd"])
                t_da = sc.inc("t", lambda e: e.matmul(
                    ps_da[:], ones16f[:], daflat[:], start=True, stop=True))
                sc.wait("v", "t", t_da)
                sc.inc("v", lambda e: e.tensor_copy(dacol16[:], ps_da[:]))
            sc.barrier()
            with contextlib.ExitStack() as pp:
                ps_g = pp.enter_context(cx.PSUM("ps_g", [128, 128], F32))
                ps_ld = pp.enter_context(cx.PSUM("ps_ld", [128, 128], F32))
                ps_y = pp.enter_context(cx.PSUM("ps_y", [128, 256], F32))
                ps_bc = pp.enter_context(cx.PSUM("ps_bc", [16, 128], F32))
                ps_tr = pp.enter_context(cx.PSUM("ps_tr", [128, 256], BF16))
                ps_hs = pp.enter_context(cx.PSUM("ps_hs", [16, 256], F32))
                btrp = pp.enter_context(cx.PSUM("btrp", [128, 16], BF16))
                prev = {}
                for h in range(NH):
                    ldx = None
                    for half in range(2):
                        ldx = sc.dma("y", xh[half][:],
                                     xbc_scr.ap()[(2 * h + half) * 128:
                                                  (2 * h + half + 1) * 128,
                                                  3:2051])
                    for g16 in range(16):
                        sl = slice(g16 * 128, (g16 + 1) * 128)
                        sc.wait("t", "y", ldx)
                        sc.wait("t", "v", prev.get("xc"))
                        t_tr = None
                        for half in range(2):
                            t_tr = sc.inc("t", lambda e, _h=half, _g=g16:
                                          e.transpose(
                                              ps_tr[:, _h * 128:
                                                    (_h + 1) * 128],
                                              xh[_h][:, _g * 128:
                                                     (_g + 1) * 128],
                                              ident[:]))
                        sc.wait("v", "t", t_tr)
                        prev["xc"] = sc.inc("v", lambda e: e.tensor_copy(
                            xcnk[:], ps_tr[:]))
                        sc.wait("t", "v", prev.get("lp"))
                        t_b1 = sc.inc("t", lambda e, _h=h, _g=g16: e.matmul(
                            ps_bc[:], ones16[:],
                            bdtrows[0:1, _h * TH + _g * 128:
                                    _h * TH + (_g + 1) * 128],
                            start=True, stop=True))
                        sc.wait("v", "t", t_b1)
                        sc.inc("v", lambda e, _g=g16: e.tensor_tensor(
                            btl[:], bt16[:, _g * 128:(_g + 1) * 128],
                            ps_bc[:], OP.mult))
                        prev["bc"] = sc.cnt["v"]
                        sc.wait("t", "v", prev.get("m"))
                        sc.raw("t", lambda e, _g=g16: e.matmul(
                            ps_g[:], bt16[:, _g * 128:(_g + 1) * 128],
                            ct16[:, _g * 128:(_g + 1) * 128],
                            start=True, stop=True))
                        sc.raw("t", lambda e, _h=h, _g=g16: e.matmul(
                            ps_ld[:], onesf_r[0:1, 0:128],
                            lrows[0:1, _h * TH + _g * 128:
                                  _h * TH + (_g + 1) * 128],
                            start=True, stop=False))
                        t_ld = sc.inc("t", lambda e, _h=h, _g=g16: e.matmul(
                            ps_ld[:],
                            lrows[0:1, _h * TH + _g * 128:
                                  _h * TH + (_g + 1) * 128],
                            negf_r[0:1, 0:128], start=False, stop=True))
                        sc.wait("s", "t", t_ld)
                        sc.wait("s", "v", prev.get("m2", None))
                        s_e = sc.inc("s", lambda e: e.activation(
                            mlog[:], ps_ld[:], AF.Exp))
                        sc.wait("v", "s", s_e)
                        sc.inc("v", lambda e: e.tensor_tensor(
                            mlog[:], mlog[:], tri[:], OP.mult))
                        sc.inc("v", lambda e, _g=g16, _h=h: e.tensor_scalar(
                            mlog[:], mlog[:], dtTc[_g][:, _h:_h + 1], None,
                            OP.mult))
                        prev["m2"] = prev["m"] = sc.inc(
                            "v", lambda e: e.tensor_tensor(
                                mtl[:], mlog[:], ps_g[:], OP.mult))
                        sc.wait("t", "v", prev["m"])
                        sc.wait("t", "v", prev["xc"])
                        sc.raw("t", lambda e: e.matmul(
                            ps_y[:], mtl[:], xcnk[:], start=True,
                            stop=False))
                        sc.wait("t", "v", prev.get("bc"))
                        t_ci = sc.inc("t", lambda e, _h=h, _g=g16: e.matmul(
                            ps_bc[:], ones16f[:],
                            lrows[0:1, _h * TH + _g * 128:
                                  _h * TH + (_g + 1) * 128],
                            start=True, stop=True))
                        sc.wait("v", "t", t_ci)
                        sc.wait("v", "s", prev.get("sec"))
                        if g16 == 0:
                            sc.inc("v", lambda e: e.tensor_copy(
                                mfree[:], ps_bc[:]))
                        else:
                            sc.inc("v", lambda e, _h=h, _g=g16:
                                   e.tensor_scalar(
                                       mfree[:], ps_bc[:],
                                       lprevc[:, 0:1], None, OP.subtract))
                        v_mf = sc.cnt["v"]
                        sc.wait("s", "v", v_mf)
                        prev["sec"] = s_ec = sc.inc(
                            "s", lambda e: e.activation(
                                mfree[:], mfree[:], AF.Exp))
                        sc.wait("v", "s", s_ec)
                        sc.inc("v", lambda e, _g=g16: e.tensor_tensor(
                            ctl[:], ct16[:, _g * 128:(_g + 1) * 128],
                            mfree[:], OP.mult))
                        v_ct = sc.cnt["v"]
                        sc.wait("t", "v", v_ct)
                        t_y = sc.inc("t", lambda e, _h=h: e.matmul(
                            ps_y[:], ctl[:], hsb[_h][:], start=False,
                            stop=True))
                        sc.wait("v", "t", t_y)
                        sc.wait("v", "y", prev.get("yst"))
                        v_y = sc.inc("v", lambda e, _h=h: e.scalar_tensor_tensor(
                            ytile[:], xcnk[:], dcol128[:, _h:_h + 1],
                            ps_y[:], OP.mult, OP.add))
                        sc.wait("y", "v", v_y)
                        prev["yst"] = sc.dma(
                            "y", y_scr.ap()[g16 * 128:(g16 + 1) * 128,
                                            h * 256:(h + 1) * 256],
                            ytile[:])
                        sc.wait("t", "v", prev["bc"])
                        t_tb = sc.inc("t", lambda e: e.transpose(
                            btrp[:, 0:16], btl[:], ident[0:16, 0:16]))
                        sc.wait("v", "t", t_tb)
                        v_tb = sc.inc("v", lambda e: e.tensor_copy(
                            btr[:], btrp[:, 0:16]))
                        sc.wait("t", "v", v_tb)
                        t_hs = sc.inc("t", lambda e: e.matmul(
                            ps_hs[:], btr[:], xcnk[:], start=True,
                            stop=True))
                        sc.wait("v", "t", t_hs)
                        sc.inc("v", lambda e, _h=h, _g=g16:
                               e.scalar_tensor_tensor(
                                   hstate[_h][:], hstate[_h][:],
                                   dacol16[:, _h * 16 + _g:
                                           _h * 16 + _g + 1],
                                   ps_hs[:], OP.mult, OP.add))
                        prev["hs"] = sc.inc(
                            "v", lambda e, _h=h: e.tensor_copy(
                                hsb[_h][:], hstate[_h][:]))
                        prev["lp"] = sc.inc(
                            "v", lambda e, _h=h, _g=g16: e.tensor_copy(
                                lprevc[:, 0:1],
                                ps_bc[:, 127:128]))
                v_f = sc.cnt["v"]
                sc.wait("y", "v", v_f)
                for h in range(NH):
                    sc.dma("y", hfin_o.ap()[:, h * 256:(h + 1) * 256],
                           hstate[h][:])
        sc.finalize(es)
    return cx.nc


def build_L3b():
    cx = Ctx()
    nc, sc = cx.nc, cx.sc
    y_d = nc.dram_tensor("y_d", [TH, 2048], BF16, kind="ExternalInput").ap()
    ct_d = nc.dram_tensor("ct_d", [16, TH], BF16, kind="ExternalInput").ap()
    clg_d = nc.dram_tensor("clg_d", [8, TH], F32, kind="ExternalInput").ap()
    hin_d = nc.dram_tensor("hin_d", [16, 2048], F32, kind="ExternalInput").ap()
    qoffd = nc.dram_tensor("qoff", [1, 1], F32, kind="ExternalInput").ap()
    lng_d = nc.dram_tensor("lng_d", [1, 2048], F32, kind="ExternalInput").ap()
    lnb_d = nc.dram_tensor("lnb_d", [1, 2048], F32, kind="ExternalInput").ap()
    wgT = nc.dram_tensor("wgT", [2048, 2048], F32, kind="ExternalInput").ap()
    bg_d = nc.dram_tensor("bg_d", [128, 16], F32, kind="ExternalInput").ap()
    skip_d = nc.dram_tensor("skip_d", [2048, TH], BF16,
                            kind="ExternalInput").ap()
    wo3T = nc.dram_tensor("wo3T", [4096, C], F32, kind="ExternalInput").ap()
    idn = nc.dram_tensor("idn", [128, 128], F32, kind="ExternalInput").ap()
    s3T = nc.dram_tensor("s3T", [C, TH], F32, kind="ExternalOutput").ap()
    gtd_scr = nc.dram_tensor("gtd_scr", [2048, TH], BF16)

    NH = 8
    es = contextlib.ExitStack()
    with es:
        sbL = lambda n, shp, dt: es.enter_context(cx.SBUF(n, shp, dt))
        ident = sbL("ident", [128, 128], BF16)
        step1 = sbL("step1", [128, 1], F32)
        qcol = sbL("qcol", [128, 1], F32)
        onesb_r = sbL("onesb_r", [1, 128], BF16)
        ones16 = sbL("ones16", [1, 16], BF16)
        bgsb = sbL("bgsb", [128, 16], F32)
        sc.dma("g", ident[:], idn)
        sc.dma("y", bgsb[:], bg_d)
        ldq = sc.dma("y", qcol[0:1, :], qoffd)
        gc = sc.inc("g", lambda e: e.memset(onesb_r[:], 1.0))
        gc = sc.inc("g", lambda e: e.memset(ones16[:], 1.0))
        sc.wait("v", "y", ldq)
        sc.wait("v", "g", gc)
        sc.inc("v", lambda e: e.tensor_single_scalar(step1[0:1, :],
                                                     qcol[0:1, :], 0.5,
                                                     OP.is_gt))
        with cx.PSUM("ps_st", [128, 1], F32) as ps_st:
            stp_b = sbL("stp_b", [1, 1], BF16)
            v_c = sc.inc("v", lambda e: e.tensor_copy(stp_b[:],
                                                      step1[0:1, :]))
            sc.wait("t", "v", v_c)
            sc.wait("t", "g", gc)
            t_b = sc.inc("t", lambda e: e.matmul(ps_st[:], onesb_r[:],
                                                 stp_b[:], start=True,
                                                 stop=True))
            sc.wait("v", "t", t_b)
            sc.inc("v", lambda e: e.tensor_copy(step1[:], ps_st[:]))
        sc.barrier()

        stYT = contextlib.ExitStack()
        ynT = [stYT.enter_context(cx.SBUF(f"ynT{i}", [128, TH], BF16))
               for i in range(16)]
        with contextlib.ExitStack() as ph:
            pb = lambda n, shp, dt: ph.enter_context(cx.SBUFR(n, shp, dt))
            ct16 = pb("ct16", [16, TH], BF16)
            clg = pb("clg", [8, TH], F32)
            elg = pb("elg", [8, TH], BF16)
            elrows = pb("elrows", [1, 8 * TH], BF16)
            hsc = [pb(f"hsc{h}", [16, 256], BF16) for h in range(NH)]
            hraw = pb("hraw", [16, 2048], F32)
            ytk = pb("ytk", [128, 2048], F32)
            ytb = pb("ytb", [128, 2048], BF16)
            ctl = pb("ctl", [16, 128], BF16)
            gamb = pb("gamb", [128, 2048], BF16)
            betb = pb("betb", [128, 2048], BF16)
            gb_row = pb("gb_row", [1, 2048], BF16)
            mrow = pb("mrow", [128, 1], F32)
            vrow = pb("vrow", [128, 1], F32)
            sqr = pb("sqr", [128, 2048], F32)
            epsc = pb("epsc", [128, 1], F32)
            sc.inc("g", lambda e: e.memset(epsc[:], 1e-5))
            ld1 = sc.dma("y", ct16[:], ct_d)
            ld1 = sc.dma("y", clg[:], clg_d)
            ld1 = sc.dma("y", hraw[:], hin_d)
            sc.wait("v", "y", ld1)
            for h in range(NH):
                sc.inc("v", lambda e, _h=h: e.tensor_scalar(
                    hsc[_h][:], hraw[:, _h * 256:(_h + 1) * 256],
                    step1[0:16, 0:1], None, OP.mult))
            v_e = sc.cnt["v"]
            sc.wait("s", "v", v_e)
            s_e = sc.inc("s", lambda e: e.activation(elg[:], clg[:], AF.Exp))
            sc.wait("g", "s", s_e)
            for h in range(NH):
                gd = sc.dma("g", elrows[0:1, h * TH:(h + 1) * TH],
                            elg[h:h + 1, :])
            with cx.PSUM("ps_gb", [128, 2048], F32) as ps_gb:
                for (dst, srcd) in ((gamb, lng_d), (betb, lnb_d)):
                    sc.wait("g", "t", sc.cnt["t"])
                    lg = sc.dma("g", gb_row[:], srcd)
                    sc.wait("t", "gd", lg)
                    sc.wait("t", "v", sc.cnt["v"])
                    t_g = None
                    for nn in range(4):
                        sl = slice(nn * 512, (nn + 1) * 512)
                        t_g = sc.inc("t", lambda e, _sl=sl: e.matmul(
                            ps_gb[:, _sl], onesb_r[:], gb_row[0:1, _sl],
                            start=True, stop=True))
                    sc.wait("v", "t", t_g)
                    sc.inc("v", lambda e, _d=dst: e.tensor_copy(_d[:],
                                                                ps_gb[:]))
            sc.barrier()
            with contextlib.ExitStack() as pp:
                ps_yc = pp.enter_context(cx.PSUM("ps_yc", [128, 256], F32))
                ps_bc = pp.enter_context(cx.PSUM("ps_bc", [16, 128], F32))
                ps_tr = pp.enter_context(cx.PSUM("ps_tr", [128, 2048], BF16))
                prev = {}
                for g16 in range(16):
                    sc.wait("g", "v", sc.cnt["v"])
                    ldy = sc.dma("g", ytk[:],
                                 y_d[g16 * 128:(g16 + 1) * 128, :])
                    for h in range(NH):
                        sc.wait("t", "v", prev.get("ctl"))
                        t_b1 = sc.inc("t", lambda e, _h=h, _g=g16: e.matmul(
                            ps_bc[:], ones16[:],
                            elrows[0:1, _h * TH + _g * 128:
                                   _h * TH + (_g + 1) * 128],
                            start=True, stop=True))
                        sc.wait("v", "t", t_b1)
                        sc.wait("v", "t", prev.get("yc"))
                        prev["ctl"] = sc.inc(
                            "v", lambda e, _g=g16: e.tensor_tensor(
                                ctl[:], ct16[:, _g * 128:(_g + 1) * 128],
                                ps_bc[:], OP.mult))
                        sc.wait("t", "v", prev["ctl"])
                        prev["yc"] = sc.inc("t", lambda e, _h=h: e.matmul(
                            ps_yc[:], ctl[:], hsc[_h][:], start=True,
                            stop=True))
                        sc.wait("v", "t", prev["yc"])
                        sc.wait("v", "gd", ldy)
                        sc.inc("v", lambda e, _h=h: e.tensor_tensor(
                            ytk[:, _h * 256:(_h + 1) * 256],
                            ytk[:, _h * 256:(_h + 1) * 256], ps_yc[:],
                            OP.add))
                        prev["yadd"] = sc.cnt["v"]
                    sc.inc("v", lambda e: e.reduce_sum(mrow[:], ytk[:],
                                                       axis=AX))
                    sc.inc("v", lambda e: e.tensor_scalar(
                        mrow[:], mrow[:], 1.0 / 2048, None, OP.mult))
                    v_m = sc.cnt["v"]
                    sc.wait("s", "v", v_m)
                    s_sq = sc.inc("s", lambda e: e.activation(
                        sqr[:], ytk[:], AF.Square))
                    sc.wait("v", "s", s_sq)
                    sc.inc("v", lambda e: e.reduce_sum(vrow[:], sqr[:],
                                                       axis=AX))
                    sc.inc("v", lambda e: e.tensor_scalar(
                        vrow[:], vrow[:], 1.0 / 2048, None, OP.mult))
                    sc.inc("v", lambda e: e.tensor_tensor(
                        sqr[:, 0:1], mrow[:], mrow[:], OP.mult))
                    v_v = sc.inc("v", lambda e: e.tensor_tensor(
                        vrow[:], vrow[:], sqr[:, 0:1], OP.subtract))
                    sc.wait("s", "v", v_v)
                    sc.inc("s", lambda e: e.activation(vrow[:], vrow[:],
                                                       AF.Ln, bias=epsc[:]))
                    s_r = sc.inc("s", lambda e: e.activation(
                        vrow[:], vrow[:], AF.Exp, scale=-0.5))
                    sc.wait("v", "s", s_r)
                    sc.inc("v", lambda e: e.tensor_scalar(
                        ytk[:], ytk[:], mrow[:], vrow[:], OP.subtract,
                        OP.mult))
                    sc.inc("v", lambda e: e.tensor_tensor(
                        ytk[:], ytk[:], gamb[:], OP.mult))
                    v_n = sc.inc("v", lambda e: e.tensor_tensor(
                        ytb[:], ytk[:], betb[:], OP.add))
                    sc.wait("t", "v", v_n)
                    sc.wait("t", "v", prev.get("trc"))
                    t_tr = None
                    for ccx in range(16):
                        t_tr = sc.inc("t", lambda e, _c=ccx: e.transpose(
                            ps_tr[:, _c * 128:(_c + 1) * 128],
                            ytb[:, _c * 128:(_c + 1) * 128], ident[:]))
                    sc.wait("v", "t", t_tr)
                    for ccx in range(16):
                        prev["trc"] = sc.inc(
                            "v", lambda e, _c=ccx, _g=g16: e.tensor_copy(
                                ynT[_c][:, _g * 128:(_g + 1) * 128],
                                ps_tr[:, _c * 128:(_c + 1) * 128]))
        sc.barrier()

        with contextlib.ExitStack() as ph:
            pb = lambda n, shp, dt: ph.enter_context(cx.SBUFR(n, shp, dt))
            wg_s = [pb(f"wg_s{i}", [128, 2048], BF16) for i in range(16)]
            sgb = pb("sgb", [128, TH], BF16)
            for i in range(16):
                lw = sc.dma("g", wg_s[i][:], wgT[i * 128:(i + 1) * 128, :])
            sc.wait("t", "gd", lw)
            with contextlib.ExitStack() as pp:
                psg = [pp.enter_context(cx.PSUM(f"psg{j}", [128, 512], F32))
                       for j in range(4)]
                last = {}
                for mo in range(16):
                    for nn in range(4):
                        j = nn
                        sc.wait("t", "y", last.get(j))
                        mm = mm_accum(
                            sc, psg[j][:],
                            [(wg_s[cc][:, mo * 128:(mo + 1) * 128],
                              ynT[cc][:, nn * 512:(nn + 1) * 512])
                             for cc in range(16)], )
                        sc.wait("s", "t", mm)
                        sl = slice(nn * 512, (nn + 1) * 512)
                        s_g = sc.inc("s", lambda e, _j=j, _mo=mo, _sl=sl:
                                     e.activation(sgb[:, _sl], psg[_j][:],
                                                  AF.Sigmoid,
                                                  bias=bgsb[:, _mo:_mo + 1]))
                        sc.wait("v", "s", s_g)
                        v_g = sc.inc("v", lambda e, _mo=mo, _sl=sl:
                                     e.tensor_tensor(sgb[:, _sl], sgb[:, _sl],
                                                     ynT[_mo][:, _sl],
                                                     OP.mult))
                        sc.wait("y", "v", v_g)
                        last[j] = sc.dma(
                            "y", gtd_scr.ap()[mo * 128:(mo + 1) * 128,
                                              nn * 512:(nn + 1) * 512],
                            sgb[:, sl])
        stYT.close()
        sc.barrier()

        with contextlib.ExitStack() as ph:
            pb = lambda n, shp, dt: ph.enter_context(cx.SBUFR(n, shp, dt))
            wo_s = [pb(f"wo_s{i}", [128, C], BF16) for i in range(32)]
            rb = [pb(f"rb{i}", [128, TH], BF16) for i in range(2)]
            sout = pb("sout", [128, TH], F32)
            for i in range(32):
                lw = sc.dma("g", wo_s[i][:],
                            wo3T[i * 128:(i + 1) * 128, :])
            sc.wait("t", "gd", lw)
            with contextlib.ExitStack() as pp:
                pso = [pp.enter_context(cx.PSUM(f"pso{j}", [128, 512], F32))
                       for j in range(4)]
                prev_ev = None
                prev_st = None
                for mo in range(8):
                    sc.wait("y", "v", prev_ev)
                    ldk = {}
                    ldk[0] = sc.dma("y", rb[0][:], skip_d[0:128, :])
                    for kc in range(32):
                        if kc + 1 < 32:
                            if kc >= 1:
                                sc.wait("y", "t", tk_done.get(kc - 1))
                            srcd = (skip_d if kc + 1 < 16 else gtd_scr.ap())
                            krow = ((kc + 1) * 128 if kc + 1 < 16 else
                                    (kc + 1 - 16) * 128)
                            ldk[kc + 1] = sc.dma(
                                "y", rb[(kc + 1) % 2][:],
                                srcd[krow:krow + 128, :])
                        sc.wait("t", "y", ldk[kc])
                        if kc == 0:
                            tk_done = {}
                        t_last = None
                        for nn in range(4):
                            sl = slice(nn * 512, (nn + 1) * 512)
                            fn = lambda e, _kc=kc, _mo=mo, _sl=sl, _n=nn: \
                                e.matmul(pso[_n][:],
                                         wo_s[_kc][:, _mo * 128:
                                                   (_mo + 1) * 128],
                                         rb[_kc % 2][:, _sl],
                                         start=(_kc == 0), stop=(_kc == 31))
                            if nn == 3:
                                t_last = sc.inc("t", fn)
                            else:
                                sc.raw("t", fn)
                        tk_done[kc] = t_last
                    sc.wait("v", "t", t_last)
                    sc.wait("v", "y", prev_st)
                    vv = None
                    for nn in range(4):
                        sl = slice(nn * 512, (nn + 1) * 512)
                        vv = sc.inc("v", lambda e, _n=nn, _sl=sl:
                                    e.tensor_copy(sout[:, _sl], pso[_n][:]))
                    prev_ev = vv
                    sc.wait("y", "v", vv)
                    prev_st = sc.dma("y", s3T[mo * 128:(mo + 1) * 128, :],
                                     sout[:])
        sc.wait("y", "y", sc.cnt["y"])
        sc.finalize(es)
    return cx.nc



def build_L3f():
    cx = Ctx()
    nc, sc = cx.nc, cx.sc
    x3T = nc.dram_tensor("x3T", [C, L], F32, kind="ExternalInput").ap()
    s3Th = nc.dram_tensor("s3Th", [C, TH], F32, kind="ExternalInput").ap()
    cnw1 = nc.dram_tensor("cnw1", [C, C], F32, kind="ExternalInput").ap()
    cnb1 = nc.dram_tensor("cnb1", [128, 8], F32, kind="ExternalInput").ap()
    cnw2 = nc.dram_tensor("cnw2", [C, 512], F32, kind="ExternalInput").ap()
    cnb2 = nc.dram_tensor("cnb2", [128, 4], F32, kind="ExternalInput").ap()
    cnw3 = nc.dram_tensor("cnw3", [512, 256], F32, kind="ExternalInput").ap()
    cnb3 = nc.dram_tensor("cnb3", [128, 2], F32, kind="ExternalInput").ap()
    cnw4 = nc.dram_tensor("cnw4", [256, 1], F32, kind="ExternalInput").ap()
    cnb4 = nc.dram_tensor("cnb4", [1, 1], F32, kind="ExternalInput").ap()
    gcw1 = nc.dram_tensor("gcw1", [C, 512], F32, kind="ExternalInput").ap()
    gcb1 = nc.dram_tensor("gcb1", [128, 4], F32, kind="ExternalInput").ap()
    gcw2 = nc.dram_tensor("gcw2", [512, 256], F32, kind="ExternalInput").ap()
    gcb2 = nc.dram_tensor("gcb2", [128, 2], F32, kind="ExternalInput").ap()
    gcw3 = nc.dram_tensor("gcw3", [256, 1], F32, kind="ExternalInput").ap()
    gcb3 = nc.dram_tensor("gcb3", [1, 1], F32, kind="ExternalInput").ap()
    ar_d = nc.dram_tensor("ar_d", [1, 1], F32, kind="ExternalInput").ap()
    x4T = nc.dram_tensor("x4T", [C, TH], F32, kind="ExternalOutput").ap()

    es = contextlib.ExitStack()
    with es:
        sbL = lambda n, shp, dt: es.enter_context(cx.SBUF(n, shp, dt))
        onesb_r = sbL("onesb_r", [1, 128], BF16)
        w1b = [sbL(f"w1b{i}", [128, C], BF16) for i in range(8)]
        w2b = [sbL(f"w2b{i}", [128, 512], BF16) for i in range(8)]
        w3b = [sbL(f"w3b{i}", [128, 256], BF16) for i in range(4)]
        w4b = [sbL(f"w4b{i}", [128, 1], BF16) for i in range(2)]
        b1c = sbL("b1c", [128, 8], F32)
        b2c = sbL("b2c", [128, 4], F32)
        b3c = sbL("b3c", [128, 2], F32)
        b4c = sbL("b4c", [1, 1], F32)
        arv = sbL("arv", [1, 1], F32)
        gfv = sbL("gfv", [1, 1], F32)
        gf7 = sbL("gf7", [1, 1], F32)
        gc8 = [sbL(f"gc8_{i}", [128, 1], F32) for i in range(8)]
        gc = sc.inc("g", lambda e: e.memset(onesb_r[:], 1.0))
        for i in range(8):
            sc.dma("g", w1b[i][:], cnw1[i * 128:(i + 1) * 128, :])
            sc.dma("g", w2b[i][:], cnw2[i * 128:(i + 1) * 128, :])
        for i in range(4):
            sc.dma("g", w3b[i][:], cnw3[i * 128:(i + 1) * 128, :])
        for i in range(2):
            sc.dma("g", w4b[i][:], cnw4[i * 128:(i + 1) * 128, :])
        sc.dma("y", b1c[:], cnb1)
        sc.dma("y", b2c[:], cnb2)
        sc.dma("y", b3c[:], cnb3)
        sc.dma("y", b4c[:], cnb4)
        sc.dma("y", arv[:], ar_d)

        stXN = contextlib.ExitStack()
        xn = emit_ln_cm(cx, stXN, x3T, L)
        sc.barrier()
        vv = None
        for i in range(8):
            sc.inc("v", lambda e, _i=i: e.reduce_sum(gc8[_i][:], xn[_i][:],
                                                     axis=AX))
            vv = sc.inc("v", lambda e, _i=i: e.tensor_scalar(
                gc8[_i][:], gc8[_i][:], 1.0 / L, None, OP.mult))
        stOU = contextlib.ExitStack()
        outA = [stOU.enter_context(cx.SBUFR(f"outA{i}", [128, TH], BF16))
                for i in range(8)]
        outB = [stOU.enter_context(cx.SBUFR(f"outB{i}", [128, TH], BF16))
                for i in range(8)]
        for i in range(8):
            vv = sc.inc("v", lambda e, _i=i: e.tensor_copy(
                outA[_i][:], xn[_i][:, 0:TH]))
        sc.barrier()
        stXN.close()

        with contextlib.ExitStack() as ph:
            pb = lambda n, shp, dt: ph.enter_context(cx.SBUFR(n, shp, dt))
            g1b = [pb(f"g1b{i}", [128, 512], BF16) for i in range(8)]
            g2b = [pb(f"g2b{i}", [128, 256], BF16) for i in range(4)]
            g3b = [pb(f"g3b{i}", [128, 1], BF16) for i in range(2)]
            gb1c = pb("gb1c", [128, 4], F32)
            gb2c = pb("gb2c", [128, 2], F32)
            gb3c = pb("gb3c", [1, 1], F32)
            hh1 = pb("hh1", [128, 4], BF16)
            hh2 = pb("hh2", [128, 2], BF16)
            gcc = pb("gcc", [128, 8], BF16)
            for i in range(8):
                sc.dma("g", g1b[i][:], gcw1[i * 128:(i + 1) * 128, :])
            for i in range(4):
                sc.dma("g", g2b[i][:], gcw2[i * 128:(i + 1) * 128, :])
            for i in range(2):
                lw = sc.dma("g", g3b[i][:], gcw3[i * 128:(i + 1) * 128, :])
            sc.dma("y", gb1c[:], gcb1)
            sc.dma("y", gb2c[:], gcb2)
            ldb = sc.dma("y", gb3c[:], gcb3)
            sc.wait("v", "gd", lw)
            sc.wait("v", "y", ldb)
            vv = None
            for i in range(8):
                vv = sc.inc("v", lambda e, _i=i: e.tensor_copy(
                    gcc[:, _i:_i + 1], gc8[_i][:]))
            sc.wait("t", "v", vv)
            with cx.PSUM("ps_g1", [128, 4], F32) as ps1, \
                    cx.PSUM("ps_g2", [128, 2], F32) as ps2, \
                    cx.PSUM("ps_g3", [1, 1], F32) as ps3:
                t1 = None
                for mo in range(4):
                    t1 = mm_accum(sc, ps1[:, mo:mo + 1],
                                  [(g1b[cc][:, mo * 128:(mo + 1) * 128],
                                    gcc[:, cc:cc + 1]) for cc in range(8)])
                sc.wait("s", "t", t1)
                s1 = None
                for mo in range(4):
                    s1 = sc.inc("s", lambda e, _m=mo: e.activation(
                        hh1[:, _m:_m + 1], ps1[:, _m:_m + 1], AF.Gelu,
                        bias=gb1c[:, _m:_m + 1]))
                sc.wait("t", "s", s1)
                t2 = None
                for mo in range(2):
                    t2 = mm_accum(sc, ps2[:, mo:mo + 1],
                                  [(g2b[cc][:, mo * 128:(mo + 1) * 128],
                                    hh1[:, cc:cc + 1]) for cc in range(4)],)
                sc.wait("s", "t", t2)
                s2 = None
                for mo in range(2):
                    s2 = sc.inc("s", lambda e, _m=mo: e.activation(
                        hh2[:, _m:_m + 1], ps2[:, _m:_m + 1], AF.Gelu,
                        bias=gb2c[:, _m:_m + 1]))
                sc.wait("t", "s", s2)
                t3 = mm_accum(sc, ps3[:],
                              [(g3b[cc][:], hh2[:, cc:cc + 1])
                               for cc in range(2)])
                sc.wait("s", "t", t3)
                s3 = sc.inc("s", lambda e: e.activation(
                    gfv[:], ps3[:], AF.Sigmoid, bias=gb3c[:]))
        sc.barrier()

        sigma = [4, 5, 6, 7, 0, 1, 2, 3]
        signs = [1, 1, -1, -1, -1, -1, 1, 1]
        cur, nxt = outA, outB
        with contextlib.ExitStack() as ph:
            pb = lambda n, shp, dt: ph.enter_context(cx.SBUFR(n, shp, dt))
            h1t = [pb(f"h1t{i}", [128, TH], BF16) for i in range(8)]
            h2t = [pb(f"h2t{i}", [128, TH], BF16) for i in range(4)]
            h3t = [pb(f"h3t{i}", [128, TH], BF16) for i in range(2)]
            tfr = pb("tfr", [1, TH], F32)
            cpr = pb("cpr", [1, TH], BF16)
            tmpc = pb("tmpc", [128, 512], F32)
            cbt = pb("cbt", [128, TH], F32)
            with contextlib.ExitStack() as pp:
                psc = [pp.enter_context(cx.PSUM(f"psc{j}", [128, 512],
                                                F32)) for j in range(4)]
                ps_cb = pp.enter_context(cx.PSUM("ps_cb", [128, 512], F32))
                lastc = {}
                for cyc in range(3):
                    sc.wait("t", "s", sc.cnt["s"])
                    sc.wait("t", "v", sc.cnt["v"])
                    for mo in range(8):
                        for nn in range(4):
                            j = nn
                            sc.wait("t", "s", lastc.get(j))
                            mm = mm_accum(
                                sc, psc[j][:],
                                [(w1b[cc][:, mo * 128:(mo + 1) * 128],
                                  cur[cc][:, nn * 512:(nn + 1) * 512])
                                 for cc in range(8)])
                            sc.wait("s", "t", mm)
                            sl = slice(nn * 512, (nn + 1) * 512)
                            lastc[j] = sc.inc(
                                "s", lambda e, _j=j, _m=mo, _sl=sl:
                                e.activation(h1t[_m][:, _sl], psc[_j][:],
                                             AF.Gelu,
                                             bias=b1c[:, _m:_m + 1]))
                    for mo in range(4):
                        for nn in range(4):
                            j = nn
                            sc.wait("t", "s", lastc.get(j))
                            mm = mm_accum(
                                sc, psc[j][:],
                                [(w2b[cc][:, mo * 128:(mo + 1) * 128],
                                  h1t[cc][:, nn * 512:(nn + 1) * 512])
                                 for cc in range(8)])
                            sc.wait("s", "t", mm)
                            sl = slice(nn * 512, (nn + 1) * 512)
                            lastc[j] = sc.inc(
                                "s", lambda e, _j=j, _m=mo, _sl=sl:
                                e.activation(h2t[_m][:, _sl], psc[_j][:],
                                             AF.Gelu,
                                             bias=b2c[:, _m:_m + 1]))
                    for mo in range(2):
                        for nn in range(4):
                            j = nn
                            sc.wait("t", "s", lastc.get(j))
                            mm = mm_accum(
                                sc, psc[j][:],
                                [(w3b[cc][:, mo * 128:(mo + 1) * 128],
                                  h2t[cc][:, nn * 512:(nn + 1) * 512])
                                 for cc in range(4)])
                            sc.wait("s", "t", mm)
                            sl = slice(nn * 512, (nn + 1) * 512)
                            lastc[j] = sc.inc(
                                "s", lambda e, _j=j, _m=mo, _sl=sl:
                                e.activation(h3t[_m][:, _sl], psc[_j][:],
                                             AF.Gelu,
                                             bias=b3c[:, _m:_m + 1]))
                    s_tf = None
                    for nn in range(4):
                        sl = slice(nn * 512, (nn + 1) * 512)
                        sc.wait("t", "s", sc.cnt["s"])
                        t4 = mm_accum(
                            sc, ps_cb[0:1, :],
                            [(w4b[cc][:], h3t[cc][:, sl])
                             for cc in range(2)])
                        sc.wait("s", "t", t4)
                        s_tf = sc.inc("s", lambda e, _sl=sl: e.activation(
                            tfr[:, _sl], ps_cb[0:1, :], AF.Sigmoid,
                            bias=b4c[:]))
                    sc.wait("v", "s", s_tf)
                    sc.inc("v", lambda e: e.tensor_single_scalar(
                        tfr[:], tfr[:], 0.3, OP.mult))
                    sc.inc("v", lambda e: e.tensor_scalar(
                        gf7[:], gfv[:], 0.7, None, OP.mult))
                    sc.inc("v", lambda e: e.tensor_scalar(
                        tfr[:], tfr[:], gf7[0:1, 0:1], -0.5, OP.add, OP.add))
                    sc.inc("v", lambda e: e.tensor_scalar(
                        tfr[:], tfr[:], 2.0, None, OP.mult))
                    v_cp = sc.inc("v", lambda e: e.tensor_scalar(
                        cpr[:], tfr[:], arv[0:1, 0:1], 0.1, OP.mult, OP.add))
                    sc.wait("t", "v", v_cp)
                    sc.wait("t", "s", sc.cnt["s"])
                    v_bc = None
                    for nn in range(4):
                        sl = slice(nn * 512, (nn + 1) * 512)
                        sc.wait("t", "v", v_bc)
                        t_cb = sc.inc("t", lambda e, _sl=sl: e.matmul(
                            ps_cb[:], onesb_r[:], cpr[0:1, _sl],
                            start=True, stop=True))
                        sc.wait("v", "t", t_cb)
                        v_bc = sc.inc("v", lambda e, _sl=sl: e.tensor_copy(
                            cbt[:, _sl], ps_cb[:]))
                    vv = None
                    for i in range(8):
                        si = sigma[i]
                        op2 = OP.add if signs[i] > 0 else OP.subtract
                        for nn in range(4):
                            sl = slice(nn * 512, (nn + 1) * 512)
                            sc.inc("v", lambda e, _s=cur[si], _sl=sl:
                                   e.tensor_tensor(tmpc[:],
                                                   _s[:, _sl],
                                                   cbt[:, _sl], OP.mult))
                            vv = sc.inc("v", lambda e, _n=nxt[i], _c=cur[i],
                                        _sl=sl, _o=op2:
                                        e.tensor_tensor(_n[:, _sl],
                                                        _c[:, _sl],
                                                        tmpc[:], _o))
                    cur, nxt = nxt, cur
        sc.barrier()

        with contextlib.ExitStack() as ph:
            pb = lambda n, shp, dt: ph.enter_context(cx.SBUFR(n, shp, dt))
            x3b = pb("x3b", [128, TH], F32)
            s3b = pb("s3b", [128, TH], F32)
            gsl = pb("gsl", [128, TH], BF16)
            tmpc2 = pb("tmpc2", [128, 512], F32)
            xob = pb("xob", [128, TH], F32)
            prev_st = None
            prev_v = None
            for ccx in range(8):
                sc.wait("y", "v", prev_v)
                ldx = sc.dma("y", x3b[:], x3T[ccx * 128:(ccx + 1) * 128,
                                              0:TH])
                lds = sc.dma("y", s3b[:], s3Th[ccx * 128:(ccx + 1) * 128, :])
                v_c = sc.cnt["v"]
                sc.wait("s", "v", v_c)
                sc.wait("s", "y", lds)
                s_g = sc.inc("s", lambda e, _c=ccx: e.activation(
                    gsl[:], cur[_c][:], AF.Silu))
                sc.wait("v", "s", s_g)
                sc.wait("v", "y", lds)
                vv = None
                for nn in range(4):
                    sl = slice(nn * 512, (nn + 1) * 512)
                    sc.inc("v", lambda e, _sl=sl: e.tensor_tensor(
                        tmpc2[:], s3b[:, _sl], gsl[:, _sl], OP.mult))
                    vv = sc.inc("v", lambda e, _sl=sl: e.tensor_tensor(
                        xob[:, _sl], tmpc2[:], x3b[:, _sl], OP.add))
                prev_v = vv
                sc.wait("y", "v", vv)
                prev_st = sc.dma("y", x4T[ccx * 128:(ccx + 1) * 128, :],
                                 xob[:])
                sc.wait("v", "y", prev_st)
        stOU.close()
        sc.wait("y", "y", sc.cnt["y"])
        sc.finalize(es)
    return cx.nc



def prep_L3_inputs(inputs, x3):
    W_in = f32c(inputs["L3_attn_W_in"])
    rows = np.concatenate([W_in[0:4096], W_in[6144:8232]], axis=0)
    wuT = np.zeros((C, 6272), np.float32)
    wuT[:, 0:6184] = rows.T
    cw = f32c(inputs["L3_attn_conv_w"]).reshape(2080, 4)
    cwp = np.zeros((2176, 4), np.float32)
    cwp[0:2080] = cw
    cw4 = np.ascontiguousarray(
        cwp.reshape(17, 128, 4).transpose(1, 0, 2).reshape(128, 68))
    cbp = np.zeros((2176,), np.float32)
    cbp[0:2080] = f32c(inputs["L3_attn_conv_b"])
    cb17 = np.ascontiguousarray(cbp.reshape(17, 128).T)
    dtb = np.zeros((128, 1), np.float32)
    dtb[32:40, 0] = f32c(inputs["L3_attn_dt_bias"])
    alog = f32c(inputs["L3_attn_A_log"]).reshape(8, 1)
    dpar = f32c(inputs["L3_attn_D_param"]).reshape(1, 8)
    idn = np.eye(128, dtype=np.float32)
    lng = f32c(inputs["L3_attn_ln_gamma"]).reshape(1, 2048)
    lnb = f32c(inputs["L3_attn_ln_beta"]).reshape(1, 2048)
    wgT = np.ascontiguousarray(f32c(inputs["L3_attn_W_gate"]).T)
    bg = np.ascontiguousarray(
        f32c(inputs["L3_attn_b_gate"]).reshape(16, 128).T)
    wo3T = np.ascontiguousarray(f32c(inputs["L3_attn_W_out"]).T)
    mapsG, mapsH = [], []
    for core in range(NC8):
        b, th = core // 2, core % 2
        xT = np.ascontiguousarray(x3[b].T)
        rot = np.concatenate([xT[:, th * TH:(th + 1) * TH],
                              xT[:, (1 - th) * TH:(2 - th) * TH]], axis=1)
        mapsG.append({
            "x3T": np.ascontiguousarray(rot),
            "qoff": np.array([[th * TH]], dtype=np.float32),
            "wuT": wuT, "cw4": cw4, "cb17": cb17, "dtb": dtb,
            "alog": alog, "dpar": dpar, "idn": idn,
        })
        mapsH.append({
            "qoff": np.array([[th * TH]], dtype=np.float32),
            "lng_d": lng, "lnb_d": lnb, "wgT": wgT, "bg_d": bg,
            "wo3T": wo3T, "idn": idn,
        })
    return mapsG, mapsH


def prep_L3f_inputs(inputs, x3, resH):
    def cb(v, nch):
        v = f32c(v)
        return np.ascontiguousarray(v.reshape(nch, 128).T) if nch > 0 else v
    maps = []
    for core in range(NC8):
        b, th = core // 2, core % 2
        xT = np.ascontiguousarray(x3[b].T)
        rot = np.concatenate([xT[:, th * TH:(th + 1) * TH],
                              xT[:, (1 - th) * TH:(2 - th) * TH]], axis=1)
        maps.append({
            "x3T": rot,
            "s3Th": np.asarray(resH[core]["s3T"]),
            "cnw1": f32c(inputs["L3_ffn_cn_w1"]),
            "cnb1": cb(inputs["L3_ffn_cn_b1"], 8),
            "cnw2": f32c(inputs["L3_ffn_cn_w2"]),
            "cnb2": cb(inputs["L3_ffn_cn_b2"], 4),
            "cnw3": f32c(inputs["L3_ffn_cn_w3"]),
            "cnb3": cb(inputs["L3_ffn_cn_b3"], 2),
            "cnw4": f32c(inputs["L3_ffn_cn_w4"]),
            "cnb4": f32c(inputs["L3_ffn_cn_b4"]).reshape(1, 1),
            "gcw1": f32c(inputs["L3_ffn_gc_w1"]),
            "gcb1": cb(inputs["L3_ffn_gc_b1"], 4),
            "gcw2": f32c(inputs["L3_ffn_gc_w2"]),
            "gcb2": cb(inputs["L3_ffn_gc_b2"], 2),
            "gcw3": f32c(inputs["L3_ffn_gc_w3"]),
            "gcb3": f32c(inputs["L3_ffn_gc_b3"]).reshape(1, 1),
            "ar_d": f32c(inputs["L3_ffn_adaptive_range"]).reshape(1, 1),
        })
    return maps


def run_L3(inputs, x3):
    for nm, bld in (("L3a", build_L3a), ("L3b", build_L3b),
                    ("L3f", build_L3f)):
        if nm not in _builder_cache:
            _builder_cache[nm] = bld()
    mapsG, mapsH = prep_L3_inputs(inputs, x3)
    resG = run_bass_kernel_spmd(_builder_cache["L3a"], mapsG,
                                core_ids=list(range(NC8))).results
    for core in range(NC8):
        b, th = core // 2, core % 2
        pair = 2 * b + (1 - th)
        mapsH[core]["y_d"] = np.asarray(resG[core]["y_scr"])
        mapsH[core]["ct_d"] = np.ascontiguousarray(
            np.asarray(resG[core]["bc_o"])[16:32])
        mapsH[core]["clg_d"] = np.asarray(resG[core]["clg_o"])
        mapsH[core]["hin_d"] = np.asarray(resG[pair]["hfin_o"])
        mapsH[core]["skip_d"] = np.asarray(resG[core]["skip_scr"])
    resH = run_bass_kernel_spmd(_builder_cache["L3b"], mapsH,
                                core_ids=list(range(NC8))).results
    mapsI = prep_L3f_inputs(inputs, x3, resH)
    resI = run_bass_kernel_spmd(_builder_cache["L3f"], mapsI,
                                core_ids=list(range(NC8))).results
    x4 = np.empty((B, L, C), dtype=np.float32)
    for core in range(NC8):
        b, th = core // 2, core % 2
        x4[b, th * TH:(th + 1) * TH, :] = resI[core]["x4T"].T
    return x4
